# version 05
# baseline (speedup 1.0000x reference)
import numpy as np

import concourse.bass as bass
import concourse.mybir as mybir
from concourse.bass_utils import run_bass_kernel_spmd

F32 = mybir.dt.float32
BF16 = mybir.dt.bfloat16
I32 = mybir.dt.int32
AF = mybir.ActivationFunctionType
ALU = mybir.AluOpType
AXX = mybir.AxisListType.X

NCORES = 8
EPS = 1e-5
AWQ_INV = [0, 2, 4, 6, 1, 3, 5, 7]


def make_cfg(HID=4096, NH=128, HD=64, NST=128, KC=4, I=11008, B=2, L=8192):
    G = 8
    QG = 128
    c = dict(HID=HID, NH=NH, HD=HD, NST=NST, KC=KC, I=I, B=B, L=L, G=G, QG=QG)
    c["DIN"] = NH * HD
    c["CONVD"] = c["DIN"] + 2 * G * NST
    c["HPC"] = NH // NCORES
    c["DPC"] = c["HPC"] * HD
    c["NDPC"] = c["DPC"] // 128
    c["TPC"] = B * L // NCORES
    c["NTOK"] = B * L
    c["NDT"] = HID // 128
    c["ABLK"] = 512
    c["NAB"] = c["NTOK"] // c["ABLK"]
    c["AB_PER_BATCH"] = c["NAB"] // B
    c["CSS"] = 128
    c["NSUB"] = c["ABLK"] // c["CSS"]
    c["CA1"] = c["DPC"] + 2 * NST + c["HPC"]
    c["NCONVT"] = (c["DPC"] + 2 * NST) // 128
    c["DSH"] = HID // NCORES
    c["NGT"] = c["DSH"] // 128
    c["NGRP"] = c["DSH"] // QG
    c["IPAD"] = -(-I // (128 * NCORES)) * (128 * NCORES)
    c["ISH"] = c["IPAD"] // NCORES
    c["NIT_SH"] = c["ISH"] // 128
    c["NI"] = I // 128
    c["NTT"] = c["TPC"] // 128
    assert NST == 128 and c["DPC"] % 128 == 0 and HID % 128 == 0
    assert L % c["ABLK"] == 0 and c["DSH"] % QG == 0
    return c


REAL_CFG = make_cfg()

_EK2NAME = {"v": "vector", "s": "scalar", "t": "tensor", "g": "gpsimd", "y": "sync"}


_QS_OF = {"y": "qy", "s": "qs", "g": "qg"}


class Prog:
    def __init__(self, nc, block, sems):
        self.nc = nc
        self.block = block
        self.sem = sems
        self.cnt = {k: 0 for k in sems}
        self.lastw = {e: {k: 0 for k in sems} for e in _EK2NAME}
        self.maxw = {k: 0 for k in sems}

    def sec(self, ek, fn):
        def wrapped(eng):
            q = _QS_OF.get(ek)
            if q is not None and self.maxw[q] > self.lastw[ek][q]:
                eng.wait_ge(self.sem[q], self.maxw[q])
                self.lastw[ek][q] = self.maxw[q]
            fn(eng)
        getattr(self.block, _EK2NAME[ek])(wrapped)

    def w(self, eng, ek, tokens):
        for tok in tokens:
            if tok is None:
                continue
            sk, val = tok
            if sk.startswith("q"):
                val = self.cnt[sk]
                if val > self.maxw[sk]:
                    self.maxw[sk] = val
            if self.lastw[ek][sk] < val:
                eng.wait_ge(self.sem[sk], val)
                self.lastw[ek][sk] = val

    def i(self, instr, sk):
        amount = 16 if sk.startswith("q") else 1
        self.cnt[sk] += amount
        instr.then_inc(self.sem[sk], amount)
        return (sk, self.cnt[sk])

    def icc(self, instr):
        self.cnt["cc"] += 1
        instr.then_inc(self.sem["cc"])
        return ("cc", self.cnt["cc"])

    def tok(self, sk):
        return (sk, self.cnt[sk])

    def bar(self):
        return [self.tok(k) for k in self.sem]


def sap(handle_or_ap, dims, offset=0):
    base = handle_or_ap if isinstance(handle_or_ap, bass.AP) else handle_or_ap[:]
    return bass.AP(base.tensor, base.offset + offset, [list(d) for d in dims])


def fexp(ap2d, ncols, reps, col0=0):
    return bass.AP(ap2d.tensor, ap2d.offset + col0,
                   [list(ap2d.ap[0]), [1, ncols], [0, reps]])


def frep(ap2d, ncols, reps, col0=0):
    return bass.AP(ap2d.tensor, ap2d.offset + col0,
                   [list(ap2d.ap[0]), [0, reps], [1, ncols]])


def build_program(cfg):
    C = cfg
    HID, I, Bsz = C["HID"], C["I"], C["B"]
    HD, HPC, DPC, NDPC = C["HD"], C["HPC"], C["DPC"], C["NDPC"]
    TPC, NTOK, NDT = C["TPC"], C["NTOK"], C["NDT"]
    ABLK, NAB, CSS, NSUB = C["ABLK"], C["NAB"], C["CSS"], C["NSUB"]
    CA1, NCONVT = C["CA1"], C["NCONVT"]
    DSH, NGT, NGRP, QG = C["DSH"], C["NGT"], C["NGRP"], C["QG"]
    IPAD, ISH, NIT_SH, NI = C["IPAD"], C["ISH"], C["NIT_SH"], C["NI"]
    NST, KC, DIN, NTT = C["NST"], C["KC"], C["DIN"], C["NTT"]
    IP8, HP8 = I // 8, HID // 8
    AH = ABLK + KC - 1
    NKD = DIN // 128

    nc = bass.Bass("TRN2", target_bir_lowering=False)

    ei = lambda n, sh, dt=F32: nc.dram_tensor(n, sh, dt, kind="ExternalInput").ap()
    x_in = ei("x", [TPC, HID])
    wa1_in = ei("wa1", [HID, CA1])
    wa2_in = ei("wa2", [HID, DPC])
    convw_in = ei("convw", [128, NCONVT * KC])
    convb_in = ei("convb", [128, NCONVT])
    dtb_in = ei("dtb", [HPC, 1])
    alog_in = ei("alog", [HPC, 1])
    dvec_in = ei("dvec", [HPC, 1])
    normw_in = ei("normw", [128, NDPC])
    woutt_in = ei("woutt", [DPC, HID])
    gq_in = ei("gq", [DSH, IP8], I32)
    gs_in = ei("gs", [NGRP, I])
    gz_in = ei("gz", [NGRP, IP8], I32)
    uq_in = ei("uq", [DSH, IP8], I32)
    us_in = ei("us", [NGRP, I])
    uz_in = ei("uz", [NGRP, IP8], I32)
    dq_in = ei("dq", [ISH, HP8], I32)
    ds_in = ei("ds", [NIT_SH, HID])
    dz_in = ei("dz", [NIT_SH, HP8], I32)
    out_t = nc.dram_tensor("out", [TPC, HID], F32, kind="ExternalOutput").ap()

    di = lambda n, sh, dt: nc.dram_tensor(n, sh, dt).ap()
    xln_tok_d = di("xln_tok", [TPC, HID], BF16)
    xlnT_sl_d = di("xlnT_sl", [HID, TPC], BF16)
    xlnT_all_d = di("xlnT_all", [NCORES * HID, TPC], BF16)
    gsl_d = di("gsl", [DSH, I], BF16)
    g_all_d = di("g_all", [HID, I], BF16)
    usl_d = di("usl", [DSH, I], BF16)
    u_all_d = di("u_all", [HID, I], BF16)
    wdsl_d = di("wdsl", [ISH, HID], BF16)
    wd_all_d = di("wd_all", [IPAD, HID], BF16)
    woT_sl_d = di("woT_sl", [DPC, HID], BF16)
    woT_all_d = di("woT_all", [DIN, HID], BF16)
    zsg_d = di("zsg", [NGRP, I], F32)
    zsu_d = di("zsu", [NGRP, I], F32)
    zsd_d = di("zsd", [NIT_SH, HID], F32)
    dbounce_d = di("dbounce", [1, HPC], F32)
    acb_d = di("acb", [NAB * NSUB * HPC, CSS], F32)
    prow_d = di("prow", [NAB * NSUB, HPC], F32)
    y_raw_d = di("y_raw", [NTOK, DPC], BF16)
    y_tok_d = di("y_tok", [NTOK, DPC], BF16)
    yT_sl_d = di("yT_sl", [DPC, NTOK], BF16)
    y_all_d = di("y_all", [DIN, NTOK], BF16)
    x_new_d = di("x_new", [TPC, HID], F32)
    xln2_tok_d = di("xln2_tok", [TPC, HID], BF16)
    h_d = di("h", [I, TPC], BF16)

    with (
        nc.semaphore("sv") as _sv,
        nc.semaphore("ss") as _ss,
        nc.semaphore("sp") as _sp,
        nc.semaphore("sg") as _sg,
        nc.semaphore("qy") as _qy,
        nc.semaphore("qs") as _qs,
        nc.semaphore("qg") as _qg,
        nc.semaphore("cc") as _cc,
        nc.Block() as block,
    ):
        P = Prog(nc, block, dict(sv=_sv, ss=_ss, sp=_sp, sg=_sg,
                                 qy=_qy, qs=_qs, qg=_qg, cc=_cc))

        pers = nc.auto_arena()
        ident_bf = pers.sbuf_tensor("ident_bf", [128, 128], BF16)
        ident_f = pers.sbuf_tensor("ident_f", [128, 128], F32)
        ut_sb = pers.sbuf_tensor("ut_sb", [128, CSS], F32)
        dexp_sb = pers.sbuf_tensor("dexp_sb", [128, DPC], F32)
        convw_sb = pers.sbuf_tensor("convw_sb", [128, NCONVT * KC], F32)
        convb_sb = pers.sbuf_tensor("convb_sb", [128, NCONVT], F32)
        dtb_sb = pers.sbuf_tensor("dtb_sb", [HPC, 1], F32)
        acol_sb = pers.sbuf_tensor("acol_sb", [HPC, 1], F32)
        dvec_sb = pers.sbuf_tensor("dvec_sb", [HPC, 1], F32)
        ones_sb = pers.sbuf_tensor("ones_sb", [1, 128], F32)
        eps_sb = pers.sbuf_tensor("eps_sb", [128, 1], F32)
        drow_sb = pers.sbuf_tensor("drow_sb", [1, HPC], F32)
        pers.allocate()

        def y_setup(y):
            P.i(y.dma_start(out=convw_sb[:], in_=convw_in), "qy")
            P.i(y.dma_start(out=convb_sb[:], in_=convb_in), "qy")
            for dst, src in ((dtb_sb, dtb_in), (acol_sb, alog_in), (dvec_sb, dvec_in)):
                P.i(y.dma_start(out=dst[:], in_=src), "qy")
        P.sec("y", y_setup)
        ld_small = P.tok("qy")

        def v_setup(v):
            P.w(v, "v", [ld_small])
            P.i(v.memset(ident_bf[:], 1.0), "sv")
            P.i(v.memset(ident_f[:], 1.0), "sv")
            P.i(v.memset(ut_sb[:], 0.0), "sv")
            P.i(v.memset(ones_sb[:], 1.0), "sv")
            P.i(v.memset(eps_sb[:], EPS), "sv")
        P.sec("v", v_setup)
        ident_tok0 = P.tok("sv")

        def g_setup(g):
            P.w(g, "g", [ident_tok0])
            P.i(g.affine_select(out=ut_sb[:], in_=ut_sb[:], pattern=[[1, CSS]],
                                compare_op=ALU.is_ge, fill=-1e9, base=0,
                                channel_multiplier=-1), "sg")
            P.i(g.affine_select(out=ident_bf[:], in_=ident_bf[:],
                                pattern=[[1, 128]], compare_op=ALU.is_equal,
                                fill=0.0, base=0, channel_multiplier=-1), "sg")
            P.i(g.affine_select(out=ident_f[:], in_=ident_f[:],
                                pattern=[[1, 128]], compare_op=ALU.is_equal,
                                fill=0.0, base=0, channel_multiplier=-1), "sg")
        P.sec("g", g_setup)
        ut_tok = P.tok("sg")
        ident_tok = P.tok("sg")

        def s_setup(s):
            P.w(s, "s", [ld_small])
            P.i(s.activation(out=acol_sb[:], in_=acol_sb[:], func=AF.Exp), "ss")
        P.sec("s", s_setup)

        def v_setup2(v):
            P.w(v, "v", [P.tok("ss")])
            P.i(v.tensor_scalar_mul(out=acol_sb[:], in0=acol_sb[:], scalar1=-1.0), "sv")
        P.sec("v", v_setup2)
        a_tok = P.tok("sv")

        def y_setup3(y):
            P.w(y, "y", [ld_small])
            P.i(y.dma_start(out=sap(dbounce_d, [[1, HPC], [0, 1]]), in_=dvec_sb[:]),
                "qy")
            P.w(y, "y", [P.tok("qy")])
            P.i(y.dma_start(out=drow_sb[:], in_=dbounce_d), "qy")
        P.sec("y", y_setup3)
        drow_tok = P.tok("qy")

        with nc.psum_tensor("PDX", [128, DPC], F32) as PDX:
            def t_setup(t):
                P.w(t, "t", [drow_tok, ident_tok])
                step = min(512, DPC)
                rhs_full = bass.AP(drow_sb[:].tensor, drow_sb[:].offset,
                                   [[HPC, 1], [1, HPC], [0, HD]])
                for i0 in range(0, DPC, step):
                    rhs = bass.AP(rhs_full.tensor,
                                  rhs_full.offset + i0 // HD,
                                  [[HPC, 1], [1, step // HD], [0, HD]])
                    mm = t.matmul(PDX[:, i0:i0 + step], ones_sb[:], rhs,
                                  start=True, stop=True)
                P.i(mm, "sp")
            P.sec("t", t_setup)
            dx_mm = P.tok("sp")

            def v_setup3(v):
                P.w(v, "v", [dx_mm])
                P.i(v.tensor_copy(out=dexp_sb[:], in_=PDX[:]), "sv")
            P.sec("v", v_setup3)
        dexp_tok = P.tok("sv")

        ph0a = nc.auto_arena()
        p0x = ph0a.sbuf_tensor("p0x", [128, HID], F32)
        SQCH = min(1024, HID)
        p0sq = ph0a.sbuf_tensor("p0sq", [128, SQCH], F32)
        p0s1 = ph0a.sbuf_tensor("p0s1", [128, 1], F32)
        p0s2 = ph0a.sbuf_tensor("p0s2", [128, 1], F32)
        p0pt = ph0a.sbuf_tensor("p0pt", [128, 1], F32)
        p0m = ph0a.sbuf_tensor("p0m", [128, 1], F32)
        p0v = ph0a.sbuf_tensor("p0v", [128, 1], F32)
        p0r = ph0a.sbuf_tensor("p0r", [128, 1], F32)
        p0o = ph0a.sbuf_tensor("p0o", [128, HID], BF16)
        p0t = ph0a.sbuf_tensor("p0t", [128, TPC], BF16)
        ph0a.allocate()

        def ln_pass(src_dram, dst_dram, rows, x_sb, sq_sb, s1_sb, s2_sb, pt_sb,
                    m_sb, v_sb, r_sb, o_sb, pre_bar):
            written = None
            nsq = HID // SQCH
            for it in range(rows // 128):
                r0 = it * 128

                def y_fn(y):
                    P.w(y, "y", pre_bar + [P.tok("sv"), P.tok("qs")])
                    P.i(y.dma_start(out=x_sb[:], in_=src_dram[r0:r0 + 128, :]), "qy")
                P.sec("y", y_fn)
                xt = P.tok("qy")

                def v_fn(v):
                    P.w(v, "v", pre_bar + [xt])
                    P.i(v.tensor_reduce(out=s1_sb[:], in_=x_sb[:], axis=AXX,
                                        op=ALU.add), "sv")
                    for ic in range(nsq):
                        P.w(v, "v", [P.tok("sv")])
                        P.i(v.tensor_tensor(out=sq_sb[:],
                                            in0=x_sb[:, ic * SQCH:(ic + 1) * SQCH],
                                            in1=x_sb[:, ic * SQCH:(ic + 1) * SQCH],
                                            op=ALU.mult), "sv")
                        P.w(v, "v", [P.tok("sv")])
                        P.i(v.tensor_reduce(out=pt_sb[:], in_=sq_sb[:], axis=AXX,
                                            op=ALU.add), "sv")
                        P.w(v, "v", [P.tok("sv")])
                        if ic == 0:
                            P.i(v.tensor_copy(out=s2_sb[:], in_=pt_sb[:]), "sv")
                        else:
                            P.i(v.tensor_tensor(out=s2_sb[:], in0=s2_sb[:],
                                                in1=pt_sb[:], op=ALU.add), "sv")
                    P.w(v, "v", [P.tok("sv")])
                    P.i(v.tensor_scalar_mul(out=m_sb[:], in0=s1_sb[:],
                                            scalar1=1.0 / HID), "sv")
                    P.w(v, "v", [P.tok("sv")])
                    P.i(v.tensor_tensor(out=v_sb[:], in0=m_sb[:], in1=m_sb[:],
                                        op=ALU.mult), "sv")
                    P.w(v, "v", [P.tok("sv")])
                    P.i(v.scalar_tensor_tensor(out=v_sb[:], in0=s2_sb[:],
                                               scalar=1.0 / HID, in1=v_sb[:],
                                               op0=ALU.mult, op1=ALU.subtract), "sv")
                P.sec("v", v_fn)
                var_tok = P.tok("sv")

                def s_fn(s):
                    P.w(s, "s", [var_tok])
                    P.i(s.activation(out=r_sb[:], in_=v_sb[:], func=AF.Ln,
                                     bias=eps_sb[:]), "ss")
                P.sec("s", s_fn)
                ln_tok = P.tok("ss")

                def vr_fn(v):
                    P.w(v, "v", [ln_tok])
                    P.i(v.tensor_scalar_mul(out=r_sb[:], in0=r_sb[:],
                                            scalar1=-0.5), "sv")
                P.sec("v", vr_fn)
                neg_tok = P.tok("sv")

                def s1b_fn(s):
                    P.w(s, "s", [neg_tok])
                    P.i(s.activation(out=r_sb[:], in_=r_sb[:], func=AF.Exp), "ss")
                P.sec("s", s1b_fn)
                rs_tok = P.tok("ss")

                def v2_fn(v):
                    P.w(v, "v", [rs_tok])
                    P.i(v.tensor_scalar(out=o_sb[:], in0=x_sb[:], scalar1=m_sb[:],
                                        scalar2=r_sb[:], op0=ALU.subtract,
                                        op1=ALU.mult), "sv")
                P.sec("v", v2_fn)
                o_tok = P.tok("sv")

                def s2_fn(s):
                    P.w(s, "s", [o_tok])
                    P.i(s.dma_start(out=dst_dram[r0:r0 + 128, :], in_=o_sb[:]), "qs")
                P.sec("s", s2_fn)
                written = P.tok("qs")
            return written

        xln_w = ln_pass(x_in, xln_tok_d, TPC, p0x, p0sq, p0s1, p0s2, p0pt,
                        p0m, p0v, p0r, p0o, [])

        def y_xlt(y):
            P.w(y, "y", [xln_w])
            for kt in range(NDT):
                P.w(y, "y", [P.tok("qy")])
                P.i(y.dma_start_transpose(out=p0t[:],
                                          in_=xln_tok_d[:, kt * 128:(kt + 1) * 128]),
                    "qy")
                P.w(y, "y", [P.tok("qy")])
                P.i(y.dma_start(out=xlnT_sl_d[kt * 128:(kt + 1) * 128, :],
                                in_=p0t[:]), "qy")
        P.sec("y", y_xlt)
        xlt_w = P.tok("qy")

        def g_ag1(g):
            P.w(g, "g", [xlt_w])
            P.icc(g.collective_compute(
                "AllGather", ALU.bypass, replica_groups=[list(range(NCORES))],
                ins=[xlnT_sl_d.opt()], outs=[xlnT_all_d.opt()]))
        P.sec("g", g_ag1)
        ag_xln = P.tok("cc")
        ph0a.free()

        ICH = I // 2 if I > 2048 else I
        NICH = I // ICH
        ph0b = nc.auto_arena()
        dqq = ph0b.sbuf_tensor("dqq", [128, ICH // 8], I32)
        dqi = ph0b.sbuf_tensor("dqi", [128, ICH], I32)
        dqn = ph0b.sbuf_tensor("dqn", [128, ICH], F32)
        dqs = ph0b.sbuf_tensor("dqs", [128, ICH], F32)
        dqz = ph0b.sbuf_tensor("dqz", [128, ICH], F32)
        dqw = ph0b.sbuf_tensor("dqw", [128, ICH], BF16)
        zqq = ph0b.sbuf_tensor("zqq", [NGRP, IP8], I32)
        zqi = ph0b.sbuf_tensor("zqi", [NGRP, ICH], I32)
        zqs = ph0b.sbuf_tensor("zqs", [NGRP, ICH], F32)
        zsc = ph0b.sbuf_tensor("zsc", [NGRP, ICH], F32)
        ph0b.allocate()
        bar0b = P.bar()

        def dequant_gu(qin, sin, zin, zs_dram, out_dram):
            def y_fn(y):
                P.w(y, "y", bar0b + [P.tok("sv")])
                P.i(y.dma_start(out=zqq[:], in_=zin), "qy")
            P.sec("y", y_fn)
            zl = P.tok("qy")

            for ic in range(NICH):
                c0 = ic * ICH

                def yz_fn(y):
                    P.w(y, "y", [P.tok("sv")])
                    P.i(y.dma_start(out=zsc[:], in_=sin[:, c0:c0 + ICH]), "qy")
                P.sec("y", yz_fn)
                zscl = P.tok("qy")

                def v_fn(v):
                    P.w(v, "v", bar0b + [zl, zscl, P.tok("qy")])
                    for sft in range(8):
                        P.i(v.tensor_scalar(out=zqi[:, AWQ_INV[sft]::8],
                                            in0=zqq[:, c0 // 8:(c0 + ICH) // 8],
                                            scalar1=4 * sft, scalar2=15,
                                            op0=ALU.logical_shift_right,
                                            op1=ALU.bitwise_and), "sv")
                    P.w(v, "v", [P.tok("sv")])
                    P.i(v.tensor_copy(out=zqs[:], in_=zqi[:]), "sv")
                    P.w(v, "v", [P.tok("sv")])
                    P.i(v.tensor_tensor(out=zqs[:], in0=zqs[:], in1=zsc[:],
                                        op=ALU.mult), "sv")
                P.sec("v", v_fn)
                zs_tok = P.tok("sv")

                def y2_fn(y):
                    P.w(y, "y", [zs_tok])
                    P.i(y.dma_start(out=zs_dram[:, c0:c0 + ICH], in_=zqs[:]), "qy")
                P.sec("y", y2_fn)
            zs_w = P.tok("qy")

            for gt in range(NGT):
                for ic in range(NICH):
                    c0 = ic * ICH

                    def y3_fn(y):
                        P.w(y, "y", [zs_w, P.tok("sv")])
                        P.i(y.dma_start(
                            out=dqq[:],
                            in_=qin[gt * 128:(gt + 1) * 128,
                                    c0 // 8:(c0 + ICH) // 8]), "qy")
                        P.i(y.dma_start(out=dqs[:],
                                        in_=sap(sin, [[0, 128], [1, ICH]],
                                                gt * I + c0)), "qy")
                        P.i(y.dma_start(out=dqz[:],
                                        in_=sap(zs_dram, [[0, 128], [1, ICH]],
                                                gt * I + c0)), "qy")
                    P.sec("y", y3_fn)
                    ql = P.tok("qy")

                    def v3_fn(v):
                        P.w(v, "v", [ql, P.tok("qs")])
                        for sft in range(8):
                            P.i(v.tensor_scalar(out=dqi[:, AWQ_INV[sft]::8],
                                                in0=dqq[:],
                                                scalar1=4 * sft, scalar2=15,
                                                op0=ALU.logical_shift_right,
                                                op1=ALU.bitwise_and), "sv")
                        P.w(v, "v", [P.tok("sv")])
                        P.i(v.tensor_copy(out=dqn[:], in_=dqi[:]), "sv")
                        P.w(v, "v", [P.tok("sv")])
                        P.i(v.tensor_tensor(out=dqn[:], in0=dqn[:], in1=dqs[:],
                                            op=ALU.mult), "sv")
                        P.w(v, "v", [P.tok("sv")])
                        P.i(v.tensor_tensor(out=dqw[:], in0=dqn[:], in1=dqz[:],
                                            op=ALU.subtract), "sv")
                    P.sec("v", v3_fn)
                    wt = P.tok("sv")

                    def s3_fn(s):
                        P.w(s, "s", [wt])
                        P.i(s.dma_start(
                            out=out_dram[gt * 128:(gt + 1) * 128, c0:c0 + ICH],
                            in_=dqw[:]), "qs")
                    P.sec("s", s3_fn)

        dequant_gu(gq_in, gs_in, gz_in, zsg_d, gsl_d)
        dequant_gu(uq_in, us_in, uz_in, zsu_d, usl_d)
        ph0b.free()

        ph0c = nc.auto_arena()
        dnq = ph0c.sbuf_tensor("dnq", [128, HP8], I32)
        dni = ph0c.sbuf_tensor("dni", [128, HID], I32)
        dnn = ph0c.sbuf_tensor("dnn", [128, HID], F32)
        dns = ph0c.sbuf_tensor("dns", [128, HID], F32)
        dnz = ph0c.sbuf_tensor("dnz", [128, HID], F32)
        dnw = ph0c.sbuf_tensor("dnw", [128, HID], BF16)
        znq = ph0c.sbuf_tensor("znq", [NIT_SH, HP8], I32)
        zni = ph0c.sbuf_tensor("zni", [NIT_SH, HID], I32)
        zns = ph0c.sbuf_tensor("zns", [NIT_SH, HID], F32)
        znc = ph0c.sbuf_tensor("znc", [NIT_SH, HID], F32)
        ph0c.allocate()
        bar0c = P.bar()

        def y_dn(y):
            P.w(y, "y", bar0c + [P.tok("sv")])
            P.i(y.dma_start(out=znq[:], in_=dz_in), "qy")
            P.i(y.dma_start(out=znc[:], in_=ds_in), "qy")
        P.sec("y", y_dn)
        znl = P.tok("qy")

        def v_dn(v):
            P.w(v, "v", bar0c + [znl])
            for sft in range(8):
                P.i(v.tensor_scalar(out=zni[:, AWQ_INV[sft]::8], in0=znq[:],
                                    scalar1=4 * sft, scalar2=15,
                                    op0=ALU.logical_shift_right,
                                    op1=ALU.bitwise_and), "sv")
            P.w(v, "v", [P.tok("sv")])
            P.i(v.tensor_copy(out=zns[:], in_=zni[:]), "sv")
            P.w(v, "v", [P.tok("sv")])
            P.i(v.tensor_tensor(out=zns[:], in0=zns[:], in1=znc[:], op=ALU.mult),
                "sv")
        P.sec("v", v_dn)
        zns_tok = P.tok("sv")

        def y_dn2(y):
            P.w(y, "y", [zns_tok])
            P.i(y.dma_start(out=zsd_d, in_=zns[:]), "qy")
        P.sec("y", y_dn2)
        zsd_w = P.tok("qy")

        for itl in range(NIT_SH):
            def y_dn3(y):
                P.w(y, "y", [zsd_w, P.tok("sv")])
                P.i(y.dma_start(out=dnq[:], in_=dq_in[itl * 128:(itl + 1) * 128, :]),
                    "qy")
                P.i(y.dma_start(out=dns[:],
                                in_=sap(ds_in, [[0, 128], [1, HID]], itl * HID)),
                    "qy")
                P.i(y.dma_start(out=dnz[:],
                                in_=sap(zsd_d, [[0, 128], [1, HID]], itl * HID)),
                    "qy")
            P.sec("y", y_dn3)
            ql = P.tok("qy")

            def v_dn3(v):
                P.w(v, "v", [ql, P.tok("qs")])
                for sft in range(8):
                    P.i(v.tensor_scalar(out=dni[:, AWQ_INV[sft]::8], in0=dnq[:],
                                        scalar1=4 * sft, scalar2=15,
                                        op0=ALU.logical_shift_right,
                                        op1=ALU.bitwise_and), "sv")
                P.w(v, "v", [P.tok("sv")])
                P.i(v.tensor_copy(out=dnn[:], in_=dni[:]), "sv")
                P.w(v, "v", [P.tok("sv")])
                P.i(v.tensor_tensor(out=dnn[:], in0=dnn[:], in1=dns[:],
                                    op=ALU.mult), "sv")
                P.w(v, "v", [P.tok("sv")])
                P.i(v.tensor_tensor(out=dnw[:], in0=dnn[:], in1=dnz[:],
                                    op=ALU.subtract), "sv")
            P.sec("v", v_dn3)
            wt = P.tok("sv")

            def s_dn3(s):
                P.w(s, "s", [wt])
                P.i(s.dma_start(out=wdsl_d[itl * 128:(itl + 1) * 128, :],
                                in_=dnw[:]), "qs")
            P.sec("s", s_dn3)
        ph0c.free()

        ph0d = nc.auto_arena()
        wof = ph0d.sbuf_tensor("wof", [128, HID], F32)
        wob = ph0d.sbuf_tensor("wob", [128, HID], BF16)
        wnw = ph0d.sbuf_tensor("wnw", [128, NDPC], F32)
        ph0d.allocate()
        bar0d = P.bar()

        def y_nw(y):
            P.w(y, "y", bar0d)
            P.i(y.dma_start(out=wnw[:], in_=normw_in), "qy")
        P.sec("y", y_nw)
        nw_tok = P.tok("qy")

        for dt_ in range(NDPC):
            def y_wo(y):
                P.w(y, "y", [nw_tok, P.tok("sv")])
                P.i(y.dma_start(out=wof[:],
                                in_=woutt_in[dt_ * 128:(dt_ + 1) * 128, :]), "qy")
            P.sec("y", y_wo)
            wl = P.tok("qy")

            def v_wo(v):
                P.w(v, "v", bar0d + [wl, P.tok("qs")])
                P.i(v.tensor_scalar(out=wob[:], in0=wof[:],
                                    scalar1=wnw[:, dt_:dt_ + 1],
                                    scalar2=None, op0=ALU.mult), "sv")
            P.sec("v", v_wo)
            wc = P.tok("sv")

            def s_wo(s):
                P.w(s, "s", [wc])
                P.i(s.dma_start(out=woT_sl_d[dt_ * 128:(dt_ + 1) * 128, :],
                                in_=wob[:]), "qs")
            P.sec("s", s_wo)
        wo_w = P.tok("qs")

        def g_agw(g):
            P.w(g, "g", [wo_w, P.tok("qs"), P.tok("qy")])
            P.icc(g.collective_compute(
                "AllGather", ALU.bypass, replica_groups=[list(range(NCORES))],
                ins=[gsl_d.opt()], outs=[g_all_d.opt()]))
            P.icc(g.collective_compute(
                "AllGather", ALU.bypass, replica_groups=[list(range(NCORES))],
                ins=[usl_d.opt()], outs=[u_all_d.opt()]))
            P.icc(g.collective_compute(
                "AllGather", ALU.bypass, replica_groups=[list(range(NCORES))],
                ins=[wdsl_d.opt()], outs=[wd_all_d.opt()]))
            P.icc(g.collective_compute(
                "AllGather", ALU.bypass, replica_groups=[list(range(NCORES))],
                ins=[woT_sl_d.opt()], outs=[woT_all_d.opt()]))
        P.sec("g", g_agw)
        ag_w = P.tok("cc")
        ph0d.free()

        a1 = nc.auto_arena()
        wa1_sb = a1.sbuf_tensor("wa1_sb", [128, NDT * CA1], BF16)
        wstg = a1.sbuf_tensor("wstg", [128, CA1], F32)
        xfm_sb = a1.sbuf_tensor("xfm_sb", [128, NDT * ABLK], BF16)
        xbc_sb = a1.sbuf_tensor("xbc_sb", [128, NCONVT * AH], BF16)
        xba_sb = a1.sbuf_tensor("xba_sb", [128, NCONVT * ABLK], BF16)
        cacc_sb = a1.sbuf_tensor("cacc_sb", [128, 2 * ABLK], F32)
        dtp_sb = a1.sbuf_tensor("dtp_sb", [HPC, ABLK], F32)
        dts_sb = a1.sbuf_tensor("dts_sb", [HPC, ABLK], F32)
        dta_sb = a1.sbuf_tensor("dta_sb", [HPC, ABLK], F32)
        stk_sb = a1.sbuf_tensor("stk_sb", [128, ABLK], F32)
        dtf_sb = a1.sbuf_tensor("dtf_sb", [HPC, ABLK], F32)
        actok_sb = a1.sbuf_tensor("actok_sb", [128, 128], F32)
        srep_sb = a1.sbuf_tensor("srep_sb", [128, HPC * CSS], F32)
        prep_sb = a1.sbuf_tensor("prep_sb", [128, HPC], F32)
        acm_sb = srep_sb
        lall_sb = a1.sbuf_tensor("lall_sb", [128, HPC * CSS], BF16)
        gm_sb = a1.sbuf_tensor("gm_sb", [128, CSS], BF16)
        gml_sb = a1.sbuf_tensor("gml_sb", [128, HPC * CSS], BF16)
        xtok_sb = a1.sbuf_tensor("xtok_sb", [128, DPC], BF16)
        btok_sb = a1.sbuf_tensor("btok_sb", [128, NST], BF16)
        xs_sb = a1.sbuf_tensor("xs_sb", [128, DPC], BF16)
        xsd_sb = a1.sbuf_tensor("xsd_sb", [128, DPC], BF16)
        z_sb = a1.sbuf_tensor("z_sb", [128, DPC], F32)
        zbf_sb = a1.sbuf_tensor("zbf_sb", [128, DPC], BF16)
        zt_sb = a1.sbuf_tensor("zt_sb", [128, DPC], F32)
        yo_sb = a1.sbuf_tensor("yo_sb", [128, DPC], F32)
        dtm_sb = a1.sbuf_tensor("dtm_sb", [128, DPC], F32)
        yc_sb = yo_sb
        yout_sb = a1.sbuf_tensor("yout_sb", [128, DPC], BF16)
        a1.allocate()

        bar0 = P.bar()
        with (
            nc.psum_tensor("PA0", [128, 512], F32) as PA0,
            nc.psum_tensor("PA1", [128, 512], F32) as PA1,
            nc.psum_tensor("PTRB", [128, DPC], BF16) as PTRB,
            nc.psum_tensor("PYD", [128, DPC], F32) as PYD,
            nc.psum_tensor("PSC", [128, DPC], F32) as PSC,
        ):
            def v_stk0(v):
                P.w(v, "v", bar0)
                P.i(v.memset(stk_sb[:], 0.0), "sv")
            P.sec("v", v_stk0)

            wconv_tok = None
            for kt in range(NDT):
                def y_w(y):
                    P.w(y, "y", bar0 + [wconv_tok])
                    P.i(y.dma_start(out=wstg[:],
                                    in_=wa1_in[kt * 128:(kt + 1) * 128, :]), "qy")
                P.sec("y", y_w)
                wl = P.tok("qy")

                def v_w(v):
                    P.w(v, "v", bar0 + [wl])
                    P.i(v.tensor_copy(out=wa1_sb[:, kt * CA1:(kt + 1) * CA1],
                                      in_=wstg[:]), "sv")
                P.sec("v", v_w)
                wconv_tok = P.tok("sv")
            wa1_tok = P.tok("sv")

            ftiles = [(d * 128, 128, "x", d) for d in range(NDPC)]
            ftiles += [(DPC, NST, "B", NDPC), (DPC + NST, NST, "C", NDPC + 1),
                       (DPC + 2 * NST, HPC, "dt", -1)]

            st = dict(xfm=None, xfm_pe=None,
                      pa_mm=[None, None], pa_ev=[None, None],
                      z=None, zuser=None, halo=None, conv=None,
                      xbc_user=None, dtuser=None, stk_pe=None, xba_pe=None,
                      yd_ev=None, sc_ev=None)
            yraw_w = None

            for b in range(NAB):
                T0 = b * ABLK
                cb_src = T0 // TPC
                tloc = T0 - cb_src * TPC
                batch_start = (b % C["AB_PER_BATCH"]) == 0

                def y_xfm(y):
                    P.w(y, "y", [ag_xln, st["xfm_pe"]])
                    src = sap(xlnT_all_d, [[TPC, 128], [128 * TPC, NDT], [1, ABLK]],
                              cb_src * HID * TPC + tloc)
                    P.i(y.dma_start(
                        out=xfm_sb[:].rearrange("p (k t) -> p k t", k=NDT),
                        in_=src), "qy")
                P.sec("y", y_xfm)
                st["xfm"] = P.tok("qy")

                if batch_start:
                    def v_rst(v):
                        P.w(v, "v", [st["zuser"], st["z"], st["xbc_user"]])
                        P.i(v.memset(z_sb[:], 0.0), "sv")
                        halo = sap(xbc_sb, [[NCONVT * AH, 128], [AH, NCONVT],
                                            [1, KC - 1]])
                        P.i(v.memset(halo, 0.0), "sv")
                    P.sec("v", v_rst)
                    st["z"] = P.tok("sv")
                    st["halo"] = P.tok("sv")

                for fi, (off, fw, kind, dpos) in enumerate(ftiles):
                    pp = (PA0, PA1)[fi % 2]
                    pi = fi % 2

                    def t_ip(t):
                        P.w(t, "t", [wa1_tok, st["xfm"], st["pa_ev"][pi]])
                        for kt in range(NDT):
                            mm = t.matmul(
                                pp[0:fw, 0:ABLK],
                                wa1_sb[:, kt * CA1 + off:kt * CA1 + off + fw],
                                xfm_sb[:, kt * ABLK:(kt + 1) * ABLK],
                                start=(kt == 0), stop=(kt == NDT - 1))
                        P.i(mm, "sp")
                    P.sec("t", t_ip)
                    st["pa_mm"][pi] = P.tok("sp")
                    if fi == len(ftiles) - 1:
                        st["xfm_pe"] = P.tok("sp")

                    def v_ev(v):
                        P.w(v, "v", [st["pa_mm"][pi], st["halo"], st["conv"]])
                        if kind == "dt":
                            P.i(v.tensor_copy(out=dtp_sb[:], in_=pp[0:HPC, 0:ABLK]),
                                "sv")
                        else:
                            dst = xbc_sb[:, dpos * AH + KC - 1:dpos * AH + AH]
                            P.i(v.tensor_copy(out=dst, in_=pp[:, 0:ABLK]), "sv")
                    P.sec("v", v_ev)
                    st["pa_ev"][pi] = P.tok("sv")
                xbc_tok = P.tok("sv")

                conv_sil = []
                for ct in range(NCONVT):
                    acc = cacc_sb[:, (ct % 2) * ABLK:(ct % 2 + 1) * ABLK]

                    def v_cv(v):
                        deps = [xbc_tok, st["xba_pe"]]
                        if ct >= 2:
                            deps.append(conv_sil[ct - 2])
                        P.w(v, "v", deps)
                        base = ct * AH
                        P.i(v.tensor_scalar(out=acc, in0=xbc_sb[:, base:base + ABLK],
                                            scalar1=convw_sb[:, ct * KC:ct * KC + 1],
                                            scalar2=convb_sb[:, ct:ct + 1],
                                            op0=ALU.mult, op1=ALU.add), "sv")
                        for tau in range(1, KC):
                            P.w(v, "v", [P.tok("sv")])
                            P.i(v.scalar_tensor_tensor(
                                out=acc, in0=xbc_sb[:, base + tau:base + tau + ABLK],
                                scalar=convw_sb[:, ct * KC + tau:ct * KC + tau + 1],
                                in1=acc, op0=ALU.mult, op1=ALU.add), "sv")
                    P.sec("v", v_cv)
                    cv_tok = P.tok("sv")

                    def s_cv(s):
                        P.w(s, "s", [cv_tok])
                        P.i(s.activation(out=xba_sb[:, ct * ABLK:(ct + 1) * ABLK],
                                         in_=acc, func=AF.Silu), "ss")
                    P.sec("s", s_cv)
                    conv_sil.append(P.tok("ss"))
                conv_tok = P.tok("ss")
                st["conv"] = conv_tok

                def v_hl(v):
                    P.w(v, "v", [conv_tok, xbc_tok])
                    src = sap(xbc_sb, [[NCONVT * AH, 128], [AH, NCONVT],
                                       [1, KC - 1]], ABLK)
                    dst = sap(xbc_sb, [[NCONVT * AH, 128], [AH, NCONVT],
                                       [1, KC - 1]])
                    P.i(v.tensor_copy(out=dst, in_=src), "sv")
                P.sec("v", v_hl)
                st["halo"] = P.tok("sv")

                def v_dt(v):
                    P.w(v, "v", [st["pa_ev"][(len(ftiles) - 1) % 2], a_tok,
                                 st["stk_pe"], st["dtuser"]])
                    P.i(v.tensor_scalar(out=dts_sb[:], in0=dtp_sb[:],
                                        scalar1=dtb_sb[:], scalar2=None,
                                        op0=ALU.add), "sv")
                P.sec("v", v_dt)
                dt1 = P.tok("sv")

                def s_dt(s):
                    P.w(s, "s", [dt1])
                    P.i(s.activation(out=dta_sb[:], in_=dts_sb[:], func=AF.Abs),
                        "ss")
                    P.w(s, "s", [P.tok("ss")])
                    P.i(s.activation(out=dta_sb[:], in_=dta_sb[:], func=AF.Exp,
                                     scale=-1.0), "ss")
                    P.w(s, "s", [P.tok("ss")])
                    P.i(s.activation(out=dta_sb[:], in_=dta_sb[:], func=AF.Ln,
                                     bias=1.0), "ss")
                P.sec("s", s_dt)
                dt2 = P.tok("ss")

                def v_dt2(v):
                    P.w(v, "v", [dt2])
                    P.i(v.tensor_relu(out=dts_sb[:], in_=dts_sb[:]), "sv")
                    P.w(v, "v", [P.tok("sv")])
                    P.i(v.tensor_tensor(out=dtf_sb[:],
                                        in0=dts_sb[:], in1=dta_sb[:], op=ALU.add),
                        "sv")
                    P.w(v, "v", [P.tok("sv")])
                    P.i(v.tensor_scalar(out=dta_sb[:],
                                        in0=dtf_sb[:],
                                        scalar1=acol_sb[:], scalar2=None,
                                        op0=ALU.mult), "sv")
                    for s_ in range(NSUB):
                        P.w(v, "v", [P.tok("sv")])
                        sl = slice(s_ * CSS, (s_ + 1) * CSS)
                        P.i(v.tensor_tensor_scan(out=stk_sb[0:HPC, sl],
                                                 data0=dta_sb[:, sl],
                                                 data1=dta_sb[:, sl], initial=0.0,
                                                 op0=ALU.add, op1=ALU.bypass), "sv")
                P.sec("v", v_dt2)
                ac_tok = P.tok("sv")
                st["dtuser"] = ac_tok

                def s_dt2(s):
                    P.w(s, "s", [ac_tok])
                    for s_ in range(NSUB):
                        sl = slice(s_ * CSS, (s_ + 1) * CSS)
                        last = s_ * CSS + CSS - 1
                        P.i(s.activation(out=stk_sb[32:32 + HPC, sl],
                                         in_=stk_sb[0:HPC, sl], func=AF.Exp,
                                         scale=-1.0,
                                         bias=stk_sb[0:HPC, last:last + 1]), "ss")
                        P.i(s.activation(out=stk_sb[64:64 + HPC, sl],
                                         in_=stk_sb[0:HPC, sl], func=AF.Exp), "ss")
                P.sec("s", s_dt2)
                stk_tok = P.tok("ss")

                def s_dt3(s):
                    P.w(s, "s", [stk_tok])
                    for s_ in range(NSUB):
                        sl = slice(s_ * CSS, (s_ + 1) * CSS)
                        r0 = (b * NSUB + s_) * HPC
                        P.i(s.dma_start(out=acb_d[r0:r0 + HPC, :],
                                        in_=stk_sb[0:HPC, sl]), "qs")
                        last = s_ * CSS + CSS - 1
                        P.i(s.dma_start(
                            out=sap(prow_d, [[1, HPC], [0, 1]],
                                    (b * NSUB + s_) * HPC),
                            in_=stk_sb[64:64 + HPC, last:last + 1]), "qs")
                P.sec("s", s_dt3)
                acb_w = P.tok("qs")

                for s_ in range(NSUB):
                    sub = b * NSUB + s_
                    sc = s_ * CSS

                    def y_rep(y):
                        P.w(y, "y", [acb_w, P.tok("sv"), P.tok("ss")])
                        P.i(y.dma_start(out=srep_sb[:],
                                        in_=sap(acb_d, [[0, 128], [1, HPC * CSS]],
                                                sub * HPC * CSS)), "qy")
                        P.i(y.dma_start(out=prep_sb[:],
                                        in_=sap(prow_d, [[0, 128], [1, HPC]],
                                                sub * HPC)), "qy")
                    P.sec("y", y_rep)
                    rep_tok = P.tok("qy")

                    def t_tr(t):
                        P.w(t, "t", [conv_tok, P.tok("sv")])
                        for d in range(NDPC):
                            tr = t.transpose(PTRB[:, d * 128:(d + 1) * 128],
                                             xba_sb[:, d * ABLK + sc:d * ABLK + sc + CSS],
                                             ident_bf[:])
                        P.i(tr, "sp")
                    P.sec("t", t_tr)
                    xtr = P.tok("sp")

                    def v_tr(v):
                        P.w(v, "v", [xtr])
                        P.i(v.tensor_copy(out=xtok_sb[:], in_=PTRB[:]), "sv")
                    P.sec("v", v_tr)
                    xtok_tok = P.tok("sv")

                    def t_tr2(t):
                        P.w(t, "t", [xtok_tok, stk_tok])
                        tr = t.transpose(PTRB[:, 0:NST],
                                         xba_sb[:, NDPC * ABLK + sc:NDPC * ABLK + sc + CSS],
                                         ident_bf[:])
                        P.i(tr, "sp")
                        P.w(t, "t", [P.tok("sv")])
                        t.transpose(PSC[:, 0:64 + HPC],
                                    stk_sb[0:64 + HPC, sc:sc + CSS],
                                    ident_f[0:64 + HPC, 0:64 + HPC])
                        tr2 = t.transpose(PSC[:, 96:96 + HPC],
                                          dtf_sb[:, sc:sc + CSS],
                                          ident_f[0:HPC, 0:HPC])
                        P.i(tr2, "sp")
                    P.sec("t", t_tr2)
                    btr = ("sp", P.cnt["sp"] - 1)
                    str_ = P.tok("sp")
                    st["stk_pe"] = str_

                    def v_tr2(v):
                        P.w(v, "v", [btr])
                        P.i(v.tensor_copy(out=btok_sb[:], in_=PTRB[:, 0:NST]), "sv")
                        P.w(v, "v", [str_])
                        P.i(v.tensor_copy(out=actok_sb[:, 0:64 + HPC],
                                          in_=PSC[:, 0:64 + HPC]), "sv")
                        P.i(v.tensor_copy(out=actok_sb[:, 96:96 + HPC],
                                          in_=PSC[:, 96:96 + HPC]), "sv")
                    P.sec("v", v_tr2)
                    btok_tok = ("sv", P.cnt["sv"] - 1)
                    actok_tok = P.tok("sv")

                    def v_fo(v):
                        P.w(v, "v", [actok_tok, st["z"], dexp_tok, rep_tok, ut_tok])
                        P.i(v.tensor_tensor(out=xs_sb[:], in0=xtok_sb[:],
                                            in1=fexp(actok_sb[:], HPC, HD, 96),
                                            op=ALU.mult), "sv")
                        P.w(v, "v", [P.tok("sv")])
                        P.i(v.tensor_tensor(out=xsd_sb[:], in0=xs_sb[:],
                                            in1=fexp(actok_sb[:], HPC, HD, 32),
                                            op=ALU.mult), "sv")
                        P.i(v.tensor_copy(out=zbf_sb[:], in_=z_sb[:]), "sv")
                        P.i(v.tensor_tensor(out=acm_sb[:], in0=srep_sb[:],
                                            in1=fexp(actok_sb[:], HPC, CSS, 0),
                                            op=ALU.subtract), "sv")
                        P.w(v, "v", [P.tok("sv")])
                        P.i(v.tensor_tensor(out=acm_sb[:], in0=acm_sb[:],
                                            in1=frep(ut_sb[:], CSS, HPC),
                                            op=ALU.add), "sv")
                    P.sec("v", v_fo)
                    folds_tok = P.tok("sv")
                    zbf_tok = ("sv", P.cnt["sv"] - 2)

                    def s_l(s):
                        P.w(s, "s", [folds_tok])
                        P.i(s.activation(out=lall_sb[:], in_=acm_sb[:],
                                         func=AF.Exp), "ss")
                    P.sec("s", s_l)
                    lall_tok = P.tok("ss")

                    def t_gm(t):
                        P.w(t, "t", [conv_tok, st["pa_ev"][0]])
                        bfm = xba_sb[:, NDPC * ABLK + sc:NDPC * ABLK + sc + CSS]
                        cfm = xba_sb[:, (NDPC + 1) * ABLK + sc:(NDPC + 1) * ABLK + sc + CSS]
                        P.i(t.matmul(PA0[:, 0:CSS], bfm, cfm, start=True,
                                     stop=True), "sp")
                    P.sec("t", t_gm)
                    gm_mm = P.tok("sp")

                    def v_gm(v):
                        P.w(v, "v", [gm_mm])
                        P.i(v.tensor_copy(out=gm_sb[:], in_=PA0[:, 0:CSS]), "sv")
                        P.w(v, "v", [P.tok("sv"), lall_tok])
                        for h in range(HPC):
                            P.i(v.tensor_tensor(out=gml_sb[:, h * CSS:(h + 1) * CSS],
                                                in0=gm_sb[:],
                                                in1=lall_sb[:, h * CSS:(h + 1) * CSS],
                                                op=ALU.mult), "sv")
                    P.sec("v", v_gm)
                    gm_ev = ("sv", P.cnt["sv"] - HPC)
                    gml_tok = P.tok("sv")
                    st["pa_ev"][0] = gm_ev

                    def t_y(t):
                        P.w(t, "t", [gml_tok, btok_tok, zbf_tok, folds_tok,
                                     st["pa_ev"][0], st["pa_ev"][1],
                                     st["yd_ev"], st["sc_ev"]])
                        for h in range(HPC):
                            mm = t.matmul(PYD[:, h * HD:(h + 1) * HD],
                                          gml_sb[:, h * CSS:(h + 1) * CSS],
                                          xs_sb[:, h * HD:(h + 1) * HD],
                                          start=((h * HD) % 512 == 0),
                                          stop=(((h + 1) * HD) % 512 == 0
                                                or h == HPC - 1))
                        P.i(mm, "sp")
                        cfm = xba_sb[:, (NDPC + 1) * ABLK + sc:(NDPC + 1) * ABLK + sc + CSS]
                        stepd = min(512, DPC)
                        for i0 in range(0, DPC, stepd):
                            pp = (PA1, PA0)[(i0 // stepd) % 2]
                            P.i(t.matmul(pp[:, 0:stepd], cfm,
                                         zbf_sb[:, i0:i0 + stepd],
                                         start=True, stop=True), "sp")
                        for i0 in range(0, DPC, stepd):
                            mm = t.matmul(PSC[:, i0:i0 + stepd], btok_sb[:],
                                          xsd_sb[:, i0:i0 + stepd],
                                          start=True, stop=True)
                        P.i(mm, "sp")
                    P.sec("t", t_y)
                    sc_mm = P.tok("sp")
                    st["xba_pe"] = sc_mm
                    nyo = max(1, DPC // 512)
                    yo_mms = [("sp", P.cnt["sp"] - 1 - nyo + 1 + i)
                              for i in range(nyo)]
                    yd_mm = ("sp", P.cnt["sp"] - 1 - nyo)

                    def v_y(v):
                        stepd = min(512, DPC)
                        P.w(v, "v", yo_mms + [yd_mm])
                        for i in range(nyo):
                            pp = (PA1, PA0)[i % 2]
                            hph = stepd // HD
                            P.i(v.tensor_tensor(
                                out=yo_sb[:, i * stepd:(i + 1) * stepd],
                                in0=pp[:, 0:stepd],
                                in1=fexp(actok_sb[:], hph, HD,
                                         64 + i * hph),
                                op=ALU.mult), "sv")
                        P.i(v.tensor_tensor(out=dtm_sb[:], in0=xtok_sb[:],
                                            in1=dexp_sb[:], op=ALU.mult), "sv")
                        P.w(v, "v", [P.tok("sv")])
                        P.i(v.tensor_tensor(out=yc_sb[:], in0=PYD[:], in1=yo_sb[:],
                                            op=ALU.add), "sv")
                        P.w(v, "v", [P.tok("sv")])
                        P.i(v.tensor_tensor(out=yout_sb[:], in0=yc_sb[:],
                                            in1=dtm_sb[:], op=ALU.add), "sv")
                    P.sec("v", v_y)
                    y_tok = P.tok("sv")
                    st["pa_ev"][0] = y_tok
                    st["pa_ev"][1] = y_tok
                    st["yd_ev"] = y_tok

                    def s_y(s):
                        P.w(s, "s", [y_tok])
                        P.i(s.dma_start(out=y_raw_d[T0 + sc:T0 + sc + CSS, :],
                                        in_=yout_sb[:]), "qs")
                    P.sec("s", s_y)
                    yraw_w = P.tok("qs")

                    def v_z(v):
                        P.w(v, "v", [sc_mm, rep_tok, zbf_tok])
                        P.i(v.tensor_tensor(out=zt_sb[:], in0=z_sb[:],
                                            in1=fexp(prep_sb[:], HPC, HD),
                                            op=ALU.mult), "sv")
                        P.w(v, "v", [P.tok("sv")])
                        P.i(v.tensor_tensor(out=z_sb[:], in0=zt_sb[:], in1=PSC[:],
                                            op=ALU.add), "sv")
                    P.sec("v", v_z)
                    st["z"] = P.tok("sv")
                    st["zuser"] = P.tok("sv")
                    st["sc_ev"] = P.tok("sv")
                st["xbc_user"] = P.tok("sp")
            a1_done = yraw_w
        a1.free()

        a2 = nc.auto_arena()
        wa2_sb = a2.sbuf_tensor("wa2_sb", [128, NDT * DPC], BF16)
        wstg2 = a2.sbuf_tensor("wstg2", [128, DPC], F32)
        xfm2_sb = a2.sbuf_tensor("xfm2_sb", [128, NDT * ABLK], BF16)
        gact_sb = a2.sbuf_tensor("gact_sb", [128, NDPC * ABLK], BF16)
        gtok_sb = a2.sbuf_tensor("gtok_sb", [128, DPC], BF16)
        yr_sb = a2.sbuf_tensor("yr_sb", [128, DPC], BF16)
        yg_sb = a2.sbuf_tensor("yg_sb", [128, DPC], F32)
        ysq_sb = a2.sbuf_tensor("ysq_sb", [128, DPC], F32)
        ssum_sb = a2.sbuf_tensor("ssum_sb", [128, 1], F32)
        rstd_sb = a2.sbuf_tensor("rstd_sb", [128, 1], F32)
        yn_sb = a2.sbuf_tensor("yn_sb", [128, DPC], BF16)
        ytr_sb = a2.sbuf_tensor("ytr_sb", [128, NTOK], BF16)
        a2.allocate()

        bar1 = P.bar()
        with (
            nc.psum_tensor("PB0_", [128, 512], F32) as PB0_,
            nc.psum_tensor("PB1_", [128, 512], F32) as PB1_,
            nc.psum_tensor("PTR2", [128, DPC], BF16) as PTR2,
        ):
            wconv2 = None
            for kt in range(NDT):
                def y_w2(y):
                    P.w(y, "y", bar1 + [wconv2])
                    P.i(y.dma_start(out=wstg2[:],
                                    in_=wa2_in[kt * 128:(kt + 1) * 128, :]), "qy")
                P.sec("y", y_w2)
                wl = P.tok("qy")

                def v_w2(v):
                    P.w(v, "v", bar1 + [wl])
                    P.i(v.tensor_copy(out=wa2_sb[:, kt * DPC:(kt + 1) * DPC],
                                      in_=wstg2[:]), "sv")
                P.sec("v", v_w2)
                wconv2 = P.tok("sv")
            wa2_tok = P.tok("sv")

            st2 = dict(xfm=None, xfm_pe=None, ev=[None, None], tr_ev=None,
                       yn_ev=None)
            ytok_w = None
            for b in range(NAB):
                T0 = b * ABLK
                cb_src = T0 // TPC
                tloc = T0 - cb_src * TPC

                def y_xfm2(y):
                    P.w(y, "y", [ag_xln, st2["xfm_pe"]])
                    src = sap(xlnT_all_d, [[TPC, 128], [128 * TPC, NDT], [1, ABLK]],
                              cb_src * HID * TPC + tloc)
                    P.i(y.dma_start(
                        out=xfm2_sb[:].rearrange("p (k t) -> p k t", k=NDT),
                        in_=src), "qy")
                P.sec("y", y_xfm2)
                st2["xfm"] = P.tok("qy")

                for d in range(NDPC):
                    pp = (PB0_, PB1_)[d % 2]
                    pi = d % 2

                    def t_g(t):
                        P.w(t, "t", [wa2_tok, st2["xfm"], st2["ev"][pi]])
                        for kt in range(NDT):
                            mm = t.matmul(
                                pp[:],
                                wa2_sb[:, kt * DPC + d * 128:kt * DPC + (d + 1) * 128],
                                xfm2_sb[:, kt * ABLK:(kt + 1) * ABLK],
                                start=(kt == 0), stop=(kt == NDT - 1))
                        P.i(mm, "sp")
                    P.sec("t", t_g)
                    g_mm = P.tok("sp")
                    if d == NDPC - 1:
                        st2["xfm_pe"] = P.tok("sp")

                    def s_g(s):
                        P.w(s, "s", [g_mm])
                        P.i(s.activation(out=gact_sb[:, d * ABLK:(d + 1) * ABLK],
                                         in_=pp[:], func=AF.Silu), "ss")
                    P.sec("s", s_g)
                    st2["ev"][pi] = P.tok("ss")
                gact_tok = P.tok("ss")

                for s_ in range(NSUB):
                    sc = s_ * CSS
                    T0s = T0 + sc

                    def y_yr(y):
                        P.w(y, "y", [a1_done, P.tok("sv")])
                        P.i(y.dma_start(out=yr_sb[:], in_=y_raw_d[T0s:T0s + CSS, :]),
                            "qy")
                    P.sec("y", y_yr)
                    yr_tok = P.tok("qy")

                    def t_gt(t):
                        P.w(t, "t", [gact_tok, st2["tr_ev"]])
                        for d in range(NDPC):
                            tr = t.transpose(PTR2[:, d * 128:(d + 1) * 128],
                                             gact_sb[:, d * ABLK + sc:d * ABLK + sc + CSS],
                                             ident_bf[:])
                        P.i(tr, "sp")
                    P.sec("t", t_gt)
                    gt_tok = P.tok("sp")

                    def v_g2(v):
                        P.w(v, "v", [gt_tok])
                        P.i(v.tensor_copy(out=gtok_sb[:], in_=PTR2[:]), "sv")
                        P.w(v, "v", [P.tok("sv"), yr_tok])
                        P.i(v.tensor_tensor(out=yg_sb[:], in0=yr_sb[:],
                                            in1=gtok_sb[:], op=ALU.mult), "sv")
                        P.w(v, "v", [P.tok("sv")])
                        P.i(v.tensor_tensor(out=ysq_sb[:], in0=yg_sb[:],
                                            in1=yg_sb[:], op=ALU.mult), "sv")
                        P.w(v, "v", [P.tok("sv")])
                        P.i(v.tensor_reduce(out=ssum_sb[:], in_=ysq_sb[:], axis=AXX,
                                            op=ALU.add), "sv")
                    P.sec("v", v_g2)
                    st2["tr_ev"] = ("sv", P.cnt["sv"] - 3)
                    ssm_tok = P.tok("sv")

                    def v_r0(v):
                        P.w(v, "v", [ssm_tok])
                        P.i(v.tensor_scalar_mul(out=rstd_sb[:], in0=ssum_sb[:],
                                                scalar1=1.0 / DPC), "sv")
                    P.sec("v", v_r0)
                    sc_tok2 = P.tok("sv")

                    def s_r(s):
                        P.w(s, "s", [sc_tok2])
                        P.i(s.activation(out=rstd_sb[:], in_=rstd_sb[:], func=AF.Ln,
                                         bias=eps_sb[:]), "ss")
                    P.sec("s", s_r)
                    ln_tok2 = P.tok("ss")

                    def v_r1(v):
                        P.w(v, "v", [ln_tok2])
                        P.i(v.tensor_scalar_mul(out=rstd_sb[:], in0=rstd_sb[:],
                                                scalar1=-0.5), "sv")
                    P.sec("v", v_r1)
                    ng_tok2 = P.tok("sv")

                    def s_r2(s):
                        P.w(s, "s", [ng_tok2])
                        P.i(s.activation(out=rstd_sb[:], in_=rstd_sb[:],
                                         func=AF.Exp), "ss")
                    P.sec("s", s_r2)
                    rs_tok = P.tok("ss")

                    def v_n(v):
                        P.w(v, "v", [rs_tok, st2["yn_ev"]])
                        P.i(v.tensor_scalar(out=yn_sb[:], in0=yg_sb[:],
                                            scalar1=rstd_sb[:], scalar2=None,
                                            op0=ALU.mult), "sv")
                    P.sec("v", v_n)
                    yn_tok = P.tok("sv")

                    def s_yw(s):
                        P.w(s, "s", [yn_tok])
                        P.i(s.dma_start(out=y_tok_d[T0s:T0s + CSS, :], in_=yn_sb[:]),
                            "qs")
                    P.sec("s", s_yw)
                    st2["yn_ev"] = P.tok("qs")
            ytok_w = P.tok("qs")

            def y_ytr(y):
                P.w(y, "y", [ytok_w])
                for dt_ in range(NDPC):
                    P.w(y, "y", [P.tok("qy")])
                    P.i(y.dma_start_transpose(
                        out=ytr_sb[:], in_=y_tok_d[:, dt_ * 128:(dt_ + 1) * 128]),
                        "qy")
                    P.w(y, "y", [P.tok("qy")])
                    P.i(y.dma_start(out=yT_sl_d[dt_ * 128:(dt_ + 1) * 128, :],
                                    in_=ytr_sb[:]), "qy")
            P.sec("y", y_ytr)
            ya2a_w = P.tok("qy")

            def g_agy(g):
                P.w(g, "g", [ya2a_w])
                P.icc(g.collective_compute(
                    "AllGather", ALU.bypass, replica_groups=[list(range(NCORES))],
                    ins=[yT_sl_d.opt()], outs=[y_all_d.opt()]))
            P.sec("g", g_agy)
            ag_y = P.tok("cc")
        a2.free()

        SUPB1 = 256
        NSUPB1 = TPC // SUPB1
        QW = min(1024, HID)
        NQ = HID // QW
        b1 = nc.auto_arena()
        wo_sb = b1.sbuf_tensor("wo_sb", [128, NKD * QW], BF16)
        ysup_sb = b1.sbuf_tensor("ysup_sb", [128, NKD * SUPB1], BF16)
        xres_sb = b1.sbuf_tensor("xres_sb", [128, QW], F32)
        xn_sb = b1.sbuf_tensor("xn_sb", [128, QW], F32)
        b1.allocate()

        bar2 = P.bar()
        with nc.psum_tensor("PBQ", [128, (SUPB1 // 128) * QW], F32) as PBQ:
            y_pe = None
            wo_pe = None
            xn_w = None
            ev_b1 = [None] * (SUPB1 // 128)
            for sup in range(NSUPB1):
                t0 = sup * SUPB1

                def y_ys(y):
                    P.w(y, "y", bar2 + [ag_y, y_pe])
                    pid = y.partition_id()
                    off = pid * TPC + t0
                    src = bass.AP(y_all_d.tensor, off,
                                  [[NTOK, 128], [128 * NTOK, NKD], [1, SUPB1]])
                    P.i(y.dma_start(
                        out=ysup_sb[:].rearrange("p (k t) -> p k t", k=NKD),
                        in_=src), "qy")
                P.sec("y", y_ys)
                y_ld = P.tok("qy")

                for q in range(NQ):
                    def s_wo2(s):
                        P.w(s, "s", bar2 + [ag_w, wo_pe])
                        src = sap(woT_all_d, [[HID, 128], [128 * HID, NKD], [1, QW]],
                                  q * QW)
                        P.i(s.dma_start(
                            out=wo_sb[:].rearrange("p (k w) -> p k w", k=NKD),
                            in_=src), "qs")
                    P.sec("s", s_wo2)
                    wo_ld = P.tok("qs")

                    for tt in range(SUPB1 // 128):
                        def t_op(t):
                            P.w(t, "t", [y_ld, wo_ld, ev_b1[tt]])
                            for k in range(NKD):
                                for j in range(QW // 512):
                                    mm = t.matmul(
                                        PBQ[:, tt * QW + j * 512:tt * QW + (j + 1) * 512],
                                        ysup_sb[:, k * SUPB1 + tt * 128:k * SUPB1 + (tt + 1) * 128],
                                        wo_sb[:, k * QW + j * 512:k * QW + (j + 1) * 512],
                                        start=(k == 0), stop=(k == NKD - 1))
                            P.i(mm, "sp")
                        P.sec("t", t_op)
                        op_mm = P.tok("sp")
                        if tt == SUPB1 // 128 - 1:
                            y_pe = op_mm if q == NQ - 1 else y_pe
                            wo_pe = op_mm

                        def y_xr(y):
                            P.w(y, "y", [P.tok("sv")])
                            P.i(y.dma_start(
                                out=xres_sb[:],
                                in_=x_in[t0 + tt * 128:t0 + (tt + 1) * 128,
                                         q * QW:(q + 1) * QW]), "qy")
                        P.sec("y", y_xr)
                        xr_tok = P.tok("qy")

                        def v_xn(v):
                            P.w(v, "v", [op_mm, xr_tok, P.tok("qg")])
                            P.i(v.tensor_tensor(out=xn_sb[:],
                                                in0=PBQ[:, tt * QW:tt * QW + QW],
                                                in1=xres_sb[:], op=ALU.add), "sv")
                        P.sec("v", v_xn)
                        ev_b1[tt] = P.tok("sv")
                        xn_tok = P.tok("sv")

                        def g_xn(g):
                            P.w(g, "g", [xn_tok])
                            P.i(g.dma_start(
                                out=x_new_d[t0 + tt * 128:t0 + (tt + 1) * 128,
                                            q * QW:(q + 1) * QW],
                                in_=xn_sb[:]), "qg")
                        P.sec("g", g_xn)
                        xn_w = P.tok("qg")
        b1.free()
        xnew_done = xn_w

        b2 = nc.auto_arena()
        x2_sb = b2.sbuf_tensor("x2_sb", [128, HID], F32)
        sq2_sb = b2.sbuf_tensor("sq2_sb", [128, SQCH], F32)
        s21_sb = b2.sbuf_tensor("s21_sb", [128, 1], F32)
        s22_sb = b2.sbuf_tensor("s22_sb", [128, 1], F32)
        pt2_sb = b2.sbuf_tensor("pt2_sb", [128, 1], F32)
        m2_sb = b2.sbuf_tensor("m2_sb", [128, 1], F32)
        v2_sb = b2.sbuf_tensor("v2_sb", [128, 1], F32)
        r2_sb = b2.sbuf_tensor("r2_sb", [128, 1], F32)
        o2_sb = b2.sbuf_tensor("o2_sb", [128, HID], BF16)
        xl2T_sb = b2.sbuf_tensor("xl2T_sb", [128, NDT * TPC], BF16)
        b2.allocate()

        bar3 = P.bar()
        xln2_w = ln_pass(x_new_d, xln2_tok_d, TPC, x2_sb, sq2_sb, s21_sb, s22_sb,
                         pt2_sb, m2_sb, v2_sb, r2_sb, o2_sb, bar3 + [xnew_done])

        def y_l2t(y):
            P.w(y, "y", [xln2_w])
            for kt in range(NDT):
                P.i(y.dma_start_transpose(
                    out=xl2T_sb[:, kt * TPC:(kt + 1) * TPC],
                    in_=xln2_tok_d[:, kt * 128:(kt + 1) * 128]), "qy")
        P.sec("y", y_l2t)
        xl2T_tok = P.tok("qy")

        b3 = nc.auto_arena()
        wg_sb = b3.sbuf_tensor("wg_sb", [128, 2 * NDT * 128], BF16)
        wu_sb = b3.sbuf_tensor("wu_sb", [128, 2 * NDT * 128], BF16)
        hs_sb = b3.sbuf_tensor("hs_sb", [128, TPC], BF16)
        ho_sb = b3.sbuf_tensor("ho_sb", [128, TPC], BF16)
        b3.allocate()

        NTW = TPC // 512
        bar4 = P.bar()
        with (
            nc.psum_tensor("PG_", [128, 4 * 512], F32) as PG_,
            nc.psum_tensor("PU_", [128, 4 * 512], F32) as PU_,
        ):
            w_pe = [None, None]
            ev3 = None
            h_w = None
            for it in range(NI):
                par = it % 2
                wslice = slice(par * NDT * 128, (par + 1) * NDT * 128)

                def s_wl(s):
                    P.w(s, "s", bar4 + [ag_w, w_pe[par]])
                    for wsb, wall in ((wg_sb, g_all_d), (wu_sb, u_all_d)):
                        src = sap(wall, [[I, 128], [128 * I, NDT], [1, 128]],
                                  it * 128)
                        P.i(s.dma_start(
                            out=wsb[:, wslice].rearrange("p (k i) -> p k i", k=NDT),
                            in_=src), "qs")
                P.sec("s", s_wl)
                w_ld = P.tok("qs")

                def t_f(t):
                    P.w(t, "t", [w_ld, xl2T_tok, ev3])
                    for kt in range(NDT):
                        for tw in range(NTW):
                            mm = t.matmul(
                                PG_[:, tw * 512:(tw + 1) * 512],
                                wg_sb[:, par * NDT * 128 + kt * 128:
                                      par * NDT * 128 + (kt + 1) * 128],
                                xl2T_sb[:, kt * TPC + tw * 512:kt * TPC + (tw + 1) * 512],
                                start=(kt == 0), stop=(kt == NDT - 1))
                    P.i(mm, "sp")
                    for kt in range(NDT):
                        for tw in range(NTW):
                            mm = t.matmul(
                                PU_[:, tw * 512:(tw + 1) * 512],
                                wu_sb[:, par * NDT * 128 + kt * 128:
                                      par * NDT * 128 + (kt + 1) * 128],
                                xl2T_sb[:, kt * TPC + tw * 512:kt * TPC + (tw + 1) * 512],
                                start=(kt == 0), stop=(kt == NDT - 1))
                    P.i(mm, "sp")
                P.sec("t", t_f)
                u_mm = P.tok("sp")
                g_mm = ("sp", P.cnt["sp"] - 1)
                w_pe[par] = u_mm

                def s_sil(s):
                    P.w(s, "s", [g_mm, P.tok("sv")])
                    for tw in range(NTW):
                        P.i(s.activation(out=hs_sb[:, tw * 512:(tw + 1) * 512],
                                         in_=PG_[:, tw * 512:(tw + 1) * 512],
                                         func=AF.Silu), "ss")
                P.sec("s", s_sil)
                sil_tok = P.tok("ss")

                def v_h(v):
                    P.w(v, "v", [sil_tok, u_mm, P.tok("qg")])
                    for tw in range(NTW):
                        P.i(v.tensor_tensor(out=ho_sb[:, tw * 512:(tw + 1) * 512],
                                            in0=PU_[:, tw * 512:(tw + 1) * 512],
                                            in1=hs_sb[:, tw * 512:(tw + 1) * 512],
                                            op=ALU.mult), "sv")
                P.sec("v", v_h)
                ev3 = P.tok("sv")
                h_tok = P.tok("sv")

                def g_h(g):
                    P.w(g, "g", [h_tok])
                    P.i(g.dma_start(out=h_d[it * 128:(it + 1) * 128, :],
                                    in_=ho_sb[:]), "qg")
                P.sec("g", g_h)
                h_w = P.tok("qg")
            h_written = h_w
        b3.free()
        b2.free()

        b4 = nc.auto_arena()
        hsup_sb = b4.sbuf_tensor("hsup_sb", [128, NI * 512], BF16)
        KCH = -(-NI // 4)
        wd_sb = b4.sbuf_tensor("wd_sb", [128, 2 * KCH * 1024], BF16)
        xr4_sb = b4.sbuf_tensor("xr4_sb", [128, 1024], F32)
        o4_sb = b4.sbuf_tensor("o4_sb", [128, 1024], F32)
        b4.allocate()

        SUP4 = 512
        NSUP4 = TPC // SUP4
        SPW = min(1024, HID)
        NSP = HID // SPW
        bar5 = P.bar()
        with nc.psum_tensor("PD_", [128, 8 * 512], F32) as PD_:
            h_pe = None
            out_w = None
            ev4 = None
            for sup in range(NSUP4):
                t0 = sup * SUP4

                def y_h(y):
                    P.w(y, "y", bar5 + [h_written, h_pe])
                    src = sap(h_d, [[TPC, 128], [128 * TPC, NI], [1, 512]], t0)
                    P.i(y.dma_start(
                        out=hsup_sb[:].rearrange("p (k t) -> p k t", k=NI),
                        in_=src), "qy")
                P.sec("y", y_h)
                h_ld = P.tok("qy")

                for sp in range(NSP):
                    wd_toks = []
                    nchunk = -(-NI // KCH)
                    for ch in range(nchunk):
                        k0 = ch * KCH
                        k1 = min(NI, k0 + KCH)
                        par = ch % 2

                        def s_wd(s):
                            prev = wd_toks[ch - 2][1] if ch >= 2 else None
                            P.w(s, "s", bar5 + [ag_w, prev])
                            src = sap(wd_all_d,
                                      [[HID, 128], [128 * HID, k1 - k0], [1, SPW]],
                                      k0 * 128 * HID + sp * SPW)
                            P.i(s.dma_start(
                                out=wd_sb[:, par * KCH * 1024:
                                          par * KCH * 1024 + (k1 - k0) * SPW]
                                .rearrange("p (k w) -> p k w", k=k1 - k0),
                                in_=src), "qs")
                        P.sec("s", s_wd)
                        wd_ld = P.tok("qs")

                        def t_dn(t):
                            P.w(t, "t", [wd_ld, h_ld, ev4])
                            for ki in range(k0, k1):
                                for tt in range(4):
                                    for j in range(SPW // 512):
                                        mm = t.matmul(
                                            PD_[:, (tt * (SPW // 512) + j) * 512:
                                                (tt * (SPW // 512) + j + 1) * 512],
                                            hsup_sb[:, ki * 512 + tt * 128:
                                                    ki * 512 + (tt + 1) * 128],
                                            wd_sb[:, par * KCH * 1024 + (ki - k0) * SPW
                                                  + j * 512:
                                                  par * KCH * 1024 + (ki - k0) * SPW
                                                  + (j + 1) * 512],
                                            start=(ki == 0), stop=(ki == NI - 1))
                            P.i(mm, "sp")
                        P.sec("t", t_dn)
                        wd_toks.append((wd_ld, P.tok("sp")))
                    dn_mm = P.tok("sp")
                    h_pe = dn_mm

                    for tt in range(4):
                        def y_x4(y):
                            P.w(y, "y", [P.tok("sv")])
                            P.i(y.dma_start(
                                out=xr4_sb[:, 0:SPW],
                                in_=x_new_d[t0 + tt * 128:t0 + (tt + 1) * 128,
                                            sp * SPW:(sp + 1) * SPW]), "qy")
                        P.sec("y", y_x4)
                        xr_tok = P.tok("qy")

                        def v_o(v):
                            P.w(v, "v", [dn_mm, xr_tok, P.tok("qs")])
                            for j in range(SPW // 512):
                                P.i(v.tensor_tensor(
                                    out=o4_sb[:, j * 512:(j + 1) * 512],
                                    in0=PD_[:, (tt * (SPW // 512) + j) * 512:
                                            (tt * (SPW // 512) + j + 1) * 512],
                                    in1=xr4_sb[:, j * 512:(j + 1) * 512],
                                    op=ALU.add), "sv")
                        P.sec("v", v_o)
                        o_tok = P.tok("sv")
                        ev4 = o_tok

                        def s_o(s):
                            P.w(s, "s", [o_tok])
                            P.i(s.dma_start(
                                out=out_t[t0 + tt * 128:t0 + (tt + 1) * 128,
                                          sp * SPW:(sp + 1) * SPW],
                                in_=o4_sb[:, 0:SPW]), "qs")
                        P.sec("s", s_o)
                        out_w = P.tok("qs")
        b4.free()

        def s_end(s):
            P.w(s, "s", [out_w, P.tok("qg"), P.tok("qy")])
        P.sec("s", s_end)

        def y_end(y):
            P.w(y, "y", [out_w])
        P.sec("y", y_end)

    return nc



def shard_inputs(cfg, inputs):
    c = cfg
    HID, I, Bsz, L = c["HID"], c["I"], c["B"], c["L"]
    DIN, NST, HPC, DPC = c["DIN"], c["NST"], c["HPC"], c["DPC"]
    TPC, QG, G = c["TPC"], c["QG"], c["G"]
    DSH, NGRP = c["DSH"], c["NGRP"]
    ISH, NIT_SH, IPAD = c["ISH"], c["NIT_SH"], c["IPAD"]
    HP8 = HID // 8

    x = np.ascontiguousarray(
        np.asarray(inputs["L0_attn_hidden_states"]).reshape(Bsz * L, HID))
    wi = np.asarray(inputs["L0_attn_in_proj_w"])
    convw = np.asarray(inputs["L0_attn_conv_w"])[:, 0, :]
    convb = np.asarray(inputs["L0_attn_conv_b"])
    dtb = np.asarray(inputs["L0_attn_dt_bias"])
    alog = np.asarray(inputs["L0_attn_A_log"])
    dvec = np.asarray(inputs["L0_attn_D"])
    normw = np.asarray(inputs["L0_attn_norm_w"])
    wo = np.asarray(inputs["L0_attn_out_proj_w"])
    gq = np.asarray(inputs["L0_ffn_gate_qweight"])
    gs = np.asarray(inputs["L0_ffn_gate_scales"])
    gz = np.asarray(inputs["L0_ffn_gate_qzeros"])
    uq = np.asarray(inputs["L0_ffn_up_qweight"])
    us = np.asarray(inputs["L0_ffn_up_scales"])
    uz = np.asarray(inputs["L0_ffn_up_qzeros"])
    dq = np.asarray(inputs["L0_ffn_down_qweight"])
    ds = np.asarray(inputs["L0_ffn_down_scales"])
    dz = np.asarray(inputs["L0_ffn_down_qzeros"])

    dqp = np.zeros((IPAD, HP8), np.int32); dqp[:I] = dq
    dsp = np.zeros((IPAD // QG, HID), np.float32); dsp[:ds.shape[0]] = ds
    dzp = np.zeros((IPAD // QG, HP8), np.int32); dzp[:dz.shape[0]] = dz

    maps = []
    for cid in range(NCORES):
        xr = slice(DIN + DPC * cid, DIN + DPC * (cid + 1))
        br = slice(2 * DIN + NST * cid, 2 * DIN + NST * (cid + 1))
        cr = slice(2 * DIN + G * NST + NST * cid, 2 * DIN + G * NST + NST * (cid + 1))
        dtr = slice(DIN + c["CONVD"] + HPC * cid, DIN + c["CONVD"] + HPC * (cid + 1))
        gr = slice(DPC * cid, DPC * (cid + 1))
        m = dict(
            x=np.ascontiguousarray(x[TPC * cid:TPC * (cid + 1)]),
            wa1=np.ascontiguousarray(
                np.concatenate([wi[xr], wi[br], wi[cr], wi[dtr]], axis=0).T),
            wa2=np.ascontiguousarray(wi[gr].T),
            convw=np.ascontiguousarray(np.concatenate(
                [convw[DPC * cid:DPC * (cid + 1)],
                 convw[DIN + NST * cid:DIN + NST * (cid + 1)],
                 convw[DIN + G * NST + NST * cid:DIN + G * NST + NST * (cid + 1)]],
                axis=0).reshape(-1, 128, convw.shape[1]).transpose(1, 0, 2)
                .reshape(128, -1)),
            convb=np.ascontiguousarray(np.concatenate(
                [convb[DPC * cid:DPC * (cid + 1)],
                 convb[DIN + NST * cid:DIN + NST * (cid + 1)],
                 convb[DIN + G * NST + NST * cid:DIN + G * NST + NST * (cid + 1)]],
                axis=0).reshape(-1, 128).T),
            dtb=np.ascontiguousarray(dtb[HPC * cid:HPC * (cid + 1)])[:, None],
            alog=np.ascontiguousarray(alog[HPC * cid:HPC * (cid + 1)])[:, None],
            dvec=np.ascontiguousarray(dvec[HPC * cid:HPC * (cid + 1)])[:, None],
            normw=np.ascontiguousarray(
                normw[DPC * cid:DPC * (cid + 1)].reshape(-1, 128).T),
            woutt=np.ascontiguousarray(wo[:, DPC * cid:DPC * (cid + 1)].T),
            gq=np.ascontiguousarray(gq[DSH * cid:DSH * (cid + 1)]),
            gs=np.ascontiguousarray(gs[NGRP * cid:NGRP * (cid + 1)]),
            gz=np.ascontiguousarray(gz[NGRP * cid:NGRP * (cid + 1)]),
            uq=np.ascontiguousarray(uq[DSH * cid:DSH * (cid + 1)]),
            us=np.ascontiguousarray(us[NGRP * cid:NGRP * (cid + 1)]),
            uz=np.ascontiguousarray(uz[NGRP * cid:NGRP * (cid + 1)]),
            dq=np.ascontiguousarray(dqp[ISH * cid:ISH * (cid + 1)]),
            ds=np.ascontiguousarray(dsp[NIT_SH * cid:NIT_SH * (cid + 1)]),
            dz=np.ascontiguousarray(dzp[NIT_SH * cid:NIT_SH * (cid + 1)]),
        )
        maps.append(m)
    return maps


_CACHE = {}


def kernel(**inputs):
    cfg = REAL_CFG
    maps = shard_inputs(cfg, inputs)
    if "nc" not in _CACHE:
        _CACHE["nc"] = build_program(cfg)
    res = run_bass_kernel_spmd(_CACHE["nc"], maps, core_ids=list(range(NCORES)))
    out = np.concatenate([np.asarray(r["out"]) for r in res.results], axis=0)
    return np.ascontiguousarray(
        out.reshape(cfg["B"], cfg["L"], cfg["HID"]).astype(np.float32))


# version 09
# speedup vs baseline: 1.0282x; 1.0282x over previous
import numpy as np
from contextlib import ExitStack

import concourse.bass as bass
import concourse.mybir as mybir
from concourse.bass_utils import run_bass_kernel_spmd

F32 = mybir.dt.float32
BF16 = mybir.dt.bfloat16
I32 = mybir.dt.int32
AF = mybir.ActivationFunctionType
ALU = mybir.AluOpType
AXX = mybir.AxisListType.X

NCORES = 8
EPS = 1e-5
AWQ_INV = [0, 2, 4, 6, 1, 3, 5, 7]


def make_cfg(HID=4096, NH=128, HD=64, NST=128, KC=4, I=11008, B=2, L=8192):
    G = 8
    QG = 128
    c = dict(HID=HID, NH=NH, HD=HD, NST=NST, KC=KC, I=I, B=B, L=L, G=G, QG=QG)
    c["DIN"] = NH * HD
    c["CONVD"] = c["DIN"] + 2 * G * NST
    c["HPC"] = NH // NCORES
    c["DPC"] = c["HPC"] * HD
    c["NDPC"] = c["DPC"] // 128
    c["TPC"] = B * L // NCORES
    c["NTOK"] = B * L
    c["NDT"] = HID // 128
    c["ABLK"] = 512
    c["NAB"] = c["NTOK"] // c["ABLK"]
    c["AB_PER_BATCH"] = c["NAB"] // B
    c["CSS"] = 128
    c["NSUB"] = c["ABLK"] // c["CSS"]
    c["CA1"] = c["DPC"] + 2 * NST + c["HPC"]
    c["NCONVT"] = (c["DPC"] + 2 * NST) // 128
    c["DSH"] = HID // NCORES
    c["NGT"] = c["DSH"] // 128
    c["NGRP"] = c["DSH"] // QG
    c["IPAD"] = -(-I // (128 * NCORES)) * (128 * NCORES)
    c["ISH"] = c["IPAD"] // NCORES
    c["NIT_SH"] = c["ISH"] // 128
    c["NI"] = I // 128
    c["NTT"] = c["TPC"] // 128
    assert NST == 128 and c["DPC"] % 128 == 0 and HID % 128 == 0
    assert L % c["ABLK"] == 0 and c["DSH"] % QG == 0
    return c


REAL_CFG = make_cfg()

_EK2NAME = {"v": "vector", "s": "scalar", "t": "tensor", "g": "gpsimd", "y": "sync"}


_QS_OF = {"y": "qy", "s": "qs", "g": "qg"}


class Prog:
    def __init__(self, nc, block, sems):
        self.nc = nc
        self.block = block
        self.sem = sems
        self.cnt = {k: 0 for k in sems}
        self.lastw = {e: {k: 0 for k in sems} for e in _EK2NAME}
        self.maxw = {k: 0 for k in sems}
        self.issuer = {e: set([q]) if q else set()
                       for e, q in ((ek, _QS_OF.get(ek)) for ek in _EK2NAME)}

    def sec(self, ek, fn):
        def wrapped(eng):
            for q in self.issuer.get(ek, ()):
                if self.maxw[q] > self.lastw[ek][q]:
                    eng.wait_ge(self.sem[q], self.maxw[q])
                    self.lastw[ek][q] = self.maxw[q]
            fn(eng)
        getattr(self.block, _EK2NAME[ek])(wrapped)

    def w(self, eng, ek, tokens):
        for tok in tokens:
            if tok is None:
                continue
            sk, val = tok
            if sk.startswith("q"):
                val = self.cnt[sk]
                if val > self.maxw[sk]:
                    self.maxw[sk] = val
            if self.lastw[ek][sk] < val:
                eng.wait_ge(self.sem[sk], val)
                self.lastw[ek][sk] = val

    def i(self, instr, sk, ek=None):
        amount = 16 if sk.startswith("q") else 1
        self.cnt[sk] += amount
        instr.then_inc(self.sem[sk], amount)
        if ek is not None:
            self.issuer[ek].add(sk)
        return (sk, self.cnt[sk])

    def icc(self, instr):
        self.cnt["cc"] += 1
        instr.then_inc(self.sem["cc"])
        return ("cc", self.cnt["cc"])

    def tok(self, sk):
        return (sk, self.cnt[sk])

    def bar(self):
        return [self.tok(k) for k in self.sem]


def sap(handle_or_ap, dims, offset=0):
    base = handle_or_ap if isinstance(handle_or_ap, bass.AP) else handle_or_ap[:]
    return bass.AP(base.tensor, base.offset + offset, [list(d) for d in dims])


def fexp(ap2d, ncols, reps, col0=0):
    return bass.AP(ap2d.tensor, ap2d.offset + col0,
                   [list(ap2d.ap[0]), [1, ncols], [0, reps]])


def frep(ap2d, ncols, reps, col0=0):
    return bass.AP(ap2d.tensor, ap2d.offset + col0,
                   [list(ap2d.ap[0]), [0, reps], [1, ncols]])


def build_program(cfg):
    C = cfg
    HID, I, Bsz = C["HID"], C["I"], C["B"]
    HD, HPC, DPC, NDPC = C["HD"], C["HPC"], C["DPC"], C["NDPC"]
    TPC, NTOK, NDT = C["TPC"], C["NTOK"], C["NDT"]
    ABLK, NAB, CSS, NSUB = C["ABLK"], C["NAB"], C["CSS"], C["NSUB"]
    CA1, NCONVT = C["CA1"], C["NCONVT"]
    DSH, NGT, NGRP, QG = C["DSH"], C["NGT"], C["NGRP"], C["QG"]
    IPAD, ISH, NIT_SH, NI = C["IPAD"], C["ISH"], C["NIT_SH"], C["NI"]
    NST, KC, DIN, NTT = C["NST"], C["KC"], C["DIN"], C["NTT"]
    IP8, HP8 = I // 8, HID // 8
    AH = ABLK + KC - 1
    NKD = DIN // 128

    nc = bass.Bass("TRN2", target_bir_lowering=False)

    ei = lambda n, sh, dt=F32: nc.dram_tensor(n, sh, dt, kind="ExternalInput").ap()
    x_in = ei("x", [TPC, HID])
    wa1_in = ei("wa1", [HID, CA1])
    wa2_in = ei("wa2", [HID, DPC])
    convw_in = ei("convw", [128, NCONVT * KC])
    convb_in = ei("convb", [128, NCONVT])
    dtb_in = ei("dtb", [HPC, 1])
    alog_in = ei("alog", [HPC, 1])
    dvec_in = ei("dvec", [HPC, 1])
    normw_in = ei("normw", [128, NDPC])
    woutt_in = ei("woutt", [DPC, HID])
    gq_in = ei("gq", [DSH, IP8], I32)
    gs_in = ei("gs", [NGRP, I])
    gz_in = ei("gz", [NGRP, IP8], I32)
    uq_in = ei("uq", [DSH, IP8], I32)
    us_in = ei("us", [NGRP, I])
    uz_in = ei("uz", [NGRP, IP8], I32)
    dq_in = ei("dq", [ISH, HP8], I32)
    ds_in = ei("ds", [NIT_SH, HID])
    dz_in = ei("dz", [NIT_SH, HP8], I32)
    out_t = nc.dram_tensor("out", [TPC, HID], F32, kind="ExternalOutput").ap()

    di = lambda n, sh, dt: nc.dram_tensor(n, sh, dt).ap()
    xln_tok_d = di("xln_tok", [TPC, HID], BF16)
    xlnT_sl_d = di("xlnT_sl", [HID, TPC], BF16)
    xlnT_all_d = di("xlnT_all", [NCORES * HID, TPC], BF16)
    gsl_d = di("gsl", [DSH, I], BF16)
    g_all_d = di("g_all", [HID, I], BF16)
    usl_d = di("usl", [DSH, I], BF16)
    u_all_d = di("u_all", [HID, I], BF16)
    wdsl_d = di("wdsl", [ISH, HID], BF16)
    wd_all_d = di("wd_all", [IPAD, HID], BF16)
    woT_sl_d = di("woT_sl", [DPC, HID], BF16)
    woT_all_d = di("woT_all", [DIN, HID], BF16)
    zsg_d = di("zsg", [NGRP, I], F32)
    zsu_d = di("zsu", [NGRP, I], F32)
    zsd_d = di("zsd", [NIT_SH, HID], F32)
    dbounce_d = di("dbounce", [1, HPC], F32)
    acb_d = di("acb", [NAB * NSUB * HPC, CSS], F32)
    prow_d = di("prow", [NAB * NSUB, HPC], F32)
    y_raw_d = di("y_raw", [NTOK, DPC], BF16)
    y_tok_d = di("y_tok", [NTOK, DPC], BF16)
    yT_sl_d = di("yT_sl", [DPC, NTOK], BF16)
    y_all_d = di("y_all", [DIN, NTOK], BF16)
    y_b_d = di("y_b", [DIN, TPC], BF16)
    x_new_d = di("x_new", [TPC, HID], F32)
    xln2_tok_d = di("xln2_tok", [TPC, HID], BF16)
    h_d = di("h", [I, TPC], BF16)

    with ExitStack() as _es:
        _sems = {}
        for _sk in ("sv", "ss", "sp", "sg", "qy", "qs", "qg",
                    "qa", "qb", "qc", "qd", "qe", "qf", "cc"):
            _sems[_sk] = _es.enter_context(nc.semaphore(_sk))
        (_sv, _ss, _sp, _sg, _qy, _qs, _qg, _qa, _qb, _qc, _qd, _qe, _qf,
         _cc) = (_sems[k] for k in ("sv", "ss", "sp", "sg", "qy", "qs", "qg",
                                    "qa", "qb", "qc", "qd", "qe", "qf", "cc"))
        block = _es.enter_context(nc.Block())
        P = Prog(nc, block, dict(sv=_sv, ss=_ss, sp=_sp, sg=_sg,
                                 qy=_qy, qs=_qs, qg=_qg, cc=_cc,
                                 qa=_qa, qb=_qb, qc=_qc, qd=_qd, qe=_qe, qf=_qf))

        pers = nc.auto_arena()
        ident_bf = pers.sbuf_tensor("ident_bf", [128, 128], BF16)
        ident_f = pers.sbuf_tensor("ident_f", [128, 128], F32)
        ut_sb = pers.sbuf_tensor("ut_sb", [128, CSS], F32)
        dexp_sb = pers.sbuf_tensor("dexp_sb", [128, DPC], F32)
        convw_sb = pers.sbuf_tensor("convw_sb", [128, NCONVT * KC], F32)
        convb_sb = pers.sbuf_tensor("convb_sb", [128, NCONVT], F32)
        dtb_sb = pers.sbuf_tensor("dtb_sb", [HPC, 1], F32)
        acol_sb = pers.sbuf_tensor("acol_sb", [HPC, 1], F32)
        dvec_sb = pers.sbuf_tensor("dvec_sb", [HPC, 1], F32)
        ones_sb = pers.sbuf_tensor("ones_sb", [1, 128], F32)
        eps_sb = pers.sbuf_tensor("eps_sb", [128, 1], F32)
        drow_sb = pers.sbuf_tensor("drow_sb", [1, HPC], F32)
        pers.allocate()

        def y_setup(y):
            P.i(y.dma_start(out=convw_sb[:], in_=convw_in), "qy")
            P.i(y.dma_start(out=convb_sb[:], in_=convb_in), "qy")
            for dst, src in ((dtb_sb, dtb_in), (acol_sb, alog_in), (dvec_sb, dvec_in)):
                P.i(y.dma_start(out=dst[:], in_=src), "qy")
        P.sec("y", y_setup)
        ld_small = P.tok("qy")

        def v_setup(v):
            P.w(v, "v", [ld_small])
            P.i(v.memset(ident_bf[:], 1.0), "sv")
            P.i(v.memset(ident_f[:], 1.0), "sv")
            P.i(v.memset(ut_sb[:], 0.0), "sv")
            P.i(v.memset(ones_sb[:], 1.0), "sv")
            P.i(v.memset(eps_sb[:], EPS), "sv")
        P.sec("v", v_setup)
        ident_tok0 = P.tok("sv")

        def g_setup(g):
            P.w(g, "g", [ident_tok0])
            P.i(g.affine_select(out=ut_sb[:], in_=ut_sb[:], pattern=[[1, CSS]],
                                compare_op=ALU.is_ge, fill=-1e9, base=0,
                                channel_multiplier=-1), "sg")
            P.i(g.affine_select(out=ident_bf[:], in_=ident_bf[:],
                                pattern=[[1, 128]], compare_op=ALU.is_equal,
                                fill=0.0, base=0, channel_multiplier=-1), "sg")
            P.i(g.affine_select(out=ident_f[:], in_=ident_f[:],
                                pattern=[[1, 128]], compare_op=ALU.is_equal,
                                fill=0.0, base=0, channel_multiplier=-1), "sg")
        P.sec("g", g_setup)
        ut_tok = P.tok("sg")
        ident_tok = P.tok("sg")

        def s_setup(s):
            P.w(s, "s", [ld_small])
            P.i(s.activation(out=acol_sb[:], in_=acol_sb[:], func=AF.Exp), "ss")
        P.sec("s", s_setup)

        def v_setup2(v):
            P.w(v, "v", [P.tok("ss")])
            P.i(v.tensor_scalar_mul(out=acol_sb[:], in0=acol_sb[:], scalar1=-1.0), "sv")
        P.sec("v", v_setup2)
        a_tok = P.tok("sv")

        def y_setup3(y):
            P.w(y, "y", [ld_small])
            P.i(y.dma_start(out=sap(dbounce_d, [[1, HPC], [0, 1]]), in_=dvec_sb[:]),
                "qy")
            P.w(y, "y", [P.tok("qy")])
            P.i(y.dma_start(out=drow_sb[:], in_=dbounce_d), "qy")
        P.sec("y", y_setup3)
        drow_tok = P.tok("qy")

        with ExitStack() as _esdx:
            PDX = _esdx.enter_context(nc.psum_tensor("PDX", [128, DPC], F32))
            def t_setup(t):
                P.w(t, "t", [drow_tok, ident_tok])
                step = min(512, DPC)
                rhs_full = bass.AP(drow_sb[:].tensor, drow_sb[:].offset,
                                   [[HPC, 1], [1, HPC], [0, HD]])
                for i0 in range(0, DPC, step):
                    rhs = bass.AP(rhs_full.tensor,
                                  rhs_full.offset + i0 // HD,
                                  [[HPC, 1], [1, step // HD], [0, HD]])
                    mm = t.matmul(PDX[:, i0:i0 + step], ones_sb[:], rhs,
                                  start=True, stop=True)
                P.i(mm, "sp")
            P.sec("t", t_setup)
            dx_mm = P.tok("sp")

            def v_setup3(v):
                P.w(v, "v", [dx_mm])
                P.i(v.tensor_copy(out=dexp_sb[:], in_=PDX[:]), "sv")
            P.sec("v", v_setup3)
        dexp_tok = P.tok("sv")

        ph0a = nc.auto_arena()
        p0x = ph0a.sbuf_tensor("p0x", [128, HID], F32)
        SQCH = min(1024, HID)
        p0sq = ph0a.sbuf_tensor("p0sq", [128, SQCH], F32)
        p0s1 = ph0a.sbuf_tensor("p0s1", [128, 1], F32)
        p0s2 = ph0a.sbuf_tensor("p0s2", [128, 1], F32)
        p0pt = ph0a.sbuf_tensor("p0pt", [128, 1], F32)
        p0m = ph0a.sbuf_tensor("p0m", [128, 1], F32)
        p0v = ph0a.sbuf_tensor("p0v", [128, 1], F32)
        p0r = ph0a.sbuf_tensor("p0r", [128, 1], F32)
        p0o = ph0a.sbuf_tensor("p0o", [128, HID], BF16)
        p0t = ph0a.sbuf_tensor("p0t", [128, TPC], BF16)
        ph0a.allocate()

        def ln_pass(src_dram, dst_dram, rows, x_sb, sq_sb, s1_sb, s2_sb, pt_sb,
                    m_sb, v_sb, r_sb, o_sb, pre_bar):
            written = None
            nsq = HID // SQCH
            for it in range(rows // 128):
                r0 = it * 128

                def y_fn(y):
                    P.w(y, "y", pre_bar + [P.tok("sv"), P.tok("qs")])
                    P.i(y.dma_start(out=x_sb[:], in_=src_dram[r0:r0 + 128, :]), "qy")
                P.sec("y", y_fn)
                xt = P.tok("qy")

                def v_fn(v):
                    P.w(v, "v", pre_bar + [xt])
                    P.i(v.tensor_reduce(out=s1_sb[:], in_=x_sb[:], axis=AXX,
                                        op=ALU.add), "sv")
                    for ic in range(nsq):
                        P.w(v, "v", [P.tok("sv")])
                        P.i(v.tensor_tensor(out=sq_sb[:],
                                            in0=x_sb[:, ic * SQCH:(ic + 1) * SQCH],
                                            in1=x_sb[:, ic * SQCH:(ic + 1) * SQCH],
                                            op=ALU.mult), "sv")
                        P.w(v, "v", [P.tok("sv")])
                        P.i(v.tensor_reduce(out=pt_sb[:], in_=sq_sb[:], axis=AXX,
                                            op=ALU.add), "sv")
                        P.w(v, "v", [P.tok("sv")])
                        if ic == 0:
                            P.i(v.tensor_copy(out=s2_sb[:], in_=pt_sb[:]), "sv")
                        else:
                            P.i(v.tensor_tensor(out=s2_sb[:], in0=s2_sb[:],
                                                in1=pt_sb[:], op=ALU.add), "sv")
                    P.w(v, "v", [P.tok("sv")])
                    P.i(v.tensor_scalar_mul(out=m_sb[:], in0=s1_sb[:],
                                            scalar1=1.0 / HID), "sv")
                    P.w(v, "v", [P.tok("sv")])
                    P.i(v.tensor_tensor(out=v_sb[:], in0=m_sb[:], in1=m_sb[:],
                                        op=ALU.mult), "sv")
                    P.w(v, "v", [P.tok("sv")])
                    P.i(v.scalar_tensor_tensor(out=v_sb[:], in0=s2_sb[:],
                                               scalar=1.0 / HID, in1=v_sb[:],
                                               op0=ALU.mult, op1=ALU.subtract), "sv")
                P.sec("v", v_fn)
                var_tok = P.tok("sv")

                def s_fn(s):
                    P.w(s, "s", [var_tok])
                    P.i(s.activation(out=r_sb[:], in_=v_sb[:], func=AF.Ln,
                                     bias=eps_sb[:]), "ss")
                P.sec("s", s_fn)
                ln_tok = P.tok("ss")

                def vr_fn(v):
                    P.w(v, "v", [ln_tok])
                    P.i(v.tensor_scalar_mul(out=r_sb[:], in0=r_sb[:],
                                            scalar1=-0.5), "sv")
                P.sec("v", vr_fn)
                neg_tok = P.tok("sv")

                def s1b_fn(s):
                    P.w(s, "s", [neg_tok])
                    P.i(s.activation(out=r_sb[:], in_=r_sb[:], func=AF.Exp), "ss")
                P.sec("s", s1b_fn)
                rs_tok = P.tok("ss")

                def v2_fn(v):
                    P.w(v, "v", [rs_tok])
                    P.i(v.tensor_scalar(out=o_sb[:], in0=x_sb[:], scalar1=m_sb[:],
                                        scalar2=r_sb[:], op0=ALU.subtract,
                                        op1=ALU.mult), "sv")
                P.sec("v", v2_fn)
                o_tok = P.tok("sv")

                def s2_fn(s):
                    P.w(s, "s", [o_tok])
                    P.i(s.dma_start(out=dst_dram[r0:r0 + 128, :], in_=o_sb[:]), "qs")
                P.sec("s", s2_fn)
                written = P.tok("qs")
            return written

        xln_w = ln_pass(x_in, xln_tok_d, TPC, p0x, p0sq, p0s1, p0s2, p0pt,
                        p0m, p0v, p0r, p0o, [])

        def y_xlt(y):
            P.w(y, "y", [xln_w])
            for kt in range(NDT):
                P.w(y, "y", [P.tok("qy")])
                P.i(y.dma_start_transpose(out=p0t[:],
                                          in_=xln_tok_d[:, kt * 128:(kt + 1) * 128]),
                    "qy")
                P.w(y, "y", [P.tok("qy")])
                P.i(y.dma_start(out=xlnT_sl_d[kt * 128:(kt + 1) * 128, :],
                                in_=p0t[:]), "qy")
        P.sec("y", y_xlt)
        xlt_w = P.tok("qy")

        def g_ag1(g):
            P.w(g, "g", [xlt_w])
            P.icc(g.collective_compute(
                "AllGather", ALU.bypass, replica_groups=[list(range(NCORES))],
                ins=[xlnT_sl_d.opt()], outs=[xlnT_all_d.opt()]))
        P.sec("g", g_ag1)
        ag_xln = P.tok("cc")
        ph0a.free()

        ICH = I // 4 if I > 2048 else I
        NICH = I // ICH
        ph0b = nc.auto_arena()
        dqq = ph0b.sbuf_tensor("dqq", [128, 2 * (ICH // 8)], I32)
        dqi = ph0b.sbuf_tensor("dqi", [128, ICH], I32)
        dqn = ph0b.sbuf_tensor("dqn", [128, ICH], F32)
        dqs = ph0b.sbuf_tensor("dqs", [128, 2 * ICH], F32)
        dqz = ph0b.sbuf_tensor("dqz", [128, 2 * ICH], F32)
        dqw = ph0b.sbuf_tensor("dqw", [128, ICH], BF16)
        zqq = ph0b.sbuf_tensor("zqq", [NGRP, IP8], I32)
        zqi = ph0b.sbuf_tensor("zqi", [NGRP, ICH], I32)
        zqs = ph0b.sbuf_tensor("zqs", [NGRP, ICH], F32)
        zsc = ph0b.sbuf_tensor("zsc", [NGRP, ICH], F32)
        ph0b.allocate()
        bar0b = P.bar()

        def dequant_gu(qin, sin, zin, zs_dram, out_dram):
            def y_fn(y):
                P.w(y, "y", bar0b + [P.tok("sv")])
                P.i(y.dma_start(out=zqq[:], in_=zin), "qy")
            P.sec("y", y_fn)
            zl = P.tok("qy")

            for ic in range(NICH):
                c0 = ic * ICH

                def yz_fn(y):
                    P.w(y, "y", [P.tok("sv")])
                    P.i(y.dma_start(out=zsc[:], in_=sin[:, c0:c0 + ICH]), "qy")
                P.sec("y", yz_fn)
                zscl = P.tok("qy")

                def v_fn(v):
                    P.w(v, "v", bar0b + [zl, zscl, P.tok("qy")])
                    for sft in range(8):
                        P.i(v.tensor_scalar(out=zqi[:, AWQ_INV[sft]::8],
                                            in0=zqq[:, c0 // 8:(c0 + ICH) // 8],
                                            scalar1=4 * sft, scalar2=15,
                                            op0=ALU.logical_shift_right,
                                            op1=ALU.bitwise_and), "sv")
                    P.w(v, "v", [P.tok("sv")])
                    P.i(v.tensor_copy(out=zqs[:], in_=zqi[:]), "sv")
                    P.w(v, "v", [P.tok("sv")])
                    P.i(v.tensor_tensor(out=zqs[:], in0=zqs[:], in1=zsc[:],
                                        op=ALU.mult), "sv")
                P.sec("v", v_fn)
                zs_tok = P.tok("sv")

                def y2_fn(y):
                    P.w(y, "y", [zs_tok])
                    P.i(y.dma_start(out=zs_dram[:, c0:c0 + ICH], in_=zqs[:]), "qy")
                P.sec("y", y2_fn)
            zs_w = P.tok("qy")

            prev_ext = [None, None]
            dqi_free = None
            it_idx = 0
            for gt in range(NGT):
                for ic in range(NICH):
                    c0 = ic * ICH
                    par = it_idx % 2
                    it_idx += 1
                    qb8 = par * (ICH // 8)
                    qbc = par * ICH

                    def y3_fn(y):
                        P.w(y, "y", [zs_w, prev_ext[par]])
                        P.i(y.dma_start(
                            out=dqq[:, qb8:qb8 + ICH // 8],
                            in_=qin[gt * 128:(gt + 1) * 128,
                                    c0 // 8:(c0 + ICH) // 8]),
                            ("qa", "qb")[par], "y")
                        P.i(y.dma_start(out=dqs[:, qbc:qbc + ICH],
                                        in_=sap(sin, [[0, 128], [1, ICH]],
                                                gt * I + c0)),
                            ("qa", "qb")[par], "y")
                        P.i(y.dma_start(out=dqz[:, qbc:qbc + ICH],
                                        in_=sap(zs_dram, [[0, 128], [1, ICH]],
                                                gt * I + c0)),
                            ("qa", "qb")[par], "y")
                    P.sec("y", y3_fn)
                    ql = P.tok(("qa", "qb")[par])

                    def v3_fn(v):
                        P.w(v, "v", [ql, dqi_free])
                        for sft in range(8):
                            P.i(v.tensor_scalar(out=dqi[:, AWQ_INV[sft]::8],
                                                in0=dqq[:, qb8:qb8 + ICH // 8],
                                                scalar1=4 * sft, scalar2=15,
                                                op0=ALU.logical_shift_right,
                                                op1=ALU.bitwise_and), "sv")
                    P.sec("v", v3_fn)
                    ext_v = P.tok("sv")
                    ext_g = ext_v
                    prev_ext[par] = ext_v

                    dqi_free_local = [None]

                    def v4_fn(v):
                        P.w(v, "v", [ext_v, ext_g])
                        P.i(v.tensor_copy(out=dqn[:], in_=dqi[:]), "sv")
                        dqi_free_local[0] = P.tok("sv")
                        P.w(v, "v", [P.tok("sv")])
                        P.i(v.tensor_tensor(out=dqn[:], in0=dqn[:],
                                            in1=dqs[:, qbc:qbc + ICH],
                                            op=ALU.mult), "sv")
                        P.w(v, "v", [P.tok("sv")])
                        P.i(v.tensor_tensor(out=dqw[:], in0=dqn[:],
                                            in1=dqz[:, qbc:qbc + ICH],
                                            op=ALU.subtract), "sv")
                    P.sec("v", v4_fn)
                    dqi_free = dqi_free_local[0]
                    wt = P.tok("sv")

                    def s3_fn(s):
                        P.w(s, "s", [wt])
                        P.i(s.dma_start(
                            out=out_dram[gt * 128:(gt + 1) * 128, c0:c0 + ICH],
                            in_=dqw[:]), "qs")
                    P.sec("s", s3_fn)

        dequant_gu(gq_in, gs_in, gz_in, zsg_d, gsl_d)
        dequant_gu(uq_in, us_in, uz_in, zsu_d, usl_d)
        ph0b.free()

        ph0c = nc.auto_arena()
        dnq = ph0c.sbuf_tensor("dnq", [128, HP8], I32)
        dni = ph0c.sbuf_tensor("dni", [128, HID], I32)
        dnn = ph0c.sbuf_tensor("dnn", [128, HID], F32)
        dns = ph0c.sbuf_tensor("dns", [128, HID], F32)
        dnz = ph0c.sbuf_tensor("dnz", [128, HID], F32)
        dnw = ph0c.sbuf_tensor("dnw", [128, HID], BF16)
        znq = ph0c.sbuf_tensor("znq", [NIT_SH, HP8], I32)
        zni = ph0c.sbuf_tensor("zni", [NIT_SH, HID], I32)
        zns = ph0c.sbuf_tensor("zns", [NIT_SH, HID], F32)
        znc = ph0c.sbuf_tensor("znc", [NIT_SH, HID], F32)
        ph0c.allocate()
        bar0c = P.bar()

        def y_dn(y):
            P.w(y, "y", bar0c + [P.tok("sv")])
            P.i(y.dma_start(out=znq[:], in_=dz_in), "qy")
            P.i(y.dma_start(out=znc[:], in_=ds_in), "qy")
        P.sec("y", y_dn)
        znl = P.tok("qy")

        def v_dn(v):
            P.w(v, "v", bar0c + [znl])
            for sft in range(8):
                P.i(v.tensor_scalar(out=zni[:, AWQ_INV[sft]::8], in0=znq[:],
                                    scalar1=4 * sft, scalar2=15,
                                    op0=ALU.logical_shift_right,
                                    op1=ALU.bitwise_and), "sv")
            P.w(v, "v", [P.tok("sv")])
            P.i(v.tensor_copy(out=zns[:], in_=zni[:]), "sv")
            P.w(v, "v", [P.tok("sv")])
            P.i(v.tensor_tensor(out=zns[:], in0=zns[:], in1=znc[:], op=ALU.mult),
                "sv")
        P.sec("v", v_dn)
        zns_tok = P.tok("sv")

        def y_dn2(y):
            P.w(y, "y", [zns_tok])
            P.i(y.dma_start(out=zsd_d, in_=zns[:]), "qy")
        P.sec("y", y_dn2)
        zsd_w = P.tok("qy")

        for itl in range(NIT_SH):
            def y_dn3(y):
                P.w(y, "y", [zsd_w, P.tok("sv")])
                P.i(y.dma_start(out=dnq[:], in_=dq_in[itl * 128:(itl + 1) * 128, :]),
                    "qy")
                P.i(y.dma_start(out=dns[:],
                                in_=sap(ds_in, [[0, 128], [1, HID]], itl * HID)),
                    "qy")
                P.i(y.dma_start(out=dnz[:],
                                in_=sap(zsd_d, [[0, 128], [1, HID]], itl * HID)),
                    "qy")
            P.sec("y", y_dn3)
            ql = P.tok("qy")

            def v_dn3(v):
                P.w(v, "v", [ql, P.tok("qs")])
                for sft in range(8):
                    P.i(v.tensor_scalar(out=dni[:, AWQ_INV[sft]::8], in0=dnq[:],
                                        scalar1=4 * sft, scalar2=15,
                                        op0=ALU.logical_shift_right,
                                        op1=ALU.bitwise_and), "sv")
                P.w(v, "v", [P.tok("sv")])
                P.i(v.tensor_copy(out=dnn[:], in_=dni[:]), "sv")
                P.w(v, "v", [P.tok("sv")])
                P.i(v.tensor_tensor(out=dnn[:], in0=dnn[:], in1=dns[:],
                                    op=ALU.mult), "sv")
                P.w(v, "v", [P.tok("sv")])
                P.i(v.tensor_tensor(out=dnw[:], in0=dnn[:], in1=dnz[:],
                                    op=ALU.subtract), "sv")
            P.sec("v", v_dn3)
            wt = P.tok("sv")

            def s_dn3(s):
                P.w(s, "s", [wt])
                P.i(s.dma_start(out=wdsl_d[itl * 128:(itl + 1) * 128, :],
                                in_=dnw[:]), "qs")
            P.sec("s", s_dn3)
        ph0c.free()

        ph0d = nc.auto_arena()
        wof = ph0d.sbuf_tensor("wof", [128, HID], F32)
        wob = ph0d.sbuf_tensor("wob", [128, HID], BF16)
        wnw = ph0d.sbuf_tensor("wnw", [128, NDPC], F32)
        ph0d.allocate()
        bar0d = P.bar()

        def y_nw(y):
            P.w(y, "y", bar0d)
            P.i(y.dma_start(out=wnw[:], in_=normw_in), "qy")
        P.sec("y", y_nw)
        nw_tok = P.tok("qy")

        for dt_ in range(NDPC):
            def y_wo(y):
                P.w(y, "y", [nw_tok, P.tok("sv")])
                P.i(y.dma_start(out=wof[:],
                                in_=woutt_in[dt_ * 128:(dt_ + 1) * 128, :]), "qy")
            P.sec("y", y_wo)
            wl = P.tok("qy")

            def v_wo(v):
                P.w(v, "v", bar0d + [wl, P.tok("qs")])
                P.i(v.tensor_scalar(out=wob[:], in0=wof[:],
                                    scalar1=wnw[:, dt_:dt_ + 1],
                                    scalar2=None, op0=ALU.mult), "sv")
            P.sec("v", v_wo)
            wc = P.tok("sv")

            def s_wo(s):
                P.w(s, "s", [wc])
                P.i(s.dma_start(out=woT_sl_d[dt_ * 128:(dt_ + 1) * 128, :],
                                in_=wob[:]), "qs")
            P.sec("s", s_wo)
        wo_w = P.tok("qs")

        def g_agw(g):
            P.w(g, "g", [wo_w, P.tok("qs"), P.tok("qy")])
            P.icc(g.collective_compute(
                "AllGather", ALU.bypass, replica_groups=[list(range(NCORES))],
                ins=[gsl_d.opt()], outs=[g_all_d.opt()]))
            P.icc(g.collective_compute(
                "AllGather", ALU.bypass, replica_groups=[list(range(NCORES))],
                ins=[usl_d.opt()], outs=[u_all_d.opt()]))
            P.icc(g.collective_compute(
                "AllGather", ALU.bypass, replica_groups=[list(range(NCORES))],
                ins=[wdsl_d.opt()], outs=[wd_all_d.opt()]))
            P.icc(g.collective_compute(
                "AllGather", ALU.bypass, replica_groups=[list(range(NCORES))],
                ins=[woT_sl_d.opt()], outs=[woT_all_d.opt()]))
        P.sec("g", g_agw)
        ag_w = P.tok("cc")
        ph0d.free()

        a1 = nc.auto_arena()
        wa1_sb = a1.sbuf_tensor("wa1_sb", [128, NDT * CA1], BF16)
        wstg = a1.sbuf_tensor("wstg", [128, CA1], F32)
        xfm_sb = a1.sbuf_tensor("xfm_sb", [128, NDT * ABLK], BF16)
        xbc_sb = a1.sbuf_tensor("xbc_sb", [128, NCONVT * AH], BF16)
        xba_sb = a1.sbuf_tensor("xba_sb", [128, NCONVT * ABLK], BF16)
        cacc_sb = a1.sbuf_tensor("cacc_sb", [128, 2 * ABLK], F32)
        dtp_sb = a1.sbuf_tensor("dtp_sb", [HPC, ABLK], F32)
        dts_sb = a1.sbuf_tensor("dts_sb", [HPC, ABLK], F32)
        dta_sb = a1.sbuf_tensor("dta_sb", [HPC, ABLK], F32)
        stk_sb = a1.sbuf_tensor("stk_sb", [128, ABLK], F32)
        dtf_sb = a1.sbuf_tensor("dtf_sb", [HPC, ABLK], F32)
        actok_sb = a1.sbuf_tensor("actok_sb", [128, 128], F32)
        srep_sb = a1.sbuf_tensor("srep_sb", [128, HPC * CSS], F32)
        prep_sb = a1.sbuf_tensor("prep_sb", [128, HPC], F32)
        acm_sb = srep_sb
        lall_sb = a1.sbuf_tensor("lall_sb", [128, HPC * CSS], BF16)
        gm_sb = a1.sbuf_tensor("gm_sb", [128, CSS], BF16)
        gml_sb = a1.sbuf_tensor("gml_sb", [128, HPC * CSS], BF16)
        xtok_sb = a1.sbuf_tensor("xtok_sb", [128, DPC], BF16)
        btok_sb = a1.sbuf_tensor("btok_sb", [128, NST], BF16)
        xs_sb = a1.sbuf_tensor("xs_sb", [128, DPC], BF16)
        xsd_sb = a1.sbuf_tensor("xsd_sb", [128, DPC], BF16)
        z_sb = a1.sbuf_tensor("z_sb", [128, DPC], F32)
        zbf_sb = a1.sbuf_tensor("zbf_sb", [128, DPC], BF16)
        zt_sb = a1.sbuf_tensor("zt_sb", [128, DPC], F32)
        yo_sb = a1.sbuf_tensor("yo_sb", [128, DPC], F32)
        dtm_sb = a1.sbuf_tensor("dtm_sb", [128, DPC], F32)
        yc_sb = yo_sb
        yout_sb = a1.sbuf_tensor("yout_sb", [128, DPC], BF16)
        a1.allocate()

        bar0 = P.bar()
        with ExitStack() as _esa1:
            PA0 = _esa1.enter_context(nc.psum_tensor("PA0", [128, 512], F32))
            PA1 = _esa1.enter_context(nc.psum_tensor("PA1", [128, 512], F32))
            PTRB = _esa1.enter_context(nc.psum_tensor("PTRB", [128, DPC], BF16))
            PYD = _esa1.enter_context(nc.psum_tensor("PYD", [128, DPC], F32))
            PSC = _esa1.enter_context(nc.psum_tensor("PSC", [128, DPC], F32))
            def v_stk0(v):
                P.w(v, "v", bar0)
                P.i(v.memset(stk_sb[:], 0.0), "sv")
            P.sec("v", v_stk0)

            wconv_tok = None
            for kt in range(NDT):
                def y_w(y):
                    P.w(y, "y", bar0 + [wconv_tok])
                    P.i(y.dma_start(out=wstg[:],
                                    in_=wa1_in[kt * 128:(kt + 1) * 128, :]), "qy")
                P.sec("y", y_w)
                wl = P.tok("qy")

                def v_w(v):
                    P.w(v, "v", bar0 + [wl])
                    P.i(v.tensor_copy(out=wa1_sb[:, kt * CA1:(kt + 1) * CA1],
                                      in_=wstg[:]), "sv")
                P.sec("v", v_w)
                wconv_tok = P.tok("sv")
            wa1_tok = P.tok("sv")

            ftiles = [(d * 128, 128, "x", d) for d in range(NDPC)]
            ftiles += [(DPC, NST, "B", NDPC), (DPC + NST, NST, "C", NDPC + 1),
                       (DPC + 2 * NST, HPC, "dt", -1)]

            st = dict(xfm=None, xfm_pe=None,
                      pa_mm=[None, None], pa_ev=[None, None],
                      z=None, zuser=None, halo=None, conv=None,
                      xbc_user=None, dtuser=None, stk_pe=None, xba_pe=None,
                      yd_ev=None, sc_ev=None)
            yraw_w = None

            for b in range(NAB):
                T0 = b * ABLK
                cb_src = T0 // TPC
                tloc = T0 - cb_src * TPC
                batch_start = (b % C["AB_PER_BATCH"]) == 0

                def y_xfm(y):
                    P.w(y, "y", [ag_xln, st["xfm_pe"]])
                    src = sap(xlnT_all_d, [[TPC, 128], [128 * TPC, NDT], [1, ABLK]],
                              cb_src * HID * TPC + tloc)
                    P.i(y.dma_start(
                        out=xfm_sb[:].rearrange("p (k t) -> p k t", k=NDT),
                        in_=src), "qy")
                P.sec("y", y_xfm)
                st["xfm"] = P.tok("qy")

                if batch_start:
                    def v_rst(v):
                        P.w(v, "v", [st["zuser"], st["z"], st["xbc_user"]])
                        P.i(v.memset(z_sb[:], 0.0), "sv")
                        halo = sap(xbc_sb, [[NCONVT * AH, 128], [AH, NCONVT],
                                            [1, KC - 1]])
                        P.i(v.memset(halo, 0.0), "sv")
                    P.sec("v", v_rst)
                    st["z"] = P.tok("sv")
                    st["halo"] = P.tok("sv")

                evac_toks = []
                for fi, (off, fw, kind, dpos) in enumerate(ftiles):
                    pp = (PA0, PA1)[fi % 2]
                    pi = fi % 2

                    def t_ip(t):
                        P.w(t, "t", [wa1_tok, st["xfm"], st["pa_ev"][pi]])
                        for kt in range(NDT):
                            mm = t.matmul(
                                pp[0:fw, 0:ABLK],
                                wa1_sb[:, kt * CA1 + off:kt * CA1 + off + fw],
                                xfm_sb[:, kt * ABLK:(kt + 1) * ABLK],
                                start=(kt == 0), stop=(kt == NDT - 1))
                        P.i(mm, "sp")
                    P.sec("t", t_ip)
                    st["pa_mm"][pi] = P.tok("sp")
                    if fi == len(ftiles) - 1:
                        st["xfm_pe"] = P.tok("sp")

                    def v_ev(v):
                        P.w(v, "v", [st["pa_mm"][pi], st["halo"], st["conv"]])
                        if kind == "dt":
                            P.i(v.tensor_copy(out=dtp_sb[:], in_=pp[0:HPC, 0:ABLK]),
                                "sv")
                        else:
                            dst = xbc_sb[:, dpos * AH + KC - 1:dpos * AH + AH]
                            P.i(v.tensor_copy(out=dst, in_=pp[:, 0:ABLK]), "sv")
                    P.sec("v", v_ev)
                    st["pa_ev"][pi] = P.tok("sv")
                    evac_toks.append(P.tok("sv"))
                xbc_tok = P.tok("sv")

                conv_sil = []
                for ct in range(NCONVT):
                    acc = cacc_sb[:, (ct % 2) * ABLK:(ct % 2 + 1) * ABLK]

                    def v_cv(v):
                        deps = [evac_toks[ct], st["halo"], st["xba_pe"]]
                        if ct >= 2:
                            deps.append(conv_sil[ct - 2])
                        P.w(v, "v", deps)
                        base = ct * AH
                        P.i(v.tensor_scalar(out=acc, in0=xbc_sb[:, base:base + ABLK],
                                            scalar1=convw_sb[:, ct * KC:ct * KC + 1],
                                            scalar2=convb_sb[:, ct:ct + 1],
                                            op0=ALU.mult, op1=ALU.add), "sv")
                        for tau in range(1, KC):
                            P.w(v, "v", [P.tok("sv")])
                            P.i(v.scalar_tensor_tensor(
                                out=acc, in0=xbc_sb[:, base + tau:base + tau + ABLK],
                                scalar=convw_sb[:, ct * KC + tau:ct * KC + tau + 1],
                                in1=acc, op0=ALU.mult, op1=ALU.add), "sv")
                    P.sec("v", v_cv)
                    cv_tok = P.tok("sv")

                    def s_cv(s):
                        P.w(s, "s", [cv_tok])
                        P.i(s.activation(out=xba_sb[:, ct * ABLK:(ct + 1) * ABLK],
                                         in_=acc, func=AF.Silu), "ss")
                    P.sec("s", s_cv)
                    conv_sil.append(P.tok("ss"))
                conv_tok = P.tok("ss")
                st["conv"] = conv_tok

                def v_hl(v):
                    P.w(v, "v", [conv_tok, xbc_tok])
                    src = sap(xbc_sb, [[NCONVT * AH, 128], [AH, NCONVT],
                                       [1, KC - 1]], ABLK)
                    dst = sap(xbc_sb, [[NCONVT * AH, 128], [AH, NCONVT],
                                       [1, KC - 1]])
                    P.i(v.tensor_copy(out=dst, in_=src), "sv")
                P.sec("v", v_hl)
                st["halo"] = P.tok("sv")

                def v_dt(v):
                    P.w(v, "v", [st["pa_ev"][(len(ftiles) - 1) % 2], a_tok,
                                 st["stk_pe"], st["dtuser"]])
                    P.i(v.tensor_scalar(out=dts_sb[:], in0=dtp_sb[:],
                                        scalar1=dtb_sb[:], scalar2=None,
                                        op0=ALU.add), "sv")
                P.sec("v", v_dt)
                dt1 = P.tok("sv")

                def s_dt(s):
                    P.w(s, "s", [dt1])
                    P.i(s.activation(out=dta_sb[:], in_=dts_sb[:], func=AF.Abs),
                        "ss")
                    P.w(s, "s", [P.tok("ss")])
                    P.i(s.activation(out=dta_sb[:], in_=dta_sb[:], func=AF.Exp,
                                     scale=-1.0), "ss")
                    P.w(s, "s", [P.tok("ss")])
                    P.i(s.activation(out=dta_sb[:], in_=dta_sb[:], func=AF.Ln,
                                     bias=1.0), "ss")
                P.sec("s", s_dt)
                dt2 = P.tok("ss")

                def v_dt2(v):
                    P.w(v, "v", [dt2])
                    P.i(v.tensor_relu(out=dts_sb[:], in_=dts_sb[:]), "sv")
                    P.w(v, "v", [P.tok("sv")])
                    P.i(v.tensor_tensor(out=dtf_sb[:],
                                        in0=dts_sb[:], in1=dta_sb[:], op=ALU.add),
                        "sv")
                    P.w(v, "v", [P.tok("sv")])
                    P.i(v.tensor_scalar(out=dta_sb[:],
                                        in0=dtf_sb[:],
                                        scalar1=acol_sb[:], scalar2=None,
                                        op0=ALU.mult), "sv")
                    for s_ in range(NSUB):
                        P.w(v, "v", [P.tok("sv")])
                        sl = slice(s_ * CSS, (s_ + 1) * CSS)
                        P.i(v.tensor_tensor_scan(out=stk_sb[0:HPC, sl],
                                                 data0=dta_sb[:, sl],
                                                 data1=dta_sb[:, sl], initial=0.0,
                                                 op0=ALU.add, op1=ALU.bypass), "sv")
                P.sec("v", v_dt2)
                ac_tok = P.tok("sv")
                st["dtuser"] = ac_tok

                def s_dt2(s):
                    P.w(s, "s", [ac_tok])
                    for s_ in range(NSUB):
                        sl = slice(s_ * CSS, (s_ + 1) * CSS)
                        last = s_ * CSS + CSS - 1
                        P.i(s.activation(out=stk_sb[32:32 + HPC, sl],
                                         in_=stk_sb[0:HPC, sl], func=AF.Exp,
                                         scale=-1.0,
                                         bias=stk_sb[0:HPC, last:last + 1]), "ss")
                        P.i(s.activation(out=stk_sb[64:64 + HPC, sl],
                                         in_=stk_sb[0:HPC, sl], func=AF.Exp), "ss")
                P.sec("s", s_dt2)
                stk_tok = P.tok("ss")

                def s_dt3(s):
                    P.w(s, "s", [stk_tok])
                    for s_ in range(NSUB):
                        sl = slice(s_ * CSS, (s_ + 1) * CSS)
                        r0 = (b * NSUB + s_) * HPC
                        P.i(s.dma_start(out=acb_d[r0:r0 + HPC, :],
                                        in_=stk_sb[0:HPC, sl]), "qs")
                        last = s_ * CSS + CSS - 1
                        P.i(s.dma_start(
                            out=sap(prow_d, [[1, HPC], [0, 1]],
                                    (b * NSUB + s_) * HPC),
                            in_=stk_sb[64:64 + HPC, last:last + 1]), "qs")
                P.sec("s", s_dt3)
                acb_w = P.tok("qs")

                for s_ in range(NSUB):
                    sub = b * NSUB + s_
                    sc = s_ * CSS

                    def y_rep(y):
                        P.w(y, "y", [acb_w, P.tok("sv"), P.tok("ss")])
                        P.i(y.dma_start(out=srep_sb[:],
                                        in_=sap(acb_d, [[0, 128], [1, HPC * CSS]],
                                                sub * HPC * CSS)), "qy")
                        P.i(y.dma_start(out=prep_sb[:],
                                        in_=sap(prow_d, [[0, 128], [1, HPC]],
                                                sub * HPC)), "qy")
                    P.sec("y", y_rep)
                    rep_tok = P.tok("qy")

                    def t_tr(t):
                        P.w(t, "t", [conv_tok, P.tok("sv")])
                        for d in range(NDPC):
                            tr = t.transpose(PTRB[:, d * 128:(d + 1) * 128],
                                             xba_sb[:, d * ABLK + sc:d * ABLK + sc + CSS],
                                             ident_bf[:])
                        P.i(tr, "sp")
                    P.sec("t", t_tr)
                    xtr = P.tok("sp")

                    def v_tr(v):
                        P.w(v, "v", [xtr])
                        P.i(v.tensor_copy(out=xtok_sb[:], in_=PTRB[:]), "sv")
                    P.sec("v", v_tr)
                    xtok_tok = P.tok("sv")

                    def t_tr2(t):
                        P.w(t, "t", [xtok_tok, stk_tok])
                        tr = t.transpose(PTRB[:, 0:NST],
                                         xba_sb[:, NDPC * ABLK + sc:NDPC * ABLK + sc + CSS],
                                         ident_bf[:])
                        P.i(tr, "sp")
                        P.w(t, "t", [P.tok("sv")])
                        t.transpose(PSC[:, 0:64 + HPC],
                                    stk_sb[0:64 + HPC, sc:sc + CSS],
                                    ident_f[0:64 + HPC, 0:64 + HPC])
                        tr2 = t.transpose(PSC[:, 96:96 + HPC],
                                          dtf_sb[:, sc:sc + CSS],
                                          ident_f[0:HPC, 0:HPC])
                        P.i(tr2, "sp")
                    P.sec("t", t_tr2)
                    btr = ("sp", P.cnt["sp"] - 1)
                    str_ = P.tok("sp")
                    st["stk_pe"] = str_

                    def v_tr2(v):
                        P.w(v, "v", [btr])
                        P.i(v.tensor_copy(out=btok_sb[:], in_=PTRB[:, 0:NST]), "sv")
                        P.w(v, "v", [str_])
                        P.i(v.tensor_copy(out=actok_sb[:, 0:64 + HPC],
                                          in_=PSC[:, 0:64 + HPC]), "sv")
                        P.i(v.tensor_copy(out=actok_sb[:, 96:96 + HPC],
                                          in_=PSC[:, 96:96 + HPC]), "sv")
                    P.sec("v", v_tr2)
                    btok_tok = ("sv", P.cnt["sv"] - 1)
                    actok_tok = P.tok("sv")

                    def v_fo(v):
                        P.w(v, "v", [actok_tok, st["z"], dexp_tok, rep_tok, ut_tok])
                        P.i(v.tensor_tensor(out=xs_sb[:], in0=xtok_sb[:],
                                            in1=fexp(actok_sb[:], HPC, HD, 96),
                                            op=ALU.mult), "sv")
                        P.w(v, "v", [P.tok("sv")])
                        P.i(v.tensor_tensor(out=xsd_sb[:], in0=xs_sb[:],
                                            in1=fexp(actok_sb[:], HPC, HD, 32),
                                            op=ALU.mult), "sv")
                        P.i(v.tensor_copy(out=zbf_sb[:], in_=z_sb[:]), "sv")
                        P.i(v.tensor_tensor(out=acm_sb[:], in0=srep_sb[:],
                                            in1=fexp(actok_sb[:], HPC, CSS, 0),
                                            op=ALU.subtract), "sv")
                        P.w(v, "v", [P.tok("sv")])
                        P.i(v.tensor_tensor(out=acm_sb[:], in0=acm_sb[:],
                                            in1=frep(ut_sb[:], CSS, HPC),
                                            op=ALU.add), "sv")
                    P.sec("v", v_fo)
                    folds_tok = P.tok("sv")
                    zbf_tok = ("sv", P.cnt["sv"] - 2)

                    def s_l(s):
                        P.w(s, "s", [folds_tok])
                        P.i(s.activation(out=lall_sb[:], in_=acm_sb[:],
                                         func=AF.Exp), "ss")
                    P.sec("s", s_l)
                    lall_tok = P.tok("ss")

                    def t_gm(t):
                        P.w(t, "t", [conv_tok, st["pa_ev"][0]])
                        bfm = xba_sb[:, NDPC * ABLK + sc:NDPC * ABLK + sc + CSS]
                        cfm = xba_sb[:, (NDPC + 1) * ABLK + sc:(NDPC + 1) * ABLK + sc + CSS]
                        P.i(t.matmul(PA0[:, 0:CSS], bfm, cfm, start=True,
                                     stop=True), "sp")
                    P.sec("t", t_gm)
                    gm_mm = P.tok("sp")

                    def v_gm(v):
                        P.w(v, "v", [gm_mm])
                        P.i(v.tensor_copy(out=gm_sb[:], in_=PA0[:, 0:CSS]), "sv")
                        P.w(v, "v", [P.tok("sv"), lall_tok])
                        for h in range(HPC):
                            P.i(v.tensor_tensor(out=gml_sb[:, h * CSS:(h + 1) * CSS],
                                                in0=gm_sb[:],
                                                in1=lall_sb[:, h * CSS:(h + 1) * CSS],
                                                op=ALU.mult), "sv")
                    P.sec("v", v_gm)
                    gm_ev = ("sv", P.cnt["sv"] - HPC)
                    gml_tok = P.tok("sv")
                    st["pa_ev"][0] = gm_ev

                    def t_y(t):
                        P.w(t, "t", [gml_tok, btok_tok, zbf_tok, folds_tok,
                                     st["pa_ev"][0], st["pa_ev"][1],
                                     st["yd_ev"], st["sc_ev"]])
                        for h in range(HPC):
                            mm = t.matmul(PYD[:, h * HD:(h + 1) * HD],
                                          gml_sb[:, h * CSS:(h + 1) * CSS],
                                          xs_sb[:, h * HD:(h + 1) * HD],
                                          start=((h * HD) % 512 == 0),
                                          stop=(((h + 1) * HD) % 512 == 0
                                                or h == HPC - 1))
                        P.i(mm, "sp")
                        cfm = xba_sb[:, (NDPC + 1) * ABLK + sc:(NDPC + 1) * ABLK + sc + CSS]
                        stepd = min(512, DPC)
                        for i0 in range(0, DPC, stepd):
                            pp = (PA1, PA0)[(i0 // stepd) % 2]
                            P.i(t.matmul(pp[:, 0:stepd], cfm,
                                         zbf_sb[:, i0:i0 + stepd],
                                         start=True, stop=True), "sp")
                        for i0 in range(0, DPC, stepd):
                            mm = t.matmul(PSC[:, i0:i0 + stepd], btok_sb[:],
                                          xsd_sb[:, i0:i0 + stepd],
                                          start=True, stop=True)
                        P.i(mm, "sp")
                    P.sec("t", t_y)
                    sc_mm = P.tok("sp")
                    st["xba_pe"] = sc_mm
                    nyo = max(1, DPC // 512)
                    yo_mms = [("sp", P.cnt["sp"] - 1 - nyo + 1 + i)
                              for i in range(nyo)]
                    yd_mm = ("sp", P.cnt["sp"] - 1 - nyo)

                    def v_y(v):
                        stepd = min(512, DPC)
                        P.w(v, "v", yo_mms + [yd_mm])
                        for i in range(nyo):
                            pp = (PA1, PA0)[i % 2]
                            hph = stepd // HD
                            P.i(v.tensor_tensor(
                                out=yo_sb[:, i * stepd:(i + 1) * stepd],
                                in0=pp[:, 0:stepd],
                                in1=fexp(actok_sb[:], hph, HD,
                                         64 + i * hph),
                                op=ALU.mult), "sv")
                        P.i(v.tensor_tensor(out=dtm_sb[:], in0=xtok_sb[:],
                                            in1=dexp_sb[:], op=ALU.mult), "sv")
                        P.w(v, "v", [P.tok("sv")])
                        P.i(v.tensor_tensor(out=yc_sb[:], in0=PYD[:], in1=yo_sb[:],
                                            op=ALU.add), "sv")
                        P.w(v, "v", [P.tok("sv")])
                        P.i(v.tensor_tensor(out=yout_sb[:], in0=yc_sb[:],
                                            in1=dtm_sb[:], op=ALU.add), "sv")
                    P.sec("v", v_y)
                    y_tok = P.tok("sv")
                    st["pa_ev"][0] = y_tok
                    st["pa_ev"][1] = y_tok
                    st["yd_ev"] = y_tok

                    def s_y(s):
                        P.w(s, "s", [y_tok])
                        P.i(s.dma_start(out=y_raw_d[T0 + sc:T0 + sc + CSS, :],
                                        in_=yout_sb[:]), "qs")
                    P.sec("s", s_y)
                    yraw_w = P.tok("qs")

                    def v_z(v):
                        P.w(v, "v", [sc_mm, rep_tok, zbf_tok])
                        P.i(v.tensor_tensor(out=zt_sb[:], in0=z_sb[:],
                                            in1=fexp(prep_sb[:], HPC, HD),
                                            op=ALU.mult), "sv")
                        P.w(v, "v", [P.tok("sv")])
                        P.i(v.tensor_tensor(out=z_sb[:], in0=zt_sb[:], in1=PSC[:],
                                            op=ALU.add), "sv")
                    P.sec("v", v_z)
                    st["z"] = P.tok("sv")
                    st["zuser"] = P.tok("sv")
                    st["sc_ev"] = P.tok("sv")
                st["xbc_user"] = P.tok("sp")
            a1_done = yraw_w
        a1.free()

        a2 = nc.auto_arena()
        wa2_sb = a2.sbuf_tensor("wa2_sb", [128, NDT * DPC], BF16)
        wstg2 = a2.sbuf_tensor("wstg2", [128, DPC], F32)
        xfm2_sb = a2.sbuf_tensor("xfm2_sb", [128, 2 * NDT * ABLK], BF16)
        gact_sb = a2.sbuf_tensor("gact_sb", [128, NDPC * ABLK], BF16)
        gtok_sb = a2.sbuf_tensor("gtok_sb", [128, DPC], BF16)
        yr_sb = a2.sbuf_tensor("yr_sb", [128, DPC], BF16)
        yg_sb = a2.sbuf_tensor("yg_sb", [128, DPC], F32)
        ysq_sb = a2.sbuf_tensor("ysq_sb", [128, DPC], F32)
        ssum_sb = a2.sbuf_tensor("ssum_sb", [128, 1], F32)
        rstd_sb = a2.sbuf_tensor("rstd_sb", [128, 1], F32)
        yn_sb = a2.sbuf_tensor("yn_sb", [128, DPC], BF16)
        ytr_sb = a2.sbuf_tensor("ytr_sb", [128, NTOK], BF16)
        a2.allocate()

        bar1 = P.bar()
        with ExitStack() as _esa2:
            PB0_ = _esa2.enter_context(nc.psum_tensor("PB0_", [128, 512], F32))
            PB1_ = _esa2.enter_context(nc.psum_tensor("PB1_", [128, 512], F32))
            PTR2 = _esa2.enter_context(nc.psum_tensor("PTR2", [128, DPC], BF16))
            wconv2 = None
            for kt in range(NDT):
                def y_w2(y):
                    P.w(y, "y", bar1 + [wconv2])
                    P.i(y.dma_start(out=wstg2[:],
                                    in_=wa2_in[kt * 128:(kt + 1) * 128, :]), "qy")
                P.sec("y", y_w2)
                wl = P.tok("qy")

                def v_w2(v):
                    P.w(v, "v", bar1 + [wl])
                    P.i(v.tensor_copy(out=wa2_sb[:, kt * DPC:(kt + 1) * DPC],
                                      in_=wstg2[:]), "sv")
                P.sec("v", v_w2)
                wconv2 = P.tok("sv")
            wa2_tok = P.tok("sv")

            st2 = dict(xfm=None, xfm_pe=[None, None], ev=[None, None], tr_ev=None,
                       yn_ev=None)
            ytok_w = None
            for b in range(NAB):
                T0 = b * ABLK
                cb_src = T0 // TPC
                tloc = T0 - cb_src * TPC
                bpar = b % 2
                xbase2 = bpar * NDT * ABLK

                def y_xfm2(y):
                    P.w(y, "y", [ag_xln, st2["xfm_pe"][bpar]])
                    src = sap(xlnT_all_d, [[TPC, 128], [128 * TPC, NDT], [1, ABLK]],
                              cb_src * HID * TPC + tloc)
                    P.i(y.dma_start(
                        out=xfm2_sb[:, xbase2:xbase2 + NDT * ABLK].rearrange(
                            "p (k t) -> p k t", k=NDT),
                        in_=src), ("qa", "qb")[bpar], "y")
                P.sec("y", y_xfm2)
                st2["xfm"] = P.tok(("qa", "qb")[bpar])

                for d in range(NDPC):
                    pp = (PB0_, PB1_)[d % 2]
                    pi = d % 2

                    def t_g(t):
                        P.w(t, "t", [wa2_tok, st2["xfm"], st2["ev"][pi]])
                        for kt in range(NDT):
                            mm = t.matmul(
                                pp[:],
                                wa2_sb[:, kt * DPC + d * 128:kt * DPC + (d + 1) * 128],
                                xfm2_sb[:, xbase2 + kt * ABLK:
                                        xbase2 + (kt + 1) * ABLK],
                                start=(kt == 0), stop=(kt == NDT - 1))
                        P.i(mm, "sp")
                    P.sec("t", t_g)
                    g_mm = P.tok("sp")
                    if d == NDPC - 1:
                        st2["xfm_pe"][bpar] = P.tok("sp")

                    def s_g(s):
                        P.w(s, "s", [g_mm])
                        P.i(s.activation(out=gact_sb[:, d * ABLK:(d + 1) * ABLK],
                                         in_=pp[:], func=AF.Silu), "ss")
                    P.sec("s", s_g)
                    st2["ev"][pi] = P.tok("ss")
                gact_tok = P.tok("ss")

                for s_ in range(NSUB):
                    sc = s_ * CSS
                    T0s = T0 + sc

                    def y_yr(y):
                        P.w(y, "y", [a1_done, P.tok("sv")])
                        P.i(y.dma_start(out=yr_sb[:], in_=y_raw_d[T0s:T0s + CSS, :]),
                            "qy")
                    P.sec("y", y_yr)
                    yr_tok = P.tok("qy")

                    def t_gt(t):
                        P.w(t, "t", [gact_tok, st2["tr_ev"]])
                        for d in range(NDPC):
                            tr = t.transpose(PTR2[:, d * 128:(d + 1) * 128],
                                             gact_sb[:, d * ABLK + sc:d * ABLK + sc + CSS],
                                             ident_bf[:])
                        P.i(tr, "sp")
                    P.sec("t", t_gt)
                    gt_tok = P.tok("sp")

                    def v_g2(v):
                        P.w(v, "v", [gt_tok])
                        P.i(v.tensor_copy(out=gtok_sb[:], in_=PTR2[:]), "sv")
                        P.w(v, "v", [P.tok("sv"), yr_tok])
                        P.i(v.tensor_tensor(out=yg_sb[:], in0=yr_sb[:],
                                            in1=gtok_sb[:], op=ALU.mult), "sv")
                        P.w(v, "v", [P.tok("sv")])
                        P.i(v.tensor_tensor(out=ysq_sb[:], in0=yg_sb[:],
                                            in1=yg_sb[:], op=ALU.mult), "sv")
                        P.w(v, "v", [P.tok("sv")])
                        P.i(v.tensor_reduce(out=ssum_sb[:], in_=ysq_sb[:], axis=AXX,
                                            op=ALU.add), "sv")
                    P.sec("v", v_g2)
                    st2["tr_ev"] = ("sv", P.cnt["sv"] - 3)
                    ssm_tok = P.tok("sv")

                    def v_r0(v):
                        P.w(v, "v", [ssm_tok])
                        P.i(v.tensor_scalar_mul(out=rstd_sb[:], in0=ssum_sb[:],
                                                scalar1=1.0 / DPC), "sv")
                    P.sec("v", v_r0)
                    sc_tok2 = P.tok("sv")

                    def s_r(s):
                        P.w(s, "s", [sc_tok2])
                        P.i(s.activation(out=rstd_sb[:], in_=rstd_sb[:], func=AF.Ln,
                                         bias=eps_sb[:]), "ss")
                    P.sec("s", s_r)
                    ln_tok2 = P.tok("ss")

                    def v_r1(v):
                        P.w(v, "v", [ln_tok2])
                        P.i(v.tensor_scalar_mul(out=rstd_sb[:], in0=rstd_sb[:],
                                                scalar1=-0.5), "sv")
                    P.sec("v", v_r1)
                    ng_tok2 = P.tok("sv")

                    def s_r2(s):
                        P.w(s, "s", [ng_tok2])
                        P.i(s.activation(out=rstd_sb[:], in_=rstd_sb[:],
                                         func=AF.Exp), "ss")
                    P.sec("s", s_r2)
                    rs_tok = P.tok("ss")

                    def v_n(v):
                        P.w(v, "v", [rs_tok, st2["yn_ev"]])
                        P.i(v.tensor_scalar(out=yn_sb[:], in0=yg_sb[:],
                                            scalar1=rstd_sb[:], scalar2=None,
                                            op0=ALU.mult), "sv")
                    P.sec("v", v_n)
                    yn_tok = P.tok("sv")

                    def s_yw(s):
                        P.w(s, "s", [yn_tok])
                        P.i(s.dma_start(out=y_tok_d[T0s:T0s + CSS, :], in_=yn_sb[:]),
                            "qs")
                    P.sec("s", s_yw)
                    st2["yn_ev"] = P.tok("qs")
            ytok_w = P.tok("qs")

            def y_ytr(y):
                P.w(y, "y", [ytok_w])
                for dt_ in range(NDPC):
                    P.w(y, "y", [P.tok("qy")])
                    P.i(y.dma_start_transpose(
                        out=ytr_sb[:], in_=y_tok_d[:, dt_ * 128:(dt_ + 1) * 128]),
                        "qy")
                    P.w(y, "y", [P.tok("qy")])
                    P.i(y.dma_start(out=yT_sl_d[dt_ * 128:(dt_ + 1) * 128, :],
                                    in_=ytr_sb[:]), "qy")
            P.sec("y", y_ytr)
            ya2a_w = P.tok("qy")

            def g_agy(g):
                P.w(g, "g", [ya2a_w])
                P.icc(g.collective_compute(
                    "AllGather", ALU.bypass, replica_groups=[list(range(NCORES))],
                    ins=[yT_sl_d.opt()], outs=[y_all_d.opt()]))
            P.sec("g", g_agy)
            ag_y = P.tok("cc")

            def y_slice(y):
                P.w(y, "y", [ag_y])
                off = y.partition_id() * TPC
                src2 = bass.AP(y_all_d.tensor, off,
                               [[NTOK, 128], [128 * NTOK, NKD], [1, TPC]])
                dst2 = sap(y_b_d, [[TPC, 128], [128 * TPC, NKD], [1, TPC]])
                P.i(y.dma_start(out=dst2, in_=src2), "qy")
            P.sec("y", y_slice)
            ag_y = P.tok("qy")
        a2.free()

        SUPB1 = 128
        NSUPB1 = TPC // SUPB1
        QW = min(1024, HID)
        NQ = HID // QW
        NTTB1 = SUPB1 // 128
        b1 = nc.auto_arena()
        wo_sb = b1.sbuf_tensor("wo_sb", [128, NKD * QW], BF16)
        ysup_sb = b1.sbuf_tensor("ysup_sb", [128, 2 * NKD * SUPB1], BF16)
        xres_sb = b1.sbuf_tensor("xres_sb", [128, QW], F32)
        xn_sb = b1.sbuf_tensor("xn_sb", [128, QW], F32)
        b1.allocate()

        bar2 = P.bar()
        with ExitStack() as _esb1:
            PBQ = _esb1.enter_context(
                nc.psum_tensor("PBQ", [128, 2 * NTTB1 * QW], F32))
            xn_w = None
            y_pe = [None, None]
            wo_pe = None
            ev_b1 = {}
            for q in range(NQ):
                def s_wo2(s):
                    P.w(s, "s", bar2 + [ag_w, wo_pe])
                    src2 = sap(woT_all_d, [[HID, 128], [128 * HID, NKD], [1, QW]],
                               q * QW)
                    P.i(s.dma_start(
                        out=wo_sb[:].rearrange("p (k w) -> p k w", k=NKD),
                        in_=src2), "qc", "s")
                P.sec("s", s_wo2)
                wo_ld = P.tok("qc")

                for sup in range(NSUPB1):
                    t0 = sup * SUPB1
                    par = sup % 2
                    ybase = par * NKD * SUPB1

                    def y_ys(y):
                        P.w(y, "y", bar2 + [ag_y, y_pe[par]])
                        src2 = sap(y_b_d, [[TPC, 128], [128 * TPC, NKD], [1, SUPB1]],
                                   t0)
                        P.i(y.dma_start(
                            out=ysup_sb[:, ybase:ybase + NKD * SUPB1].rearrange(
                                "p (k t) -> p k t", k=NKD),
                            in_=src2), ("qa", "qb")[par], "y")
                    P.sec("y", y_ys)
                    y_ld = P.tok(("qa", "qb")[par])

                    for tt in range(NTTB1):
                        pbase = (par * NTTB1 + tt) * QW

                        def t_op(t):
                            P.w(t, "t", [y_ld, wo_ld, ev_b1.get((par, tt))])
                            for k in range(NKD):
                                for j in range(QW // 512):
                                    mm = t.matmul(
                                        PBQ[:, pbase + j * 512:pbase + (j + 1) * 512],
                                        ysup_sb[:, ybase + k * SUPB1 + tt * 128:
                                                ybase + k * SUPB1 + (tt + 1) * 128],
                                        wo_sb[:, k * QW + j * 512:k * QW + (j + 1) * 512],
                                        start=(k == 0), stop=(k == NKD - 1))
                            P.i(mm, "sp")
                        P.sec("t", t_op)
                        op_mm = P.tok("sp")
                        if tt == NTTB1 - 1:
                            y_pe[par] = op_mm
                            if sup == NSUPB1 - 1:
                                wo_pe = op_mm

                        def y_xr(y):
                            P.w(y, "y", [P.tok("sv")])
                            P.i(y.dma_start(
                                out=xres_sb[:],
                                in_=x_in[t0 + tt * 128:t0 + (tt + 1) * 128,
                                         q * QW:(q + 1) * QW]), "qy")
                        P.sec("y", y_xr)
                        xr_tok = P.tok("qy")

                        def v_xn(v):
                            P.w(v, "v", [op_mm, xr_tok, P.tok("qg")])
                            P.i(v.tensor_tensor(out=xn_sb[:],
                                                in0=PBQ[:, pbase:pbase + QW],
                                                in1=xres_sb[:], op=ALU.add), "sv")
                        P.sec("v", v_xn)
                        ev_b1[(par, tt)] = P.tok("sv")
                        xn_tok = P.tok("sv")

                        def g_xn(g):
                            P.w(g, "g", [xn_tok])
                            P.i(g.dma_start(
                                out=x_new_d[t0 + tt * 128:t0 + (tt + 1) * 128,
                                            q * QW:(q + 1) * QW],
                                in_=xn_sb[:]), "qg")
                        P.sec("g", g_xn)
                        xn_w = P.tok("qg")
        b1.free()
        xnew_done = xn_w

        b2 = nc.auto_arena()
        x2_sb = b2.sbuf_tensor("x2_sb", [128, HID], F32)
        sq2_sb = b2.sbuf_tensor("sq2_sb", [128, SQCH], F32)
        s21_sb = b2.sbuf_tensor("s21_sb", [128, 1], F32)
        s22_sb = b2.sbuf_tensor("s22_sb", [128, 1], F32)
        pt2_sb = b2.sbuf_tensor("pt2_sb", [128, 1], F32)
        m2_sb = b2.sbuf_tensor("m2_sb", [128, 1], F32)
        v2_sb = b2.sbuf_tensor("v2_sb", [128, 1], F32)
        r2_sb = b2.sbuf_tensor("r2_sb", [128, 1], F32)
        o2_sb = b2.sbuf_tensor("o2_sb", [128, HID], BF16)
        xl2T_sb = b2.sbuf_tensor("xl2T_sb", [128, NDT * TPC], BF16)
        b2.allocate()

        bar3 = P.bar()
        xln2_w = ln_pass(x_new_d, xln2_tok_d, TPC, x2_sb, sq2_sb, s21_sb, s22_sb,
                         pt2_sb, m2_sb, v2_sb, r2_sb, o2_sb, bar3 + [xnew_done])

        def y_l2t(y):
            P.w(y, "y", [xln2_w])
            for kt in range(NDT):
                P.i(y.dma_start_transpose(
                    out=xl2T_sb[:, kt * TPC:(kt + 1) * TPC],
                    in_=xln2_tok_d[:, kt * 128:(kt + 1) * 128]), "qy")
        P.sec("y", y_l2t)
        xl2T_tok = P.tok("qy")

        b3 = nc.auto_arena()
        wg_sb = b3.sbuf_tensor("wg_sb", [128, 2 * NDT * 128], BF16)
        wu_sb = b3.sbuf_tensor("wu_sb", [128, 2 * NDT * 128], BF16)
        hs_sb = b3.sbuf_tensor("hs_sb", [128, TPC], BF16)
        ho_sb = b3.sbuf_tensor("ho_sb", [128, TPC], BF16)
        b3.allocate()

        NTW = TPC // 512
        bar4 = P.bar()
        with ExitStack() as _esb3:
            PG_ = _esb3.enter_context(nc.psum_tensor("PG_", [128, 4 * 512], F32))
            PU_ = _esb3.enter_context(nc.psum_tensor("PU_", [128, 4 * 512], F32))
            w_pe = [None, None]
            ev3 = None
            h_w = None
            for it in range(NI):
                par = it % 2
                wslice = slice(par * NDT * 128, (par + 1) * NDT * 128)

                def s_wl(s):
                    P.w(s, "s", bar4 + [ag_w, w_pe[par]])
                    for wsb, wall in ((wg_sb, g_all_d), (wu_sb, u_all_d)):
                        src = sap(wall, [[I, 128], [128 * I, NDT], [1, 128]],
                                  it * 128)
                        P.i(s.dma_start(
                            out=wsb[:, wslice].rearrange("p (k i) -> p k i", k=NDT),
                            in_=src), ("qa", "qb")[par], "s")
                P.sec("s", s_wl)
                w_ld = P.tok(("qa", "qb")[par])

                def t_f(t):
                    P.w(t, "t", [w_ld, xl2T_tok, ev3])
                    for kt in range(NDT):
                        for tw in range(NTW):
                            mm = t.matmul(
                                PG_[:, tw * 512:(tw + 1) * 512],
                                wg_sb[:, par * NDT * 128 + kt * 128:
                                      par * NDT * 128 + (kt + 1) * 128],
                                xl2T_sb[:, kt * TPC + tw * 512:kt * TPC + (tw + 1) * 512],
                                start=(kt == 0), stop=(kt == NDT - 1))
                    P.i(mm, "sp")
                    for kt in range(NDT):
                        for tw in range(NTW):
                            mm = t.matmul(
                                PU_[:, tw * 512:(tw + 1) * 512],
                                wu_sb[:, par * NDT * 128 + kt * 128:
                                      par * NDT * 128 + (kt + 1) * 128],
                                xl2T_sb[:, kt * TPC + tw * 512:kt * TPC + (tw + 1) * 512],
                                start=(kt == 0), stop=(kt == NDT - 1))
                    P.i(mm, "sp")
                P.sec("t", t_f)
                u_mm = P.tok("sp")
                g_mm = ("sp", P.cnt["sp"] - 1)
                w_pe[par] = u_mm

                def s_sil(s):
                    P.w(s, "s", [g_mm, P.tok("sv")])
                    for tw in range(NTW):
                        P.i(s.activation(out=hs_sb[:, tw * 512:(tw + 1) * 512],
                                         in_=PG_[:, tw * 512:(tw + 1) * 512],
                                         func=AF.Silu), "ss")
                P.sec("s", s_sil)
                sil_tok = P.tok("ss")

                def v_h(v):
                    P.w(v, "v", [sil_tok, u_mm, P.tok("qg")])
                    for tw in range(NTW):
                        P.i(v.tensor_tensor(out=ho_sb[:, tw * 512:(tw + 1) * 512],
                                            in0=PU_[:, tw * 512:(tw + 1) * 512],
                                            in1=hs_sb[:, tw * 512:(tw + 1) * 512],
                                            op=ALU.mult), "sv")
                P.sec("v", v_h)
                ev3 = P.tok("sv")
                h_tok = P.tok("sv")

                def g_h(g):
                    P.w(g, "g", [h_tok])
                    P.i(g.dma_start(out=h_d[it * 128:(it + 1) * 128, :],
                                    in_=ho_sb[:]), "qg")
                P.sec("g", g_h)
                h_w = P.tok("qg")
            h_written = h_w
        b3.free()
        b2.free()

        b4 = nc.auto_arena()
        hsup_sb = b4.sbuf_tensor("hsup_sb", [128, NI * 512], BF16)
        KCH = -(-NI // 4)
        wd_sb = b4.sbuf_tensor("wd_sb", [128, 2 * KCH * 1024], BF16)
        xr4_sb = b4.sbuf_tensor("xr4_sb", [128, 1024], F32)
        o4_sb = b4.sbuf_tensor("o4_sb", [128, 1024], F32)
        b4.allocate()

        SUP4 = 512
        NSUP4 = TPC // SUP4
        SPW = min(1024, HID)
        NSP = HID // SPW
        bar5 = P.bar()
        with ExitStack() as _esb4:
            PD_ = _esb4.enter_context(nc.psum_tensor("PD_", [128, 8 * 512], F32))
            h_pe = None
            out_w = None
            ev4 = None
            for sup in range(NSUP4):
                t0 = sup * SUP4

                def y_h(y):
                    P.w(y, "y", bar5 + [h_written, h_pe])
                    src = sap(h_d, [[TPC, 128], [128 * TPC, NI], [1, 512]], t0)
                    P.i(y.dma_start(
                        out=hsup_sb[:].rearrange("p (k t) -> p k t", k=NI),
                        in_=src), "qy")
                P.sec("y", y_h)
                h_ld = P.tok("qy")

                for sp in range(NSP):
                    wd_toks = []
                    nchunk = -(-NI // KCH)
                    for ch in range(nchunk):
                        k0 = ch * KCH
                        k1 = min(NI, k0 + KCH)
                        par = ch % 2

                        def s_wd(s):
                            prev = wd_toks[ch - 2][1] if ch >= 2 else None
                            P.w(s, "s", bar5 + [ag_w, prev])
                            src = sap(wd_all_d,
                                      [[HID, 128], [128 * HID, k1 - k0], [1, SPW]],
                                      k0 * 128 * HID + sp * SPW)
                            P.i(s.dma_start(
                                out=wd_sb[:, par * KCH * 1024:
                                          par * KCH * 1024 + (k1 - k0) * SPW]
                                .rearrange("p (k w) -> p k w", k=k1 - k0),
                                in_=src), ("qa", "qb")[par], "s")
                        P.sec("s", s_wd)
                        wd_ld = P.tok(("qa", "qb")[par])

                        def t_dn(t):
                            P.w(t, "t", [wd_ld, h_ld, ev4])
                            for ki in range(k0, k1):
                                for tt in range(4):
                                    for j in range(SPW // 512):
                                        mm = t.matmul(
                                            PD_[:, (tt * (SPW // 512) + j) * 512:
                                                (tt * (SPW // 512) + j + 1) * 512],
                                            hsup_sb[:, ki * 512 + tt * 128:
                                                    ki * 512 + (tt + 1) * 128],
                                            wd_sb[:, par * KCH * 1024 + (ki - k0) * SPW
                                                  + j * 512:
                                                  par * KCH * 1024 + (ki - k0) * SPW
                                                  + (j + 1) * 512],
                                            start=(ki == 0), stop=(ki == NI - 1))
                            P.i(mm, "sp")
                        P.sec("t", t_dn)
                        wd_toks.append((wd_ld, P.tok("sp")))
                    dn_mm = P.tok("sp")
                    h_pe = dn_mm

                    for tt in range(4):
                        def y_x4(y):
                            P.w(y, "y", [P.tok("sv")])
                            P.i(y.dma_start(
                                out=xr4_sb[:, 0:SPW],
                                in_=x_new_d[t0 + tt * 128:t0 + (tt + 1) * 128,
                                            sp * SPW:(sp + 1) * SPW]), "qy")
                        P.sec("y", y_x4)
                        xr_tok = P.tok("qy")

                        def v_o(v):
                            P.w(v, "v", [dn_mm, xr_tok, P.tok("qs")])
                            for j in range(SPW // 512):
                                P.i(v.tensor_tensor(
                                    out=o4_sb[:, j * 512:(j + 1) * 512],
                                    in0=PD_[:, (tt * (SPW // 512) + j) * 512:
                                            (tt * (SPW // 512) + j + 1) * 512],
                                    in1=xr4_sb[:, j * 512:(j + 1) * 512],
                                    op=ALU.add), "sv")
                        P.sec("v", v_o)
                        o_tok = P.tok("sv")
                        ev4 = o_tok

                        def s_o(s):
                            P.w(s, "s", [o_tok])
                            P.i(s.dma_start(
                                out=out_t[t0 + tt * 128:t0 + (tt + 1) * 128,
                                          sp * SPW:(sp + 1) * SPW],
                                in_=o4_sb[:, 0:SPW]), "qs")
                        P.sec("s", s_o)
                        out_w = P.tok("qs")
        b4.free()

        def s_end(s):
            P.w(s, "s", [out_w, P.tok("qg"), P.tok("qy")])
        P.sec("s", s_end)

        def y_end(y):
            P.w(y, "y", [out_w])
        P.sec("y", y_end)

    return nc



def shard_inputs(cfg, inputs):
    c = cfg
    HID, I, Bsz, L = c["HID"], c["I"], c["B"], c["L"]
    DIN, NST, HPC, DPC = c["DIN"], c["NST"], c["HPC"], c["DPC"]
    TPC, QG, G = c["TPC"], c["QG"], c["G"]
    DSH, NGRP = c["DSH"], c["NGRP"]
    ISH, NIT_SH, IPAD = c["ISH"], c["NIT_SH"], c["IPAD"]
    HP8 = HID // 8

    x = np.ascontiguousarray(
        np.asarray(inputs["L0_attn_hidden_states"]).reshape(Bsz * L, HID))
    wi = np.asarray(inputs["L0_attn_in_proj_w"])
    convw = np.asarray(inputs["L0_attn_conv_w"])[:, 0, :]
    convb = np.asarray(inputs["L0_attn_conv_b"])
    dtb = np.asarray(inputs["L0_attn_dt_bias"])
    alog = np.asarray(inputs["L0_attn_A_log"])
    dvec = np.asarray(inputs["L0_attn_D"])
    normw = np.asarray(inputs["L0_attn_norm_w"])
    wo = np.asarray(inputs["L0_attn_out_proj_w"])
    gq = np.asarray(inputs["L0_ffn_gate_qweight"])
    gs = np.asarray(inputs["L0_ffn_gate_scales"])
    gz = np.asarray(inputs["L0_ffn_gate_qzeros"])
    uq = np.asarray(inputs["L0_ffn_up_qweight"])
    us = np.asarray(inputs["L0_ffn_up_scales"])
    uz = np.asarray(inputs["L0_ffn_up_qzeros"])
    dq = np.asarray(inputs["L0_ffn_down_qweight"])
    ds = np.asarray(inputs["L0_ffn_down_scales"])
    dz = np.asarray(inputs["L0_ffn_down_qzeros"])

    dqp = np.zeros((IPAD, HP8), np.int32); dqp[:I] = dq
    dsp = np.zeros((IPAD // QG, HID), np.float32); dsp[:ds.shape[0]] = ds
    dzp = np.zeros((IPAD // QG, HP8), np.int32); dzp[:dz.shape[0]] = dz

    maps = []
    for cid in range(NCORES):
        xr = slice(DIN + DPC * cid, DIN + DPC * (cid + 1))
        br = slice(2 * DIN + NST * cid, 2 * DIN + NST * (cid + 1))
        cr = slice(2 * DIN + G * NST + NST * cid, 2 * DIN + G * NST + NST * (cid + 1))
        dtr = slice(DIN + c["CONVD"] + HPC * cid, DIN + c["CONVD"] + HPC * (cid + 1))
        gr = slice(DPC * cid, DPC * (cid + 1))
        m = dict(
            x=np.ascontiguousarray(x[TPC * cid:TPC * (cid + 1)]),
            wa1=np.ascontiguousarray(
                np.concatenate([wi[xr], wi[br], wi[cr], wi[dtr]], axis=0).T),
            wa2=np.ascontiguousarray(wi[gr].T),
            convw=np.ascontiguousarray(np.concatenate(
                [convw[DPC * cid:DPC * (cid + 1)],
                 convw[DIN + NST * cid:DIN + NST * (cid + 1)],
                 convw[DIN + G * NST + NST * cid:DIN + G * NST + NST * (cid + 1)]],
                axis=0).reshape(-1, 128, convw.shape[1]).transpose(1, 0, 2)
                .reshape(128, -1)),
            convb=np.ascontiguousarray(np.concatenate(
                [convb[DPC * cid:DPC * (cid + 1)],
                 convb[DIN + NST * cid:DIN + NST * (cid + 1)],
                 convb[DIN + G * NST + NST * cid:DIN + G * NST + NST * (cid + 1)]],
                axis=0).reshape(-1, 128).T),
            dtb=np.ascontiguousarray(dtb[HPC * cid:HPC * (cid + 1)])[:, None],
            alog=np.ascontiguousarray(alog[HPC * cid:HPC * (cid + 1)])[:, None],
            dvec=np.ascontiguousarray(dvec[HPC * cid:HPC * (cid + 1)])[:, None],
            normw=np.ascontiguousarray(
                normw[DPC * cid:DPC * (cid + 1)].reshape(-1, 128).T),
            woutt=np.ascontiguousarray(wo[:, DPC * cid:DPC * (cid + 1)].T),
            gq=np.ascontiguousarray(gq[DSH * cid:DSH * (cid + 1)]),
            gs=np.ascontiguousarray(gs[NGRP * cid:NGRP * (cid + 1)]),
            gz=np.ascontiguousarray(gz[NGRP * cid:NGRP * (cid + 1)]),
            uq=np.ascontiguousarray(uq[DSH * cid:DSH * (cid + 1)]),
            us=np.ascontiguousarray(us[NGRP * cid:NGRP * (cid + 1)]),
            uz=np.ascontiguousarray(uz[NGRP * cid:NGRP * (cid + 1)]),
            dq=np.ascontiguousarray(dqp[ISH * cid:ISH * (cid + 1)]),
            ds=np.ascontiguousarray(dsp[NIT_SH * cid:NIT_SH * (cid + 1)]),
            dz=np.ascontiguousarray(dzp[NIT_SH * cid:NIT_SH * (cid + 1)]),
        )
        maps.append(m)
    return maps


_CACHE = {}


def kernel(**inputs):
    cfg = REAL_CFG
    maps = shard_inputs(cfg, inputs)
    if "nc" not in _CACHE:
        _CACHE["nc"] = build_program(cfg)
    res = run_bass_kernel_spmd(_CACHE["nc"], maps, core_ids=list(range(NCORES)))
    out = np.concatenate([np.asarray(r["out"]) for r in res.results], axis=0)
    return np.ascontiguousarray(
        out.reshape(cfg["B"], cfg["L"], cfg["HID"]).astype(np.float32))


# version 12
# speedup vs baseline: 1.0622x; 1.0331x over previous
import numpy as np
from contextlib import ExitStack

import concourse.bass as bass
import concourse.mybir as mybir
from concourse.bass_utils import run_bass_kernel_spmd

F32 = mybir.dt.float32
BF16 = mybir.dt.bfloat16
I32 = mybir.dt.int32
AF = mybir.ActivationFunctionType
ALU = mybir.AluOpType
AXX = mybir.AxisListType.X

NCORES = 8
EPS = 1e-5
AWQ_INV = [0, 2, 4, 6, 1, 3, 5, 7]


def make_cfg(HID=4096, NH=128, HD=64, NST=128, KC=4, I=11008, B=2, L=8192):
    G = 8
    QG = 128
    c = dict(HID=HID, NH=NH, HD=HD, NST=NST, KC=KC, I=I, B=B, L=L, G=G, QG=QG)
    c["DIN"] = NH * HD
    c["CONVD"] = c["DIN"] + 2 * G * NST
    c["HPC"] = NH // NCORES
    c["DPC"] = c["HPC"] * HD
    c["NDPC"] = c["DPC"] // 128
    c["TPC"] = B * L // NCORES
    c["NTOK"] = B * L
    c["NDT"] = HID // 128
    c["ABLK"] = 512
    c["NAB"] = c["NTOK"] // c["ABLK"]
    c["AB_PER_BATCH"] = c["NAB"] // B
    c["CSS"] = 128
    c["NSUB"] = c["ABLK"] // c["CSS"]
    c["CA1"] = c["DPC"] + 2 * NST + c["HPC"]
    c["NCONVT"] = (c["DPC"] + 2 * NST) // 128
    c["DSH"] = HID // NCORES
    c["NGT"] = c["DSH"] // 128
    c["NGRP"] = c["DSH"] // QG
    c["IPAD"] = -(-I // (128 * NCORES)) * (128 * NCORES)
    c["ISH"] = c["IPAD"] // NCORES
    c["NIT_SH"] = c["ISH"] // 128
    c["NI"] = I // 128
    c["NTT"] = c["TPC"] // 128
    assert NST == 128 and c["DPC"] % 128 == 0 and HID % 128 == 0
    assert L % c["ABLK"] == 0 and c["DSH"] % QG == 0
    return c


REAL_CFG = make_cfg()

_EK2NAME = {"v": "vector", "s": "scalar", "t": "tensor", "g": "gpsimd", "y": "sync"}


_QS_OF = {"y": "qy", "s": "qs", "g": "qg"}


class Prog:
    def __init__(self, nc, block, sems):
        self.nc = nc
        self.block = block
        self.sem = sems
        self.cnt = {k: 0 for k in sems}
        self.lastw = {e: {k: 0 for k in sems} for e in _EK2NAME}
        self.maxw = {k: 0 for k in sems}
        self.issuer = {e: set([q]) if q else set()
                       for e, q in ((ek, _QS_OF.get(ek)) for ek in _EK2NAME)}

    def sec(self, ek, fn, dq=None):
        def wrapped(eng):
            qs = dq if dq is not None else self.issuer.get(ek, ())
            for q in qs:
                if self.maxw[q] > self.lastw[ek][q]:
                    eng.wait_ge(self.sem[q], self.maxw[q])
                    self.lastw[ek][q] = self.maxw[q]
            fn(eng)
        getattr(self.block, _EK2NAME[ek])(wrapped)

    def w(self, eng, ek, tokens):
        for tok in tokens:
            if tok is None:
                continue
            sk, val = tok
            if sk.startswith("q"):
                val = self.cnt[sk]
                if val > self.maxw[sk]:
                    self.maxw[sk] = val
            if self.lastw[ek][sk] < val:
                eng.wait_ge(self.sem[sk], val)
                self.lastw[ek][sk] = val

    def i(self, instr, sk, ek=None):
        amount = 16 if sk.startswith("q") else 1
        self.cnt[sk] += amount
        instr.then_inc(self.sem[sk], amount)
        if ek is not None:
            self.issuer[ek].add(sk)
        return (sk, self.cnt[sk])

    def icc(self, instr):
        self.cnt["cc"] += 1
        instr.then_inc(self.sem["cc"])
        return ("cc", self.cnt["cc"])

    def tok(self, sk):
        return (sk, self.cnt[sk])

    def bar(self):
        return [self.tok(k) for k in self.sem]


def sap(handle_or_ap, dims, offset=0):
    base = handle_or_ap if isinstance(handle_or_ap, bass.AP) else handle_or_ap[:]
    return bass.AP(base.tensor, base.offset + offset, [list(d) for d in dims])


def fexp(ap2d, ncols, reps, col0=0):
    return bass.AP(ap2d.tensor, ap2d.offset + col0,
                   [list(ap2d.ap[0]), [1, ncols], [0, reps]])


def frep(ap2d, ncols, reps, col0=0):
    return bass.AP(ap2d.tensor, ap2d.offset + col0,
                   [list(ap2d.ap[0]), [0, reps], [1, ncols]])


def build_program(cfg):
    C = cfg
    HID, I, Bsz = C["HID"], C["I"], C["B"]
    HD, HPC, DPC, NDPC = C["HD"], C["HPC"], C["DPC"], C["NDPC"]
    TPC, NTOK, NDT = C["TPC"], C["NTOK"], C["NDT"]
    ABLK, NAB, CSS, NSUB = C["ABLK"], C["NAB"], C["CSS"], C["NSUB"]
    CA1, NCONVT = C["CA1"], C["NCONVT"]
    DSH, NGT, NGRP, QG = C["DSH"], C["NGT"], C["NGRP"], C["QG"]
    IPAD, ISH, NIT_SH, NI = C["IPAD"], C["ISH"], C["NIT_SH"], C["NI"]
    NST, KC, DIN, NTT = C["NST"], C["KC"], C["DIN"], C["NTT"]
    IP8, HP8 = I // 8, HID // 8
    AH = ABLK + KC - 1
    NKD = DIN // 128

    nc = bass.Bass("TRN2", target_bir_lowering=False)

    ei = lambda n, sh, dt=F32: nc.dram_tensor(n, sh, dt, kind="ExternalInput").ap()
    x_in = ei("x", [TPC, HID])
    wa1_in = ei("wa1", [HID, CA1])
    wa2_in = ei("wa2", [HID, DPC])
    convw_in = ei("convw", [128, NCONVT * KC])
    convb_in = ei("convb", [128, NCONVT])
    dtb_in = ei("dtb", [HPC, 1])
    alog_in = ei("alog", [HPC, 1])
    dvec_in = ei("dvec", [HPC, 1])
    normw_in = ei("normw", [128, NDPC])
    woutt_in = ei("woutt", [DPC, HID])
    gq_in = ei("gq", [DSH, IP8], I32)
    gs_in = ei("gs", [NGRP, I])
    gz_in = ei("gz", [NGRP, IP8], I32)
    uq_in = ei("uq", [DSH, IP8], I32)
    us_in = ei("us", [NGRP, I])
    uz_in = ei("uz", [NGRP, IP8], I32)
    dq_in = ei("dq", [ISH, HP8], I32)
    ds_in = ei("ds", [NIT_SH, HID])
    dz_in = ei("dz", [NIT_SH, HP8], I32)
    out_t = nc.dram_tensor("out", [TPC, HID], F32, kind="ExternalOutput").ap()

    di = lambda n, sh, dt: nc.dram_tensor(n, sh, dt).ap()
    xln_tok_d = di("xln_tok", [TPC, HID], BF16)
    xlnT_sl_d = di("xlnT_sl", [HID, TPC], BF16)
    xlnT_all_d = di("xlnT_all", [NCORES * HID, TPC], BF16)
    gsl_d = di("gsl", [DSH, I], BF16)
    g_all_d = di("g_all", [HID, I], BF16)
    usl_d = di("usl", [DSH, I], BF16)
    u_all_d = di("u_all", [HID, I], BF16)
    wdsl_d = di("wdsl", [ISH, HID], BF16)
    wd_all_d = di("wd_all", [IPAD, HID], BF16)
    woT_sl_d = di("woT_sl", [DPC, HID], BF16)
    woT_all_d = di("woT_all", [DIN, HID], BF16)
    zsg_d = di("zsg", [NGRP, I], F32)
    zsu_d = di("zsu", [NGRP, I], F32)
    zsd_d = di("zsd", [NIT_SH, HID], F32)
    dbounce_d = di("dbounce", [1, HPC], F32)
    acb_d = di("acb", [NAB * NSUB * HPC, CSS], F32)
    prow_d = di("prow", [NAB * NSUB, HPC], F32)
    y_raw_d = di("y_raw", [NTOK, DPC], BF16)
    y_tok_d = di("y_tok", [NTOK, DPC], BF16)
    yT_sl_d = di("yT_sl", [DPC, NTOK], BF16)
    y_all_d = di("y_all", [DIN, NTOK], BF16)
    y_b_d = di("y_b", [DIN, TPC], BF16)
    x_new_d = di("x_new", [TPC, HID], F32)
    xln2_tok_d = di("xln2_tok", [TPC, HID], BF16)
    h_d = di("h", [I, TPC], BF16)

    with ExitStack() as _es:
        _sems = {}
        for _sk in ("sv", "ss", "sp", "sg", "qy", "qs", "qg",
                    "qa", "qb", "qc", "qd", "qe", "qf", "cc"):
            _sems[_sk] = _es.enter_context(nc.semaphore(_sk))
        (_sv, _ss, _sp, _sg, _qy, _qs, _qg, _qa, _qb, _qc, _qd, _qe, _qf,
         _cc) = (_sems[k] for k in ("sv", "ss", "sp", "sg", "qy", "qs", "qg",
                                    "qa", "qb", "qc", "qd", "qe", "qf", "cc"))
        block = _es.enter_context(nc.Block())
        P = Prog(nc, block, dict(sv=_sv, ss=_ss, sp=_sp, sg=_sg,
                                 qy=_qy, qs=_qs, qg=_qg, cc=_cc,
                                 qa=_qa, qb=_qb, qc=_qc, qd=_qd, qe=_qe, qf=_qf))

        pers = nc.auto_arena()
        ident_bf = pers.sbuf_tensor("ident_bf", [128, 128], BF16)
        ident_f = pers.sbuf_tensor("ident_f", [128, 128], F32)
        ut_sb = pers.sbuf_tensor("ut_sb", [128, CSS], F32)
        dexp_sb = pers.sbuf_tensor("dexp_sb", [128, DPC], F32)
        convw_sb = pers.sbuf_tensor("convw_sb", [128, NCONVT * KC], F32)
        convb_sb = pers.sbuf_tensor("convb_sb", [128, NCONVT], F32)
        dtb_sb = pers.sbuf_tensor("dtb_sb", [HPC, 1], F32)
        acol_sb = pers.sbuf_tensor("acol_sb", [HPC, 1], F32)
        dvec_sb = pers.sbuf_tensor("dvec_sb", [HPC, 1], F32)
        ones_sb = pers.sbuf_tensor("ones_sb", [1, 128], F32)
        eps_sb = pers.sbuf_tensor("eps_sb", [128, 1], F32)
        drow_sb = pers.sbuf_tensor("drow_sb", [1, HPC], F32)
        pers.allocate()

        def y_setup(y):
            P.i(y.dma_start(out=convw_sb[:], in_=convw_in), "qy")
            P.i(y.dma_start(out=convb_sb[:], in_=convb_in), "qy")
            for dst, src in ((dtb_sb, dtb_in), (acol_sb, alog_in), (dvec_sb, dvec_in)):
                P.i(y.dma_start(out=dst[:], in_=src), "qy")
        P.sec("y", y_setup)
        ld_small = P.tok("qy")

        def v_setup(v):
            P.w(v, "v", [ld_small])
            P.i(v.memset(ident_bf[:], 1.0), "sv")
            P.i(v.memset(ident_f[:], 1.0), "sv")
            P.i(v.memset(ut_sb[:], 0.0), "sv")
            P.i(v.memset(ones_sb[:], 1.0), "sv")
            P.i(v.memset(eps_sb[:], EPS), "sv")
        P.sec("v", v_setup)
        ident_tok0 = P.tok("sv")

        def g_setup(g):
            P.w(g, "g", [ident_tok0])
            P.i(g.affine_select(out=ut_sb[:], in_=ut_sb[:], pattern=[[1, CSS]],
                                compare_op=ALU.is_ge, fill=-1e9, base=0,
                                channel_multiplier=-1), "sg")
            P.i(g.affine_select(out=ident_bf[:], in_=ident_bf[:],
                                pattern=[[1, 128]], compare_op=ALU.is_equal,
                                fill=0.0, base=0, channel_multiplier=-1), "sg")
            P.i(g.affine_select(out=ident_f[:], in_=ident_f[:],
                                pattern=[[1, 128]], compare_op=ALU.is_equal,
                                fill=0.0, base=0, channel_multiplier=-1), "sg")
        P.sec("g", g_setup)
        ut_tok = P.tok("sg")
        ident_tok = P.tok("sg")

        def s_setup(s):
            P.w(s, "s", [ld_small])
            P.i(s.activation(out=acol_sb[:], in_=acol_sb[:], func=AF.Exp), "ss")
        P.sec("s", s_setup)

        def v_setup2(v):
            P.w(v, "v", [P.tok("ss")])
            P.i(v.tensor_scalar_mul(out=acol_sb[:], in0=acol_sb[:], scalar1=-1.0), "sv")
        P.sec("v", v_setup2)
        a_tok = P.tok("sv")

        def y_setup3(y):
            P.w(y, "y", [ld_small])
            P.i(y.dma_start(out=sap(dbounce_d, [[1, HPC], [0, 1]]), in_=dvec_sb[:]),
                "qy")
            P.w(y, "y", [P.tok("qy")])
            P.i(y.dma_start(out=drow_sb[:], in_=dbounce_d), "qy")
        P.sec("y", y_setup3)
        drow_tok = P.tok("qy")

        with ExitStack() as _esdx:
            PDX = _esdx.enter_context(nc.psum_tensor("PDX", [128, DPC], F32))
            def t_setup(t):
                P.w(t, "t", [drow_tok, ident_tok])
                step = min(512, DPC)
                rhs_full = bass.AP(drow_sb[:].tensor, drow_sb[:].offset,
                                   [[HPC, 1], [1, HPC], [0, HD]])
                for i0 in range(0, DPC, step):
                    rhs = bass.AP(rhs_full.tensor,
                                  rhs_full.offset + i0 // HD,
                                  [[HPC, 1], [1, step // HD], [0, HD]])
                    mm = t.matmul(PDX[:, i0:i0 + step], ones_sb[:], rhs,
                                  start=True, stop=True)
                P.i(mm, "sp")
            P.sec("t", t_setup)
            dx_mm = P.tok("sp")

            def v_setup3(v):
                P.w(v, "v", [dx_mm])
                P.i(v.tensor_copy(out=dexp_sb[:], in_=PDX[:]), "sv")
            P.sec("v", v_setup3)
        dexp_tok = P.tok("sv")

        ph0a = nc.auto_arena()
        p0x = ph0a.sbuf_tensor("p0x", [128, HID], F32)
        SQCH = min(1024, HID)
        p0sq = ph0a.sbuf_tensor("p0sq", [128, SQCH], F32)
        p0s1 = ph0a.sbuf_tensor("p0s1", [128, 1], F32)
        p0s2 = ph0a.sbuf_tensor("p0s2", [128, 1], F32)
        p0pt = ph0a.sbuf_tensor("p0pt", [128, 1], F32)
        p0m = ph0a.sbuf_tensor("p0m", [128, 1], F32)
        p0v = ph0a.sbuf_tensor("p0v", [128, 1], F32)
        p0r = ph0a.sbuf_tensor("p0r", [128, 1], F32)
        p0o = ph0a.sbuf_tensor("p0o", [128, HID], BF16)
        p0t = ph0a.sbuf_tensor("p0t", [128, TPC], BF16)
        ph0a.allocate()

        def ln_pass(src_dram, dst_dram, rows, x_sb, sq_sb, s1_sb, s2_sb, pt_sb,
                    m_sb, v_sb, r_sb, o_sb, pre_bar):
            written = None
            nsq = HID // SQCH
            for it in range(rows // 128):
                r0 = it * 128

                def y_fn(y):
                    P.w(y, "y", pre_bar + [P.tok("sv"), P.tok("qs")])
                    P.i(y.dma_start(out=x_sb[:], in_=src_dram[r0:r0 + 128, :]), "qy")
                P.sec("y", y_fn)
                xt = P.tok("qy")

                def v_fn(v):
                    P.w(v, "v", pre_bar + [xt])
                    P.i(v.tensor_reduce(out=s1_sb[:], in_=x_sb[:], axis=AXX,
                                        op=ALU.add), "sv")
                    for ic in range(nsq):
                        P.w(v, "v", [P.tok("sv")])
                        P.i(v.tensor_tensor(out=sq_sb[:],
                                            in0=x_sb[:, ic * SQCH:(ic + 1) * SQCH],
                                            in1=x_sb[:, ic * SQCH:(ic + 1) * SQCH],
                                            op=ALU.mult), "sv")
                        P.w(v, "v", [P.tok("sv")])
                        P.i(v.tensor_reduce(out=pt_sb[:], in_=sq_sb[:], axis=AXX,
                                            op=ALU.add), "sv")
                        P.w(v, "v", [P.tok("sv")])
                        if ic == 0:
                            P.i(v.tensor_copy(out=s2_sb[:], in_=pt_sb[:]), "sv")
                        else:
                            P.i(v.tensor_tensor(out=s2_sb[:], in0=s2_sb[:],
                                                in1=pt_sb[:], op=ALU.add), "sv")
                    P.w(v, "v", [P.tok("sv")])
                    P.i(v.tensor_scalar_mul(out=m_sb[:], in0=s1_sb[:],
                                            scalar1=1.0 / HID), "sv")
                    P.w(v, "v", [P.tok("sv")])
                    P.i(v.tensor_tensor(out=v_sb[:], in0=m_sb[:], in1=m_sb[:],
                                        op=ALU.mult), "sv")
                    P.w(v, "v", [P.tok("sv")])
                    P.i(v.scalar_tensor_tensor(out=v_sb[:], in0=s2_sb[:],
                                               scalar=1.0 / HID, in1=v_sb[:],
                                               op0=ALU.mult, op1=ALU.subtract), "sv")
                P.sec("v", v_fn)
                var_tok = P.tok("sv")

                def s_fn(s):
                    P.w(s, "s", [var_tok])
                    P.i(s.activation(out=r_sb[:], in_=v_sb[:], func=AF.Ln,
                                     bias=eps_sb[:]), "ss")
                P.sec("s", s_fn)
                ln_tok = P.tok("ss")

                def vr_fn(v):
                    P.w(v, "v", [ln_tok])
                    P.i(v.tensor_scalar_mul(out=r_sb[:], in0=r_sb[:],
                                            scalar1=-0.5), "sv")
                P.sec("v", vr_fn)
                neg_tok = P.tok("sv")

                def s1b_fn(s):
                    P.w(s, "s", [neg_tok])
                    P.i(s.activation(out=r_sb[:], in_=r_sb[:], func=AF.Exp), "ss")
                P.sec("s", s1b_fn)
                rs_tok = P.tok("ss")

                def v2_fn(v):
                    P.w(v, "v", [rs_tok, P.tok("qs")])
                    P.i(v.tensor_scalar(out=o_sb[:], in0=x_sb[:], scalar1=m_sb[:],
                                        scalar2=r_sb[:], op0=ALU.subtract,
                                        op1=ALU.mult), "sv")
                P.sec("v", v2_fn)
                o_tok = P.tok("sv")

                def s2_fn(s):
                    P.w(s, "s", [o_tok])
                    P.i(s.dma_start(out=dst_dram[r0:r0 + 128, :], in_=o_sb[:]), "qs")
                P.sec("s", s2_fn)
                written = P.tok("qs")
            return written

        xln_w = ln_pass(x_in, xln_tok_d, TPC, p0x, p0sq, p0s1, p0s2, p0pt,
                        p0m, p0v, p0r, p0o, [])

        def y_xlt(y):
            P.w(y, "y", [xln_w])
            for kt in range(NDT):
                P.w(y, "y", [P.tok("qy")])
                P.i(y.dma_start_transpose(out=p0t[:],
                                          in_=xln_tok_d[:, kt * 128:(kt + 1) * 128]),
                    "qy")
                P.w(y, "y", [P.tok("qy")])
                P.i(y.dma_start(out=xlnT_sl_d[kt * 128:(kt + 1) * 128, :],
                                in_=p0t[:]), "qy")
        P.sec("y", y_xlt)
        xlt_w = P.tok("qy")

        def g_ag1(g):
            P.w(g, "g", [xlt_w])
            P.icc(g.collective_compute(
                "AllGather", ALU.bypass, replica_groups=[list(range(NCORES))],
                ins=[xlnT_sl_d.opt()], outs=[xlnT_all_d.opt()]))
        P.sec("g", g_ag1)
        ag_xln = P.tok("cc")
        ph0a.free()

        ICH = I // 4 if I > 2048 else I
        NICH = I // ICH
        ph0b = nc.auto_arena()
        dqq = ph0b.sbuf_tensor("dqq", [128, 2 * (ICH // 8)], I32)
        dqi = ph0b.sbuf_tensor("dqi", [128, ICH], I32)
        dqn = ph0b.sbuf_tensor("dqn", [128, ICH], F32)
        dqs = ph0b.sbuf_tensor("dqs", [128, 2 * ICH], F32)
        dqz = ph0b.sbuf_tensor("dqz", [128, 2 * ICH], F32)
        dqw = ph0b.sbuf_tensor("dqw", [128, ICH], BF16)
        zqq = ph0b.sbuf_tensor("zqq", [NGRP, IP8], I32)
        zqi = ph0b.sbuf_tensor("zqi", [NGRP, ICH], I32)
        zqs = ph0b.sbuf_tensor("zqs", [NGRP, ICH], F32)
        zsc = ph0b.sbuf_tensor("zsc", [NGRP, ICH], F32)
        ph0b.allocate()
        bar0b = P.bar()

        def dequant_gu(qin, sin, zin, zs_dram, out_dram):
            def y_fn(y):
                P.w(y, "y", bar0b + [P.tok("sv")])
                P.i(y.dma_start(out=zqq[:], in_=zin), "qy")
            P.sec("y", y_fn)
            zl = P.tok("qy")

            for ic in range(NICH):
                c0 = ic * ICH

                def yz_fn(y):
                    P.w(y, "y", [P.tok("sv")])
                    P.i(y.dma_start(out=zsc[:], in_=sin[:, c0:c0 + ICH]), "qy")
                P.sec("y", yz_fn)
                zscl = P.tok("qy")

                def v_fn(v):
                    P.w(v, "v", bar0b + [zl, zscl, P.tok("qy")])
                    for sft in range(8):
                        P.i(v.tensor_scalar(out=zqi[:, AWQ_INV[sft]::8],
                                            in0=zqq[:, c0 // 8:(c0 + ICH) // 8],
                                            scalar1=4 * sft, scalar2=15,
                                            op0=ALU.logical_shift_right,
                                            op1=ALU.bitwise_and), "sv")
                    P.w(v, "v", [P.tok("sv")])
                    P.i(v.tensor_copy(out=zqs[:], in_=zqi[:]), "sv")
                    P.w(v, "v", [P.tok("sv")])
                    P.i(v.tensor_tensor(out=zqs[:], in0=zqs[:], in1=zsc[:],
                                        op=ALU.mult), "sv")
                P.sec("v", v_fn)
                zs_tok = P.tok("sv")

                def y2_fn(y):
                    P.w(y, "y", [zs_tok])
                    P.i(y.dma_start(out=zs_dram[:, c0:c0 + ICH], in_=zqs[:]), "qy")
                P.sec("y", y2_fn)
            zs_w = P.tok("qy")

            prev_ext = [None, None]
            dqi_free = None
            it_idx = 0
            for gt in range(NGT):
                for ic in range(NICH):
                    c0 = ic * ICH
                    par = it_idx % 2
                    it_idx += 1
                    qb8 = par * (ICH // 8)
                    qbc = par * ICH

                    def y3_fn(y):
                        P.w(y, "y", [zs_w, prev_ext[par]])
                        P.i(y.dma_start(
                            out=dqq[:, qb8:qb8 + ICH // 8],
                            in_=qin[gt * 128:(gt + 1) * 128,
                                    c0 // 8:(c0 + ICH) // 8]),
                            ("qa", "qb")[par], "y")
                        P.i(y.dma_start(out=dqs[:, qbc:qbc + ICH],
                                        in_=sap(sin, [[0, 128], [1, ICH]],
                                                gt * I + c0)),
                            ("qa", "qb")[par], "y")
                        P.i(y.dma_start(out=dqz[:, qbc:qbc + ICH],
                                        in_=sap(zs_dram, [[0, 128], [1, ICH]],
                                                gt * I + c0)),
                            ("qa", "qb")[par], "y")
                    P.sec("y", y3_fn, dq=(("qa", "qb")[par],))
                    ql = P.tok(("qa", "qb")[par])

                    def v3_fn(v):
                        P.w(v, "v", [ql, dqi_free])
                        for sft in range(8):
                            P.i(v.tensor_scalar(out=dqi[:, AWQ_INV[sft]::8],
                                                in0=dqq[:, qb8:qb8 + ICH // 8],
                                                scalar1=4 * sft, scalar2=15,
                                                op0=ALU.logical_shift_right,
                                                op1=ALU.bitwise_and), "sv")
                    P.sec("v", v3_fn)
                    ext_v = P.tok("sv")
                    ext_g = ext_v

                    dqi_free_local = [None]

                    def v4_fn(v):
                        P.w(v, "v", [ext_v, ext_g])
                        P.i(v.tensor_copy(out=dqn[:], in_=dqi[:]), "sv")
                        dqi_free_local[0] = P.tok("sv")
                        P.w(v, "v", [P.tok("sv")])
                        P.i(v.tensor_tensor(out=dqn[:], in0=dqn[:],
                                            in1=dqs[:, qbc:qbc + ICH],
                                            op=ALU.mult), "sv")
                        P.w(v, "v", [P.tok("sv"), P.tok("qs")])
                        P.i(v.tensor_tensor(out=dqw[:], in0=dqn[:],
                                            in1=dqz[:, qbc:qbc + ICH],
                                            op=ALU.subtract), "sv")
                    P.sec("v", v4_fn)
                    dqi_free = dqi_free_local[0]
                    wt = P.tok("sv")
                    prev_ext[par] = wt

                    def s3_fn(s):
                        P.w(s, "s", [wt])
                        P.i(s.dma_start(
                            out=out_dram[gt * 128:(gt + 1) * 128, c0:c0 + ICH],
                            in_=dqw[:]), "qs")
                    P.sec("s", s3_fn)

        dequant_gu(gq_in, gs_in, gz_in, zsg_d, gsl_d)
        dequant_gu(uq_in, us_in, uz_in, zsu_d, usl_d)
        ph0b.free()

        ph0c = nc.auto_arena()
        dnq = ph0c.sbuf_tensor("dnq", [128, HP8], I32)
        dni = ph0c.sbuf_tensor("dni", [128, HID], I32)
        dnn = ph0c.sbuf_tensor("dnn", [128, HID], F32)
        dns = ph0c.sbuf_tensor("dns", [128, HID], F32)
        dnz = ph0c.sbuf_tensor("dnz", [128, HID], F32)
        dnw = ph0c.sbuf_tensor("dnw", [128, HID], BF16)
        znq = ph0c.sbuf_tensor("znq", [NIT_SH, HP8], I32)
        zni = ph0c.sbuf_tensor("zni", [NIT_SH, HID], I32)
        zns = ph0c.sbuf_tensor("zns", [NIT_SH, HID], F32)
        znc = ph0c.sbuf_tensor("znc", [NIT_SH, HID], F32)
        ph0c.allocate()
        bar0c = P.bar()

        def y_dn(y):
            P.w(y, "y", bar0c + [P.tok("sv")])
            P.i(y.dma_start(out=znq[:], in_=dz_in), "qy")
            P.i(y.dma_start(out=znc[:], in_=ds_in), "qy")
        P.sec("y", y_dn)
        znl = P.tok("qy")

        def v_dn(v):
            P.w(v, "v", bar0c + [znl])
            for sft in range(8):
                P.i(v.tensor_scalar(out=zni[:, AWQ_INV[sft]::8], in0=znq[:],
                                    scalar1=4 * sft, scalar2=15,
                                    op0=ALU.logical_shift_right,
                                    op1=ALU.bitwise_and), "sv")
            P.w(v, "v", [P.tok("sv")])
            P.i(v.tensor_copy(out=zns[:], in_=zni[:]), "sv")
            P.w(v, "v", [P.tok("sv")])
            P.i(v.tensor_tensor(out=zns[:], in0=zns[:], in1=znc[:], op=ALU.mult),
                "sv")
        P.sec("v", v_dn)
        zns_tok = P.tok("sv")

        def y_dn2(y):
            P.w(y, "y", [zns_tok])
            P.i(y.dma_start(out=zsd_d, in_=zns[:]), "qy")
        P.sec("y", y_dn2)
        zsd_w = P.tok("qy")

        for itl in range(NIT_SH):
            def y_dn3(y):
                P.w(y, "y", [zsd_w, P.tok("sv")])
                P.i(y.dma_start(out=dnq[:], in_=dq_in[itl * 128:(itl + 1) * 128, :]),
                    "qy")
                P.i(y.dma_start(out=dns[:],
                                in_=sap(ds_in, [[0, 128], [1, HID]], itl * HID)),
                    "qy")
                P.i(y.dma_start(out=dnz[:],
                                in_=sap(zsd_d, [[0, 128], [1, HID]], itl * HID)),
                    "qy")
            P.sec("y", y_dn3)
            ql = P.tok("qy")

            def v_dn3(v):
                P.w(v, "v", [ql, P.tok("qs")])
                for sft in range(8):
                    P.i(v.tensor_scalar(out=dni[:, AWQ_INV[sft]::8], in0=dnq[:],
                                        scalar1=4 * sft, scalar2=15,
                                        op0=ALU.logical_shift_right,
                                        op1=ALU.bitwise_and), "sv")
                P.w(v, "v", [P.tok("sv")])
                P.i(v.tensor_copy(out=dnn[:], in_=dni[:]), "sv")
                P.w(v, "v", [P.tok("sv")])
                P.i(v.tensor_tensor(out=dnn[:], in0=dnn[:], in1=dns[:],
                                    op=ALU.mult), "sv")
                P.w(v, "v", [P.tok("sv")])
                P.i(v.tensor_tensor(out=dnw[:], in0=dnn[:], in1=dnz[:],
                                    op=ALU.subtract), "sv")
            P.sec("v", v_dn3)
            wt = P.tok("sv")

            def s_dn3(s):
                P.w(s, "s", [wt])
                P.i(s.dma_start(out=wdsl_d[itl * 128:(itl + 1) * 128, :],
                                in_=dnw[:]), "qs")
            P.sec("s", s_dn3)
        ph0c.free()

        ph0d = nc.auto_arena()
        wof = ph0d.sbuf_tensor("wof", [128, HID], F32)
        wob = ph0d.sbuf_tensor("wob", [128, HID], BF16)
        wnw = ph0d.sbuf_tensor("wnw", [128, NDPC], F32)
        ph0d.allocate()
        bar0d = P.bar()

        def y_nw(y):
            P.w(y, "y", bar0d)
            P.i(y.dma_start(out=wnw[:], in_=normw_in), "qy")
        P.sec("y", y_nw)
        nw_tok = P.tok("qy")

        for dt_ in range(NDPC):
            def y_wo(y):
                P.w(y, "y", [nw_tok, P.tok("sv")])
                P.i(y.dma_start(out=wof[:],
                                in_=woutt_in[dt_ * 128:(dt_ + 1) * 128, :]), "qy")
            P.sec("y", y_wo)
            wl = P.tok("qy")

            def v_wo(v):
                P.w(v, "v", bar0d + [wl, P.tok("qs")])
                P.i(v.tensor_scalar(out=wob[:], in0=wof[:],
                                    scalar1=wnw[:, dt_:dt_ + 1],
                                    scalar2=None, op0=ALU.mult), "sv")
            P.sec("v", v_wo)
            wc = P.tok("sv")

            def s_wo(s):
                P.w(s, "s", [wc])
                P.i(s.dma_start(out=woT_sl_d[dt_ * 128:(dt_ + 1) * 128, :],
                                in_=wob[:]), "qs")
            P.sec("s", s_wo)
        wo_w = P.tok("qs")

        def g_agw(g):
            P.w(g, "g", [wo_w, P.tok("qs"), P.tok("qy")])
            P.icc(g.collective_compute(
                "AllGather", ALU.bypass, replica_groups=[list(range(NCORES))],
                ins=[gsl_d.opt()], outs=[g_all_d.opt()]))
            P.icc(g.collective_compute(
                "AllGather", ALU.bypass, replica_groups=[list(range(NCORES))],
                ins=[usl_d.opt()], outs=[u_all_d.opt()]))
            P.icc(g.collective_compute(
                "AllGather", ALU.bypass, replica_groups=[list(range(NCORES))],
                ins=[wdsl_d.opt()], outs=[wd_all_d.opt()]))
            P.icc(g.collective_compute(
                "AllGather", ALU.bypass, replica_groups=[list(range(NCORES))],
                ins=[woT_sl_d.opt()], outs=[woT_all_d.opt()]))
        P.sec("g", g_agw)
        ag_w = P.tok("cc")
        ph0d.free()

        a1 = nc.auto_arena()
        wa1_sb = a1.sbuf_tensor("wa1_sb", [128, NDT * CA1], BF16)
        wstg = a1.sbuf_tensor("wstg", [128, CA1], F32)
        xfm_sb = a1.sbuf_tensor("xfm_sb", [128, NDT * ABLK], BF16)
        xbc_sb = a1.sbuf_tensor("xbc_sb", [128, NCONVT * AH], BF16)
        xba_sb = a1.sbuf_tensor("xba_sb", [128, NCONVT * ABLK], BF16)
        cacc_sb = a1.sbuf_tensor("cacc_sb", [128, 2 * ABLK], F32)
        dtp_sb = a1.sbuf_tensor("dtp_sb", [HPC, ABLK], F32)
        dts_sb = a1.sbuf_tensor("dts_sb", [HPC, ABLK], F32)
        dta_sb = a1.sbuf_tensor("dta_sb", [HPC, ABLK], F32)
        stk_sb = a1.sbuf_tensor("stk_sb", [128, ABLK], F32)
        dtf_sb = a1.sbuf_tensor("dtf_sb", [HPC, ABLK], F32)
        actok_sb = a1.sbuf_tensor("actok_sb", [128, 128], F32)
        srep_sb = a1.sbuf_tensor("srep_sb", [128, HPC * CSS], F32)
        prep_sb = a1.sbuf_tensor("prep_sb", [128, HPC], F32)
        acm_sb = srep_sb
        lall_sb = a1.sbuf_tensor("lall_sb", [128, HPC * CSS], BF16)
        gm_sb = a1.sbuf_tensor("gm_sb", [128, CSS], BF16)
        gml_sb = a1.sbuf_tensor("gml_sb", [128, HPC * CSS], BF16)
        xtok_sb = a1.sbuf_tensor("xtok_sb", [128, DPC], BF16)
        btok_sb = a1.sbuf_tensor("btok_sb", [128, NST], BF16)
        xs_sb = a1.sbuf_tensor("xs_sb", [128, DPC], BF16)
        xsd_sb = a1.sbuf_tensor("xsd_sb", [128, DPC], BF16)
        z_sb = a1.sbuf_tensor("z_sb", [128, DPC], F32)
        zbf_sb = a1.sbuf_tensor("zbf_sb", [128, DPC], BF16)
        zt_sb = a1.sbuf_tensor("zt_sb", [128, DPC], F32)
        yo_sb = a1.sbuf_tensor("yo_sb", [128, DPC], F32)
        dtm_sb = a1.sbuf_tensor("dtm_sb", [128, DPC], F32)
        yc_sb = yo_sb
        yout_sb = a1.sbuf_tensor("yout_sb", [128, DPC], BF16)
        a1.allocate()

        bar0 = P.bar()
        with ExitStack() as _esa1:
            PA0 = _esa1.enter_context(nc.psum_tensor("PA0", [128, 512], F32))
            PA1 = _esa1.enter_context(nc.psum_tensor("PA1", [128, 512], F32))
            PTRB = _esa1.enter_context(nc.psum_tensor("PTRB", [128, DPC], BF16))
            PYD = _esa1.enter_context(nc.psum_tensor("PYD", [128, DPC], F32))
            PSC = _esa1.enter_context(nc.psum_tensor("PSC", [128, DPC], F32))
            def v_stk0(v):
                P.w(v, "v", bar0)
                P.i(v.memset(stk_sb[:], 0.0), "sv")
            P.sec("v", v_stk0)

            wconv_tok = None
            for kt in range(NDT):
                def y_w(y):
                    P.w(y, "y", bar0 + [wconv_tok])
                    P.i(y.dma_start(out=wstg[:],
                                    in_=wa1_in[kt * 128:(kt + 1) * 128, :]), "qy")
                P.sec("y", y_w)
                wl = P.tok("qy")

                def v_w(v):
                    P.w(v, "v", bar0 + [wl])
                    P.i(v.tensor_copy(out=wa1_sb[:, kt * CA1:(kt + 1) * CA1],
                                      in_=wstg[:]), "sv")
                P.sec("v", v_w)
                wconv_tok = P.tok("sv")
            wa1_tok = P.tok("sv")

            ftiles = [(d * 128, 128, "x", d) for d in range(NDPC)]
            ftiles += [(DPC, NST, "B", NDPC), (DPC + NST, NST, "C", NDPC + 1),
                       (DPC + 2 * NST, HPC, "dt", -1)]

            st = dict(xfm=None, xfm_pe=None,
                      pa_mm=[None, None], pa_ev=[None, None],
                      z=None, zuser=None, halo=None, conv=None,
                      xbc_user=None, dtuser=None, stk_pe=None, xba_pe=None,
                      yd_ev=None, sc_ev=None)
            yraw_w = None

            for b in range(NAB):
                T0 = b * ABLK
                cb_src = T0 // TPC
                tloc = T0 - cb_src * TPC
                batch_start = (b % C["AB_PER_BATCH"]) == 0

                def y_xfm(y):
                    P.w(y, "y", [ag_xln, st["xfm_pe"]])
                    src = sap(xlnT_all_d, [[TPC, 128], [128 * TPC, NDT], [1, ABLK]],
                              cb_src * HID * TPC + tloc)
                    P.i(y.dma_start(
                        out=xfm_sb[:].rearrange("p (k t) -> p k t", k=NDT),
                        in_=src), "qy")
                P.sec("y", y_xfm)
                st["xfm"] = P.tok("qy")

                if batch_start:
                    def v_rst(v):
                        P.w(v, "v", [st["zuser"], st["z"], st["xbc_user"]])
                        P.i(v.memset(z_sb[:], 0.0), "sv")
                        halo = sap(xbc_sb, [[NCONVT * AH, 128], [AH, NCONVT],
                                            [1, KC - 1]])
                        P.i(v.memset(halo, 0.0), "sv")
                    P.sec("v", v_rst)
                    st["z"] = P.tok("sv")
                    st["halo"] = P.tok("sv")

                evac_toks = []
                for fi, (off, fw, kind, dpos) in enumerate(ftiles):
                    pp = (PA0, PA1)[fi % 2]
                    pi = fi % 2

                    def t_ip(t):
                        P.w(t, "t", [wa1_tok, st["xfm"], st["pa_ev"][pi]])
                        for kt in range(NDT):
                            mm = t.matmul(
                                pp[0:fw, 0:ABLK],
                                wa1_sb[:, kt * CA1 + off:kt * CA1 + off + fw],
                                xfm_sb[:, kt * ABLK:(kt + 1) * ABLK],
                                start=(kt == 0), stop=(kt == NDT - 1))
                        P.i(mm, "sp")
                    P.sec("t", t_ip)
                    st["pa_mm"][pi] = P.tok("sp")
                    if fi == len(ftiles) - 1:
                        st["xfm_pe"] = P.tok("sp")

                    def v_ev(v):
                        P.w(v, "v", [st["pa_mm"][pi], st["halo"], st["conv"]])
                        if kind == "dt":
                            P.i(v.tensor_copy(out=dtp_sb[:], in_=pp[0:HPC, 0:ABLK]),
                                "sv")
                        else:
                            dst = xbc_sb[:, dpos * AH + KC - 1:dpos * AH + AH]
                            P.i(v.tensor_copy(out=dst, in_=pp[:, 0:ABLK]), "sv")
                    P.sec("v", v_ev)
                    st["pa_ev"][pi] = P.tok("sv")
                    evac_toks.append(P.tok("sv"))
                xbc_tok = P.tok("sv")

                conv_sil = []
                for ct in range(NCONVT):
                    acc = cacc_sb[:, (ct % 2) * ABLK:(ct % 2 + 1) * ABLK]

                    def v_cv(v):
                        deps = [evac_toks[ct], st["halo"], st["xba_pe"]]
                        if ct >= 2:
                            deps.append(conv_sil[ct - 2])
                        P.w(v, "v", deps)
                        base = ct * AH
                        P.i(v.tensor_scalar(out=acc, in0=xbc_sb[:, base:base + ABLK],
                                            scalar1=convw_sb[:, ct * KC:ct * KC + 1],
                                            scalar2=convb_sb[:, ct:ct + 1],
                                            op0=ALU.mult, op1=ALU.add), "sv")
                        for tau in range(1, KC):
                            P.w(v, "v", [P.tok("sv")])
                            P.i(v.scalar_tensor_tensor(
                                out=acc, in0=xbc_sb[:, base + tau:base + tau + ABLK],
                                scalar=convw_sb[:, ct * KC + tau:ct * KC + tau + 1],
                                in1=acc, op0=ALU.mult, op1=ALU.add), "sv")
                    P.sec("v", v_cv)
                    cv_tok = P.tok("sv")

                    def s_cv(s):
                        P.w(s, "s", [cv_tok])
                        P.i(s.activation(out=xba_sb[:, ct * ABLK:(ct + 1) * ABLK],
                                         in_=acc, func=AF.Silu), "ss")
                    P.sec("s", s_cv)
                    conv_sil.append(P.tok("ss"))
                conv_tok = P.tok("ss")
                st["conv"] = conv_tok

                def v_hl(v):
                    P.w(v, "v", [conv_tok, xbc_tok])
                    src = sap(xbc_sb, [[NCONVT * AH, 128], [AH, NCONVT],
                                       [1, KC - 1]], ABLK)
                    dst = sap(xbc_sb, [[NCONVT * AH, 128], [AH, NCONVT],
                                       [1, KC - 1]])
                    P.i(v.tensor_copy(out=dst, in_=src), "sv")
                P.sec("v", v_hl)
                st["halo"] = P.tok("sv")

                def v_dt(v):
                    P.w(v, "v", [st["pa_ev"][(len(ftiles) - 1) % 2], a_tok,
                                 st["stk_pe"], st["dtuser"]])
                    P.i(v.tensor_scalar(out=dts_sb[:], in0=dtp_sb[:],
                                        scalar1=dtb_sb[:], scalar2=None,
                                        op0=ALU.add), "sv")
                P.sec("v", v_dt)
                dt1 = P.tok("sv")

                def s_dt(s):
                    P.w(s, "s", [dt1])
                    P.i(s.activation(out=dta_sb[:], in_=dts_sb[:], func=AF.Abs),
                        "ss")
                    P.w(s, "s", [P.tok("ss")])
                    P.i(s.activation(out=dta_sb[:], in_=dta_sb[:], func=AF.Exp,
                                     scale=-1.0), "ss")
                    P.w(s, "s", [P.tok("ss")])
                    P.i(s.activation(out=dta_sb[:], in_=dta_sb[:], func=AF.Ln,
                                     bias=1.0), "ss")
                P.sec("s", s_dt)
                dt2 = P.tok("ss")

                def v_dt2(v):
                    P.w(v, "v", [dt2, P.tok("qs")])
                    P.i(v.tensor_relu(out=dts_sb[:], in_=dts_sb[:]), "sv")
                    P.w(v, "v", [P.tok("sv")])
                    P.i(v.tensor_tensor(out=dtf_sb[:],
                                        in0=dts_sb[:], in1=dta_sb[:], op=ALU.add),
                        "sv")
                    P.w(v, "v", [P.tok("sv")])
                    P.i(v.tensor_scalar(out=dta_sb[:],
                                        in0=dtf_sb[:],
                                        scalar1=acol_sb[:], scalar2=None,
                                        op0=ALU.mult), "sv")
                    for s_ in range(NSUB):
                        P.w(v, "v", [P.tok("sv")])
                        sl = slice(s_ * CSS, (s_ + 1) * CSS)
                        P.i(v.tensor_tensor_scan(out=stk_sb[0:HPC, sl],
                                                 data0=dta_sb[:, sl],
                                                 data1=dta_sb[:, sl], initial=0.0,
                                                 op0=ALU.add, op1=ALU.bypass), "sv")
                P.sec("v", v_dt2)
                ac_tok = P.tok("sv")
                st["dtuser"] = ac_tok

                def s_dt2(s):
                    P.w(s, "s", [ac_tok])
                    for s_ in range(NSUB):
                        sl = slice(s_ * CSS, (s_ + 1) * CSS)
                        last = s_ * CSS + CSS - 1
                        P.i(s.activation(out=stk_sb[32:32 + HPC, sl],
                                         in_=stk_sb[0:HPC, sl], func=AF.Exp,
                                         scale=-1.0,
                                         bias=stk_sb[0:HPC, last:last + 1]), "ss")
                        P.i(s.activation(out=stk_sb[64:64 + HPC, sl],
                                         in_=stk_sb[0:HPC, sl], func=AF.Exp), "ss")
                P.sec("s", s_dt2)
                stk_tok = P.tok("ss")

                def s_dt3(s):
                    P.w(s, "s", [stk_tok])
                    for s_ in range(NSUB):
                        sl = slice(s_ * CSS, (s_ + 1) * CSS)
                        r0 = (b * NSUB + s_) * HPC
                        P.i(s.dma_start(out=acb_d[r0:r0 + HPC, :],
                                        in_=stk_sb[0:HPC, sl]), "qs")
                        last = s_ * CSS + CSS - 1
                        P.i(s.dma_start(
                            out=sap(prow_d, [[1, HPC], [0, 1]],
                                    (b * NSUB + s_) * HPC),
                            in_=stk_sb[64:64 + HPC, last:last + 1]), "qs")
                P.sec("s", s_dt3)
                acb_w = P.tok("qs")

                for s_ in range(NSUB):
                    sub = b * NSUB + s_
                    sc = s_ * CSS

                    def y_rep(y):
                        P.w(y, "y", [acb_w, P.tok("sv"), P.tok("ss")])
                        P.i(y.dma_start(out=srep_sb[:],
                                        in_=sap(acb_d, [[0, 128], [1, HPC * CSS]],
                                                sub * HPC * CSS)), "qy")
                        P.i(y.dma_start(out=prep_sb[:],
                                        in_=sap(prow_d, [[0, 128], [1, HPC]],
                                                sub * HPC)), "qy")
                    P.sec("y", y_rep)
                    rep_tok = P.tok("qy")

                    def t_tr(t):
                        P.w(t, "t", [conv_tok, P.tok("sv")])
                        for d in range(NDPC):
                            tr = t.transpose(PTRB[:, d * 128:(d + 1) * 128],
                                             xba_sb[:, d * ABLK + sc:d * ABLK + sc + CSS],
                                             ident_bf[:])
                        P.i(tr, "sp")
                    P.sec("t", t_tr)
                    xtr = P.tok("sp")

                    def v_tr(v):
                        P.w(v, "v", [xtr])
                        P.i(v.tensor_copy(out=xtok_sb[:], in_=PTRB[:]), "sv")
                    P.sec("v", v_tr)
                    xtok_tok = P.tok("sv")

                    def t_tr2(t):
                        P.w(t, "t", [xtok_tok, stk_tok])
                        tr = t.transpose(PTRB[:, 0:NST],
                                         xba_sb[:, NDPC * ABLK + sc:NDPC * ABLK + sc + CSS],
                                         ident_bf[:])
                        P.i(tr, "sp")
                        P.w(t, "t", [P.tok("sv")])
                        t.transpose(PSC[:, 0:64 + HPC],
                                    stk_sb[0:64 + HPC, sc:sc + CSS],
                                    ident_f[0:64 + HPC, 0:64 + HPC])
                        tr2 = t.transpose(PSC[:, 96:96 + HPC],
                                          dtf_sb[:, sc:sc + CSS],
                                          ident_f[0:HPC, 0:HPC])
                        P.i(tr2, "sp")
                    P.sec("t", t_tr2)
                    btr = ("sp", P.cnt["sp"] - 1)
                    str_ = P.tok("sp")
                    st["stk_pe"] = str_

                    def v_tr2(v):
                        P.w(v, "v", [btr])
                        P.i(v.tensor_copy(out=btok_sb[:], in_=PTRB[:, 0:NST]), "sv")
                        P.w(v, "v", [str_])
                        P.i(v.tensor_copy(out=actok_sb[:, 0:64 + HPC],
                                          in_=PSC[:, 0:64 + HPC]), "sv")
                        P.i(v.tensor_copy(out=actok_sb[:, 96:96 + HPC],
                                          in_=PSC[:, 96:96 + HPC]), "sv")
                    P.sec("v", v_tr2)
                    btok_tok = ("sv", P.cnt["sv"] - 1)
                    actok_tok = P.tok("sv")

                    def v_fo(v):
                        P.w(v, "v", [actok_tok, st["z"], dexp_tok, rep_tok, ut_tok])
                        P.i(v.tensor_tensor(out=xs_sb[:], in0=xtok_sb[:],
                                            in1=fexp(actok_sb[:], HPC, HD, 96),
                                            op=ALU.mult), "sv")
                        P.w(v, "v", [P.tok("sv")])
                        P.i(v.tensor_tensor(out=xsd_sb[:], in0=xs_sb[:],
                                            in1=fexp(actok_sb[:], HPC, HD, 32),
                                            op=ALU.mult), "sv")
                        P.i(v.tensor_copy(out=zbf_sb[:], in_=z_sb[:]), "sv")
                        P.i(v.tensor_tensor(out=acm_sb[:], in0=srep_sb[:],
                                            in1=fexp(actok_sb[:], HPC, CSS, 0),
                                            op=ALU.subtract), "sv")
                        P.w(v, "v", [P.tok("sv")])
                        P.i(v.tensor_tensor(out=acm_sb[:], in0=acm_sb[:],
                                            in1=frep(ut_sb[:], CSS, HPC),
                                            op=ALU.add), "sv")
                    P.sec("v", v_fo)
                    folds_tok = P.tok("sv")
                    zbf_tok = ("sv", P.cnt["sv"] - 2)

                    def s_l(s):
                        P.w(s, "s", [folds_tok])
                        P.i(s.activation(out=lall_sb[:], in_=acm_sb[:],
                                         func=AF.Exp), "ss")
                    P.sec("s", s_l)
                    lall_tok = P.tok("ss")

                    def t_gm(t):
                        P.w(t, "t", [conv_tok, st["pa_ev"][0]])
                        bfm = xba_sb[:, NDPC * ABLK + sc:NDPC * ABLK + sc + CSS]
                        cfm = xba_sb[:, (NDPC + 1) * ABLK + sc:(NDPC + 1) * ABLK + sc + CSS]
                        P.i(t.matmul(PA0[:, 0:CSS], bfm, cfm, start=True,
                                     stop=True), "sp")
                    P.sec("t", t_gm)
                    gm_mm = P.tok("sp")

                    def v_gm(v):
                        P.w(v, "v", [gm_mm])
                        P.i(v.tensor_copy(out=gm_sb[:], in_=PA0[:, 0:CSS]), "sv")
                        P.w(v, "v", [P.tok("sv"), lall_tok])
                        for h in range(HPC):
                            P.i(v.tensor_tensor(out=gml_sb[:, h * CSS:(h + 1) * CSS],
                                                in0=gm_sb[:],
                                                in1=lall_sb[:, h * CSS:(h + 1) * CSS],
                                                op=ALU.mult), "sv")
                    P.sec("v", v_gm)
                    gm_ev = ("sv", P.cnt["sv"] - HPC)
                    gml_tok = P.tok("sv")
                    st["pa_ev"][0] = gm_ev

                    def t_y(t):
                        P.w(t, "t", [gml_tok, btok_tok, zbf_tok, folds_tok,
                                     st["pa_ev"][0], st["pa_ev"][1],
                                     st["yd_ev"], st["sc_ev"]])
                        for h in range(HPC):
                            mm = t.matmul(PYD[:, h * HD:(h + 1) * HD],
                                          gml_sb[:, h * CSS:(h + 1) * CSS],
                                          xs_sb[:, h * HD:(h + 1) * HD],
                                          start=((h * HD) % 512 == 0),
                                          stop=(((h + 1) * HD) % 512 == 0
                                                or h == HPC - 1))
                        P.i(mm, "sp")
                        cfm = xba_sb[:, (NDPC + 1) * ABLK + sc:(NDPC + 1) * ABLK + sc + CSS]
                        stepd = min(512, DPC)
                        for i0 in range(0, DPC, stepd):
                            pp = (PA1, PA0)[(i0 // stepd) % 2]
                            P.i(t.matmul(pp[:, 0:stepd], cfm,
                                         zbf_sb[:, i0:i0 + stepd],
                                         start=True, stop=True), "sp")
                        for i0 in range(0, DPC, stepd):
                            mm = t.matmul(PSC[:, i0:i0 + stepd], btok_sb[:],
                                          xsd_sb[:, i0:i0 + stepd],
                                          start=True, stop=True)
                        P.i(mm, "sp")
                    P.sec("t", t_y)
                    sc_mm = P.tok("sp")
                    st["xba_pe"] = sc_mm
                    nyo = max(1, DPC // 512)
                    yo_mms = [("sp", P.cnt["sp"] - 1 - nyo + 1 + i)
                              for i in range(nyo)]
                    yd_mm = ("sp", P.cnt["sp"] - 1 - nyo)

                    def v_y(v):
                        stepd = min(512, DPC)
                        P.w(v, "v", yo_mms + [yd_mm, P.tok("qs")])
                        for i in range(nyo):
                            pp = (PA1, PA0)[i % 2]
                            hph = stepd // HD
                            P.i(v.tensor_tensor(
                                out=yo_sb[:, i * stepd:(i + 1) * stepd],
                                in0=pp[:, 0:stepd],
                                in1=fexp(actok_sb[:], hph, HD,
                                         64 + i * hph),
                                op=ALU.mult), "sv")
                        P.i(v.tensor_tensor(out=dtm_sb[:], in0=xtok_sb[:],
                                            in1=dexp_sb[:], op=ALU.mult), "sv")
                        P.w(v, "v", [P.tok("sv")])
                        P.i(v.tensor_tensor(out=yc_sb[:], in0=PYD[:], in1=yo_sb[:],
                                            op=ALU.add), "sv")
                        P.w(v, "v", [P.tok("sv")])
                        P.i(v.tensor_tensor(out=yout_sb[:], in0=yc_sb[:],
                                            in1=dtm_sb[:], op=ALU.add), "sv")
                    P.sec("v", v_y)
                    y_tok = P.tok("sv")
                    st["pa_ev"][0] = y_tok
                    st["pa_ev"][1] = y_tok
                    st["yd_ev"] = y_tok

                    def s_y(s):
                        P.w(s, "s", [y_tok])
                        P.i(s.dma_start(out=y_raw_d[T0 + sc:T0 + sc + CSS, :],
                                        in_=yout_sb[:]), "qs")
                    P.sec("s", s_y)
                    yraw_w = P.tok("qs")

                    def v_z(v):
                        P.w(v, "v", [sc_mm, rep_tok, zbf_tok])
                        P.i(v.tensor_tensor(out=zt_sb[:], in0=z_sb[:],
                                            in1=fexp(prep_sb[:], HPC, HD),
                                            op=ALU.mult), "sv")
                        P.w(v, "v", [P.tok("sv")])
                        P.i(v.tensor_tensor(out=z_sb[:], in0=zt_sb[:], in1=PSC[:],
                                            op=ALU.add), "sv")
                    P.sec("v", v_z)
                    st["z"] = P.tok("sv")
                    st["zuser"] = P.tok("sv")
                    st["sc_ev"] = P.tok("sv")
                st["xbc_user"] = P.tok("sp")
            a1_done = yraw_w
        a1.free()

        a2 = nc.auto_arena()
        wa2_sb = a2.sbuf_tensor("wa2_sb", [128, NDT * DPC], BF16)
        wstg2 = a2.sbuf_tensor("wstg2", [128, DPC], F32)
        xfm2_sb = a2.sbuf_tensor("xfm2_sb", [128, 2 * NDT * ABLK], BF16)
        gact_sb = a2.sbuf_tensor("gact_sb", [128, NDPC * ABLK], BF16)
        gtok_sb = a2.sbuf_tensor("gtok_sb", [128, DPC], BF16)
        yr_sb = a2.sbuf_tensor("yr_sb", [128, DPC], BF16)
        yg_sb = a2.sbuf_tensor("yg_sb", [128, DPC], F32)
        ysq_sb = a2.sbuf_tensor("ysq_sb", [128, DPC], F32)
        ssum_sb = a2.sbuf_tensor("ssum_sb", [128, 1], F32)
        rstd_sb = a2.sbuf_tensor("rstd_sb", [128, 1], F32)
        yn_sb = a2.sbuf_tensor("yn_sb", [128, DPC], BF16)
        ytr_sb = a2.sbuf_tensor("ytr_sb", [128, NTOK], BF16)
        a2.allocate()

        bar1 = P.bar()
        with ExitStack() as _esa2:
            PB0_ = _esa2.enter_context(nc.psum_tensor("PB0_", [128, 512], F32))
            PB1_ = _esa2.enter_context(nc.psum_tensor("PB1_", [128, 512], F32))
            PTR2 = _esa2.enter_context(nc.psum_tensor("PTR2", [128, DPC], BF16))
            wconv2 = None
            for kt in range(NDT):
                def y_w2(y):
                    P.w(y, "y", bar1 + [wconv2])
                    P.i(y.dma_start(out=wstg2[:],
                                    in_=wa2_in[kt * 128:(kt + 1) * 128, :]), "qy")
                P.sec("y", y_w2)
                wl = P.tok("qy")

                def v_w2(v):
                    P.w(v, "v", bar1 + [wl])
                    P.i(v.tensor_copy(out=wa2_sb[:, kt * DPC:(kt + 1) * DPC],
                                      in_=wstg2[:]), "sv")
                P.sec("v", v_w2)
                wconv2 = P.tok("sv")
            wa2_tok = P.tok("sv")

            st2 = dict(xfm=None, xfm_pe=[None, None], ev=[None, None], tr_ev=None,
                       yn_ev=None)
            ytok_w = None
            for b in range(NAB):
                T0 = b * ABLK
                cb_src = T0 // TPC
                tloc = T0 - cb_src * TPC
                bpar = b % 2
                xbase2 = bpar * NDT * ABLK

                def y_xfm2(y):
                    P.w(y, "y", [ag_xln, st2["xfm_pe"][bpar]])
                    src = sap(xlnT_all_d, [[TPC, 128], [128 * TPC, NDT], [1, ABLK]],
                              cb_src * HID * TPC + tloc)
                    P.i(y.dma_start(
                        out=xfm2_sb[:, xbase2:xbase2 + NDT * ABLK].rearrange(
                            "p (k t) -> p k t", k=NDT),
                        in_=src), ("qa", "qb")[bpar], "y")
                P.sec("y", y_xfm2, dq=(("qa", "qb")[bpar],))
                st2["xfm"] = P.tok(("qa", "qb")[bpar])

                for d in range(NDPC):
                    pp = (PB0_, PB1_)[d % 2]
                    pi = d % 2

                    def t_g(t):
                        P.w(t, "t", [wa2_tok, st2["xfm"], st2["ev"][pi]])
                        for kt in range(NDT):
                            mm = t.matmul(
                                pp[:],
                                wa2_sb[:, kt * DPC + d * 128:kt * DPC + (d + 1) * 128],
                                xfm2_sb[:, xbase2 + kt * ABLK:
                                        xbase2 + (kt + 1) * ABLK],
                                start=(kt == 0), stop=(kt == NDT - 1))
                        P.i(mm, "sp")
                    P.sec("t", t_g)
                    g_mm = P.tok("sp")
                    if d == NDPC - 1:
                        st2["xfm_pe"][bpar] = P.tok("sp")

                    def s_g(s):
                        P.w(s, "s", [g_mm])
                        P.i(s.activation(out=gact_sb[:, d * ABLK:(d + 1) * ABLK],
                                         in_=pp[:], func=AF.Silu), "ss")
                    P.sec("s", s_g)
                    st2["ev"][pi] = P.tok("ss")
                gact_tok = P.tok("ss")

                for s_ in range(NSUB):
                    sc = s_ * CSS
                    T0s = T0 + sc

                    def y_yr(y):
                        P.w(y, "y", [a1_done, P.tok("sv")])
                        P.i(y.dma_start(out=yr_sb[:], in_=y_raw_d[T0s:T0s + CSS, :]),
                            "qy")
                    P.sec("y", y_yr)
                    yr_tok = P.tok("qy")

                    def t_gt(t):
                        P.w(t, "t", [gact_tok, st2["tr_ev"]])
                        for d in range(NDPC):
                            tr = t.transpose(PTR2[:, d * 128:(d + 1) * 128],
                                             gact_sb[:, d * ABLK + sc:d * ABLK + sc + CSS],
                                             ident_bf[:])
                        P.i(tr, "sp")
                    P.sec("t", t_gt)
                    gt_tok = P.tok("sp")

                    def v_g2(v):
                        P.w(v, "v", [gt_tok])
                        P.i(v.tensor_copy(out=gtok_sb[:], in_=PTR2[:]), "sv")
                        P.w(v, "v", [P.tok("sv"), yr_tok])
                        P.i(v.tensor_tensor(out=yg_sb[:], in0=yr_sb[:],
                                            in1=gtok_sb[:], op=ALU.mult), "sv")
                        P.w(v, "v", [P.tok("sv")])
                        P.i(v.tensor_tensor(out=ysq_sb[:], in0=yg_sb[:],
                                            in1=yg_sb[:], op=ALU.mult), "sv")
                        P.w(v, "v", [P.tok("sv")])
                        P.i(v.tensor_reduce(out=ssum_sb[:], in_=ysq_sb[:], axis=AXX,
                                            op=ALU.add), "sv")
                    P.sec("v", v_g2)
                    st2["tr_ev"] = ("sv", P.cnt["sv"] - 3)
                    ssm_tok = P.tok("sv")

                    def v_r0(v):
                        P.w(v, "v", [ssm_tok])
                        P.i(v.tensor_scalar_mul(out=rstd_sb[:], in0=ssum_sb[:],
                                                scalar1=1.0 / DPC), "sv")
                    P.sec("v", v_r0)
                    sc_tok2 = P.tok("sv")

                    def s_r(s):
                        P.w(s, "s", [sc_tok2])
                        P.i(s.activation(out=rstd_sb[:], in_=rstd_sb[:], func=AF.Ln,
                                         bias=eps_sb[:]), "ss")
                    P.sec("s", s_r)
                    ln_tok2 = P.tok("ss")

                    def v_r1(v):
                        P.w(v, "v", [ln_tok2])
                        P.i(v.tensor_scalar_mul(out=rstd_sb[:], in0=rstd_sb[:],
                                                scalar1=-0.5), "sv")
                    P.sec("v", v_r1)
                    ng_tok2 = P.tok("sv")

                    def s_r2(s):
                        P.w(s, "s", [ng_tok2])
                        P.i(s.activation(out=rstd_sb[:], in_=rstd_sb[:],
                                         func=AF.Exp), "ss")
                    P.sec("s", s_r2)
                    rs_tok = P.tok("ss")

                    def v_n(v):
                        P.w(v, "v", [rs_tok, st2["yn_ev"]])
                        P.i(v.tensor_scalar(out=yn_sb[:], in0=yg_sb[:],
                                            scalar1=rstd_sb[:], scalar2=None,
                                            op0=ALU.mult), "sv")
                    P.sec("v", v_n)
                    yn_tok = P.tok("sv")

                    def s_yw(s):
                        P.w(s, "s", [yn_tok])
                        P.i(s.dma_start(out=y_tok_d[T0s:T0s + CSS, :], in_=yn_sb[:]),
                            "qs")
                    P.sec("s", s_yw)
                    st2["yn_ev"] = P.tok("qs")
            ytok_w = P.tok("qs")

            def y_ytr(y):
                P.w(y, "y", [ytok_w])
                for dt_ in range(NDPC):
                    P.w(y, "y", [P.tok("qy")])
                    P.i(y.dma_start_transpose(
                        out=ytr_sb[:], in_=y_tok_d[:, dt_ * 128:(dt_ + 1) * 128]),
                        "qy")
                    P.w(y, "y", [P.tok("qy")])
                    P.i(y.dma_start(out=yT_sl_d[dt_ * 128:(dt_ + 1) * 128, :],
                                    in_=ytr_sb[:]), "qy")
            P.sec("y", y_ytr)
            ya2a_w = P.tok("qy")

            def g_agy(g):
                P.w(g, "g", [ya2a_w])
                P.icc(g.collective_compute(
                    "AllGather", ALU.bypass, replica_groups=[list(range(NCORES))],
                    ins=[yT_sl_d.opt()], outs=[y_all_d.opt()]))
            P.sec("g", g_agy)
            ag_y = P.tok("cc")

            def y_slice(y):
                P.w(y, "y", [ag_y])
                off = y.partition_id() * TPC
                src2 = bass.AP(y_all_d.tensor, off,
                               [[NTOK, 128], [128 * NTOK, NKD], [1, TPC]])
                dst2 = sap(y_b_d, [[TPC, 128], [128 * TPC, NKD], [1, TPC]])
                P.i(y.dma_start(out=dst2, in_=src2), "qy")
            P.sec("y", y_slice)
            ag_y = P.tok("qy")
        a2.free()

        SUPB1 = 256
        NSUPB1 = TPC // SUPB1
        QW = min(1024, HID)
        NQ = HID // QW
        NTTB1 = SUPB1 // 128
        b1 = nc.auto_arena()
        wo_sb = b1.sbuf_tensor("wo_sb", [128, NKD * QW], BF16)
        ysup_sb = b1.sbuf_tensor("ysup_sb", [128, 2 * NKD * SUPB1], BF16)
        xres_sb = b1.sbuf_tensor("xres_sb", [128, QW], F32)
        xn_sb = b1.sbuf_tensor("xn_sb", [128, QW], F32)
        b1.allocate()

        bar2 = P.bar()
        with ExitStack() as _esb1:
            PBQ = _esb1.enter_context(
                nc.psum_tensor("PBQ", [128, 2 * NTTB1 * QW], F32))
            xn_w = None
            y_pe = [None, None]
            wo_pe = None
            ev_b1 = {}
            for q in range(NQ):
                def s_wo2(s):
                    P.w(s, "s", bar2 + [ag_w, wo_pe])
                    src2 = sap(woT_all_d, [[HID, 128], [128 * HID, NKD], [1, QW]],
                               q * QW)
                    P.i(s.dma_start(
                        out=wo_sb[:].rearrange("p (k w) -> p k w", k=NKD),
                        in_=src2), "qc", "s")
                P.sec("s", s_wo2, dq=("qc",))
                wo_ld = P.tok("qc")

                for sup in range(NSUPB1):
                    t0 = sup * SUPB1
                    par = sup % 2
                    ybase = par * NKD * SUPB1

                    def y_ys(y):
                        P.w(y, "y", bar2 + [ag_y, y_pe[par]])
                        src2 = sap(y_b_d, [[TPC, 128], [128 * TPC, NKD], [1, SUPB1]],
                                   t0)
                        P.i(y.dma_start(
                            out=ysup_sb[:, ybase:ybase + NKD * SUPB1].rearrange(
                                "p (k t) -> p k t", k=NKD),
                            in_=src2), ("qa", "qb")[par], "y")
                    P.sec("y", y_ys, dq=(("qa", "qb")[par],))
                    y_ld = P.tok(("qa", "qb")[par])

                    for tt in range(NTTB1):
                        pbase = (par * NTTB1 + tt) * QW

                        def t_op(t):
                            P.w(t, "t", [y_ld, wo_ld, ev_b1.get((par, tt))])
                            for k in range(NKD):
                                for j in range(QW // 512):
                                    mm = t.matmul(
                                        PBQ[:, pbase + j * 512:pbase + (j + 1) * 512],
                                        ysup_sb[:, ybase + k * SUPB1 + tt * 128:
                                                ybase + k * SUPB1 + (tt + 1) * 128],
                                        wo_sb[:, k * QW + j * 512:k * QW + (j + 1) * 512],
                                        start=(k == 0), stop=(k == NKD - 1))
                            P.i(mm, "sp")
                        P.sec("t", t_op)
                        op_mm = P.tok("sp")
                        if tt == NTTB1 - 1:
                            y_pe[par] = op_mm
                            if sup == NSUPB1 - 1:
                                wo_pe = op_mm

                        def y_xr(y):
                            P.w(y, "y", [P.tok("sv")])
                            P.i(y.dma_start(
                                out=xres_sb[:],
                                in_=x_in[t0 + tt * 128:t0 + (tt + 1) * 128,
                                         q * QW:(q + 1) * QW]), "qy")
                        P.sec("y", y_xr)
                        xr_tok = P.tok("qy")

                        def v_xn(v):
                            P.w(v, "v", [op_mm, xr_tok, P.tok("qg")])
                            P.i(v.tensor_tensor(out=xn_sb[:],
                                                in0=PBQ[:, pbase:pbase + QW],
                                                in1=xres_sb[:], op=ALU.add), "sv")
                        P.sec("v", v_xn)
                        ev_b1[(par, tt)] = P.tok("sv")
                        xn_tok = P.tok("sv")

                        def g_xn(g):
                            P.w(g, "g", [xn_tok])
                            P.i(g.dma_start(
                                out=x_new_d[t0 + tt * 128:t0 + (tt + 1) * 128,
                                            q * QW:(q + 1) * QW],
                                in_=xn_sb[:]), "qg")
                        P.sec("g", g_xn)
                        xn_w = P.tok("qg")
        b1.free()
        xnew_done = xn_w

        b2 = nc.auto_arena()
        x2_sb = b2.sbuf_tensor("x2_sb", [128, HID], F32)
        sq2_sb = b2.sbuf_tensor("sq2_sb", [128, SQCH], F32)
        s21_sb = b2.sbuf_tensor("s21_sb", [128, 1], F32)
        s22_sb = b2.sbuf_tensor("s22_sb", [128, 1], F32)
        pt2_sb = b2.sbuf_tensor("pt2_sb", [128, 1], F32)
        m2_sb = b2.sbuf_tensor("m2_sb", [128, 1], F32)
        v2_sb = b2.sbuf_tensor("v2_sb", [128, 1], F32)
        r2_sb = b2.sbuf_tensor("r2_sb", [128, 1], F32)
        o2_sb = b2.sbuf_tensor("o2_sb", [128, HID], BF16)
        xl2T_sb = b2.sbuf_tensor("xl2T_sb", [128, NDT * TPC], BF16)
        b2.allocate()

        bar3 = P.bar()
        xln2_w = ln_pass(x_new_d, xln2_tok_d, TPC, x2_sb, sq2_sb, s21_sb, s22_sb,
                         pt2_sb, m2_sb, v2_sb, r2_sb, o2_sb, bar3 + [xnew_done])

        def y_l2t(y):
            P.w(y, "y", [xln2_w])
            for kt in range(NDT):
                P.i(y.dma_start_transpose(
                    out=xl2T_sb[:, kt * TPC:(kt + 1) * TPC],
                    in_=xln2_tok_d[:, kt * 128:(kt + 1) * 128]), "qy")
        P.sec("y", y_l2t)
        xl2T_tok = P.tok("qy")

        b3 = nc.auto_arena()
        wg_sb = b3.sbuf_tensor("wg_sb", [128, 2 * NDT * 128], BF16)
        wu_sb = b3.sbuf_tensor("wu_sb", [128, 2 * NDT * 128], BF16)
        hs_sb = b3.sbuf_tensor("hs_sb", [128, TPC], BF16)
        ho_sb = b3.sbuf_tensor("ho_sb", [128, TPC], BF16)
        b3.allocate()

        NTW = TPC // 512
        bar4 = P.bar()
        with ExitStack() as _esb3:
            PG_ = _esb3.enter_context(nc.psum_tensor("PG_", [128, 4 * 512], F32))
            PU_ = _esb3.enter_context(nc.psum_tensor("PU_", [128, 4 * 512], F32))
            w_pe = [None, None]
            ev3 = None
            h_w = None
            for it in range(NI):
                par = it % 2
                wslice = slice(par * NDT * 128, (par + 1) * NDT * 128)

                def s_wl(s):
                    P.w(s, "s", bar4 + [ag_w, w_pe[par]])
                    for wsb, wall in ((wg_sb, g_all_d), (wu_sb, u_all_d)):
                        src = sap(wall, [[I, 128], [128 * I, NDT], [1, 128]],
                                  it * 128)
                        P.i(s.dma_start(
                            out=wsb[:, wslice].rearrange("p (k i) -> p k i", k=NDT),
                            in_=src), ("qa", "qb")[par], "s")
                P.sec("s", s_wl, dq=(("qa", "qb")[par],))
                w_ld = P.tok(("qa", "qb")[par])

                def t_f(t):
                    P.w(t, "t", [w_ld, xl2T_tok, ev3])
                    for kt in range(NDT):
                        for tw in range(NTW):
                            mm = t.matmul(
                                PG_[:, tw * 512:(tw + 1) * 512],
                                wg_sb[:, par * NDT * 128 + kt * 128:
                                      par * NDT * 128 + (kt + 1) * 128],
                                xl2T_sb[:, kt * TPC + tw * 512:kt * TPC + (tw + 1) * 512],
                                start=(kt == 0), stop=(kt == NDT - 1))
                    P.i(mm, "sp")
                    for kt in range(NDT):
                        for tw in range(NTW):
                            mm = t.matmul(
                                PU_[:, tw * 512:(tw + 1) * 512],
                                wu_sb[:, par * NDT * 128 + kt * 128:
                                      par * NDT * 128 + (kt + 1) * 128],
                                xl2T_sb[:, kt * TPC + tw * 512:kt * TPC + (tw + 1) * 512],
                                start=(kt == 0), stop=(kt == NDT - 1))
                    P.i(mm, "sp")
                P.sec("t", t_f)
                u_mm = P.tok("sp")
                g_mm = ("sp", P.cnt["sp"] - 1)
                w_pe[par] = u_mm

                def s_sil(s):
                    P.w(s, "s", [g_mm, P.tok("sv")])
                    for tw in range(NTW):
                        P.i(s.activation(out=hs_sb[:, tw * 512:(tw + 1) * 512],
                                         in_=PG_[:, tw * 512:(tw + 1) * 512],
                                         func=AF.Silu), "ss")
                P.sec("s", s_sil)
                sil_tok = P.tok("ss")

                def v_h(v):
                    P.w(v, "v", [sil_tok, u_mm, P.tok("qg")])
                    for tw in range(NTW):
                        P.i(v.tensor_tensor(out=ho_sb[:, tw * 512:(tw + 1) * 512],
                                            in0=PU_[:, tw * 512:(tw + 1) * 512],
                                            in1=hs_sb[:, tw * 512:(tw + 1) * 512],
                                            op=ALU.mult), "sv")
                P.sec("v", v_h)
                ev3 = P.tok("sv")
                h_tok = P.tok("sv")

                def g_h(g):
                    P.w(g, "g", [h_tok])
                    P.i(g.dma_start(out=h_d[it * 128:(it + 1) * 128, :],
                                    in_=ho_sb[:]), "qg")
                P.sec("g", g_h)
                h_w = P.tok("qg")
            h_written = h_w
        b3.free()
        b2.free()

        b4 = nc.auto_arena()
        hsup_sb = b4.sbuf_tensor("hsup_sb", [128, NI * 512], BF16)
        KCH = -(-NI // 4)
        wd_sb = b4.sbuf_tensor("wd_sb", [128, 2 * KCH * 1024], BF16)
        xr4_sb = b4.sbuf_tensor("xr4_sb", [128, 1024], F32)
        o4_sb = b4.sbuf_tensor("o4_sb", [128, 1024], F32)
        b4.allocate()

        SUP4 = 512
        NSUP4 = TPC // SUP4
        SPW = min(1024, HID)
        NSP = HID // SPW
        bar5 = P.bar()
        with ExitStack() as _esb4:
            PD_ = _esb4.enter_context(nc.psum_tensor("PD_", [128, 8 * 512], F32))
            h_pe = None
            out_w = None
            ev4 = None
            wd_war = {0: None, 1: None}
            for sup in range(NSUP4):
                t0 = sup * SUP4

                def y_h(y):
                    P.w(y, "y", bar5 + [h_written, h_pe])
                    src = sap(h_d, [[TPC, 128], [128 * TPC, NI], [1, 512]], t0)
                    P.i(y.dma_start(
                        out=hsup_sb[:].rearrange("p (k t) -> p k t", k=NI),
                        in_=src), "qy")
                P.sec("y", y_h)
                h_ld = P.tok("qy")

                for sp in range(NSP):
                    wd_toks = []
                    nchunk = -(-NI // KCH)
                    for ch in range(nchunk):
                        k0 = ch * KCH
                        k1 = min(NI, k0 + KCH)
                        par = ch % 2

                        def s_wd(s):
                            prev = (wd_toks[ch - 2][1] if ch >= 2
                                    else wd_war[par])
                            P.w(s, "s", bar5 + [ag_w, prev])
                            src = sap(wd_all_d,
                                      [[HID, 128], [128 * HID, k1 - k0], [1, SPW]],
                                      k0 * 128 * HID + sp * SPW)
                            P.i(s.dma_start(
                                out=wd_sb[:, par * KCH * 1024:
                                          par * KCH * 1024 + (k1 - k0) * SPW]
                                .rearrange("p (k w) -> p k w", k=k1 - k0),
                                in_=src), ("qa", "qb")[par], "s")
                        P.sec("s", s_wd, dq=(("qa", "qb")[par],))
                        wd_ld = P.tok(("qa", "qb")[par])

                        def t_dn(t):
                            P.w(t, "t", [wd_ld, h_ld, ev4])
                            for ki in range(k0, k1):
                                for tt in range(4):
                                    for j in range(SPW // 512):
                                        mm = t.matmul(
                                            PD_[:, (tt * (SPW // 512) + j) * 512:
                                                (tt * (SPW // 512) + j + 1) * 512],
                                            hsup_sb[:, ki * 512 + tt * 128:
                                                    ki * 512 + (tt + 1) * 128],
                                            wd_sb[:, par * KCH * 1024 + (ki - k0) * SPW
                                                  + j * 512:
                                                  par * KCH * 1024 + (ki - k0) * SPW
                                                  + (j + 1) * 512],
                                            start=(ki == 0), stop=(ki == NI - 1))
                            P.i(mm, "sp")
                        P.sec("t", t_dn)
                        wd_toks.append((wd_ld, P.tok("sp")))
                        wd_war[par] = P.tok("sp")
                    dn_mm = P.tok("sp")
                    h_pe = dn_mm

                    for tt in range(4):
                        def y_x4(y):
                            P.w(y, "y", [P.tok("sv")])
                            P.i(y.dma_start(
                                out=xr4_sb[:, 0:SPW],
                                in_=x_new_d[t0 + tt * 128:t0 + (tt + 1) * 128,
                                            sp * SPW:(sp + 1) * SPW]), "qy")
                        P.sec("y", y_x4)
                        xr_tok = P.tok("qy")

                        def v_o(v):
                            P.w(v, "v", [dn_mm, xr_tok, P.tok("qs")])
                            for j in range(SPW // 512):
                                P.i(v.tensor_tensor(
                                    out=o4_sb[:, j * 512:(j + 1) * 512],
                                    in0=PD_[:, (tt * (SPW // 512) + j) * 512:
                                            (tt * (SPW // 512) + j + 1) * 512],
                                    in1=xr4_sb[:, j * 512:(j + 1) * 512],
                                    op=ALU.add), "sv")
                        P.sec("v", v_o)
                        o_tok = P.tok("sv")
                        ev4 = o_tok

                        def s_o(s):
                            P.w(s, "s", [o_tok])
                            P.i(s.dma_start(
                                out=out_t[t0 + tt * 128:t0 + (tt + 1) * 128,
                                          sp * SPW:(sp + 1) * SPW],
                                in_=o4_sb[:, 0:SPW]), "qs")
                        P.sec("s", s_o)
                        out_w = P.tok("qs")
        b4.free()

        def s_end(s):
            P.w(s, "s", [out_w, P.tok("qg"), P.tok("qy")])
        P.sec("s", s_end)

        def y_end(y):
            P.w(y, "y", [out_w])
        P.sec("y", y_end)

    return nc



def shard_inputs(cfg, inputs):
    c = cfg
    HID, I, Bsz, L = c["HID"], c["I"], c["B"], c["L"]
    DIN, NST, HPC, DPC = c["DIN"], c["NST"], c["HPC"], c["DPC"]
    TPC, QG, G = c["TPC"], c["QG"], c["G"]
    DSH, NGRP = c["DSH"], c["NGRP"]
    ISH, NIT_SH, IPAD = c["ISH"], c["NIT_SH"], c["IPAD"]
    HP8 = HID // 8

    x = np.ascontiguousarray(
        np.asarray(inputs["L0_attn_hidden_states"]).reshape(Bsz * L, HID))
    wi = np.asarray(inputs["L0_attn_in_proj_w"])
    convw = np.asarray(inputs["L0_attn_conv_w"])[:, 0, :]
    convb = np.asarray(inputs["L0_attn_conv_b"])
    dtb = np.asarray(inputs["L0_attn_dt_bias"])
    alog = np.asarray(inputs["L0_attn_A_log"])
    dvec = np.asarray(inputs["L0_attn_D"])
    normw = np.asarray(inputs["L0_attn_norm_w"])
    wo = np.asarray(inputs["L0_attn_out_proj_w"])
    gq = np.asarray(inputs["L0_ffn_gate_qweight"])
    gs = np.asarray(inputs["L0_ffn_gate_scales"])
    gz = np.asarray(inputs["L0_ffn_gate_qzeros"])
    uq = np.asarray(inputs["L0_ffn_up_qweight"])
    us = np.asarray(inputs["L0_ffn_up_scales"])
    uz = np.asarray(inputs["L0_ffn_up_qzeros"])
    dq = np.asarray(inputs["L0_ffn_down_qweight"])
    ds = np.asarray(inputs["L0_ffn_down_scales"])
    dz = np.asarray(inputs["L0_ffn_down_qzeros"])

    dqp = np.zeros((IPAD, HP8), np.int32); dqp[:I] = dq
    dsp = np.zeros((IPAD // QG, HID), np.float32); dsp[:ds.shape[0]] = ds
    dzp = np.zeros((IPAD // QG, HP8), np.int32); dzp[:dz.shape[0]] = dz

    maps = []
    for cid in range(NCORES):
        xr = slice(DIN + DPC * cid, DIN + DPC * (cid + 1))
        br = slice(2 * DIN + NST * cid, 2 * DIN + NST * (cid + 1))
        cr = slice(2 * DIN + G * NST + NST * cid, 2 * DIN + G * NST + NST * (cid + 1))
        dtr = slice(DIN + c["CONVD"] + HPC * cid, DIN + c["CONVD"] + HPC * (cid + 1))
        gr = slice(DPC * cid, DPC * (cid + 1))
        m = dict(
            x=np.ascontiguousarray(x[TPC * cid:TPC * (cid + 1)]),
            wa1=np.ascontiguousarray(
                np.concatenate([wi[xr], wi[br], wi[cr], wi[dtr]], axis=0).T),
            wa2=np.ascontiguousarray(wi[gr].T),
            convw=np.ascontiguousarray(np.concatenate(
                [convw[DPC * cid:DPC * (cid + 1)],
                 convw[DIN + NST * cid:DIN + NST * (cid + 1)],
                 convw[DIN + G * NST + NST * cid:DIN + G * NST + NST * (cid + 1)]],
                axis=0).reshape(-1, 128, convw.shape[1]).transpose(1, 0, 2)
                .reshape(128, -1)),
            convb=np.ascontiguousarray(np.concatenate(
                [convb[DPC * cid:DPC * (cid + 1)],
                 convb[DIN + NST * cid:DIN + NST * (cid + 1)],
                 convb[DIN + G * NST + NST * cid:DIN + G * NST + NST * (cid + 1)]],
                axis=0).reshape(-1, 128).T),
            dtb=np.ascontiguousarray(dtb[HPC * cid:HPC * (cid + 1)])[:, None],
            alog=np.ascontiguousarray(alog[HPC * cid:HPC * (cid + 1)])[:, None],
            dvec=np.ascontiguousarray(dvec[HPC * cid:HPC * (cid + 1)])[:, None],
            normw=np.ascontiguousarray(
                normw[DPC * cid:DPC * (cid + 1)].reshape(-1, 128).T),
            woutt=np.ascontiguousarray(wo[:, DPC * cid:DPC * (cid + 1)].T),
            gq=np.ascontiguousarray(gq[DSH * cid:DSH * (cid + 1)]),
            gs=np.ascontiguousarray(gs[NGRP * cid:NGRP * (cid + 1)]),
            gz=np.ascontiguousarray(gz[NGRP * cid:NGRP * (cid + 1)]),
            uq=np.ascontiguousarray(uq[DSH * cid:DSH * (cid + 1)]),
            us=np.ascontiguousarray(us[NGRP * cid:NGRP * (cid + 1)]),
            uz=np.ascontiguousarray(uz[NGRP * cid:NGRP * (cid + 1)]),
            dq=np.ascontiguousarray(dqp[ISH * cid:ISH * (cid + 1)]),
            ds=np.ascontiguousarray(dsp[NIT_SH * cid:NIT_SH * (cid + 1)]),
            dz=np.ascontiguousarray(dzp[NIT_SH * cid:NIT_SH * (cid + 1)]),
        )
        maps.append(m)
    return maps


_CACHE = {}


def kernel(**inputs):
    cfg = REAL_CFG
    maps = shard_inputs(cfg, inputs)
    if "nc" not in _CACHE:
        _CACHE["nc"] = build_program(cfg)
    res = run_bass_kernel_spmd(_CACHE["nc"], maps, core_ids=list(range(NCORES)))
    out = np.concatenate([np.asarray(r["out"]) for r in res.results], axis=0)
    return np.ascontiguousarray(
        out.reshape(cfg["B"], cfg["L"], cfg["HID"]).astype(np.float32))


# version 13
# speedup vs baseline: 1.0865x; 1.0229x over previous
import numpy as np
from contextlib import ExitStack

import concourse.bass as bass
import concourse.mybir as mybir
from concourse.bass_utils import run_bass_kernel_spmd

F32 = mybir.dt.float32
BF16 = mybir.dt.bfloat16
I32 = mybir.dt.int32
AF = mybir.ActivationFunctionType
ALU = mybir.AluOpType
AXX = mybir.AxisListType.X

NCORES = 8
EPS = 1e-5
AWQ_INV = [0, 2, 4, 6, 1, 3, 5, 7]


def make_cfg(HID=4096, NH=128, HD=64, NST=128, KC=4, I=11008, B=2, L=8192):
    G = 8
    QG = 128
    c = dict(HID=HID, NH=NH, HD=HD, NST=NST, KC=KC, I=I, B=B, L=L, G=G, QG=QG)
    c["DIN"] = NH * HD
    c["CONVD"] = c["DIN"] + 2 * G * NST
    c["HPC"] = NH // NCORES
    c["DPC"] = c["HPC"] * HD
    c["NDPC"] = c["DPC"] // 128
    c["TPC"] = B * L // NCORES
    c["NTOK"] = B * L
    c["NDT"] = HID // 128
    c["ABLK"] = 512
    c["NAB"] = c["NTOK"] // c["ABLK"]
    c["AB_PER_BATCH"] = c["NAB"] // B
    c["CSS"] = 128
    c["NSUB"] = c["ABLK"] // c["CSS"]
    c["CA1"] = c["DPC"] + 2 * NST + c["HPC"]
    c["NCONVT"] = (c["DPC"] + 2 * NST) // 128
    c["DSH"] = HID // NCORES
    c["NGT"] = c["DSH"] // 128
    c["NGRP"] = c["DSH"] // QG
    c["IPAD"] = -(-I // (128 * NCORES)) * (128 * NCORES)
    c["ISH"] = c["IPAD"] // NCORES
    c["NIT_SH"] = c["ISH"] // 128
    c["NI"] = I // 128
    c["NTT"] = c["TPC"] // 128
    assert NST == 128 and c["DPC"] % 128 == 0 and HID % 128 == 0
    assert L % c["ABLK"] == 0 and c["DSH"] % QG == 0
    return c


REAL_CFG = make_cfg()

_EK2NAME = {"v": "vector", "s": "scalar", "t": "tensor", "g": "gpsimd", "y": "sync"}


_QS_OF = {"y": "qy", "s": "qs", "g": "qg"}


class Prog:
    def __init__(self, nc, block, sems):
        self.nc = nc
        self.block = block
        self.sem = sems
        self.cnt = {k: 0 for k in sems}
        self.lastw = {e: {k: 0 for k in sems} for e in _EK2NAME}
        self.maxw = {k: 0 for k in sems}
        self.issuer = {e: set([q]) if q else set()
                       for e, q in ((ek, _QS_OF.get(ek)) for ek in _EK2NAME)}

    def sec(self, ek, fn, dq=None):
        def wrapped(eng):
            qs = dq if dq is not None else self.issuer.get(ek, ())
            for q in qs:
                if self.maxw[q] > self.lastw[ek][q]:
                    eng.wait_ge(self.sem[q], self.maxw[q])
                    self.lastw[ek][q] = self.maxw[q]
            fn(eng)
        getattr(self.block, _EK2NAME[ek])(wrapped)

    def w(self, eng, ek, tokens):
        for tok in tokens:
            if tok is None:
                continue
            sk, val = tok
            if sk.startswith("q"):
                val = self.cnt[sk]
                if val > self.maxw[sk]:
                    self.maxw[sk] = val
            if self.lastw[ek][sk] < val:
                eng.wait_ge(self.sem[sk], val)
                self.lastw[ek][sk] = val

    def i(self, instr, sk, ek=None):
        amount = 16 if sk.startswith("q") else 1
        self.cnt[sk] += amount
        instr.then_inc(self.sem[sk], amount)
        if ek is not None:
            self.issuer[ek].add(sk)
        return (sk, self.cnt[sk])

    def icc(self, instr):
        self.cnt["cc"] += 1
        instr.then_inc(self.sem["cc"])
        return ("cc", self.cnt["cc"])

    def tok(self, sk):
        return (sk, self.cnt[sk])

    def bar(self):
        return [self.tok(k) for k in self.sem]


def sap(handle_or_ap, dims, offset=0):
    base = handle_or_ap if isinstance(handle_or_ap, bass.AP) else handle_or_ap[:]
    return bass.AP(base.tensor, base.offset + offset, [list(d) for d in dims])


def fexp(ap2d, ncols, reps, col0=0):
    return bass.AP(ap2d.tensor, ap2d.offset + col0,
                   [list(ap2d.ap[0]), [1, ncols], [0, reps]])


def frep(ap2d, ncols, reps, col0=0):
    return bass.AP(ap2d.tensor, ap2d.offset + col0,
                   [list(ap2d.ap[0]), [0, reps], [1, ncols]])


def build_program(cfg):
    C = cfg
    HID, I, Bsz = C["HID"], C["I"], C["B"]
    HD, HPC, DPC, NDPC = C["HD"], C["HPC"], C["DPC"], C["NDPC"]
    TPC, NTOK, NDT = C["TPC"], C["NTOK"], C["NDT"]
    ABLK, NAB, CSS, NSUB = C["ABLK"], C["NAB"], C["CSS"], C["NSUB"]
    CA1, NCONVT = C["CA1"], C["NCONVT"]
    DSH, NGT, NGRP, QG = C["DSH"], C["NGT"], C["NGRP"], C["QG"]
    IPAD, ISH, NIT_SH, NI = C["IPAD"], C["ISH"], C["NIT_SH"], C["NI"]
    NST, KC, DIN, NTT = C["NST"], C["KC"], C["DIN"], C["NTT"]
    IP8, HP8 = I // 8, HID // 8
    AH = ABLK + KC - 1
    NKD = DIN // 128

    nc = bass.Bass("TRN2", target_bir_lowering=False)

    ei = lambda n, sh, dt=F32: nc.dram_tensor(n, sh, dt, kind="ExternalInput").ap()
    x_in = ei("x", [TPC, HID])
    wa1_in = ei("wa1", [HID, CA1])
    wa2_in = ei("wa2", [HID, DPC])
    convw_in = ei("convw", [128, NCONVT * KC])
    convb_in = ei("convb", [128, NCONVT])
    dtb_in = ei("dtb", [HPC, 1])
    alog_in = ei("alog", [HPC, 1])
    dvec_in = ei("dvec", [HPC, 1])
    normw_in = ei("normw", [128, NDPC])
    woutt_in = ei("woutt", [DPC, HID])
    gq_in = ei("gq", [DSH, IP8], I32)
    gs_in = ei("gs", [NGRP, I])
    gz_in = ei("gz", [NGRP, IP8], I32)
    uq_in = ei("uq", [DSH, IP8], I32)
    us_in = ei("us", [NGRP, I])
    uz_in = ei("uz", [NGRP, IP8], I32)
    dq_in = ei("dq", [ISH, HP8], I32)
    ds_in = ei("ds", [NIT_SH, HID])
    dz_in = ei("dz", [NIT_SH, HP8], I32)
    out_t = nc.dram_tensor("out", [TPC, HID], F32, kind="ExternalOutput").ap()

    di = lambda n, sh, dt: nc.dram_tensor(n, sh, dt).ap()
    xln_tok_d = di("xln_tok", [TPC, HID], BF16)
    xlnT_sl_d = di("xlnT_sl", [HID, TPC], BF16)
    xlnT_all_d = di("xlnT_all", [NCORES * HID, TPC], BF16)
    gsl_d = di("gsl", [DSH, I], BF16)
    g_all_d = nc.dram_tensor("g_all", [HID, I], BF16, addr_space="Shared").ap()
    usl_d = di("usl", [DSH, I], BF16)
    u_all_d = nc.dram_tensor("u_all", [HID, I], BF16, addr_space="Shared").ap()
    wdsl_d = di("wdsl", [ISH, HID], BF16)
    wd_all_d = nc.dram_tensor("wd_all", [IPAD, HID], BF16, addr_space="Shared").ap()
    woT_sl_d = di("woT_sl", [DPC, HID], BF16)
    woT_all_d = nc.dram_tensor("woT_all", [DIN, HID], BF16, addr_space="Shared").ap()
    zsg_d = di("zsg", [NGRP, I], F32)
    zsu_d = di("zsu", [NGRP, I], F32)
    zsd_d = di("zsd", [NIT_SH, HID], F32)
    dbounce_d = di("dbounce", [1, HPC], F32)
    acb_d = di("acb", [NAB * NSUB * HPC, CSS], F32)
    prow_d = di("prow", [NAB * NSUB, HPC], F32)
    y_raw_d = di("y_raw", [NTOK, DPC], BF16)
    y_tok_d = di("y_tok", [NTOK, DPC], BF16)
    yT_sl_d = di("yT_sl", [DPC, NTOK], BF16)
    y_all_d = di("y_all", [DIN, NTOK], BF16)
    y_b_d = di("y_b", [DIN, TPC], BF16)
    x_new_d = di("x_new", [TPC, HID], F32)
    xln2_tok_d = di("xln2_tok", [TPC, HID], BF16)
    h_d = di("h", [I, TPC], BF16)

    with ExitStack() as _es:
        _sems = {}
        for _sk in ("sv", "ss", "sp", "sg", "qy", "qs", "qg",
                    "qa", "qb", "qc", "qd", "qe", "qf", "cc"):
            _sems[_sk] = _es.enter_context(nc.semaphore(_sk))
        (_sv, _ss, _sp, _sg, _qy, _qs, _qg, _qa, _qb, _qc, _qd, _qe, _qf,
         _cc) = (_sems[k] for k in ("sv", "ss", "sp", "sg", "qy", "qs", "qg",
                                    "qa", "qb", "qc", "qd", "qe", "qf", "cc"))
        block = _es.enter_context(nc.Block())
        P = Prog(nc, block, dict(sv=_sv, ss=_ss, sp=_sp, sg=_sg,
                                 qy=_qy, qs=_qs, qg=_qg, cc=_cc,
                                 qa=_qa, qb=_qb, qc=_qc, qd=_qd, qe=_qe, qf=_qf))

        pers = nc.auto_arena()
        ident_bf = pers.sbuf_tensor("ident_bf", [128, 128], BF16)
        ident_f = pers.sbuf_tensor("ident_f", [128, 128], F32)
        ut_sb = pers.sbuf_tensor("ut_sb", [128, CSS], F32)
        dexp_sb = pers.sbuf_tensor("dexp_sb", [128, DPC], F32)
        convw_sb = pers.sbuf_tensor("convw_sb", [128, NCONVT * KC], F32)
        convb_sb = pers.sbuf_tensor("convb_sb", [128, NCONVT], F32)
        dtb_sb = pers.sbuf_tensor("dtb_sb", [HPC, 1], F32)
        acol_sb = pers.sbuf_tensor("acol_sb", [HPC, 1], F32)
        dvec_sb = pers.sbuf_tensor("dvec_sb", [HPC, 1], F32)
        ones_sb = pers.sbuf_tensor("ones_sb", [1, 128], F32)
        eps_sb = pers.sbuf_tensor("eps_sb", [128, 1], F32)
        drow_sb = pers.sbuf_tensor("drow_sb", [1, HPC], F32)
        pers.allocate()

        def y_setup(y):
            P.i(y.dma_start(out=convw_sb[:], in_=convw_in), "qy")
            P.i(y.dma_start(out=convb_sb[:], in_=convb_in), "qy")
            for dst, src in ((dtb_sb, dtb_in), (acol_sb, alog_in), (dvec_sb, dvec_in)):
                P.i(y.dma_start(out=dst[:], in_=src), "qy")
        P.sec("y", y_setup)
        ld_small = P.tok("qy")

        def v_setup(v):
            P.w(v, "v", [ld_small])
            P.i(v.memset(ident_bf[:], 1.0), "sv")
            P.i(v.memset(ident_f[:], 1.0), "sv")
            P.i(v.memset(ut_sb[:], 0.0), "sv")
            P.i(v.memset(ones_sb[:], 1.0), "sv")
            P.i(v.memset(eps_sb[:], EPS), "sv")
        P.sec("v", v_setup)
        ident_tok0 = P.tok("sv")

        def g_setup(g):
            P.w(g, "g", [ident_tok0])
            P.i(g.affine_select(out=ut_sb[:], in_=ut_sb[:], pattern=[[1, CSS]],
                                compare_op=ALU.is_ge, fill=-1e9, base=0,
                                channel_multiplier=-1), "sg")
            P.i(g.affine_select(out=ident_bf[:], in_=ident_bf[:],
                                pattern=[[1, 128]], compare_op=ALU.is_equal,
                                fill=0.0, base=0, channel_multiplier=-1), "sg")
            P.i(g.affine_select(out=ident_f[:], in_=ident_f[:],
                                pattern=[[1, 128]], compare_op=ALU.is_equal,
                                fill=0.0, base=0, channel_multiplier=-1), "sg")
        P.sec("g", g_setup)
        ut_tok = P.tok("sg")
        ident_tok = P.tok("sg")

        def s_setup(s):
            P.w(s, "s", [ld_small])
            P.i(s.activation(out=acol_sb[:], in_=acol_sb[:], func=AF.Exp), "ss")
        P.sec("s", s_setup)

        def v_setup2(v):
            P.w(v, "v", [P.tok("ss")])
            P.i(v.tensor_scalar_mul(out=acol_sb[:], in0=acol_sb[:], scalar1=-1.0), "sv")
        P.sec("v", v_setup2)
        a_tok = P.tok("sv")

        def y_setup3(y):
            P.w(y, "y", [ld_small])
            P.i(y.dma_start(out=sap(dbounce_d, [[1, HPC], [0, 1]]), in_=dvec_sb[:]),
                "qy")
            P.w(y, "y", [P.tok("qy")])
            P.i(y.dma_start(out=drow_sb[:], in_=dbounce_d), "qy")
        P.sec("y", y_setup3)
        drow_tok = P.tok("qy")

        with ExitStack() as _esdx:
            PDX = _esdx.enter_context(nc.psum_tensor("PDX", [128, DPC], F32))
            def t_setup(t):
                P.w(t, "t", [drow_tok, ident_tok])
                step = min(512, DPC)
                rhs_full = bass.AP(drow_sb[:].tensor, drow_sb[:].offset,
                                   [[HPC, 1], [1, HPC], [0, HD]])
                for i0 in range(0, DPC, step):
                    rhs = bass.AP(rhs_full.tensor,
                                  rhs_full.offset + i0 // HD,
                                  [[HPC, 1], [1, step // HD], [0, HD]])
                    mm = t.matmul(PDX[:, i0:i0 + step], ones_sb[:], rhs,
                                  start=True, stop=True)
                P.i(mm, "sp")
            P.sec("t", t_setup)
            dx_mm = P.tok("sp")

            def v_setup3(v):
                P.w(v, "v", [dx_mm])
                P.i(v.tensor_copy(out=dexp_sb[:], in_=PDX[:]), "sv")
            P.sec("v", v_setup3)
        dexp_tok = P.tok("sv")

        ph0a = nc.auto_arena()
        p0x = ph0a.sbuf_tensor("p0x", [128, HID], F32)
        SQCH = min(1024, HID)
        p0sq = ph0a.sbuf_tensor("p0sq", [128, SQCH], F32)
        p0s1 = ph0a.sbuf_tensor("p0s1", [128, 1], F32)
        p0s2 = ph0a.sbuf_tensor("p0s2", [128, 1], F32)
        p0pt = ph0a.sbuf_tensor("p0pt", [128, 1], F32)
        p0m = ph0a.sbuf_tensor("p0m", [128, 1], F32)
        p0v = ph0a.sbuf_tensor("p0v", [128, 1], F32)
        p0r = ph0a.sbuf_tensor("p0r", [128, 1], F32)
        p0o = ph0a.sbuf_tensor("p0o", [128, HID], BF16)
        p0t = ph0a.sbuf_tensor("p0t", [128, TPC], BF16)
        ph0a.allocate()

        def ln_pass(src_dram, dst_dram, rows, x_sb, sq_sb, s1_sb, s2_sb, pt_sb,
                    m_sb, v_sb, r_sb, o_sb, pre_bar):
            written = None
            nsq = HID // SQCH
            for it in range(rows // 128):
                r0 = it * 128

                def y_fn(y):
                    P.w(y, "y", pre_bar + [P.tok("sv"), P.tok("qs")])
                    P.i(y.dma_start(out=x_sb[:], in_=src_dram[r0:r0 + 128, :]), "qy")
                P.sec("y", y_fn)
                xt = P.tok("qy")

                def v_fn(v):
                    P.w(v, "v", pre_bar + [xt])
                    P.i(v.tensor_reduce(out=s1_sb[:], in_=x_sb[:], axis=AXX,
                                        op=ALU.add), "sv")
                    for ic in range(nsq):
                        P.w(v, "v", [P.tok("sv")])
                        P.i(v.tensor_tensor(out=sq_sb[:],
                                            in0=x_sb[:, ic * SQCH:(ic + 1) * SQCH],
                                            in1=x_sb[:, ic * SQCH:(ic + 1) * SQCH],
                                            op=ALU.mult), "sv")
                        P.w(v, "v", [P.tok("sv")])
                        P.i(v.tensor_reduce(out=pt_sb[:], in_=sq_sb[:], axis=AXX,
                                            op=ALU.add), "sv")
                        P.w(v, "v", [P.tok("sv")])
                        if ic == 0:
                            P.i(v.tensor_copy(out=s2_sb[:], in_=pt_sb[:]), "sv")
                        else:
                            P.i(v.tensor_tensor(out=s2_sb[:], in0=s2_sb[:],
                                                in1=pt_sb[:], op=ALU.add), "sv")
                    P.w(v, "v", [P.tok("sv")])
                    P.i(v.tensor_scalar_mul(out=m_sb[:], in0=s1_sb[:],
                                            scalar1=1.0 / HID), "sv")
                    P.w(v, "v", [P.tok("sv")])
                    P.i(v.tensor_tensor(out=v_sb[:], in0=m_sb[:], in1=m_sb[:],
                                        op=ALU.mult), "sv")
                    P.w(v, "v", [P.tok("sv")])
                    P.i(v.scalar_tensor_tensor(out=v_sb[:], in0=s2_sb[:],
                                               scalar=1.0 / HID, in1=v_sb[:],
                                               op0=ALU.mult, op1=ALU.subtract), "sv")
                P.sec("v", v_fn)
                var_tok = P.tok("sv")

                def s_fn(s):
                    P.w(s, "s", [var_tok])
                    P.i(s.activation(out=r_sb[:], in_=v_sb[:], func=AF.Ln,
                                     bias=eps_sb[:]), "ss")
                P.sec("s", s_fn)
                ln_tok = P.tok("ss")

                def vr_fn(v):
                    P.w(v, "v", [ln_tok])
                    P.i(v.tensor_scalar_mul(out=r_sb[:], in0=r_sb[:],
                                            scalar1=-0.5), "sv")
                P.sec("v", vr_fn)
                neg_tok = P.tok("sv")

                def s1b_fn(s):
                    P.w(s, "s", [neg_tok])
                    P.i(s.activation(out=r_sb[:], in_=r_sb[:], func=AF.Exp), "ss")
                P.sec("s", s1b_fn)
                rs_tok = P.tok("ss")

                def v2_fn(v):
                    P.w(v, "v", [rs_tok, P.tok("qs")])
                    P.i(v.tensor_scalar(out=o_sb[:], in0=x_sb[:], scalar1=m_sb[:],
                                        scalar2=r_sb[:], op0=ALU.subtract,
                                        op1=ALU.mult), "sv")
                P.sec("v", v2_fn)
                o_tok = P.tok("sv")

                def s2_fn(s):
                    P.w(s, "s", [o_tok])
                    P.i(s.dma_start(out=dst_dram[r0:r0 + 128, :], in_=o_sb[:]), "qs")
                P.sec("s", s2_fn)
                written = P.tok("qs")
            return written

        xln_w = ln_pass(x_in, xln_tok_d, TPC, p0x, p0sq, p0s1, p0s2, p0pt,
                        p0m, p0v, p0r, p0o, [])

        def y_xlt(y):
            P.w(y, "y", [xln_w])
            for kt in range(NDT):
                P.w(y, "y", [P.tok("qy")])
                P.i(y.dma_start_transpose(out=p0t[:],
                                          in_=xln_tok_d[:, kt * 128:(kt + 1) * 128]),
                    "qy")
                P.w(y, "y", [P.tok("qy")])
                P.i(y.dma_start(out=xlnT_sl_d[kt * 128:(kt + 1) * 128, :],
                                in_=p0t[:]), "qy")
        P.sec("y", y_xlt)
        xlt_w = P.tok("qy")

        def g_ag1(g):
            P.w(g, "g", [xlt_w])
            P.icc(g.collective_compute(
                "AllGather", ALU.bypass, replica_groups=[list(range(NCORES))],
                ins=[xlnT_sl_d.opt()], outs=[xlnT_all_d.opt()]))
        P.sec("g", g_ag1)
        ag_xln = P.tok("cc")
        ph0a.free()

        ICH = I // 4 if I > 2048 else I
        NICH = I // ICH
        ph0b = nc.auto_arena()
        dqq = ph0b.sbuf_tensor("dqq", [128, 2 * (ICH // 8)], I32)
        dqi = ph0b.sbuf_tensor("dqi", [128, ICH], I32)
        dqn = ph0b.sbuf_tensor("dqn", [128, ICH], F32)
        dqs = ph0b.sbuf_tensor("dqs", [128, 2 * ICH], F32)
        dqz = ph0b.sbuf_tensor("dqz", [128, 2 * ICH], F32)
        dqw = ph0b.sbuf_tensor("dqw", [128, ICH], BF16)
        zqq = ph0b.sbuf_tensor("zqq", [NGRP, IP8], I32)
        zqi = ph0b.sbuf_tensor("zqi", [NGRP, ICH], I32)
        zqs = ph0b.sbuf_tensor("zqs", [NGRP, ICH], F32)
        zsc = ph0b.sbuf_tensor("zsc", [NGRP, ICH], F32)
        ph0b.allocate()
        bar0b = P.bar()

        def dequant_gu(qin, sin, zin, zs_dram, out_dram):
            def y_fn(y):
                P.w(y, "y", bar0b + [P.tok("sv")])
                P.i(y.dma_start(out=zqq[:], in_=zin), "qy")
            P.sec("y", y_fn)
            zl = P.tok("qy")

            for ic in range(NICH):
                c0 = ic * ICH

                def yz_fn(y):
                    P.w(y, "y", [P.tok("sv")])
                    P.i(y.dma_start(out=zsc[:], in_=sin[:, c0:c0 + ICH]), "qy")
                P.sec("y", yz_fn)
                zscl = P.tok("qy")

                def v_fn(v):
                    P.w(v, "v", bar0b + [zl, zscl, P.tok("qy")])
                    for sft in range(8):
                        P.i(v.tensor_scalar(out=zqi[:, AWQ_INV[sft]::8],
                                            in0=zqq[:, c0 // 8:(c0 + ICH) // 8],
                                            scalar1=4 * sft, scalar2=15,
                                            op0=ALU.logical_shift_right,
                                            op1=ALU.bitwise_and), "sv")
                    P.w(v, "v", [P.tok("sv")])
                    P.i(v.tensor_copy(out=zqs[:], in_=zqi[:]), "sv")
                    P.w(v, "v", [P.tok("sv")])
                    P.i(v.tensor_tensor(out=zqs[:], in0=zqs[:], in1=zsc[:],
                                        op=ALU.mult), "sv")
                P.sec("v", v_fn)
                zs_tok = P.tok("sv")

                def y2_fn(y):
                    P.w(y, "y", [zs_tok])
                    P.i(y.dma_start(out=zs_dram[:, c0:c0 + ICH], in_=zqs[:]), "qy")
                P.sec("y", y2_fn)
            zs_w = P.tok("qy")

            prev_ext = [None, None]
            dqi_free = None
            it_idx = 0
            for gt in range(NGT):
                for ic in range(NICH):
                    c0 = ic * ICH
                    par = it_idx % 2
                    it_idx += 1
                    qb8 = par * (ICH // 8)
                    qbc = par * ICH

                    def y3_fn(y):
                        P.w(y, ("y", "s")[par], [zs_w, prev_ext[par]])
                        P.i(y.dma_start(
                            out=dqq[:, qb8:qb8 + ICH // 8],
                            in_=qin[gt * 128:(gt + 1) * 128,
                                    c0 // 8:(c0 + ICH) // 8]),
                            ("qa", "qb")[par], ("y", "s")[par])
                        P.i(y.dma_start(out=dqs[:, qbc:qbc + ICH],
                                        in_=sap(sin, [[0, 128], [1, ICH]],
                                                gt * I + c0)),
                            ("qa", "qb")[par], ("y", "s")[par])
                        P.i(y.dma_start(out=dqz[:, qbc:qbc + ICH],
                                        in_=sap(zs_dram, [[0, 128], [1, ICH]],
                                                gt * I + c0)),
                            ("qa", "qb")[par], ("y", "s")[par])
                    P.sec(("y", "s")[par], y3_fn, dq=(("qa", "qb")[par],))
                    ql = P.tok(("qa", "qb")[par])

                    def v3_fn(v):
                        P.w(v, "v", [ql, dqi_free])
                        for sft in range(8):
                            P.i(v.tensor_scalar(out=dqi[:, AWQ_INV[sft]::8],
                                                in0=dqq[:, qb8:qb8 + ICH // 8],
                                                scalar1=4 * sft, scalar2=15,
                                                op0=ALU.logical_shift_right,
                                                op1=ALU.bitwise_and), "sv")
                    P.sec("v", v3_fn)
                    ext_v = P.tok("sv")
                    ext_g = ext_v

                    dqi_free_local = [None]

                    def v4_fn(v):
                        P.w(v, "v", [ext_v, ext_g])
                        P.i(v.tensor_copy(out=dqn[:], in_=dqi[:]), "sv")
                        dqi_free_local[0] = P.tok("sv")
                        P.w(v, "v", [P.tok("sv")])
                        P.i(v.tensor_tensor(out=dqn[:], in0=dqn[:],
                                            in1=dqs[:, qbc:qbc + ICH],
                                            op=ALU.mult), "sv")
                        P.w(v, "v", [P.tok("sv"), P.tok("qs")])
                        P.i(v.tensor_tensor(out=dqw[:], in0=dqn[:],
                                            in1=dqz[:, qbc:qbc + ICH],
                                            op=ALU.subtract), "sv")
                    P.sec("v", v4_fn)
                    dqi_free = dqi_free_local[0]
                    wt = P.tok("sv")
                    prev_ext[par] = wt

                    def s3_fn(s):
                        P.w(s, "s", [wt])
                        P.i(s.dma_start(
                            out=out_dram[gt * 128:(gt + 1) * 128, c0:c0 + ICH],
                            in_=dqw[:]), "qs")
                    P.sec("s", s3_fn)

        dequant_gu(gq_in, gs_in, gz_in, zsg_d, gsl_d)
        dequant_gu(uq_in, us_in, uz_in, zsu_d, usl_d)
        ph0b.free()

        ph0c = nc.auto_arena()
        dnq = ph0c.sbuf_tensor("dnq", [128, HP8], I32)
        dni = ph0c.sbuf_tensor("dni", [128, HID], I32)
        dnn = ph0c.sbuf_tensor("dnn", [128, HID], F32)
        dns = ph0c.sbuf_tensor("dns", [128, HID], F32)
        dnz = ph0c.sbuf_tensor("dnz", [128, HID], F32)
        dnw = ph0c.sbuf_tensor("dnw", [128, HID], BF16)
        znq = ph0c.sbuf_tensor("znq", [NIT_SH, HP8], I32)
        zni = ph0c.sbuf_tensor("zni", [NIT_SH, HID], I32)
        zns = ph0c.sbuf_tensor("zns", [NIT_SH, HID], F32)
        znc = ph0c.sbuf_tensor("znc", [NIT_SH, HID], F32)
        ph0c.allocate()
        bar0c = P.bar()

        def y_dn(y):
            P.w(y, "y", bar0c + [P.tok("sv")])
            P.i(y.dma_start(out=znq[:], in_=dz_in), "qy")
            P.i(y.dma_start(out=znc[:], in_=ds_in), "qy")
        P.sec("y", y_dn)
        znl = P.tok("qy")

        def v_dn(v):
            P.w(v, "v", bar0c + [znl])
            for sft in range(8):
                P.i(v.tensor_scalar(out=zni[:, AWQ_INV[sft]::8], in0=znq[:],
                                    scalar1=4 * sft, scalar2=15,
                                    op0=ALU.logical_shift_right,
                                    op1=ALU.bitwise_and), "sv")
            P.w(v, "v", [P.tok("sv")])
            P.i(v.tensor_copy(out=zns[:], in_=zni[:]), "sv")
            P.w(v, "v", [P.tok("sv")])
            P.i(v.tensor_tensor(out=zns[:], in0=zns[:], in1=znc[:], op=ALU.mult),
                "sv")
        P.sec("v", v_dn)
        zns_tok = P.tok("sv")

        def y_dn2(y):
            P.w(y, "y", [zns_tok])
            P.i(y.dma_start(out=zsd_d, in_=zns[:]), "qy")
        P.sec("y", y_dn2)
        zsd_w = P.tok("qy")

        for itl in range(NIT_SH):
            def y_dn3(y):
                P.w(y, "y", [zsd_w, P.tok("sv")])
                P.i(y.dma_start(out=dnq[:], in_=dq_in[itl * 128:(itl + 1) * 128, :]),
                    "qy")
                P.i(y.dma_start(out=dns[:],
                                in_=sap(ds_in, [[0, 128], [1, HID]], itl * HID)),
                    "qy")
                P.i(y.dma_start(out=dnz[:],
                                in_=sap(zsd_d, [[0, 128], [1, HID]], itl * HID)),
                    "qy")
            P.sec("y", y_dn3)
            ql = P.tok("qy")

            def v_dn3(v):
                P.w(v, "v", [ql, P.tok("qs")])
                for sft in range(8):
                    P.i(v.tensor_scalar(out=dni[:, AWQ_INV[sft]::8], in0=dnq[:],
                                        scalar1=4 * sft, scalar2=15,
                                        op0=ALU.logical_shift_right,
                                        op1=ALU.bitwise_and), "sv")
                P.w(v, "v", [P.tok("sv")])
                P.i(v.tensor_copy(out=dnn[:], in_=dni[:]), "sv")
                P.w(v, "v", [P.tok("sv")])
                P.i(v.tensor_tensor(out=dnn[:], in0=dnn[:], in1=dns[:],
                                    op=ALU.mult), "sv")
                P.w(v, "v", [P.tok("sv")])
                P.i(v.tensor_tensor(out=dnw[:], in0=dnn[:], in1=dnz[:],
                                    op=ALU.subtract), "sv")
            P.sec("v", v_dn3)
            wt = P.tok("sv")

            def s_dn3(s):
                P.w(s, "s", [wt])
                P.i(s.dma_start(out=wdsl_d[itl * 128:(itl + 1) * 128, :],
                                in_=dnw[:]), "qs")
            P.sec("s", s_dn3)
        ph0c.free()

        ph0d = nc.auto_arena()
        wof = ph0d.sbuf_tensor("wof", [128, HID], F32)
        wob = ph0d.sbuf_tensor("wob", [128, HID], BF16)
        wnw = ph0d.sbuf_tensor("wnw", [128, NDPC], F32)
        ph0d.allocate()
        bar0d = P.bar()

        def y_nw(y):
            P.w(y, "y", bar0d)
            P.i(y.dma_start(out=wnw[:], in_=normw_in), "qy")
        P.sec("y", y_nw)
        nw_tok = P.tok("qy")

        for dt_ in range(NDPC):
            def y_wo(y):
                P.w(y, "y", [nw_tok, P.tok("sv")])
                P.i(y.dma_start(out=wof[:],
                                in_=woutt_in[dt_ * 128:(dt_ + 1) * 128, :]), "qy")
            P.sec("y", y_wo)
            wl = P.tok("qy")

            def v_wo(v):
                P.w(v, "v", bar0d + [wl, P.tok("qs")])
                P.i(v.tensor_scalar(out=wob[:], in0=wof[:],
                                    scalar1=wnw[:, dt_:dt_ + 1],
                                    scalar2=None, op0=ALU.mult), "sv")
            P.sec("v", v_wo)
            wc = P.tok("sv")

            def s_wo(s):
                P.w(s, "s", [wc])
                P.i(s.dma_start(out=woT_sl_d[dt_ * 128:(dt_ + 1) * 128, :],
                                in_=wob[:]), "qs")
            P.sec("s", s_wo)
        wo_w = P.tok("qs")

        def g_agw(g):
            P.w(g, "g", [wo_w, P.tok("qs"), P.tok("qy")])
            P.icc(g.collective_compute(
                "AllGather", ALU.bypass, replica_groups=[list(range(NCORES))],
                ins=[gsl_d.opt()], outs=[g_all_d.opt()]))
            P.icc(g.collective_compute(
                "AllGather", ALU.bypass, replica_groups=[list(range(NCORES))],
                ins=[usl_d.opt()], outs=[u_all_d.opt()]))
            P.icc(g.collective_compute(
                "AllGather", ALU.bypass, replica_groups=[list(range(NCORES))],
                ins=[wdsl_d.opt()], outs=[wd_all_d.opt()]))
            P.icc(g.collective_compute(
                "AllGather", ALU.bypass, replica_groups=[list(range(NCORES))],
                ins=[woT_sl_d.opt()], outs=[woT_all_d.opt()]))
        P.sec("g", g_agw)
        ag_w = P.tok("cc")
        ph0d.free()

        a1 = nc.auto_arena()
        wa1_sb = a1.sbuf_tensor("wa1_sb", [128, NDT * CA1], BF16)
        wstg = a1.sbuf_tensor("wstg", [128, CA1], F32)
        xfm_sb = a1.sbuf_tensor("xfm_sb", [128, NDT * ABLK], BF16)
        xbc_sb = a1.sbuf_tensor("xbc_sb", [128, NCONVT * AH], BF16)
        xba_sb = a1.sbuf_tensor("xba_sb", [128, NCONVT * ABLK], BF16)
        cacc_sb = a1.sbuf_tensor("cacc_sb", [128, 2 * ABLK], F32)
        dtp_sb = a1.sbuf_tensor("dtp_sb", [HPC, ABLK], F32)
        dts_sb = a1.sbuf_tensor("dts_sb", [HPC, ABLK], F32)
        dta_sb = a1.sbuf_tensor("dta_sb", [HPC, ABLK], F32)
        stk_sb = a1.sbuf_tensor("stk_sb", [128, ABLK], F32)
        dtf_sb = a1.sbuf_tensor("dtf_sb", [HPC, ABLK], F32)
        actok_sb = a1.sbuf_tensor("actok_sb", [128, 128], F32)
        srep_sb = a1.sbuf_tensor("srep_sb", [128, HPC * CSS], F32)
        prep_sb = a1.sbuf_tensor("prep_sb", [128, HPC], F32)
        acm_sb = srep_sb
        lall_sb = a1.sbuf_tensor("lall_sb", [128, HPC * CSS], BF16)
        gm_sb = a1.sbuf_tensor("gm_sb", [128, CSS], BF16)
        gml_sb = a1.sbuf_tensor("gml_sb", [128, HPC * CSS], BF16)
        xtok_sb = a1.sbuf_tensor("xtok_sb", [128, DPC], BF16)
        btok_sb = a1.sbuf_tensor("btok_sb", [128, NST], BF16)
        xs_sb = a1.sbuf_tensor("xs_sb", [128, DPC], BF16)
        xsd_sb = a1.sbuf_tensor("xsd_sb", [128, DPC], BF16)
        z_sb = a1.sbuf_tensor("z_sb", [128, DPC], F32)
        zbf_sb = a1.sbuf_tensor("zbf_sb", [128, DPC], BF16)
        zt_sb = a1.sbuf_tensor("zt_sb", [128, DPC], F32)
        yo_sb = a1.sbuf_tensor("yo_sb", [128, DPC], F32)
        dtm_sb = a1.sbuf_tensor("dtm_sb", [128, DPC], F32)
        yc_sb = yo_sb
        yout_sb = a1.sbuf_tensor("yout_sb", [128, DPC], BF16)
        a1.allocate()

        bar0 = P.bar()
        with ExitStack() as _esa1:
            PA0 = _esa1.enter_context(nc.psum_tensor("PA0", [128, 512], F32))
            PA1 = _esa1.enter_context(nc.psum_tensor("PA1", [128, 512], F32))
            PTRB = _esa1.enter_context(nc.psum_tensor("PTRB", [128, DPC], BF16))
            PYD = _esa1.enter_context(nc.psum_tensor("PYD", [128, DPC], F32))
            PSC = _esa1.enter_context(nc.psum_tensor("PSC", [128, DPC], F32))
            def v_stk0(v):
                P.w(v, "v", bar0)
                P.i(v.memset(stk_sb[:], 0.0), "sv")
            P.sec("v", v_stk0)

            wconv_tok = None
            for kt in range(NDT):
                def y_w(y):
                    P.w(y, "y", bar0 + [wconv_tok])
                    P.i(y.dma_start(out=wstg[:],
                                    in_=wa1_in[kt * 128:(kt + 1) * 128, :]), "qy")
                P.sec("y", y_w)
                wl = P.tok("qy")

                def v_w(v):
                    P.w(v, "v", bar0 + [wl])
                    P.i(v.tensor_copy(out=wa1_sb[:, kt * CA1:(kt + 1) * CA1],
                                      in_=wstg[:]), "sv")
                P.sec("v", v_w)
                wconv_tok = P.tok("sv")
            wa1_tok = P.tok("sv")

            ftiles = [(d * 128, 128, "x", d) for d in range(NDPC)]
            ftiles += [(DPC, NST, "B", NDPC), (DPC + NST, NST, "C", NDPC + 1),
                       (DPC + 2 * NST, HPC, "dt", -1)]

            st = dict(xfm=None, xfm_pe=None,
                      pa_mm=[None, None], pa_ev=[None, None],
                      z=None, zuser=None, halo=None, conv=None,
                      xbc_user=None, dtuser=None, stk_pe=None, xba_pe=None,
                      yd_ev=None, sc_ev=None)
            yraw_w = None

            for b in range(NAB):
                T0 = b * ABLK
                cb_src = T0 // TPC
                tloc = T0 - cb_src * TPC
                batch_start = (b % C["AB_PER_BATCH"]) == 0

                def y_xfm(y):
                    P.w(y, "y", [ag_xln, st["xfm_pe"]])
                    src = sap(xlnT_all_d, [[TPC, 128], [128 * TPC, NDT], [1, ABLK]],
                              cb_src * HID * TPC + tloc)
                    P.i(y.dma_start(
                        out=xfm_sb[:].rearrange("p (k t) -> p k t", k=NDT),
                        in_=src), "qy")
                P.sec("y", y_xfm)
                st["xfm"] = P.tok("qy")

                if batch_start:
                    def v_rst(v):
                        P.w(v, "v", [st["zuser"], st["z"], st["xbc_user"]])
                        P.i(v.memset(z_sb[:], 0.0), "sv")
                        halo = sap(xbc_sb, [[NCONVT * AH, 128], [AH, NCONVT],
                                            [1, KC - 1]])
                        P.i(v.memset(halo, 0.0), "sv")
                    P.sec("v", v_rst)
                    st["z"] = P.tok("sv")
                    st["halo"] = P.tok("sv")

                evac_toks = []
                for fi, (off, fw, kind, dpos) in enumerate(ftiles):
                    pp = (PA0, PA1)[fi % 2]
                    pi = fi % 2

                    def t_ip(t):
                        P.w(t, "t", [wa1_tok, st["xfm"], st["pa_ev"][pi]])
                        for kt in range(NDT):
                            mm = t.matmul(
                                pp[0:fw, 0:ABLK],
                                wa1_sb[:, kt * CA1 + off:kt * CA1 + off + fw],
                                xfm_sb[:, kt * ABLK:(kt + 1) * ABLK],
                                start=(kt == 0), stop=(kt == NDT - 1))
                        P.i(mm, "sp")
                    P.sec("t", t_ip)
                    st["pa_mm"][pi] = P.tok("sp")
                    if fi == len(ftiles) - 1:
                        st["xfm_pe"] = P.tok("sp")

                    def v_ev(v):
                        P.w(v, "v", [st["pa_mm"][pi], st["halo"], st["conv"]])
                        if kind == "dt":
                            P.i(v.tensor_copy(out=dtp_sb[:], in_=pp[0:HPC, 0:ABLK]),
                                "sv")
                        else:
                            dst = xbc_sb[:, dpos * AH + KC - 1:dpos * AH + AH]
                            P.i(v.tensor_copy(out=dst, in_=pp[:, 0:ABLK]), "sv")
                    P.sec("v", v_ev)
                    st["pa_ev"][pi] = P.tok("sv")
                    evac_toks.append(P.tok("sv"))
                xbc_tok = P.tok("sv")

                conv_sil = []
                for ct in range(NCONVT):
                    acc = cacc_sb[:, (ct % 2) * ABLK:(ct % 2 + 1) * ABLK]

                    def v_cv(v):
                        deps = [evac_toks[ct], st["halo"], st["xba_pe"]]
                        if ct >= 2:
                            deps.append(conv_sil[ct - 2])
                        P.w(v, "v", deps)
                        base = ct * AH
                        P.i(v.tensor_scalar(out=acc, in0=xbc_sb[:, base:base + ABLK],
                                            scalar1=convw_sb[:, ct * KC:ct * KC + 1],
                                            scalar2=convb_sb[:, ct:ct + 1],
                                            op0=ALU.mult, op1=ALU.add), "sv")
                        for tau in range(1, KC):
                            P.w(v, "v", [P.tok("sv")])
                            P.i(v.scalar_tensor_tensor(
                                out=acc, in0=xbc_sb[:, base + tau:base + tau + ABLK],
                                scalar=convw_sb[:, ct * KC + tau:ct * KC + tau + 1],
                                in1=acc, op0=ALU.mult, op1=ALU.add), "sv")
                    P.sec("v", v_cv)
                    cv_tok = P.tok("sv")

                    def s_cv(s):
                        P.w(s, "s", [cv_tok])
                        P.i(s.activation(out=xba_sb[:, ct * ABLK:(ct + 1) * ABLK],
                                         in_=acc, func=AF.Silu), "ss")
                    P.sec("s", s_cv)
                    conv_sil.append(P.tok("ss"))
                conv_tok = P.tok("ss")
                st["conv"] = conv_tok

                def v_hl(v):
                    P.w(v, "v", [conv_tok, xbc_tok])
                    src = sap(xbc_sb, [[NCONVT * AH, 128], [AH, NCONVT],
                                       [1, KC - 1]], ABLK)
                    dst = sap(xbc_sb, [[NCONVT * AH, 128], [AH, NCONVT],
                                       [1, KC - 1]])
                    P.i(v.tensor_copy(out=dst, in_=src), "sv")
                P.sec("v", v_hl)
                st["halo"] = P.tok("sv")

                def v_dt(v):
                    P.w(v, "v", [st["pa_ev"][(len(ftiles) - 1) % 2], a_tok,
                                 st["stk_pe"], st["dtuser"]])
                    P.i(v.tensor_scalar(out=dts_sb[:], in0=dtp_sb[:],
                                        scalar1=dtb_sb[:], scalar2=None,
                                        op0=ALU.add), "sv")
                P.sec("v", v_dt)
                dt1 = P.tok("sv")

                def s_dt(s):
                    P.w(s, "s", [dt1])
                    P.i(s.activation(out=dta_sb[:], in_=dts_sb[:], func=AF.Abs),
                        "ss")
                    P.w(s, "s", [P.tok("ss")])
                    P.i(s.activation(out=dta_sb[:], in_=dta_sb[:], func=AF.Exp,
                                     scale=-1.0), "ss")
                    P.w(s, "s", [P.tok("ss")])
                    P.i(s.activation(out=dta_sb[:], in_=dta_sb[:], func=AF.Ln,
                                     bias=1.0), "ss")
                P.sec("s", s_dt)
                dt2 = P.tok("ss")

                def v_dt2(v):
                    P.w(v, "v", [dt2, P.tok("qs")])
                    P.i(v.tensor_relu(out=dts_sb[:], in_=dts_sb[:]), "sv")
                    P.w(v, "v", [P.tok("sv")])
                    P.i(v.tensor_tensor(out=dtf_sb[:],
                                        in0=dts_sb[:], in1=dta_sb[:], op=ALU.add),
                        "sv")
                    P.w(v, "v", [P.tok("sv")])
                    P.i(v.tensor_scalar(out=dta_sb[:],
                                        in0=dtf_sb[:],
                                        scalar1=acol_sb[:], scalar2=None,
                                        op0=ALU.mult), "sv")
                    for s_ in range(NSUB):
                        P.w(v, "v", [P.tok("sv")])
                        sl = slice(s_ * CSS, (s_ + 1) * CSS)
                        P.i(v.tensor_tensor_scan(out=stk_sb[0:HPC, sl],
                                                 data0=dta_sb[:, sl],
                                                 data1=dta_sb[:, sl], initial=0.0,
                                                 op0=ALU.add, op1=ALU.bypass), "sv")
                P.sec("v", v_dt2)
                ac_tok = P.tok("sv")
                st["dtuser"] = ac_tok

                def s_dt2(s):
                    P.w(s, "s", [ac_tok])
                    for s_ in range(NSUB):
                        sl = slice(s_ * CSS, (s_ + 1) * CSS)
                        last = s_ * CSS + CSS - 1
                        P.i(s.activation(out=stk_sb[32:32 + HPC, sl],
                                         in_=stk_sb[0:HPC, sl], func=AF.Exp,
                                         scale=-1.0,
                                         bias=stk_sb[0:HPC, last:last + 1]), "ss")
                        P.i(s.activation(out=stk_sb[64:64 + HPC, sl],
                                         in_=stk_sb[0:HPC, sl], func=AF.Exp), "ss")
                P.sec("s", s_dt2)
                stk_tok = P.tok("ss")

                def s_dt3(s):
                    P.w(s, "s", [stk_tok])
                    for s_ in range(NSUB):
                        sl = slice(s_ * CSS, (s_ + 1) * CSS)
                        r0 = (b * NSUB + s_) * HPC
                        P.i(s.dma_start(out=acb_d[r0:r0 + HPC, :],
                                        in_=stk_sb[0:HPC, sl]), "qs")
                        last = s_ * CSS + CSS - 1
                        P.i(s.dma_start(
                            out=sap(prow_d, [[1, HPC], [0, 1]],
                                    (b * NSUB + s_) * HPC),
                            in_=stk_sb[64:64 + HPC, last:last + 1]), "qs")
                P.sec("s", s_dt3)
                acb_w = P.tok("qs")

                for s_ in range(NSUB):
                    sub = b * NSUB + s_
                    sc = s_ * CSS

                    def y_rep(y):
                        P.w(y, "y", [acb_w, P.tok("sv"), P.tok("ss")])
                        P.i(y.dma_start(out=srep_sb[:],
                                        in_=sap(acb_d, [[0, 128], [1, HPC * CSS]],
                                                sub * HPC * CSS)), "qy")
                        P.i(y.dma_start(out=prep_sb[:],
                                        in_=sap(prow_d, [[0, 128], [1, HPC]],
                                                sub * HPC)), "qy")
                    P.sec("y", y_rep)
                    rep_tok = P.tok("qy")

                    def t_tr(t):
                        P.w(t, "t", [conv_tok, P.tok("sv")])
                        for d in range(NDPC):
                            tr = t.transpose(PTRB[:, d * 128:(d + 1) * 128],
                                             xba_sb[:, d * ABLK + sc:d * ABLK + sc + CSS],
                                             ident_bf[:])
                        P.i(tr, "sp")
                    P.sec("t", t_tr)
                    xtr = P.tok("sp")

                    def v_tr(v):
                        P.w(v, "v", [xtr])
                        P.i(v.tensor_copy(out=xtok_sb[:], in_=PTRB[:]), "sv")
                    P.sec("v", v_tr)
                    xtok_tok = P.tok("sv")

                    def t_tr2(t):
                        P.w(t, "t", [xtok_tok, stk_tok])
                        tr = t.transpose(PTRB[:, 0:NST],
                                         xba_sb[:, NDPC * ABLK + sc:NDPC * ABLK + sc + CSS],
                                         ident_bf[:])
                        P.i(tr, "sp")
                        P.w(t, "t", [P.tok("sv")])
                        t.transpose(PSC[:, 0:64 + HPC],
                                    stk_sb[0:64 + HPC, sc:sc + CSS],
                                    ident_f[0:64 + HPC, 0:64 + HPC])
                        tr2 = t.transpose(PSC[:, 96:96 + HPC],
                                          dtf_sb[:, sc:sc + CSS],
                                          ident_f[0:HPC, 0:HPC])
                        P.i(tr2, "sp")
                    P.sec("t", t_tr2)
                    btr = ("sp", P.cnt["sp"] - 1)
                    str_ = P.tok("sp")
                    st["stk_pe"] = str_

                    def v_tr2(v):
                        P.w(v, "v", [btr])
                        P.i(v.tensor_copy(out=btok_sb[:], in_=PTRB[:, 0:NST]), "sv")
                        P.w(v, "v", [str_])
                        P.i(v.tensor_copy(out=actok_sb[:, 0:64 + HPC],
                                          in_=PSC[:, 0:64 + HPC]), "sv")
                        P.i(v.tensor_copy(out=actok_sb[:, 96:96 + HPC],
                                          in_=PSC[:, 96:96 + HPC]), "sv")
                    P.sec("v", v_tr2)
                    btok_tok = ("sv", P.cnt["sv"] - 1)
                    actok_tok = P.tok("sv")

                    def v_fo(v):
                        P.w(v, "v", [actok_tok, st["z"], dexp_tok, rep_tok, ut_tok])
                        P.i(v.tensor_tensor(out=xs_sb[:], in0=xtok_sb[:],
                                            in1=fexp(actok_sb[:], HPC, HD, 96),
                                            op=ALU.mult), "sv")
                        P.w(v, "v", [P.tok("sv")])
                        P.i(v.tensor_tensor(out=xsd_sb[:], in0=xs_sb[:],
                                            in1=fexp(actok_sb[:], HPC, HD, 32),
                                            op=ALU.mult), "sv")
                        P.i(v.tensor_copy(out=zbf_sb[:], in_=z_sb[:]), "sv")
                        P.i(v.tensor_tensor(out=acm_sb[:], in0=srep_sb[:],
                                            in1=fexp(actok_sb[:], HPC, CSS, 0),
                                            op=ALU.subtract), "sv")
                        P.w(v, "v", [P.tok("sv")])
                        P.i(v.tensor_tensor(out=acm_sb[:], in0=acm_sb[:],
                                            in1=frep(ut_sb[:], CSS, HPC),
                                            op=ALU.add), "sv")
                    P.sec("v", v_fo)
                    folds_tok = P.tok("sv")
                    zbf_tok = ("sv", P.cnt["sv"] - 2)

                    def s_l(s):
                        P.w(s, "s", [folds_tok])
                        P.i(s.activation(out=lall_sb[:], in_=acm_sb[:],
                                         func=AF.Exp), "ss")
                    P.sec("s", s_l)
                    lall_tok = P.tok("ss")

                    def t_gm(t):
                        P.w(t, "t", [conv_tok, st["pa_ev"][0]])
                        bfm = xba_sb[:, NDPC * ABLK + sc:NDPC * ABLK + sc + CSS]
                        cfm = xba_sb[:, (NDPC + 1) * ABLK + sc:(NDPC + 1) * ABLK + sc + CSS]
                        P.i(t.matmul(PA0[:, 0:CSS], bfm, cfm, start=True,
                                     stop=True), "sp")
                    P.sec("t", t_gm)
                    gm_mm = P.tok("sp")

                    def v_gm(v):
                        P.w(v, "v", [gm_mm])
                        P.i(v.tensor_copy(out=gm_sb[:], in_=PA0[:, 0:CSS]), "sv")
                        P.w(v, "v", [P.tok("sv"), lall_tok])
                        for h in range(HPC):
                            P.i(v.tensor_tensor(out=gml_sb[:, h * CSS:(h + 1) * CSS],
                                                in0=gm_sb[:],
                                                in1=lall_sb[:, h * CSS:(h + 1) * CSS],
                                                op=ALU.mult), "sv")
                    P.sec("v", v_gm)
                    gm_ev = ("sv", P.cnt["sv"] - HPC)
                    gml_tok = P.tok("sv")
                    st["pa_ev"][0] = gm_ev

                    def t_y(t):
                        P.w(t, "t", [gml_tok, btok_tok, zbf_tok, folds_tok,
                                     st["pa_ev"][0], st["pa_ev"][1],
                                     st["yd_ev"], st["sc_ev"]])
                        for h in range(HPC):
                            mm = t.matmul(PYD[:, h * HD:(h + 1) * HD],
                                          gml_sb[:, h * CSS:(h + 1) * CSS],
                                          xs_sb[:, h * HD:(h + 1) * HD],
                                          start=((h * HD) % 512 == 0),
                                          stop=(((h + 1) * HD) % 512 == 0
                                                or h == HPC - 1))
                        P.i(mm, "sp")
                        cfm = xba_sb[:, (NDPC + 1) * ABLK + sc:(NDPC + 1) * ABLK + sc + CSS]
                        stepd = min(512, DPC)
                        for i0 in range(0, DPC, stepd):
                            pp = (PA1, PA0)[(i0 // stepd) % 2]
                            P.i(t.matmul(pp[:, 0:stepd], cfm,
                                         zbf_sb[:, i0:i0 + stepd],
                                         start=True, stop=True), "sp")
                        for i0 in range(0, DPC, stepd):
                            mm = t.matmul(PSC[:, i0:i0 + stepd], btok_sb[:],
                                          xsd_sb[:, i0:i0 + stepd],
                                          start=True, stop=True)
                        P.i(mm, "sp")
                    P.sec("t", t_y)
                    sc_mm = P.tok("sp")
                    st["xba_pe"] = sc_mm
                    nyo = max(1, DPC // 512)
                    yo_mms = [("sp", P.cnt["sp"] - 1 - nyo + 1 + i)
                              for i in range(nyo)]
                    yd_mm = ("sp", P.cnt["sp"] - 1 - nyo)

                    def v_y(v):
                        stepd = min(512, DPC)
                        P.w(v, "v", yo_mms + [yd_mm, P.tok("qs")])
                        for i in range(nyo):
                            pp = (PA1, PA0)[i % 2]
                            hph = stepd // HD
                            P.i(v.tensor_tensor(
                                out=yo_sb[:, i * stepd:(i + 1) * stepd],
                                in0=pp[:, 0:stepd],
                                in1=fexp(actok_sb[:], hph, HD,
                                         64 + i * hph),
                                op=ALU.mult), "sv")
                        P.i(v.tensor_tensor(out=dtm_sb[:], in0=xtok_sb[:],
                                            in1=dexp_sb[:], op=ALU.mult), "sv")
                        P.w(v, "v", [P.tok("sv")])
                        P.i(v.tensor_tensor(out=yc_sb[:], in0=PYD[:], in1=yo_sb[:],
                                            op=ALU.add), "sv")
                        P.w(v, "v", [P.tok("sv")])
                        P.i(v.tensor_tensor(out=yout_sb[:], in0=yc_sb[:],
                                            in1=dtm_sb[:], op=ALU.add), "sv")
                    P.sec("v", v_y)
                    y_tok = P.tok("sv")
                    st["pa_ev"][0] = y_tok
                    st["pa_ev"][1] = y_tok
                    st["yd_ev"] = y_tok

                    def s_y(s):
                        P.w(s, "s", [y_tok])
                        P.i(s.dma_start(out=y_raw_d[T0 + sc:T0 + sc + CSS, :],
                                        in_=yout_sb[:]), "qs")
                    P.sec("s", s_y)
                    yraw_w = P.tok("qs")

                    def v_z(v):
                        P.w(v, "v", [sc_mm, rep_tok, zbf_tok])
                        P.i(v.tensor_tensor(out=zt_sb[:], in0=z_sb[:],
                                            in1=fexp(prep_sb[:], HPC, HD),
                                            op=ALU.mult), "sv")
                        P.w(v, "v", [P.tok("sv")])
                        P.i(v.tensor_tensor(out=z_sb[:], in0=zt_sb[:], in1=PSC[:],
                                            op=ALU.add), "sv")
                    P.sec("v", v_z)
                    st["z"] = P.tok("sv")
                    st["zuser"] = P.tok("sv")
                    st["sc_ev"] = P.tok("sv")
                st["xbc_user"] = P.tok("sp")
            a1_done = yraw_w
        a1.free()

        a2 = nc.auto_arena()
        wa2_sb = a2.sbuf_tensor("wa2_sb", [128, NDT * DPC], BF16)
        wstg2 = a2.sbuf_tensor("wstg2", [128, DPC], F32)
        xfm2_sb = a2.sbuf_tensor("xfm2_sb", [128, 2 * NDT * ABLK], BF16)
        gact_sb = a2.sbuf_tensor("gact_sb", [128, NDPC * ABLK], BF16)
        gtok_sb = a2.sbuf_tensor("gtok_sb", [128, DPC], BF16)
        yr_sb = a2.sbuf_tensor("yr_sb", [128, DPC], BF16)
        yg_sb = a2.sbuf_tensor("yg_sb", [128, DPC], F32)
        ysq_sb = a2.sbuf_tensor("ysq_sb", [128, DPC], F32)
        ssum_sb = a2.sbuf_tensor("ssum_sb", [128, 1], F32)
        rstd_sb = a2.sbuf_tensor("rstd_sb", [128, 1], F32)
        yn_sb = a2.sbuf_tensor("yn_sb", [128, DPC], BF16)
        ytr_sb = a2.sbuf_tensor("ytr_sb", [128, NTOK], BF16)
        a2.allocate()

        bar1 = P.bar()
        with ExitStack() as _esa2:
            PB0_ = _esa2.enter_context(nc.psum_tensor("PB0_", [128, 512], F32))
            PB1_ = _esa2.enter_context(nc.psum_tensor("PB1_", [128, 512], F32))
            PTR2 = _esa2.enter_context(nc.psum_tensor("PTR2", [128, DPC], BF16))
            wconv2 = None
            for kt in range(NDT):
                def y_w2(y):
                    P.w(y, "y", bar1 + [wconv2])
                    P.i(y.dma_start(out=wstg2[:],
                                    in_=wa2_in[kt * 128:(kt + 1) * 128, :]), "qy")
                P.sec("y", y_w2)
                wl = P.tok("qy")

                def v_w2(v):
                    P.w(v, "v", bar1 + [wl])
                    P.i(v.tensor_copy(out=wa2_sb[:, kt * DPC:(kt + 1) * DPC],
                                      in_=wstg2[:]), "sv")
                P.sec("v", v_w2)
                wconv2 = P.tok("sv")
            wa2_tok = P.tok("sv")

            st2 = dict(xfm=None, xfm_pe=[None, None], ev=[None, None], tr_ev=None,
                       yn_ev=None)
            ytok_w = None
            for b in range(NAB):
                T0 = b * ABLK
                cb_src = T0 // TPC
                tloc = T0 - cb_src * TPC
                bpar = b % 2
                xbase2 = bpar * NDT * ABLK

                def y_xfm2(y):
                    P.w(y, "y", [ag_xln, st2["xfm_pe"][bpar]])
                    src = sap(xlnT_all_d, [[TPC, 128], [128 * TPC, NDT], [1, ABLK]],
                              cb_src * HID * TPC + tloc)
                    P.i(y.dma_start(
                        out=xfm2_sb[:, xbase2:xbase2 + NDT * ABLK].rearrange(
                            "p (k t) -> p k t", k=NDT),
                        in_=src), ("qa", "qb")[bpar], "y")
                P.sec("y", y_xfm2, dq=(("qa", "qb")[bpar],))
                st2["xfm"] = P.tok(("qa", "qb")[bpar])

                for d in range(NDPC):
                    pp = (PB0_, PB1_)[d % 2]
                    pi = d % 2

                    def t_g(t):
                        P.w(t, "t", [wa2_tok, st2["xfm"], st2["ev"][pi]])
                        for kt in range(NDT):
                            mm = t.matmul(
                                pp[:],
                                wa2_sb[:, kt * DPC + d * 128:kt * DPC + (d + 1) * 128],
                                xfm2_sb[:, xbase2 + kt * ABLK:
                                        xbase2 + (kt + 1) * ABLK],
                                start=(kt == 0), stop=(kt == NDT - 1))
                        P.i(mm, "sp")
                    P.sec("t", t_g)
                    g_mm = P.tok("sp")
                    if d == NDPC - 1:
                        st2["xfm_pe"][bpar] = P.tok("sp")

                    def s_g(s):
                        P.w(s, "s", [g_mm])
                        P.i(s.activation(out=gact_sb[:, d * ABLK:(d + 1) * ABLK],
                                         in_=pp[:], func=AF.Silu), "ss")
                    P.sec("s", s_g)
                    st2["ev"][pi] = P.tok("ss")
                gact_tok = P.tok("ss")

                for s_ in range(NSUB):
                    sc = s_ * CSS
                    T0s = T0 + sc

                    def y_yr(y):
                        P.w(y, "y", [a1_done, P.tok("sv")])
                        P.i(y.dma_start(out=yr_sb[:], in_=y_raw_d[T0s:T0s + CSS, :]),
                            "qy")
                    P.sec("y", y_yr)
                    yr_tok = P.tok("qy")

                    def t_gt(t):
                        P.w(t, "t", [gact_tok, st2["tr_ev"]])
                        for d in range(NDPC):
                            tr = t.transpose(PTR2[:, d * 128:(d + 1) * 128],
                                             gact_sb[:, d * ABLK + sc:d * ABLK + sc + CSS],
                                             ident_bf[:])
                        P.i(tr, "sp")
                    P.sec("t", t_gt)
                    gt_tok = P.tok("sp")

                    def v_g2(v):
                        P.w(v, "v", [gt_tok])
                        P.i(v.tensor_copy(out=gtok_sb[:], in_=PTR2[:]), "sv")
                        P.w(v, "v", [P.tok("sv"), yr_tok])
                        P.i(v.tensor_tensor(out=yg_sb[:], in0=yr_sb[:],
                                            in1=gtok_sb[:], op=ALU.mult), "sv")
                        P.w(v, "v", [P.tok("sv")])
                        P.i(v.tensor_tensor(out=ysq_sb[:], in0=yg_sb[:],
                                            in1=yg_sb[:], op=ALU.mult), "sv")
                        P.w(v, "v", [P.tok("sv")])
                        P.i(v.tensor_reduce(out=ssum_sb[:], in_=ysq_sb[:], axis=AXX,
                                            op=ALU.add), "sv")
                    P.sec("v", v_g2)
                    st2["tr_ev"] = ("sv", P.cnt["sv"] - 3)
                    ssm_tok = P.tok("sv")

                    def v_r0(v):
                        P.w(v, "v", [ssm_tok])
                        P.i(v.tensor_scalar_mul(out=rstd_sb[:], in0=ssum_sb[:],
                                                scalar1=1.0 / DPC), "sv")
                    P.sec("v", v_r0)
                    sc_tok2 = P.tok("sv")

                    def s_r(s):
                        P.w(s, "s", [sc_tok2])
                        P.i(s.activation(out=rstd_sb[:], in_=rstd_sb[:], func=AF.Ln,
                                         bias=eps_sb[:]), "ss")
                    P.sec("s", s_r)
                    ln_tok2 = P.tok("ss")

                    def v_r1(v):
                        P.w(v, "v", [ln_tok2])
                        P.i(v.tensor_scalar_mul(out=rstd_sb[:], in0=rstd_sb[:],
                                                scalar1=-0.5), "sv")
                    P.sec("v", v_r1)
                    ng_tok2 = P.tok("sv")

                    def s_r2(s):
                        P.w(s, "s", [ng_tok2])
                        P.i(s.activation(out=rstd_sb[:], in_=rstd_sb[:],
                                         func=AF.Exp), "ss")
                    P.sec("s", s_r2)
                    rs_tok = P.tok("ss")

                    def v_n(v):
                        P.w(v, "v", [rs_tok, st2["yn_ev"]])
                        P.i(v.tensor_scalar(out=yn_sb[:], in0=yg_sb[:],
                                            scalar1=rstd_sb[:], scalar2=None,
                                            op0=ALU.mult), "sv")
                    P.sec("v", v_n)
                    yn_tok = P.tok("sv")

                    def s_yw(s):
                        P.w(s, "s", [yn_tok])
                        P.i(s.dma_start(out=y_tok_d[T0s:T0s + CSS, :], in_=yn_sb[:]),
                            "qs")
                    P.sec("s", s_yw)
                    st2["yn_ev"] = P.tok("qs")
            ytok_w = P.tok("qs")

            def y_ytr(y):
                P.w(y, "y", [ytok_w])
                for dt_ in range(NDPC):
                    P.w(y, "y", [P.tok("qy")])
                    P.i(y.dma_start_transpose(
                        out=ytr_sb[:], in_=y_tok_d[:, dt_ * 128:(dt_ + 1) * 128]),
                        "qy")
                    P.w(y, "y", [P.tok("qy")])
                    P.i(y.dma_start(out=yT_sl_d[dt_ * 128:(dt_ + 1) * 128, :],
                                    in_=ytr_sb[:]), "qy")
            P.sec("y", y_ytr)
            ya2a_w = P.tok("qy")

            def g_agy(g):
                P.w(g, "g", [ya2a_w])
                P.icc(g.collective_compute(
                    "AllGather", ALU.bypass, replica_groups=[list(range(NCORES))],
                    ins=[yT_sl_d.opt()], outs=[y_all_d.opt()]))
            P.sec("g", g_agy)
            ag_y = P.tok("cc")

            def y_slice(y):
                P.w(y, "y", [ag_y])
                off = y.partition_id() * TPC
                src2 = bass.AP(y_all_d.tensor, off,
                               [[NTOK, 128], [128 * NTOK, NKD], [1, TPC]])
                dst2 = sap(y_b_d, [[TPC, 128], [128 * TPC, NKD], [1, TPC]])
                P.i(y.dma_start(out=dst2, in_=src2), "qy")
            P.sec("y", y_slice)
            ag_y = P.tok("qy")
        a2.free()

        SUPB1 = 256
        NSUPB1 = TPC // SUPB1
        QW = min(1024, HID)
        NQ = HID // QW
        NTTB1 = SUPB1 // 128
        b1 = nc.auto_arena()
        wo_sb = b1.sbuf_tensor("wo_sb", [128, NKD * QW], BF16)
        ysup_sb = b1.sbuf_tensor("ysup_sb", [128, 2 * NKD * SUPB1], BF16)
        xres_sb = b1.sbuf_tensor("xres_sb", [128, QW], F32)
        xn_sb = b1.sbuf_tensor("xn_sb", [128, QW], F32)
        b1.allocate()

        bar2 = P.bar()
        with ExitStack() as _esb1:
            PBQ = _esb1.enter_context(
                nc.psum_tensor("PBQ", [128, 2 * NTTB1 * QW], F32))
            xn_w = None
            y_pe = [None, None]
            wo_pe = None
            ev_b1 = {}
            for q in range(NQ):
                def s_wo2(s):
                    P.w(s, "s", bar2 + [ag_w, wo_pe])
                    src2 = sap(woT_all_d, [[HID, 128], [128 * HID, NKD], [1, QW]],
                               q * QW)
                    P.i(s.dma_start(
                        out=wo_sb[:].rearrange("p (k w) -> p k w", k=NKD),
                        in_=src2), "qc", "s")
                P.sec("s", s_wo2, dq=("qc",))
                wo_ld = P.tok("qc")

                for sup in range(NSUPB1):
                    t0 = sup * SUPB1
                    par = sup % 2
                    ybase = par * NKD * SUPB1

                    def y_ys(y):
                        P.w(y, "y", bar2 + [ag_y, y_pe[par]])
                        src2 = sap(y_b_d, [[TPC, 128], [128 * TPC, NKD], [1, SUPB1]],
                                   t0)
                        P.i(y.dma_start(
                            out=ysup_sb[:, ybase:ybase + NKD * SUPB1].rearrange(
                                "p (k t) -> p k t", k=NKD),
                            in_=src2), ("qa", "qb")[par], "y")
                    P.sec("y", y_ys, dq=(("qa", "qb")[par],))
                    y_ld = P.tok(("qa", "qb")[par])

                    for tt in range(NTTB1):
                        pbase = (par * NTTB1 + tt) * QW

                        def t_op(t):
                            P.w(t, "t", [y_ld, wo_ld, ev_b1.get((par, tt))])
                            for k in range(NKD):
                                for j in range(QW // 512):
                                    mm = t.matmul(
                                        PBQ[:, pbase + j * 512:pbase + (j + 1) * 512],
                                        ysup_sb[:, ybase + k * SUPB1 + tt * 128:
                                                ybase + k * SUPB1 + (tt + 1) * 128],
                                        wo_sb[:, k * QW + j * 512:k * QW + (j + 1) * 512],
                                        start=(k == 0), stop=(k == NKD - 1))
                            P.i(mm, "sp")
                        P.sec("t", t_op)
                        op_mm = P.tok("sp")
                        if tt == NTTB1 - 1:
                            y_pe[par] = op_mm
                            if sup == NSUPB1 - 1:
                                wo_pe = op_mm

                        def y_xr(y):
                            P.w(y, "y", [P.tok("sv")])
                            P.i(y.dma_start(
                                out=xres_sb[:],
                                in_=x_in[t0 + tt * 128:t0 + (tt + 1) * 128,
                                         q * QW:(q + 1) * QW]), "qy")
                        P.sec("y", y_xr)
                        xr_tok = P.tok("qy")

                        def v_xn(v):
                            P.w(v, "v", [op_mm, xr_tok, P.tok("qg")])
                            P.i(v.tensor_tensor(out=xn_sb[:],
                                                in0=PBQ[:, pbase:pbase + QW],
                                                in1=xres_sb[:], op=ALU.add), "sv")
                        P.sec("v", v_xn)
                        ev_b1[(par, tt)] = P.tok("sv")
                        xn_tok = P.tok("sv")

                        def g_xn(g):
                            P.w(g, "g", [xn_tok])
                            P.i(g.dma_start(
                                out=x_new_d[t0 + tt * 128:t0 + (tt + 1) * 128,
                                            q * QW:(q + 1) * QW],
                                in_=xn_sb[:]), "qg")
                        P.sec("g", g_xn)
                        xn_w = P.tok("qg")
        b1.free()
        xnew_done = xn_w

        b2 = nc.auto_arena()
        x2_sb = b2.sbuf_tensor("x2_sb", [128, HID], F32)
        sq2_sb = b2.sbuf_tensor("sq2_sb", [128, SQCH], F32)
        s21_sb = b2.sbuf_tensor("s21_sb", [128, 1], F32)
        s22_sb = b2.sbuf_tensor("s22_sb", [128, 1], F32)
        pt2_sb = b2.sbuf_tensor("pt2_sb", [128, 1], F32)
        m2_sb = b2.sbuf_tensor("m2_sb", [128, 1], F32)
        v2_sb = b2.sbuf_tensor("v2_sb", [128, 1], F32)
        r2_sb = b2.sbuf_tensor("r2_sb", [128, 1], F32)
        o2_sb = b2.sbuf_tensor("o2_sb", [128, HID], BF16)
        xl2T_sb = b2.sbuf_tensor("xl2T_sb", [128, NDT * TPC], BF16)
        b2.allocate()

        bar3 = P.bar()
        xln2_w = ln_pass(x_new_d, xln2_tok_d, TPC, x2_sb, sq2_sb, s21_sb, s22_sb,
                         pt2_sb, m2_sb, v2_sb, r2_sb, o2_sb, bar3 + [xnew_done])

        def y_l2t(y):
            P.w(y, "y", [xln2_w])
            for kt in range(NDT):
                P.i(y.dma_start_transpose(
                    out=xl2T_sb[:, kt * TPC:(kt + 1) * TPC],
                    in_=xln2_tok_d[:, kt * 128:(kt + 1) * 128]), "qy")
        P.sec("y", y_l2t)
        xl2T_tok = P.tok("qy")

        b3 = nc.auto_arena()
        wg_sb = b3.sbuf_tensor("wg_sb", [128, 2 * NDT * 128], BF16)
        wu_sb = b3.sbuf_tensor("wu_sb", [128, 2 * NDT * 128], BF16)
        hs_sb = b3.sbuf_tensor("hs_sb", [128, TPC], BF16)
        ho_sb = b3.sbuf_tensor("ho_sb", [128, TPC], BF16)
        b3.allocate()

        NTW = TPC // 512
        bar4 = P.bar()
        with ExitStack() as _esb3:
            PG_ = _esb3.enter_context(nc.psum_tensor("PG_", [128, 4 * 512], F32))
            PU_ = _esb3.enter_context(nc.psum_tensor("PU_", [128, 4 * 512], F32))
            w_pe = [None, None]
            ev3 = None
            h_w = None
            for it in range(NI):
                par = it % 2
                wslice = slice(par * NDT * 128, (par + 1) * NDT * 128)

                def s_wl(s):
                    P.w(s, "s", bar4 + [ag_w, w_pe[par]])
                    src = sap(g_all_d, [[I, 128], [128 * I, NDT], [1, 128]],
                              it * 128)
                    P.i(s.dma_start(
                        out=wg_sb[:, wslice].rearrange("p (k i) -> p k i", k=NDT),
                        in_=src), ("qa", "qb")[par], "s")
                P.sec("s", s_wl, dq=(("qa", "qb")[par],))

                def y_wl(y):
                    P.w(y, "y", bar4 + [ag_w, w_pe[par]])
                    src = sap(u_all_d, [[I, 128], [128 * I, NDT], [1, 128]],
                              it * 128)
                    P.i(y.dma_start(
                        out=wu_sb[:, wslice].rearrange("p (k i) -> p k i", k=NDT),
                        in_=src), ("qd", "qe")[par], "y")
                P.sec("y", y_wl, dq=(("qd", "qe")[par],))
                w_ld = P.tok(("qd", "qe")[par])
                w_ld_g = P.tok(("qa", "qb")[par])

                def t_f(t):
                    P.w(t, "t", [w_ld, w_ld_g, xl2T_tok, ev3])
                    for kt in range(NDT):
                        for tw in range(NTW):
                            mm = t.matmul(
                                PG_[:, tw * 512:(tw + 1) * 512],
                                wg_sb[:, par * NDT * 128 + kt * 128:
                                      par * NDT * 128 + (kt + 1) * 128],
                                xl2T_sb[:, kt * TPC + tw * 512:kt * TPC + (tw + 1) * 512],
                                start=(kt == 0), stop=(kt == NDT - 1))
                    P.i(mm, "sp")
                    for kt in range(NDT):
                        for tw in range(NTW):
                            mm = t.matmul(
                                PU_[:, tw * 512:(tw + 1) * 512],
                                wu_sb[:, par * NDT * 128 + kt * 128:
                                      par * NDT * 128 + (kt + 1) * 128],
                                xl2T_sb[:, kt * TPC + tw * 512:kt * TPC + (tw + 1) * 512],
                                start=(kt == 0), stop=(kt == NDT - 1))
                    P.i(mm, "sp")
                P.sec("t", t_f)
                u_mm = P.tok("sp")
                g_mm = ("sp", P.cnt["sp"] - 1)
                w_pe[par] = u_mm

                def s_sil(s):
                    P.w(s, "s", [g_mm, P.tok("sv")])
                    for tw in range(NTW):
                        P.i(s.activation(out=hs_sb[:, tw * 512:(tw + 1) * 512],
                                         in_=PG_[:, tw * 512:(tw + 1) * 512],
                                         func=AF.Silu), "ss")
                P.sec("s", s_sil)
                sil_tok = P.tok("ss")

                def v_h(v):
                    P.w(v, "v", [sil_tok, u_mm, P.tok("qg")])
                    for tw in range(NTW):
                        P.i(v.tensor_tensor(out=ho_sb[:, tw * 512:(tw + 1) * 512],
                                            in0=PU_[:, tw * 512:(tw + 1) * 512],
                                            in1=hs_sb[:, tw * 512:(tw + 1) * 512],
                                            op=ALU.mult), "sv")
                P.sec("v", v_h)
                ev3 = P.tok("sv")
                h_tok = P.tok("sv")

                def g_h(g):
                    P.w(g, "g", [h_tok])
                    P.i(g.dma_start(out=h_d[it * 128:(it + 1) * 128, :],
                                    in_=ho_sb[:]), "qg")
                P.sec("g", g_h)
                h_w = P.tok("qg")
            h_written = h_w
        b3.free()
        b2.free()

        b4 = nc.auto_arena()
        hsup_sb = b4.sbuf_tensor("hsup_sb", [128, NI * 512], BF16)
        KCH = -(-NI // 4)
        wd_sb = b4.sbuf_tensor("wd_sb", [128, 2 * KCH * 1024], BF16)
        xr4_sb = b4.sbuf_tensor("xr4_sb", [128, 1024], F32)
        o4_sb = b4.sbuf_tensor("o4_sb", [128, 1024], F32)
        b4.allocate()

        SUP4 = 512
        NSUP4 = TPC // SUP4
        SPW = min(1024, HID)
        NSP = HID // SPW
        bar5 = P.bar()
        with ExitStack() as _esb4:
            PD_ = _esb4.enter_context(nc.psum_tensor("PD_", [128, 8 * 512], F32))
            h_pe = None
            out_w = None
            ev4 = None
            wd_war = {0: None, 1: None}
            for sup in range(NSUP4):
                t0 = sup * SUP4

                def y_h(y):
                    P.w(y, "y", bar5 + [h_written, h_pe])
                    src = sap(h_d, [[TPC, 128], [128 * TPC, NI], [1, 512]], t0)
                    P.i(y.dma_start(
                        out=hsup_sb[:].rearrange("p (k t) -> p k t", k=NI),
                        in_=src), "qy")
                P.sec("y", y_h)
                h_ld = P.tok("qy")

                for sp in range(NSP):
                    wd_toks = []
                    nchunk = -(-NI // KCH)
                    for ch in range(nchunk):
                        k0 = ch * KCH
                        k1 = min(NI, k0 + KCH)
                        par = ch % 2

                        def s_wd(eng):
                            prev = (wd_toks[ch - 2][1] if ch >= 2
                                    else wd_war[par])
                            ekk = ("s", "y")[par]
                            P.w(eng, ekk, bar5 + [ag_w, prev])
                            src = sap(wd_all_d,
                                      [[HID, 128], [128 * HID, k1 - k0], [1, SPW]],
                                      k0 * 128 * HID + sp * SPW)
                            P.i(eng.dma_start(
                                out=wd_sb[:, par * KCH * 1024:
                                          par * KCH * 1024 + (k1 - k0) * SPW]
                                .rearrange("p (k w) -> p k w", k=k1 - k0),
                                in_=src), ("qa", "qd")[par], ("s", "y")[par])
                        P.sec(("s", "y")[par], s_wd, dq=(("qa", "qd")[par],))
                        wd_ld = P.tok(("qa", "qd")[par])

                        def t_dn(t):
                            P.w(t, "t", [wd_ld, h_ld, ev4])
                            for ki in range(k0, k1):
                                for tt in range(4):
                                    for j in range(SPW // 512):
                                        mm = t.matmul(
                                            PD_[:, (tt * (SPW // 512) + j) * 512:
                                                (tt * (SPW // 512) + j + 1) * 512],
                                            hsup_sb[:, ki * 512 + tt * 128:
                                                    ki * 512 + (tt + 1) * 128],
                                            wd_sb[:, par * KCH * 1024 + (ki - k0) * SPW
                                                  + j * 512:
                                                  par * KCH * 1024 + (ki - k0) * SPW
                                                  + (j + 1) * 512],
                                            start=(ki == 0), stop=(ki == NI - 1))
                            P.i(mm, "sp")
                        P.sec("t", t_dn)
                        wd_toks.append((wd_ld, P.tok("sp")))
                        wd_war[par] = P.tok("sp")
                    dn_mm = P.tok("sp")
                    h_pe = dn_mm

                    for tt in range(4):
                        def y_x4(y):
                            P.w(y, "y", [P.tok("sv")])
                            P.i(y.dma_start(
                                out=xr4_sb[:, 0:SPW],
                                in_=x_new_d[t0 + tt * 128:t0 + (tt + 1) * 128,
                                            sp * SPW:(sp + 1) * SPW]), "qy")
                        P.sec("y", y_x4)
                        xr_tok = P.tok("qy")

                        def v_o(v):
                            P.w(v, "v", [dn_mm, xr_tok, P.tok("qs")])
                            for j in range(SPW // 512):
                                P.i(v.tensor_tensor(
                                    out=o4_sb[:, j * 512:(j + 1) * 512],
                                    in0=PD_[:, (tt * (SPW // 512) + j) * 512:
                                            (tt * (SPW // 512) + j + 1) * 512],
                                    in1=xr4_sb[:, j * 512:(j + 1) * 512],
                                    op=ALU.add), "sv")
                        P.sec("v", v_o)
                        o_tok = P.tok("sv")
                        ev4 = o_tok

                        def s_o(s):
                            P.w(s, "s", [o_tok])
                            P.i(s.dma_start(
                                out=out_t[t0 + tt * 128:t0 + (tt + 1) * 128,
                                          sp * SPW:(sp + 1) * SPW],
                                in_=o4_sb[:, 0:SPW]), "qs")
                        P.sec("s", s_o)
                        out_w = P.tok("qs")
        b4.free()

        def s_end(s):
            P.w(s, "s", [out_w, P.tok("qg"), P.tok("qy")])
        P.sec("s", s_end)

        def y_end(y):
            P.w(y, "y", [out_w])
        P.sec("y", y_end)

    return nc



def shard_inputs(cfg, inputs):
    c = cfg
    HID, I, Bsz, L = c["HID"], c["I"], c["B"], c["L"]
    DIN, NST, HPC, DPC = c["DIN"], c["NST"], c["HPC"], c["DPC"]
    TPC, QG, G = c["TPC"], c["QG"], c["G"]
    DSH, NGRP = c["DSH"], c["NGRP"]
    ISH, NIT_SH, IPAD = c["ISH"], c["NIT_SH"], c["IPAD"]
    HP8 = HID // 8

    x = np.ascontiguousarray(
        np.asarray(inputs["L0_attn_hidden_states"]).reshape(Bsz * L, HID))
    wi = np.asarray(inputs["L0_attn_in_proj_w"])
    convw = np.asarray(inputs["L0_attn_conv_w"])[:, 0, :]
    convb = np.asarray(inputs["L0_attn_conv_b"])
    dtb = np.asarray(inputs["L0_attn_dt_bias"])
    alog = np.asarray(inputs["L0_attn_A_log"])
    dvec = np.asarray(inputs["L0_attn_D"])
    normw = np.asarray(inputs["L0_attn_norm_w"])
    wo = np.asarray(inputs["L0_attn_out_proj_w"])
    gq = np.asarray(inputs["L0_ffn_gate_qweight"])
    gs = np.asarray(inputs["L0_ffn_gate_scales"])
    gz = np.asarray(inputs["L0_ffn_gate_qzeros"])
    uq = np.asarray(inputs["L0_ffn_up_qweight"])
    us = np.asarray(inputs["L0_ffn_up_scales"])
    uz = np.asarray(inputs["L0_ffn_up_qzeros"])
    dq = np.asarray(inputs["L0_ffn_down_qweight"])
    ds = np.asarray(inputs["L0_ffn_down_scales"])
    dz = np.asarray(inputs["L0_ffn_down_qzeros"])

    dqp = np.zeros((IPAD, HP8), np.int32); dqp[:I] = dq
    dsp = np.zeros((IPAD // QG, HID), np.float32); dsp[:ds.shape[0]] = ds
    dzp = np.zeros((IPAD // QG, HP8), np.int32); dzp[:dz.shape[0]] = dz

    maps = []
    for cid in range(NCORES):
        xr = slice(DIN + DPC * cid, DIN + DPC * (cid + 1))
        br = slice(2 * DIN + NST * cid, 2 * DIN + NST * (cid + 1))
        cr = slice(2 * DIN + G * NST + NST * cid, 2 * DIN + G * NST + NST * (cid + 1))
        dtr = slice(DIN + c["CONVD"] + HPC * cid, DIN + c["CONVD"] + HPC * (cid + 1))
        gr = slice(DPC * cid, DPC * (cid + 1))
        m = dict(
            x=np.ascontiguousarray(x[TPC * cid:TPC * (cid + 1)]),
            wa1=np.ascontiguousarray(
                np.concatenate([wi[xr], wi[br], wi[cr], wi[dtr]], axis=0).T),
            wa2=np.ascontiguousarray(wi[gr].T),
            convw=np.ascontiguousarray(np.concatenate(
                [convw[DPC * cid:DPC * (cid + 1)],
                 convw[DIN + NST * cid:DIN + NST * (cid + 1)],
                 convw[DIN + G * NST + NST * cid:DIN + G * NST + NST * (cid + 1)]],
                axis=0).reshape(-1, 128, convw.shape[1]).transpose(1, 0, 2)
                .reshape(128, -1)),
            convb=np.ascontiguousarray(np.concatenate(
                [convb[DPC * cid:DPC * (cid + 1)],
                 convb[DIN + NST * cid:DIN + NST * (cid + 1)],
                 convb[DIN + G * NST + NST * cid:DIN + G * NST + NST * (cid + 1)]],
                axis=0).reshape(-1, 128).T),
            dtb=np.ascontiguousarray(dtb[HPC * cid:HPC * (cid + 1)])[:, None],
            alog=np.ascontiguousarray(alog[HPC * cid:HPC * (cid + 1)])[:, None],
            dvec=np.ascontiguousarray(dvec[HPC * cid:HPC * (cid + 1)])[:, None],
            normw=np.ascontiguousarray(
                normw[DPC * cid:DPC * (cid + 1)].reshape(-1, 128).T),
            woutt=np.ascontiguousarray(wo[:, DPC * cid:DPC * (cid + 1)].T),
            gq=np.ascontiguousarray(gq[DSH * cid:DSH * (cid + 1)]),
            gs=np.ascontiguousarray(gs[NGRP * cid:NGRP * (cid + 1)]),
            gz=np.ascontiguousarray(gz[NGRP * cid:NGRP * (cid + 1)]),
            uq=np.ascontiguousarray(uq[DSH * cid:DSH * (cid + 1)]),
            us=np.ascontiguousarray(us[NGRP * cid:NGRP * (cid + 1)]),
            uz=np.ascontiguousarray(uz[NGRP * cid:NGRP * (cid + 1)]),
            dq=np.ascontiguousarray(dqp[ISH * cid:ISH * (cid + 1)]),
            ds=np.ascontiguousarray(dsp[NIT_SH * cid:NIT_SH * (cid + 1)]),
            dz=np.ascontiguousarray(dzp[NIT_SH * cid:NIT_SH * (cid + 1)]),
        )
        maps.append(m)
    return maps


_CACHE = {}


def kernel(**inputs):
    cfg = REAL_CFG
    maps = shard_inputs(cfg, inputs)
    if "nc" not in _CACHE:
        _CACHE["nc"] = build_program(cfg)
    res = run_bass_kernel_spmd(_CACHE["nc"], maps, core_ids=list(range(NCORES)))
    out = np.concatenate([np.asarray(r["out"]) for r in res.results], axis=0)
    return np.ascontiguousarray(
        out.reshape(cfg["B"], cfg["L"], cfg["HID"]).astype(np.float32))


# version 15
# speedup vs baseline: 1.0960x; 1.0087x over previous
import numpy as np
from contextlib import ExitStack

import concourse.bass as bass
import concourse.mybir as mybir
from concourse.bass_utils import run_bass_kernel_spmd

F32 = mybir.dt.float32
BF16 = mybir.dt.bfloat16
I32 = mybir.dt.int32
AF = mybir.ActivationFunctionType
ALU = mybir.AluOpType
AXX = mybir.AxisListType.X

NCORES = 8
EPS = 1e-5
AWQ_INV = [0, 2, 4, 6, 1, 3, 5, 7]


def make_cfg(HID=4096, NH=128, HD=64, NST=128, KC=4, I=11008, B=2, L=8192):
    G = 8
    QG = 128
    c = dict(HID=HID, NH=NH, HD=HD, NST=NST, KC=KC, I=I, B=B, L=L, G=G, QG=QG)
    c["DIN"] = NH * HD
    c["CONVD"] = c["DIN"] + 2 * G * NST
    c["HPC"] = NH // NCORES
    c["DPC"] = c["HPC"] * HD
    c["NDPC"] = c["DPC"] // 128
    c["TPC"] = B * L // NCORES
    c["NTOK"] = B * L
    c["NDT"] = HID // 128
    c["ABLK"] = 512
    c["NAB"] = c["NTOK"] // c["ABLK"]
    c["AB_PER_BATCH"] = c["NAB"] // B
    c["CSS"] = 128
    c["NSUB"] = c["ABLK"] // c["CSS"]
    c["CA1"] = c["DPC"] + 2 * NST + c["HPC"]
    c["NCONVT"] = (c["DPC"] + 2 * NST) // 128
    c["DSH"] = HID // NCORES
    c["NGT"] = c["DSH"] // 128
    c["NGRP"] = c["DSH"] // QG
    c["IPAD"] = -(-I // (128 * NCORES)) * (128 * NCORES)
    c["ISH"] = c["IPAD"] // NCORES
    c["NIT_SH"] = c["ISH"] // 128
    c["NI"] = I // 128
    c["NTT"] = c["TPC"] // 128
    assert NST == 128 and c["DPC"] % 128 == 0 and HID % 128 == 0
    assert L % c["ABLK"] == 0 and c["DSH"] % QG == 0
    return c


REAL_CFG = make_cfg()

_EK2NAME = {"v": "vector", "s": "scalar", "t": "tensor", "g": "gpsimd", "y": "sync"}


_QS_OF = {"y": "qy", "s": "qs", "g": "qg"}


class Prog:
    def __init__(self, nc, block, sems):
        self.nc = nc
        self.block = block
        self.sem = sems
        self.cnt = {k: 0 for k in sems}
        self.lastw = {e: {k: 0 for k in sems} for e in _EK2NAME}
        self.maxw = {k: 0 for k in sems}
        self.issuer = {e: set([q]) if q else set()
                       for e, q in ((ek, _QS_OF.get(ek)) for ek in _EK2NAME)}

    def sec(self, ek, fn, dq=None):
        def wrapped(eng):
            qs = dq if dq is not None else self.issuer.get(ek, ())
            for q in qs:
                if self.maxw[q] > self.lastw[ek][q]:
                    eng.wait_ge(self.sem[q], self.maxw[q])
                    self.lastw[ek][q] = self.maxw[q]
            fn(eng)
        getattr(self.block, _EK2NAME[ek])(wrapped)

    def w(self, eng, ek, tokens):
        for tok in tokens:
            if tok is None:
                continue
            sk, val = tok
            if sk.startswith("q"):
                val = self.cnt[sk]
                if val > self.maxw[sk]:
                    self.maxw[sk] = val
            if self.lastw[ek][sk] < val:
                eng.wait_ge(self.sem[sk], val)
                self.lastw[ek][sk] = val

    def i(self, instr, sk, ek=None):
        amount = 16 if sk.startswith("q") else 1
        self.cnt[sk] += amount
        instr.then_inc(self.sem[sk], amount)
        if ek is not None:
            self.issuer[ek].add(sk)
        return (sk, self.cnt[sk])

    def icc(self, instr):
        self.cnt["cc"] += 1
        instr.then_inc(self.sem["cc"])
        return ("cc", self.cnt["cc"])

    def tok(self, sk):
        return (sk, self.cnt[sk])

    def bar(self):
        return [self.tok(k) for k in self.sem]


def sap(handle_or_ap, dims, offset=0):
    base = handle_or_ap if isinstance(handle_or_ap, bass.AP) else handle_or_ap[:]
    return bass.AP(base.tensor, base.offset + offset, [list(d) for d in dims])


def fexp(ap2d, ncols, reps, col0=0):
    return bass.AP(ap2d.tensor, ap2d.offset + col0,
                   [list(ap2d.ap[0]), [1, ncols], [0, reps]])


def frep(ap2d, ncols, reps, col0=0):
    return bass.AP(ap2d.tensor, ap2d.offset + col0,
                   [list(ap2d.ap[0]), [0, reps], [1, ncols]])


def build_program(cfg):
    C = cfg
    HID, I, Bsz = C["HID"], C["I"], C["B"]
    HD, HPC, DPC, NDPC = C["HD"], C["HPC"], C["DPC"], C["NDPC"]
    TPC, NTOK, NDT = C["TPC"], C["NTOK"], C["NDT"]
    ABLK, NAB, CSS, NSUB = C["ABLK"], C["NAB"], C["CSS"], C["NSUB"]
    CA1, NCONVT = C["CA1"], C["NCONVT"]
    DSH, NGT, NGRP, QG = C["DSH"], C["NGT"], C["NGRP"], C["QG"]
    IPAD, ISH, NIT_SH, NI = C["IPAD"], C["ISH"], C["NIT_SH"], C["NI"]
    NST, KC, DIN, NTT = C["NST"], C["KC"], C["DIN"], C["NTT"]
    IP8, HP8 = I // 8, HID // 8
    AH = ABLK + KC - 1
    NKD = DIN // 128

    nc = bass.Bass("TRN2", target_bir_lowering=False)

    ei = lambda n, sh, dt=F32: nc.dram_tensor(n, sh, dt, kind="ExternalInput").ap()
    x_in = ei("x", [TPC, HID])
    wa1_in = ei("wa1", [HID, CA1])
    wa2_in = ei("wa2", [HID, DPC])
    convw_in = ei("convw", [128, NCONVT * KC])
    convb_in = ei("convb", [128, NCONVT])
    dtb_in = ei("dtb", [HPC, 1])
    alog_in = ei("alog", [HPC, 1])
    dvec_in = ei("dvec", [HPC, 1])
    normw_in = ei("normw", [128, NDPC])
    woutt_in = ei("woutt", [DPC, HID])
    gq_in = ei("gq", [DSH, IP8], I32)
    gs_in = ei("gs", [NGRP, I])
    gz_in = ei("gz", [NGRP, IP8], I32)
    uq_in = ei("uq", [DSH, IP8], I32)
    us_in = ei("us", [NGRP, I])
    uz_in = ei("uz", [NGRP, IP8], I32)
    dq_in = ei("dq", [ISH, HP8], I32)
    ds_in = ei("ds", [NIT_SH, HID])
    dz_in = ei("dz", [NIT_SH, HP8], I32)
    out_t = nc.dram_tensor("out", [TPC, HID], F32, kind="ExternalOutput").ap()

    di = lambda n, sh, dt: nc.dram_tensor(n, sh, dt).ap()
    xln_tok_d = di("xln_tok", [TPC, HID], BF16)
    xlnT_sl_d = di("xlnT_sl", [HID, TPC], BF16)
    xlnT_all_d = di("xlnT_all", [NCORES * HID, TPC], BF16)
    gsl_d = di("gsl", [DSH, I], BF16)
    g_all_d = nc.dram_tensor("g_all", [HID, I], BF16, addr_space="Shared").ap()
    usl_d = di("usl", [DSH, I], BF16)
    u_all_d = nc.dram_tensor("u_all", [HID, I], BF16, addr_space="Shared").ap()
    wdsl_d = di("wdsl", [ISH, HID], BF16)
    wd_all_d = nc.dram_tensor("wd_all", [IPAD, HID], BF16, addr_space="Shared").ap()
    woT_sl_d = di("woT_sl", [DPC, HID], BF16)
    woT_all_d = nc.dram_tensor("woT_all", [DIN, HID], BF16, addr_space="Shared").ap()
    zsg_d = di("zsg", [NGRP, I], F32)
    zsu_d = di("zsu", [NGRP, I], F32)
    zsd_d = di("zsd", [NIT_SH, HID], F32)
    dbounce_d = di("dbounce", [1, HPC], F32)
    acb_d = di("acb", [NAB * NSUB * HPC, CSS], F32)
    prow_d = di("prow", [NAB * NSUB, HPC], F32)
    y_raw_d = di("y_raw", [NTOK, DPC], BF16)
    y_tok_d = di("y_tok", [NTOK, DPC], BF16)

    NYCH = 4
    YCH = NTOK // NYCH
    yT_sl_ds = [di(f"yT_sl{i}", [DPC, YCH], BF16) for i in range(NYCH)]
    y_all_ds = [di(f"y_all{i}", [DIN, YCH], BF16) for i in range(NYCH)]
    y_b_d = di("y_b", [DIN, TPC], BF16)
    x_new_d = di("x_new", [TPC, HID], F32)
    xln2_tok_d = di("xln2_tok", [TPC, HID], BF16)
    h_d = di("h", [I, TPC], BF16)

    with ExitStack() as _es:
        _sems = {}
        for _sk in ("sv", "ss", "sp", "sg", "qy", "qs", "qg",
                    "qa", "qb", "qc", "qd", "qe", "qf", "cc"):
            _sems[_sk] = _es.enter_context(nc.semaphore(_sk))
        (_sv, _ss, _sp, _sg, _qy, _qs, _qg, _qa, _qb, _qc, _qd, _qe, _qf,
         _cc) = (_sems[k] for k in ("sv", "ss", "sp", "sg", "qy", "qs", "qg",
                                    "qa", "qb", "qc", "qd", "qe", "qf", "cc"))
        block = _es.enter_context(nc.Block())
        P = Prog(nc, block, dict(sv=_sv, ss=_ss, sp=_sp, sg=_sg,
                                 qy=_qy, qs=_qs, qg=_qg, cc=_cc,
                                 qa=_qa, qb=_qb, qc=_qc, qd=_qd, qe=_qe, qf=_qf))

        pers = nc.auto_arena()
        ident_bf = pers.sbuf_tensor("ident_bf", [128, 128], BF16)
        ident_f = pers.sbuf_tensor("ident_f", [128, 128], F32)
        ut_sb = pers.sbuf_tensor("ut_sb", [128, CSS], F32)
        dexp_sb = pers.sbuf_tensor("dexp_sb", [128, DPC], F32)
        convw_sb = pers.sbuf_tensor("convw_sb", [128, NCONVT * KC], F32)
        convb_sb = pers.sbuf_tensor("convb_sb", [128, NCONVT], F32)
        dtb_sb = pers.sbuf_tensor("dtb_sb", [HPC, 1], F32)
        acol_sb = pers.sbuf_tensor("acol_sb", [HPC, 1], F32)
        dvec_sb = pers.sbuf_tensor("dvec_sb", [HPC, 1], F32)
        ones_sb = pers.sbuf_tensor("ones_sb", [1, 128], F32)
        eps_sb = pers.sbuf_tensor("eps_sb", [128, 1], F32)
        drow_sb = pers.sbuf_tensor("drow_sb", [1, HPC], F32)
        pers.allocate()

        def y_setup(y):
            P.i(y.dma_start(out=convw_sb[:], in_=convw_in), "qy")
            P.i(y.dma_start(out=convb_sb[:], in_=convb_in), "qy")
            for dst, src in ((dtb_sb, dtb_in), (acol_sb, alog_in), (dvec_sb, dvec_in)):
                P.i(y.dma_start(out=dst[:], in_=src), "qy")
        P.sec("y", y_setup)
        ld_small = P.tok("qy")

        def v_setup(v):
            P.w(v, "v", [ld_small])
            P.i(v.memset(ident_bf[:], 1.0), "sv")
            P.i(v.memset(ident_f[:], 1.0), "sv")
            P.i(v.memset(ut_sb[:], 0.0), "sv")
            P.i(v.memset(ones_sb[:], 1.0), "sv")
            P.i(v.memset(eps_sb[:], EPS), "sv")
        P.sec("v", v_setup)
        ident_tok0 = P.tok("sv")

        def g_setup(g):
            P.w(g, "g", [ident_tok0])
            P.i(g.affine_select(out=ut_sb[:], in_=ut_sb[:], pattern=[[1, CSS]],
                                compare_op=ALU.is_ge, fill=-1e9, base=0,
                                channel_multiplier=-1), "sg")
            P.i(g.affine_select(out=ident_bf[:], in_=ident_bf[:],
                                pattern=[[1, 128]], compare_op=ALU.is_equal,
                                fill=0.0, base=0, channel_multiplier=-1), "sg")
            P.i(g.affine_select(out=ident_f[:], in_=ident_f[:],
                                pattern=[[1, 128]], compare_op=ALU.is_equal,
                                fill=0.0, base=0, channel_multiplier=-1), "sg")
        P.sec("g", g_setup)
        ut_tok = P.tok("sg")
        ident_tok = P.tok("sg")

        def s_setup(s):
            P.w(s, "s", [ld_small])
            P.i(s.activation(out=acol_sb[:], in_=acol_sb[:], func=AF.Exp), "ss")
        P.sec("s", s_setup)

        def v_setup2(v):
            P.w(v, "v", [P.tok("ss")])
            P.i(v.tensor_scalar_mul(out=acol_sb[:], in0=acol_sb[:], scalar1=-1.0), "sv")
        P.sec("v", v_setup2)
        a_tok = P.tok("sv")

        def y_setup3(y):
            P.w(y, "y", [ld_small])
            P.i(y.dma_start(out=sap(dbounce_d, [[1, HPC], [0, 1]]), in_=dvec_sb[:]),
                "qy")
            P.w(y, "y", [P.tok("qy")])
            P.i(y.dma_start(out=drow_sb[:], in_=dbounce_d), "qy")
        P.sec("y", y_setup3)
        drow_tok = P.tok("qy")

        with ExitStack() as _esdx:
            PDX = _esdx.enter_context(nc.psum_tensor("PDX", [128, DPC], F32))
            def t_setup(t):
                P.w(t, "t", [drow_tok, ident_tok])
                step = min(512, DPC)
                rhs_full = bass.AP(drow_sb[:].tensor, drow_sb[:].offset,
                                   [[HPC, 1], [1, HPC], [0, HD]])
                for i0 in range(0, DPC, step):
                    rhs = bass.AP(rhs_full.tensor,
                                  rhs_full.offset + i0 // HD,
                                  [[HPC, 1], [1, step // HD], [0, HD]])
                    mm = t.matmul(PDX[:, i0:i0 + step], ones_sb[:], rhs,
                                  start=True, stop=True)
                P.i(mm, "sp")
            P.sec("t", t_setup)
            dx_mm = P.tok("sp")

            def v_setup3(v):
                P.w(v, "v", [dx_mm])
                P.i(v.tensor_copy(out=dexp_sb[:], in_=PDX[:]), "sv")
            P.sec("v", v_setup3)
        dexp_tok = P.tok("sv")

        ph0a = nc.auto_arena()
        p0x = ph0a.sbuf_tensor("p0x", [128, HID], F32)
        SQCH = min(1024, HID)
        p0sq = ph0a.sbuf_tensor("p0sq", [128, SQCH], F32)
        p0s1 = ph0a.sbuf_tensor("p0s1", [128, 1], F32)
        p0s2 = ph0a.sbuf_tensor("p0s2", [128, 1], F32)
        p0pt = ph0a.sbuf_tensor("p0pt", [128, 1], F32)
        p0m = ph0a.sbuf_tensor("p0m", [128, 1], F32)
        p0v = ph0a.sbuf_tensor("p0v", [128, 1], F32)
        p0r = ph0a.sbuf_tensor("p0r", [128, 1], F32)
        p0o = ph0a.sbuf_tensor("p0o", [128, HID], BF16)
        p0t = ph0a.sbuf_tensor("p0t", [128, TPC], BF16)
        ph0a.allocate()

        def ln_pass(src_dram, dst_dram, rows, x_sb, sq_sb, s1_sb, s2_sb, pt_sb,
                    m_sb, v_sb, r_sb, o_sb, pre_bar):
            written = None
            nsq = HID // SQCH
            for it in range(rows // 128):
                r0 = it * 128

                def y_fn(y):
                    P.w(y, "y", pre_bar + [P.tok("sv"), P.tok("qs")])
                    P.i(y.dma_start(out=x_sb[:], in_=src_dram[r0:r0 + 128, :]), "qy")
                P.sec("y", y_fn)
                xt = P.tok("qy")

                def v_fn(v):
                    P.w(v, "v", pre_bar + [xt])
                    P.i(v.tensor_reduce(out=s1_sb[:], in_=x_sb[:], axis=AXX,
                                        op=ALU.add), "sv")
                    for ic in range(nsq):
                        P.w(v, "v", [P.tok("sv")])
                        P.i(v.tensor_tensor(out=sq_sb[:],
                                            in0=x_sb[:, ic * SQCH:(ic + 1) * SQCH],
                                            in1=x_sb[:, ic * SQCH:(ic + 1) * SQCH],
                                            op=ALU.mult), "sv")
                        P.w(v, "v", [P.tok("sv")])
                        P.i(v.tensor_reduce(out=pt_sb[:], in_=sq_sb[:], axis=AXX,
                                            op=ALU.add), "sv")
                        P.w(v, "v", [P.tok("sv")])
                        if ic == 0:
                            P.i(v.tensor_copy(out=s2_sb[:], in_=pt_sb[:]), "sv")
                        else:
                            P.i(v.tensor_tensor(out=s2_sb[:], in0=s2_sb[:],
                                                in1=pt_sb[:], op=ALU.add), "sv")
                    P.w(v, "v", [P.tok("sv")])
                    P.i(v.tensor_scalar_mul(out=m_sb[:], in0=s1_sb[:],
                                            scalar1=1.0 / HID), "sv")
                    P.w(v, "v", [P.tok("sv")])
                    P.i(v.tensor_tensor(out=v_sb[:], in0=m_sb[:], in1=m_sb[:],
                                        op=ALU.mult), "sv")
                    P.w(v, "v", [P.tok("sv")])
                    P.i(v.scalar_tensor_tensor(out=v_sb[:], in0=s2_sb[:],
                                               scalar=1.0 / HID, in1=v_sb[:],
                                               op0=ALU.mult, op1=ALU.subtract), "sv")
                P.sec("v", v_fn)
                var_tok = P.tok("sv")

                def s_fn(s):
                    P.w(s, "s", [var_tok])
                    P.i(s.activation(out=r_sb[:], in_=v_sb[:], func=AF.Ln,
                                     bias=eps_sb[:]), "ss")
                P.sec("s", s_fn)
                ln_tok = P.tok("ss")

                def vr_fn(v):
                    P.w(v, "v", [ln_tok])
                    P.i(v.tensor_scalar_mul(out=r_sb[:], in0=r_sb[:],
                                            scalar1=-0.5), "sv")
                P.sec("v", vr_fn)
                neg_tok = P.tok("sv")

                def s1b_fn(s):
                    P.w(s, "s", [neg_tok])
                    P.i(s.activation(out=r_sb[:], in_=r_sb[:], func=AF.Exp), "ss")
                P.sec("s", s1b_fn)
                rs_tok = P.tok("ss")

                def v2_fn(v):
                    P.w(v, "v", [rs_tok, P.tok("qs")])
                    P.i(v.tensor_scalar(out=o_sb[:], in0=x_sb[:], scalar1=m_sb[:],
                                        scalar2=r_sb[:], op0=ALU.subtract,
                                        op1=ALU.mult), "sv")
                P.sec("v", v2_fn)
                o_tok = P.tok("sv")

                def s2_fn(s):
                    P.w(s, "s", [o_tok])
                    P.i(s.dma_start(out=dst_dram[r0:r0 + 128, :], in_=o_sb[:]), "qs")
                P.sec("s", s2_fn)
                written = P.tok("qs")
            return written

        xln_w = ln_pass(x_in, xln_tok_d, TPC, p0x, p0sq, p0s1, p0s2, p0pt,
                        p0m, p0v, p0r, p0o, [])

        def y_xlt(y):
            P.w(y, "y", [xln_w])
            for kt in range(NDT):
                P.w(y, "y", [P.tok("qy")])
                P.i(y.dma_start_transpose(out=p0t[:],
                                          in_=xln_tok_d[:, kt * 128:(kt + 1) * 128]),
                    "qy")
                P.w(y, "y", [P.tok("qy")])
                P.i(y.dma_start(out=xlnT_sl_d[kt * 128:(kt + 1) * 128, :],
                                in_=p0t[:]), "qy")
        P.sec("y", y_xlt)
        xlt_w = P.tok("qy")

        def g_ag1(g):
            P.w(g, "g", [xlt_w])
            P.icc(g.collective_compute(
                "AllGather", ALU.bypass, replica_groups=[list(range(NCORES))],
                ins=[xlnT_sl_d.opt()], outs=[xlnT_all_d.opt()]))
        P.sec("g", g_ag1)
        ag_xln = P.tok("cc")
        ph0a.free()

        ICH = I // 4 if I > 2048 else I
        NICH = I // ICH
        ph0b = nc.auto_arena()
        dqq = ph0b.sbuf_tensor("dqq", [128, 2 * (ICH // 8)], I32)
        dqi = ph0b.sbuf_tensor("dqi", [128, ICH], I32)
        dqn = ph0b.sbuf_tensor("dqn", [128, ICH], F32)
        dqs = ph0b.sbuf_tensor("dqs", [128, 2 * ICH], F32)
        dqz = ph0b.sbuf_tensor("dqz", [128, 2 * ICH], F32)
        dqw = ph0b.sbuf_tensor("dqw", [128, ICH], BF16)
        zqq = ph0b.sbuf_tensor("zqq", [NGRP, IP8], I32)
        zqi = ph0b.sbuf_tensor("zqi", [NGRP, ICH], I32)
        zqs = ph0b.sbuf_tensor("zqs", [NGRP, ICH], F32)
        zsc = ph0b.sbuf_tensor("zsc", [NGRP, ICH], F32)
        ph0b.allocate()
        bar0b = P.bar()

        def dequant_gu(qin, sin, zin, zs_dram, out_dram):
            def y_fn(y):
                P.w(y, "y", bar0b + [P.tok("sv")])
                P.i(y.dma_start(out=zqq[:], in_=zin), "qy")
            P.sec("y", y_fn)
            zl = P.tok("qy")

            for ic in range(NICH):
                c0 = ic * ICH

                def yz_fn(y):
                    P.w(y, "y", [P.tok("sv")])
                    P.i(y.dma_start(out=zsc[:], in_=sin[:, c0:c0 + ICH]), "qy")
                P.sec("y", yz_fn)
                zscl = P.tok("qy")

                def v_fn(v):
                    P.w(v, "v", bar0b + [zl, zscl, P.tok("qy")])
                    for sft in range(8):
                        P.i(v.tensor_scalar(out=zqi[:, AWQ_INV[sft]::8],
                                            in0=zqq[:, c0 // 8:(c0 + ICH) // 8],
                                            scalar1=4 * sft, scalar2=15,
                                            op0=ALU.logical_shift_right,
                                            op1=ALU.bitwise_and), "sv")
                    P.w(v, "v", [P.tok("sv")])
                    P.i(v.tensor_copy(out=zqs[:], in_=zqi[:]), "sv")
                    P.w(v, "v", [P.tok("sv")])
                    P.i(v.tensor_tensor(out=zqs[:], in0=zqs[:], in1=zsc[:],
                                        op=ALU.mult), "sv")
                P.sec("v", v_fn)
                zs_tok = P.tok("sv")

                def y2_fn(y):
                    P.w(y, "y", [zs_tok])
                    P.i(y.dma_start(out=zs_dram[:, c0:c0 + ICH], in_=zqs[:]), "qy")
                P.sec("y", y2_fn)
            zs_w = P.tok("qy")

            prev_ext = [None, None]
            dqi_free = None
            it_idx = 0
            for gt in range(NGT):
                for ic in range(NICH):
                    c0 = ic * ICH
                    par = it_idx % 2
                    it_idx += 1
                    qb8 = par * (ICH // 8)
                    qbc = par * ICH

                    def y3_fn(y):
                        P.w(y, ("y", "s")[par], [zs_w, prev_ext[par]])
                        P.i(y.dma_start(
                            out=dqq[:, qb8:qb8 + ICH // 8],
                            in_=qin[gt * 128:(gt + 1) * 128,
                                    c0 // 8:(c0 + ICH) // 8]),
                            ("qa", "qb")[par], ("y", "s")[par])
                        P.i(y.dma_start(out=dqs[:, qbc:qbc + ICH],
                                        in_=sap(sin, [[0, 128], [1, ICH]],
                                                gt * I + c0)),
                            ("qa", "qb")[par], ("y", "s")[par])
                        P.i(y.dma_start(out=dqz[:, qbc:qbc + ICH],
                                        in_=sap(zs_dram, [[0, 128], [1, ICH]],
                                                gt * I + c0)),
                            ("qa", "qb")[par], ("y", "s")[par])
                    P.sec(("y", "s")[par], y3_fn, dq=(("qa", "qb")[par],))
                    ql = P.tok(("qa", "qb")[par])

                    def v3_fn(v):
                        P.w(v, "v", [ql, dqi_free])
                        for sft in range(8):
                            P.i(v.tensor_scalar(out=dqi[:, AWQ_INV[sft]::8],
                                                in0=dqq[:, qb8:qb8 + ICH // 8],
                                                scalar1=4 * sft, scalar2=15,
                                                op0=ALU.logical_shift_right,
                                                op1=ALU.bitwise_and), "sv")
                    P.sec("v", v3_fn)
                    ext_v = P.tok("sv")
                    ext_g = ext_v

                    dqi_free_local = [None]

                    def v4_fn(v):
                        P.w(v, "v", [ext_v, ext_g])
                        P.i(v.tensor_copy(out=dqn[:], in_=dqi[:]), "sv")
                        dqi_free_local[0] = P.tok("sv")
                        P.w(v, "v", [P.tok("sv")])
                        P.i(v.tensor_tensor(out=dqn[:], in0=dqn[:],
                                            in1=dqs[:, qbc:qbc + ICH],
                                            op=ALU.mult), "sv")
                        P.w(v, "v", [P.tok("sv"), P.tok("qs")])
                        P.i(v.tensor_tensor(out=dqw[:], in0=dqn[:],
                                            in1=dqz[:, qbc:qbc + ICH],
                                            op=ALU.subtract), "sv")
                    P.sec("v", v4_fn)
                    dqi_free = dqi_free_local[0]
                    wt = P.tok("sv")
                    prev_ext[par] = wt

                    def s3_fn(s):
                        P.w(s, "s", [wt])
                        P.i(s.dma_start(
                            out=out_dram[gt * 128:(gt + 1) * 128, c0:c0 + ICH],
                            in_=dqw[:]), "qs")
                    P.sec("s", s3_fn)

        dequant_gu(gq_in, gs_in, gz_in, zsg_d, gsl_d)
        dequant_gu(uq_in, us_in, uz_in, zsu_d, usl_d)
        ph0b.free()

        ph0c = nc.auto_arena()
        dnq = ph0c.sbuf_tensor("dnq", [128, HP8], I32)
        dni = ph0c.sbuf_tensor("dni", [128, HID], I32)
        dnn = ph0c.sbuf_tensor("dnn", [128, HID], F32)
        dns = ph0c.sbuf_tensor("dns", [128, HID], F32)
        dnz = ph0c.sbuf_tensor("dnz", [128, HID], F32)
        dnw = ph0c.sbuf_tensor("dnw", [128, HID], BF16)
        znq = ph0c.sbuf_tensor("znq", [NIT_SH, HP8], I32)
        zni = ph0c.sbuf_tensor("zni", [NIT_SH, HID], I32)
        zns = ph0c.sbuf_tensor("zns", [NIT_SH, HID], F32)
        znc = ph0c.sbuf_tensor("znc", [NIT_SH, HID], F32)
        ph0c.allocate()
        bar0c = P.bar()

        def y_dn(y):
            P.w(y, "y", bar0c + [P.tok("sv")])
            P.i(y.dma_start(out=znq[:], in_=dz_in), "qy")
            P.i(y.dma_start(out=znc[:], in_=ds_in), "qy")
        P.sec("y", y_dn)
        znl = P.tok("qy")

        def v_dn(v):
            P.w(v, "v", bar0c + [znl])
            for sft in range(8):
                P.i(v.tensor_scalar(out=zni[:, AWQ_INV[sft]::8], in0=znq[:],
                                    scalar1=4 * sft, scalar2=15,
                                    op0=ALU.logical_shift_right,
                                    op1=ALU.bitwise_and), "sv")
            P.w(v, "v", [P.tok("sv")])
            P.i(v.tensor_copy(out=zns[:], in_=zni[:]), "sv")
            P.w(v, "v", [P.tok("sv")])
            P.i(v.tensor_tensor(out=zns[:], in0=zns[:], in1=znc[:], op=ALU.mult),
                "sv")
        P.sec("v", v_dn)
        zns_tok = P.tok("sv")

        def y_dn2(y):
            P.w(y, "y", [zns_tok])
            P.i(y.dma_start(out=zsd_d, in_=zns[:]), "qy")
        P.sec("y", y_dn2)
        zsd_w = P.tok("qy")

        for itl in range(NIT_SH):
            def y_dn3(y):
                P.w(y, "y", [zsd_w, P.tok("sv")])
                P.i(y.dma_start(out=dnq[:], in_=dq_in[itl * 128:(itl + 1) * 128, :]),
                    "qy")
                P.i(y.dma_start(out=dns[:],
                                in_=sap(ds_in, [[0, 128], [1, HID]], itl * HID)),
                    "qy")
                P.i(y.dma_start(out=dnz[:],
                                in_=sap(zsd_d, [[0, 128], [1, HID]], itl * HID)),
                    "qy")
            P.sec("y", y_dn3)
            ql = P.tok("qy")

            def v_dn3(v):
                P.w(v, "v", [ql, P.tok("qs")])
                for sft in range(8):
                    P.i(v.tensor_scalar(out=dni[:, AWQ_INV[sft]::8], in0=dnq[:],
                                        scalar1=4 * sft, scalar2=15,
                                        op0=ALU.logical_shift_right,
                                        op1=ALU.bitwise_and), "sv")
                P.w(v, "v", [P.tok("sv")])
                P.i(v.tensor_copy(out=dnn[:], in_=dni[:]), "sv")
                P.w(v, "v", [P.tok("sv")])
                P.i(v.tensor_tensor(out=dnn[:], in0=dnn[:], in1=dns[:],
                                    op=ALU.mult), "sv")
                P.w(v, "v", [P.tok("sv")])
                P.i(v.tensor_tensor(out=dnw[:], in0=dnn[:], in1=dnz[:],
                                    op=ALU.subtract), "sv")
            P.sec("v", v_dn3)
            wt = P.tok("sv")

            def s_dn3(s):
                P.w(s, "s", [wt])
                P.i(s.dma_start(out=wdsl_d[itl * 128:(itl + 1) * 128, :],
                                in_=dnw[:]), "qs")
            P.sec("s", s_dn3)
        ph0c.free()

        ph0d = nc.auto_arena()
        wof = ph0d.sbuf_tensor("wof", [128, HID], F32)
        wob = ph0d.sbuf_tensor("wob", [128, HID], BF16)
        wnw = ph0d.sbuf_tensor("wnw", [128, NDPC], F32)
        ph0d.allocate()
        bar0d = P.bar()

        def y_nw(y):
            P.w(y, "y", bar0d)
            P.i(y.dma_start(out=wnw[:], in_=normw_in), "qy")
        P.sec("y", y_nw)
        nw_tok = P.tok("qy")

        for dt_ in range(NDPC):
            def y_wo(y):
                P.w(y, "y", [nw_tok, P.tok("sv")])
                P.i(y.dma_start(out=wof[:],
                                in_=woutt_in[dt_ * 128:(dt_ + 1) * 128, :]), "qy")
            P.sec("y", y_wo)
            wl = P.tok("qy")

            def v_wo(v):
                P.w(v, "v", bar0d + [wl, P.tok("qs")])
                P.i(v.tensor_scalar(out=wob[:], in0=wof[:],
                                    scalar1=wnw[:, dt_:dt_ + 1],
                                    scalar2=None, op0=ALU.mult), "sv")
            P.sec("v", v_wo)
            wc = P.tok("sv")

            def s_wo(s):
                P.w(s, "s", [wc])
                P.i(s.dma_start(out=woT_sl_d[dt_ * 128:(dt_ + 1) * 128, :],
                                in_=wob[:]), "qs")
            P.sec("s", s_wo)
        wo_w = P.tok("qs")

        def g_agw(g):
            P.w(g, "g", [wo_w, P.tok("qs"), P.tok("qy")])
            P.icc(g.collective_compute(
                "AllGather", ALU.bypass, replica_groups=[list(range(NCORES))],
                ins=[gsl_d.opt()], outs=[g_all_d.opt()]))
            P.icc(g.collective_compute(
                "AllGather", ALU.bypass, replica_groups=[list(range(NCORES))],
                ins=[usl_d.opt()], outs=[u_all_d.opt()]))
            P.icc(g.collective_compute(
                "AllGather", ALU.bypass, replica_groups=[list(range(NCORES))],
                ins=[wdsl_d.opt()], outs=[wd_all_d.opt()]))
            P.icc(g.collective_compute(
                "AllGather", ALU.bypass, replica_groups=[list(range(NCORES))],
                ins=[woT_sl_d.opt()], outs=[woT_all_d.opt()]))
        P.sec("g", g_agw)
        ag_w = P.tok("cc")
        ph0d.free()

        a1 = nc.auto_arena()
        wa1_sb = a1.sbuf_tensor("wa1_sb", [128, NDT * CA1], BF16)
        wstg = a1.sbuf_tensor("wstg", [128, CA1], F32)
        xfm_sb = a1.sbuf_tensor("xfm_sb", [128, NDT * ABLK], BF16)
        xbc_sb = a1.sbuf_tensor("xbc_sb", [128, NCONVT * AH], BF16)
        xba_sb = a1.sbuf_tensor("xba_sb", [128, NCONVT * ABLK], BF16)
        cacc_sb = a1.sbuf_tensor("cacc_sb", [128, 2 * ABLK], F32)
        dtp_sb = a1.sbuf_tensor("dtp_sb", [HPC, ABLK], F32)
        dts_sb = a1.sbuf_tensor("dts_sb", [HPC, ABLK], F32)
        dta_sb = a1.sbuf_tensor("dta_sb", [HPC, ABLK], F32)
        stk_sb = a1.sbuf_tensor("stk_sb", [128, ABLK], F32)
        dtf_sb = a1.sbuf_tensor("dtf_sb", [HPC, ABLK], F32)
        actok_sb = a1.sbuf_tensor("actok_sb", [128, 128], F32)
        srep_sb = a1.sbuf_tensor("srep_sb", [128, HPC * CSS], F32)
        prep_sb = a1.sbuf_tensor("prep_sb", [128, HPC], F32)
        acm_sb = srep_sb
        lall_sb = a1.sbuf_tensor("lall_sb", [128, HPC * CSS], BF16)
        gm_sb = a1.sbuf_tensor("gm_sb", [128, CSS], BF16)
        gml_sb = a1.sbuf_tensor("gml_sb", [128, HPC * CSS], BF16)
        xtok_sb = a1.sbuf_tensor("xtok_sb", [128, DPC], BF16)
        btok_sb = a1.sbuf_tensor("btok_sb", [128, NST], BF16)
        xs_sb = a1.sbuf_tensor("xs_sb", [128, DPC], BF16)
        xsd_sb = a1.sbuf_tensor("xsd_sb", [128, DPC], BF16)
        z_sb = a1.sbuf_tensor("z_sb", [128, DPC], F32)
        zbf_sb = a1.sbuf_tensor("zbf_sb", [128, DPC], BF16)
        zt_sb = a1.sbuf_tensor("zt_sb", [128, DPC], F32)
        yo_sb = a1.sbuf_tensor("yo_sb", [128, DPC], F32)
        dtm_sb = a1.sbuf_tensor("dtm_sb", [128, DPC], F32)
        yc_sb = yo_sb
        yout_sb = a1.sbuf_tensor("yout_sb", [128, DPC], BF16)
        a1.allocate()

        bar0 = P.bar()
        with ExitStack() as _esa1:
            PA0 = _esa1.enter_context(nc.psum_tensor("PA0", [128, 512], F32))
            PA1 = _esa1.enter_context(nc.psum_tensor("PA1", [128, 512], F32))
            PTRB = _esa1.enter_context(nc.psum_tensor("PTRB", [128, DPC], BF16))
            PYD = _esa1.enter_context(nc.psum_tensor("PYD", [128, DPC], F32))
            PSC = _esa1.enter_context(nc.psum_tensor("PSC", [128, DPC], F32))
            def v_stk0(v):
                P.w(v, "v", bar0)
                P.i(v.memset(stk_sb[:], 0.0), "sv")
            P.sec("v", v_stk0)

            wconv_tok = None
            for kt in range(NDT):
                def y_w(y):
                    P.w(y, "y", bar0 + [wconv_tok])
                    P.i(y.dma_start(out=wstg[:],
                                    in_=wa1_in[kt * 128:(kt + 1) * 128, :]), "qy")
                P.sec("y", y_w)
                wl = P.tok("qy")

                def v_w(v):
                    P.w(v, "v", bar0 + [wl])
                    P.i(v.tensor_copy(out=wa1_sb[:, kt * CA1:(kt + 1) * CA1],
                                      in_=wstg[:]), "sv")
                P.sec("v", v_w)
                wconv_tok = P.tok("sv")
            wa1_tok = P.tok("sv")

            ftiles = [(d * 128, 128, "x", d) for d in range(NDPC)]
            ftiles += [(DPC, NST, "B", NDPC), (DPC + NST, NST, "C", NDPC + 1),
                       (DPC + 2 * NST, HPC, "dt", -1)]

            st = dict(xfm=None, xfm_pe=None,
                      pa_mm=[None, None], pa_ev=[None, None],
                      z=None, zuser=None, halo=None, conv=None,
                      xbc_user=None, dtuser=None, stk_pe=None, xba_pe=None,
                      yd_ev=None, sc_ev=None)
            yraw_w = None

            for b in range(NAB):
                T0 = b * ABLK
                cb_src = T0 // TPC
                tloc = T0 - cb_src * TPC
                batch_start = (b % C["AB_PER_BATCH"]) == 0

                def y_xfm(y):
                    P.w(y, "y", [ag_xln, st["xfm_pe"]])
                    src = sap(xlnT_all_d, [[TPC, 128], [128 * TPC, NDT], [1, ABLK]],
                              cb_src * HID * TPC + tloc)
                    P.i(y.dma_start(
                        out=xfm_sb[:].rearrange("p (k t) -> p k t", k=NDT),
                        in_=src), "qy")
                P.sec("y", y_xfm)
                st["xfm"] = P.tok("qy")

                if batch_start:
                    def v_rst(v):
                        P.w(v, "v", [st["zuser"], st["z"], st["xbc_user"]])
                        P.i(v.memset(z_sb[:], 0.0), "sv")
                        halo = sap(xbc_sb, [[NCONVT * AH, 128], [AH, NCONVT],
                                            [1, KC - 1]])
                        P.i(v.memset(halo, 0.0), "sv")
                    P.sec("v", v_rst)
                    st["z"] = P.tok("sv")
                    st["halo"] = P.tok("sv")

                evac_toks = []
                for fi, (off, fw, kind, dpos) in enumerate(ftiles):
                    pp = (PA0, PA1)[fi % 2]
                    pi = fi % 2

                    def t_ip(t):
                        P.w(t, "t", [wa1_tok, st["xfm"], st["pa_ev"][pi]])
                        for kt in range(NDT):
                            mm = t.matmul(
                                pp[0:fw, 0:ABLK],
                                wa1_sb[:, kt * CA1 + off:kt * CA1 + off + fw],
                                xfm_sb[:, kt * ABLK:(kt + 1) * ABLK],
                                start=(kt == 0), stop=(kt == NDT - 1))
                        P.i(mm, "sp")
                    P.sec("t", t_ip)
                    st["pa_mm"][pi] = P.tok("sp")
                    if fi == len(ftiles) - 1:
                        st["xfm_pe"] = P.tok("sp")

                    def v_ev(v):
                        P.w(v, "v", [st["pa_mm"][pi], st["halo"], st["conv"]])
                        if kind == "dt":
                            P.i(v.tensor_copy(out=dtp_sb[:], in_=pp[0:HPC, 0:ABLK]),
                                "sv")
                        else:
                            dst = xbc_sb[:, dpos * AH + KC - 1:dpos * AH + AH]
                            P.i(v.tensor_copy(out=dst, in_=pp[:, 0:ABLK]), "sv")
                    P.sec("v", v_ev)
                    st["pa_ev"][pi] = P.tok("sv")
                    evac_toks.append(P.tok("sv"))
                xbc_tok = P.tok("sv")

                conv_sil = []
                for ct in range(NCONVT):
                    acc = cacc_sb[:, (ct % 2) * ABLK:(ct % 2 + 1) * ABLK]

                    def v_cv(v):
                        deps = [evac_toks[ct], st["halo"], st["xba_pe"]]
                        if ct >= 2:
                            deps.append(conv_sil[ct - 2])
                        P.w(v, "v", deps)
                        base = ct * AH
                        P.i(v.tensor_scalar(out=acc, in0=xbc_sb[:, base:base + ABLK],
                                            scalar1=convw_sb[:, ct * KC:ct * KC + 1],
                                            scalar2=convb_sb[:, ct:ct + 1],
                                            op0=ALU.mult, op1=ALU.add), "sv")
                        for tau in range(1, KC):
                            P.w(v, "v", [P.tok("sv")])
                            P.i(v.scalar_tensor_tensor(
                                out=acc, in0=xbc_sb[:, base + tau:base + tau + ABLK],
                                scalar=convw_sb[:, ct * KC + tau:ct * KC + tau + 1],
                                in1=acc, op0=ALU.mult, op1=ALU.add), "sv")
                    P.sec("v", v_cv)
                    cv_tok = P.tok("sv")

                    def s_cv(s):
                        P.w(s, "s", [cv_tok])
                        P.i(s.activation(out=xba_sb[:, ct * ABLK:(ct + 1) * ABLK],
                                         in_=acc, func=AF.Silu), "ss")
                    P.sec("s", s_cv)
                    conv_sil.append(P.tok("ss"))
                conv_tok = P.tok("ss")
                st["conv"] = conv_tok

                def v_hl(v):
                    P.w(v, "v", [conv_tok, xbc_tok])
                    src = sap(xbc_sb, [[NCONVT * AH, 128], [AH, NCONVT],
                                       [1, KC - 1]], ABLK)
                    dst = sap(xbc_sb, [[NCONVT * AH, 128], [AH, NCONVT],
                                       [1, KC - 1]])
                    P.i(v.tensor_copy(out=dst, in_=src), "sv")
                P.sec("v", v_hl)
                st["halo"] = P.tok("sv")

                def v_dt(v):
                    P.w(v, "v", [st["pa_ev"][(len(ftiles) - 1) % 2], a_tok,
                                 st["stk_pe"], st["dtuser"]])
                    P.i(v.tensor_scalar(out=dts_sb[:], in0=dtp_sb[:],
                                        scalar1=dtb_sb[:], scalar2=None,
                                        op0=ALU.add), "sv")
                P.sec("v", v_dt)
                dt1 = P.tok("sv")

                def s_dt(s):
                    P.w(s, "s", [dt1])
                    P.i(s.activation(out=dta_sb[:], in_=dts_sb[:], func=AF.Abs),
                        "ss")
                    P.w(s, "s", [P.tok("ss")])
                    P.i(s.activation(out=dta_sb[:], in_=dta_sb[:], func=AF.Exp,
                                     scale=-1.0), "ss")
                    P.w(s, "s", [P.tok("ss")])
                    P.i(s.activation(out=dta_sb[:], in_=dta_sb[:], func=AF.Ln,
                                     bias=1.0), "ss")
                P.sec("s", s_dt)
                dt2 = P.tok("ss")

                def v_dt2(v):
                    P.w(v, "v", [dt2, P.tok("qs")])
                    P.i(v.tensor_relu(out=dts_sb[:], in_=dts_sb[:]), "sv")
                    P.w(v, "v", [P.tok("sv")])
                    P.i(v.tensor_tensor(out=dtf_sb[:],
                                        in0=dts_sb[:], in1=dta_sb[:], op=ALU.add),
                        "sv")
                    P.w(v, "v", [P.tok("sv")])
                    P.i(v.tensor_scalar(out=dta_sb[:],
                                        in0=dtf_sb[:],
                                        scalar1=acol_sb[:], scalar2=None,
                                        op0=ALU.mult), "sv")
                    for s_ in range(NSUB):
                        P.w(v, "v", [P.tok("sv")])
                        sl = slice(s_ * CSS, (s_ + 1) * CSS)
                        P.i(v.tensor_tensor_scan(out=stk_sb[0:HPC, sl],
                                                 data0=dta_sb[:, sl],
                                                 data1=dta_sb[:, sl], initial=0.0,
                                                 op0=ALU.add, op1=ALU.bypass), "sv")
                P.sec("v", v_dt2)
                ac_tok = P.tok("sv")
                st["dtuser"] = ac_tok

                def s_dt2(s):
                    P.w(s, "s", [ac_tok])
                    for s_ in range(NSUB):
                        sl = slice(s_ * CSS, (s_ + 1) * CSS)
                        last = s_ * CSS + CSS - 1
                        P.i(s.activation(out=stk_sb[32:32 + HPC, sl],
                                         in_=stk_sb[0:HPC, sl], func=AF.Exp,
                                         scale=-1.0,
                                         bias=stk_sb[0:HPC, last:last + 1]), "ss")
                        P.i(s.activation(out=stk_sb[64:64 + HPC, sl],
                                         in_=stk_sb[0:HPC, sl], func=AF.Exp), "ss")
                P.sec("s", s_dt2)
                stk_tok = P.tok("ss")

                def s_dt3(s):
                    P.w(s, "s", [stk_tok])
                    for s_ in range(NSUB):
                        sl = slice(s_ * CSS, (s_ + 1) * CSS)
                        r0 = (b * NSUB + s_) * HPC
                        P.i(s.dma_start(out=acb_d[r0:r0 + HPC, :],
                                        in_=stk_sb[0:HPC, sl]), "qs")
                        last = s_ * CSS + CSS - 1
                        P.i(s.dma_start(
                            out=sap(prow_d, [[1, HPC], [0, 1]],
                                    (b * NSUB + s_) * HPC),
                            in_=stk_sb[64:64 + HPC, last:last + 1]), "qs")
                P.sec("s", s_dt3)
                acb_w = P.tok("qs")

                for s_ in range(NSUB):
                    sub = b * NSUB + s_
                    sc = s_ * CSS

                    def y_rep(y):
                        P.w(y, "y", [acb_w, P.tok("sv"), P.tok("ss")])
                        P.i(y.dma_start(out=srep_sb[:],
                                        in_=sap(acb_d, [[0, 128], [1, HPC * CSS]],
                                                sub * HPC * CSS)), "qy")
                        P.i(y.dma_start(out=prep_sb[:],
                                        in_=sap(prow_d, [[0, 128], [1, HPC]],
                                                sub * HPC)), "qy")
                    P.sec("y", y_rep)
                    rep_tok = P.tok("qy")

                    def t_tr(t):
                        P.w(t, "t", [conv_tok, P.tok("sv")])
                        for d in range(NDPC):
                            tr = t.transpose(PTRB[:, d * 128:(d + 1) * 128],
                                             xba_sb[:, d * ABLK + sc:d * ABLK + sc + CSS],
                                             ident_bf[:])
                        P.i(tr, "sp")
                    P.sec("t", t_tr)
                    xtr = P.tok("sp")

                    def v_tr(v):
                        P.w(v, "v", [xtr])
                        P.i(v.tensor_copy(out=xtok_sb[:], in_=PTRB[:]), "sv")
                    P.sec("v", v_tr)
                    xtok_tok = P.tok("sv")

                    def t_tr2(t):
                        P.w(t, "t", [xtok_tok, stk_tok])
                        tr = t.transpose(PTRB[:, 0:NST],
                                         xba_sb[:, NDPC * ABLK + sc:NDPC * ABLK + sc + CSS],
                                         ident_bf[:])
                        P.i(tr, "sp")
                        P.w(t, "t", [P.tok("sv")])
                        t.transpose(PSC[:, 0:64 + HPC],
                                    stk_sb[0:64 + HPC, sc:sc + CSS],
                                    ident_f[0:64 + HPC, 0:64 + HPC])
                        tr2 = t.transpose(PSC[:, 96:96 + HPC],
                                          dtf_sb[:, sc:sc + CSS],
                                          ident_f[0:HPC, 0:HPC])
                        P.i(tr2, "sp")
                    P.sec("t", t_tr2)
                    btr = ("sp", P.cnt["sp"] - 1)
                    str_ = P.tok("sp")
                    st["stk_pe"] = str_

                    def v_tr2(v):
                        P.w(v, "v", [btr])
                        P.i(v.tensor_copy(out=btok_sb[:], in_=PTRB[:, 0:NST]), "sv")
                        P.w(v, "v", [str_])
                        P.i(v.tensor_copy(out=actok_sb[:, 0:64 + HPC],
                                          in_=PSC[:, 0:64 + HPC]), "sv")
                        P.i(v.tensor_copy(out=actok_sb[:, 96:96 + HPC],
                                          in_=PSC[:, 96:96 + HPC]), "sv")
                    P.sec("v", v_tr2)
                    btok_tok = ("sv", P.cnt["sv"] - 1)
                    actok_tok = P.tok("sv")

                    def v_fo(v):
                        P.w(v, "v", [actok_tok, st["z"], dexp_tok, rep_tok, ut_tok])
                        P.i(v.tensor_tensor(out=xs_sb[:], in0=xtok_sb[:],
                                            in1=fexp(actok_sb[:], HPC, HD, 96),
                                            op=ALU.mult), "sv")
                        P.w(v, "v", [P.tok("sv")])
                        P.i(v.tensor_tensor(out=xsd_sb[:], in0=xs_sb[:],
                                            in1=fexp(actok_sb[:], HPC, HD, 32),
                                            op=ALU.mult), "sv")
                        P.i(v.tensor_copy(out=zbf_sb[:], in_=z_sb[:]), "sv")
                        P.i(v.tensor_tensor(out=acm_sb[:], in0=srep_sb[:],
                                            in1=fexp(actok_sb[:], HPC, CSS, 0),
                                            op=ALU.subtract), "sv")
                        P.w(v, "v", [P.tok("sv")])
                        P.i(v.tensor_tensor(out=acm_sb[:], in0=acm_sb[:],
                                            in1=frep(ut_sb[:], CSS, HPC),
                                            op=ALU.add), "sv")
                    P.sec("v", v_fo)
                    folds_tok = P.tok("sv")
                    zbf_tok = ("sv", P.cnt["sv"] - 2)

                    def s_l(s):
                        P.w(s, "s", [folds_tok])
                        P.i(s.activation(out=lall_sb[:], in_=acm_sb[:],
                                         func=AF.Exp), "ss")
                    P.sec("s", s_l)
                    lall_tok = P.tok("ss")

                    def t_gm(t):
                        P.w(t, "t", [conv_tok, st["pa_ev"][0]])
                        bfm = xba_sb[:, NDPC * ABLK + sc:NDPC * ABLK + sc + CSS]
                        cfm = xba_sb[:, (NDPC + 1) * ABLK + sc:(NDPC + 1) * ABLK + sc + CSS]
                        P.i(t.matmul(PA0[:, 0:CSS], bfm, cfm, start=True,
                                     stop=True), "sp")
                    P.sec("t", t_gm)
                    gm_mm = P.tok("sp")

                    def v_gm(v):
                        P.w(v, "v", [gm_mm])
                        P.i(v.tensor_copy(out=gm_sb[:], in_=PA0[:, 0:CSS]), "sv")
                        P.w(v, "v", [P.tok("sv"), lall_tok])
                        for h in range(HPC):
                            P.i(v.tensor_tensor(out=gml_sb[:, h * CSS:(h + 1) * CSS],
                                                in0=gm_sb[:],
                                                in1=lall_sb[:, h * CSS:(h + 1) * CSS],
                                                op=ALU.mult), "sv")
                    P.sec("v", v_gm)
                    gm_ev = ("sv", P.cnt["sv"] - HPC)
                    gml_tok = P.tok("sv")
                    st["pa_ev"][0] = gm_ev

                    def t_y(t):
                        P.w(t, "t", [gml_tok, btok_tok, zbf_tok, folds_tok,
                                     st["pa_ev"][0], st["pa_ev"][1],
                                     st["yd_ev"], st["sc_ev"]])
                        for h in range(HPC):
                            mm = t.matmul(PYD[:, h * HD:(h + 1) * HD],
                                          gml_sb[:, h * CSS:(h + 1) * CSS],
                                          xs_sb[:, h * HD:(h + 1) * HD],
                                          start=((h * HD) % 512 == 0),
                                          stop=(((h + 1) * HD) % 512 == 0
                                                or h == HPC - 1))
                        P.i(mm, "sp")
                        cfm = xba_sb[:, (NDPC + 1) * ABLK + sc:(NDPC + 1) * ABLK + sc + CSS]
                        stepd = min(512, DPC)
                        for i0 in range(0, DPC, stepd):
                            pp = (PA1, PA0)[(i0 // stepd) % 2]
                            P.i(t.matmul(pp[:, 0:stepd], cfm,
                                         zbf_sb[:, i0:i0 + stepd],
                                         start=True, stop=True), "sp")
                        for i0 in range(0, DPC, stepd):
                            mm = t.matmul(PSC[:, i0:i0 + stepd], btok_sb[:],
                                          xsd_sb[:, i0:i0 + stepd],
                                          start=True, stop=True)
                        P.i(mm, "sp")
                    P.sec("t", t_y)
                    sc_mm = P.tok("sp")
                    st["xba_pe"] = sc_mm
                    nyo = max(1, DPC // 512)
                    yo_mms = [("sp", P.cnt["sp"] - 1 - nyo + 1 + i)
                              for i in range(nyo)]
                    yd_mm = ("sp", P.cnt["sp"] - 1 - nyo)

                    def v_y(v):
                        stepd = min(512, DPC)
                        P.w(v, "v", yo_mms + [yd_mm, P.tok("qs")])
                        for i in range(nyo):
                            pp = (PA1, PA0)[i % 2]
                            hph = stepd // HD
                            P.i(v.tensor_tensor(
                                out=yo_sb[:, i * stepd:(i + 1) * stepd],
                                in0=pp[:, 0:stepd],
                                in1=fexp(actok_sb[:], hph, HD,
                                         64 + i * hph),
                                op=ALU.mult), "sv")
                        P.i(v.tensor_tensor(out=dtm_sb[:], in0=xtok_sb[:],
                                            in1=dexp_sb[:], op=ALU.mult), "sv")
                        P.w(v, "v", [P.tok("sv")])
                        P.i(v.tensor_tensor(out=yc_sb[:], in0=PYD[:], in1=yo_sb[:],
                                            op=ALU.add), "sv")
                        P.w(v, "v", [P.tok("sv")])
                        P.i(v.tensor_tensor(out=yout_sb[:], in0=yc_sb[:],
                                            in1=dtm_sb[:], op=ALU.add), "sv")
                    P.sec("v", v_y)
                    y_tok = P.tok("sv")
                    st["pa_ev"][0] = y_tok
                    st["pa_ev"][1] = y_tok
                    st["yd_ev"] = y_tok

                    def s_y(s):
                        P.w(s, "s", [y_tok])
                        P.i(s.dma_start(out=y_raw_d[T0 + sc:T0 + sc + CSS, :],
                                        in_=yout_sb[:]), "qs")
                    P.sec("s", s_y)
                    yraw_w = P.tok("qs")

                    def v_z(v):
                        P.w(v, "v", [sc_mm, rep_tok, zbf_tok])
                        P.i(v.tensor_tensor(out=zt_sb[:], in0=z_sb[:],
                                            in1=fexp(prep_sb[:], HPC, HD),
                                            op=ALU.mult), "sv")
                        P.w(v, "v", [P.tok("sv")])
                        P.i(v.tensor_tensor(out=z_sb[:], in0=zt_sb[:], in1=PSC[:],
                                            op=ALU.add), "sv")
                    P.sec("v", v_z)
                    st["z"] = P.tok("sv")
                    st["zuser"] = P.tok("sv")
                    st["sc_ev"] = P.tok("sv")
                st["xbc_user"] = P.tok("sp")
            a1_done = yraw_w
        a1.free()

        a2 = nc.auto_arena()
        wa2_sb = a2.sbuf_tensor("wa2_sb", [128, NDT * DPC], BF16)
        wstg2 = a2.sbuf_tensor("wstg2", [128, DPC], F32)
        xfm2_sb = a2.sbuf_tensor("xfm2_sb", [128, 2 * NDT * ABLK], BF16)
        gact_sb = a2.sbuf_tensor("gact_sb", [128, NDPC * ABLK], BF16)
        gtok_sb = a2.sbuf_tensor("gtok_sb", [128, DPC], BF16)
        yr_sb = a2.sbuf_tensor("yr_sb", [128, DPC], BF16)
        yg_sb = a2.sbuf_tensor("yg_sb", [128, DPC], F32)
        ysq_sb = a2.sbuf_tensor("ysq_sb", [128, DPC], F32)
        ssum_sb = a2.sbuf_tensor("ssum_sb", [128, 1], F32)
        rstd_sb = a2.sbuf_tensor("rstd_sb", [128, 1], F32)
        yn_sb = a2.sbuf_tensor("yn_sb", [128, DPC], BF16)
        ytr_sb = a2.sbuf_tensor("ytr_sb", [128, NTOK], BF16)
        a2.allocate()

        bar1 = P.bar()
        with ExitStack() as _esa2:
            PB0_ = _esa2.enter_context(nc.psum_tensor("PB0_", [128, 512], F32))
            PB1_ = _esa2.enter_context(nc.psum_tensor("PB1_", [128, 512], F32))
            PTR2 = _esa2.enter_context(nc.psum_tensor("PTR2", [128, DPC], BF16))
            wconv2 = None
            for kt in range(NDT):
                def y_w2(y):
                    P.w(y, "y", bar1 + [wconv2])
                    P.i(y.dma_start(out=wstg2[:],
                                    in_=wa2_in[kt * 128:(kt + 1) * 128, :]), "qy")
                P.sec("y", y_w2)
                wl = P.tok("qy")

                def v_w2(v):
                    P.w(v, "v", bar1 + [wl])
                    P.i(v.tensor_copy(out=wa2_sb[:, kt * DPC:(kt + 1) * DPC],
                                      in_=wstg2[:]), "sv")
                P.sec("v", v_w2)
                wconv2 = P.tok("sv")
            wa2_tok = P.tok("sv")

            st2 = dict(xfm=None, xfm_pe=[None, None], ev=[None, None], tr_ev=None,
                       yn_ev=None)
            ytok_w = None
            ytok_chunk_w = {}
            ag_y_ch = []
            for b in range(NAB):
                T0 = b * ABLK
                cb_src = T0 // TPC
                tloc = T0 - cb_src * TPC
                bpar = b % 2
                xbase2 = bpar * NDT * ABLK

                def y_xfm2(y):
                    P.w(y, "y", [ag_xln, st2["xfm_pe"][bpar]])
                    src = sap(xlnT_all_d, [[TPC, 128], [128 * TPC, NDT], [1, ABLK]],
                              cb_src * HID * TPC + tloc)
                    P.i(y.dma_start(
                        out=xfm2_sb[:, xbase2:xbase2 + NDT * ABLK].rearrange(
                            "p (k t) -> p k t", k=NDT),
                        in_=src), ("qa", "qb")[bpar], "y")
                P.sec("y", y_xfm2, dq=(("qa", "qb")[bpar],))
                st2["xfm"] = P.tok(("qa", "qb")[bpar])

                for d in range(NDPC):
                    pp = (PB0_, PB1_)[d % 2]
                    pi = d % 2

                    def t_g(t):
                        P.w(t, "t", [wa2_tok, st2["xfm"], st2["ev"][pi]])
                        for kt in range(NDT):
                            mm = t.matmul(
                                pp[:],
                                wa2_sb[:, kt * DPC + d * 128:kt * DPC + (d + 1) * 128],
                                xfm2_sb[:, xbase2 + kt * ABLK:
                                        xbase2 + (kt + 1) * ABLK],
                                start=(kt == 0), stop=(kt == NDT - 1))
                        P.i(mm, "sp")
                    P.sec("t", t_g)
                    g_mm = P.tok("sp")
                    if d == NDPC - 1:
                        st2["xfm_pe"][bpar] = P.tok("sp")

                    def s_g(s):
                        P.w(s, "s", [g_mm])
                        P.i(s.activation(out=gact_sb[:, d * ABLK:(d + 1) * ABLK],
                                         in_=pp[:], func=AF.Silu), "ss")
                    P.sec("s", s_g)
                    st2["ev"][pi] = P.tok("ss")
                gact_tok = P.tok("ss")

                for s_ in range(NSUB):
                    sc = s_ * CSS
                    T0s = T0 + sc

                    def y_yr(y):
                        P.w(y, "y", [a1_done, P.tok("sv")])
                        P.i(y.dma_start(out=yr_sb[:], in_=y_raw_d[T0s:T0s + CSS, :]),
                            "qy")
                    P.sec("y", y_yr)
                    yr_tok = P.tok("qy")

                    def t_gt(t):
                        P.w(t, "t", [gact_tok, st2["tr_ev"]])
                        for d in range(NDPC):
                            tr = t.transpose(PTR2[:, d * 128:(d + 1) * 128],
                                             gact_sb[:, d * ABLK + sc:d * ABLK + sc + CSS],
                                             ident_bf[:])
                        P.i(tr, "sp")
                    P.sec("t", t_gt)
                    gt_tok = P.tok("sp")

                    def v_g2(v):
                        P.w(v, "v", [gt_tok])
                        P.i(v.tensor_copy(out=gtok_sb[:], in_=PTR2[:]), "sv")
                        P.w(v, "v", [P.tok("sv"), yr_tok])
                        P.i(v.tensor_tensor(out=yg_sb[:], in0=yr_sb[:],
                                            in1=gtok_sb[:], op=ALU.mult), "sv")
                        P.w(v, "v", [P.tok("sv")])
                        P.i(v.tensor_tensor(out=ysq_sb[:], in0=yg_sb[:],
                                            in1=yg_sb[:], op=ALU.mult), "sv")
                        P.w(v, "v", [P.tok("sv")])
                        P.i(v.tensor_reduce(out=ssum_sb[:], in_=ysq_sb[:], axis=AXX,
                                            op=ALU.add), "sv")
                    P.sec("v", v_g2)
                    st2["tr_ev"] = ("sv", P.cnt["sv"] - 3)
                    ssm_tok = P.tok("sv")

                    def v_r0(v):
                        P.w(v, "v", [ssm_tok])
                        P.i(v.tensor_scalar_mul(out=rstd_sb[:], in0=ssum_sb[:],
                                                scalar1=1.0 / DPC), "sv")
                    P.sec("v", v_r0)
                    sc_tok2 = P.tok("sv")

                    def s_r(s):
                        P.w(s, "s", [sc_tok2])
                        P.i(s.activation(out=rstd_sb[:], in_=rstd_sb[:], func=AF.Ln,
                                         bias=eps_sb[:]), "ss")
                    P.sec("s", s_r)
                    ln_tok2 = P.tok("ss")

                    def v_r1(v):
                        P.w(v, "v", [ln_tok2])
                        P.i(v.tensor_scalar_mul(out=rstd_sb[:], in0=rstd_sb[:],
                                                scalar1=-0.5), "sv")
                    P.sec("v", v_r1)
                    ng_tok2 = P.tok("sv")

                    def s_r2(s):
                        P.w(s, "s", [ng_tok2])
                        P.i(s.activation(out=rstd_sb[:], in_=rstd_sb[:],
                                         func=AF.Exp), "ss")
                    P.sec("s", s_r2)
                    rs_tok = P.tok("ss")

                    def v_n(v):
                        P.w(v, "v", [rs_tok, st2["yn_ev"]])
                        P.i(v.tensor_scalar(out=yn_sb[:], in0=yg_sb[:],
                                            scalar1=rstd_sb[:], scalar2=None,
                                            op0=ALU.mult), "sv")
                    P.sec("v", v_n)
                    yn_tok = P.tok("sv")

                    def s_yw(s):
                        P.w(s, "s", [yn_tok])
                        P.i(s.dma_start(out=y_tok_d[T0s:T0s + CSS, :], in_=yn_sb[:]),
                            "qs")
                    P.sec("s", s_yw)
                    st2["yn_ev"] = P.tok("qs")
                if (T0 + ABLK) % YCH == 0:
                    ych = (T0 + ABLK) // YCH - 1
                    c0 = ych * YCH
                    yq_tok = P.tok("qs")

                    def y_ytr(y):
                        P.w(y, "y", [yq_tok])
                        for dt_ in range(NDPC):
                            P.w(y, "y", [P.tok("qy")])
                            P.i(y.dma_start_transpose(
                                out=ytr_sb[:, 0:YCH],
                                in_=y_tok_d[c0:c0 + YCH,
                                            dt_ * 128:(dt_ + 1) * 128]), "qy")
                            P.w(y, "y", [P.tok("qy")])
                            P.i(y.dma_start(
                                out=yT_sl_ds[ych][dt_ * 128:(dt_ + 1) * 128, :],
                                in_=ytr_sb[:, 0:YCH]), "qy")
                    P.sec("y", y_ytr)
                    yq_w = P.tok("qy")

                    def g_agy(g):
                        P.w(g, "g", [yq_w])
                        P.icc(g.collective_compute(
                            "AllGather", ALU.bypass,
                            replica_groups=[list(range(NCORES))],
                            ins=[yT_sl_ds[ych].opt()],
                            outs=[y_all_ds[ych].opt()]))
                    P.sec("g", g_agy)
                    ag_y_ch.append(P.tok("cc"))
            ytok_w = P.tok("qs")

            def y_slice(y):
                pid = y.partition_id()
                pin = (pid * TPC) % YCH
                qof = (pid * TPC) // YCH
                for q_ in range(NYCH):
                    P.w(y, "y", [ag_y_ch[q_]])
                    d_ = qof - q_
                    off = pin - d_ * d_ * (YCH * DIN)
                    src2 = bass.AP(y_all_ds[q_].tensor, off,
                                   [[YCH, 128], [128 * YCH, NKD], [1, TPC]])
                    dst2 = sap(y_b_d, [[TPC, 128], [128 * TPC, NKD], [1, TPC]])
                    P.i(y.dma_start(out=dst2, in_=src2,
                                    bounds_check="skip_entire_dma"), "qy")
            P.sec("y", y_slice)
            ag_y = P.tok("qy")
        a2.free()

        SUPB1 = 256
        NSUPB1 = TPC // SUPB1
        QW = min(1024, HID)
        NQ = HID // QW
        NTTB1 = SUPB1 // 128
        b1 = nc.auto_arena()
        wo_sb = b1.sbuf_tensor("wo_sb", [128, NKD * QW], BF16)
        ysup_sb = b1.sbuf_tensor("ysup_sb", [128, 2 * NKD * SUPB1], BF16)
        xres_sb = b1.sbuf_tensor("xres_sb", [128, QW], F32)
        xn_sb = b1.sbuf_tensor("xn_sb", [128, QW], F32)
        b1.allocate()

        bar2 = P.bar()
        with ExitStack() as _esb1:
            PBQ = _esb1.enter_context(
                nc.psum_tensor("PBQ", [128, 2 * NTTB1 * QW], F32))
            xn_w = None
            y_pe = [None, None]
            wo_pe = None
            ev_b1 = {}
            for q in range(NQ):
                def s_wo2(s):
                    P.w(s, "s", bar2 + [ag_w, wo_pe])
                    src2 = sap(woT_all_d, [[HID, 128], [128 * HID, NKD], [1, QW]],
                               q * QW)
                    P.i(s.dma_start(
                        out=wo_sb[:].rearrange("p (k w) -> p k w", k=NKD),
                        in_=src2), "qc", "s")
                P.sec("s", s_wo2, dq=("qc",))
                wo_ld = P.tok("qc")

                for sup in range(NSUPB1):
                    t0 = sup * SUPB1
                    par = sup % 2
                    ybase = par * NKD * SUPB1

                    def y_ys(y):
                        P.w(y, "y", bar2 + [ag_y, y_pe[par]])
                        src2 = sap(y_b_d, [[TPC, 128], [128 * TPC, NKD], [1, SUPB1]],
                                   t0)
                        P.i(y.dma_start(
                            out=ysup_sb[:, ybase:ybase + NKD * SUPB1].rearrange(
                                "p (k t) -> p k t", k=NKD),
                            in_=src2), ("qa", "qb")[par], "y")
                    P.sec("y", y_ys, dq=(("qa", "qb")[par],))
                    y_ld = P.tok(("qa", "qb")[par])

                    for tt in range(NTTB1):
                        pbase = (par * NTTB1 + tt) * QW

                        def t_op(t):
                            P.w(t, "t", [y_ld, wo_ld, ev_b1.get((par, tt))])
                            for k in range(NKD):
                                for j in range(QW // 512):
                                    mm = t.matmul(
                                        PBQ[:, pbase + j * 512:pbase + (j + 1) * 512],
                                        ysup_sb[:, ybase + k * SUPB1 + tt * 128:
                                                ybase + k * SUPB1 + (tt + 1) * 128],
                                        wo_sb[:, k * QW + j * 512:k * QW + (j + 1) * 512],
                                        start=(k == 0), stop=(k == NKD - 1))
                            P.i(mm, "sp")
                        P.sec("t", t_op)
                        op_mm = P.tok("sp")
                        if tt == NTTB1 - 1:
                            y_pe[par] = op_mm
                            if sup == NSUPB1 - 1:
                                wo_pe = op_mm

                        def y_xr(y):
                            P.w(y, "y", [P.tok("sv")])
                            P.i(y.dma_start(
                                out=xres_sb[:],
                                in_=x_in[t0 + tt * 128:t0 + (tt + 1) * 128,
                                         q * QW:(q + 1) * QW]), "qy")
                        P.sec("y", y_xr)
                        xr_tok = P.tok("qy")

                        def v_xn(v):
                            P.w(v, "v", [op_mm, xr_tok, P.tok("qf")])
                            P.i(v.tensor_tensor(out=xn_sb[:],
                                                in0=PBQ[:, pbase:pbase + QW],
                                                in1=xres_sb[:], op=ALU.add), "sv")
                        P.sec("v", v_xn)
                        ev_b1[(par, tt)] = P.tok("sv")
                        xn_tok = P.tok("sv")

                        def s_xn(s):
                            P.w(s, "s", [xn_tok])
                            P.i(s.dma_start(
                                out=x_new_d[t0 + tt * 128:t0 + (tt + 1) * 128,
                                            q * QW:(q + 1) * QW],
                                in_=xn_sb[:]), "qf", "s")
                        P.sec("s", s_xn, dq=("qf",))
                        xn_w = P.tok("qf")
        b1.free()
        xnew_done = xn_w

        b2 = nc.auto_arena()
        x2_sb = b2.sbuf_tensor("x2_sb", [128, HID], F32)
        sq2_sb = b2.sbuf_tensor("sq2_sb", [128, SQCH], F32)
        s21_sb = b2.sbuf_tensor("s21_sb", [128, 1], F32)
        s22_sb = b2.sbuf_tensor("s22_sb", [128, 1], F32)
        pt2_sb = b2.sbuf_tensor("pt2_sb", [128, 1], F32)
        m2_sb = b2.sbuf_tensor("m2_sb", [128, 1], F32)
        v2_sb = b2.sbuf_tensor("v2_sb", [128, 1], F32)
        r2_sb = b2.sbuf_tensor("r2_sb", [128, 1], F32)
        o2_sb = b2.sbuf_tensor("o2_sb", [128, HID], BF16)
        xl2T_sb = b2.sbuf_tensor("xl2T_sb", [128, NDT * TPC], BF16)
        b2.allocate()

        bar3 = P.bar()
        xln2_w = ln_pass(x_new_d, xln2_tok_d, TPC, x2_sb, sq2_sb, s21_sb, s22_sb,
                         pt2_sb, m2_sb, v2_sb, r2_sb, o2_sb, bar3 + [xnew_done])

        def y_l2t(y):
            P.w(y, "y", [xln2_w])
            for kt in range(NDT):
                P.i(y.dma_start_transpose(
                    out=xl2T_sb[:, kt * TPC:(kt + 1) * TPC],
                    in_=xln2_tok_d[:, kt * 128:(kt + 1) * 128]), "qy")
        P.sec("y", y_l2t)
        xl2T_tok = P.tok("qy")

        b3 = nc.auto_arena()
        wg_sb = b3.sbuf_tensor("wg_sb", [128, 2 * NDT * 128], BF16)
        wu_sb = b3.sbuf_tensor("wu_sb", [128, 2 * NDT * 128], BF16)
        hs_sb = b3.sbuf_tensor("hs_sb", [128, TPC], BF16)
        ho_sb = b3.sbuf_tensor("ho_sb", [128, TPC], BF16)
        b3.allocate()

        NTW = TPC // 512
        bar4 = P.bar()
        with ExitStack() as _esb3:
            PG_ = _esb3.enter_context(nc.psum_tensor("PG_", [128, 4 * 512], F32))
            PU_ = _esb3.enter_context(nc.psum_tensor("PU_", [128, 4 * 512], F32))
            w_pe = [None, None]
            ev3 = None
            h_w = None
            for it in range(NI):
                par = it % 2
                wslice = slice(par * NDT * 128, (par + 1) * NDT * 128)

                def s_wl(s):
                    P.w(s, "s", bar4 + [ag_w, w_pe[par]])
                    src = sap(g_all_d, [[I, 128], [128 * I, NDT], [1, 128]],
                              it * 128)
                    P.i(s.dma_start(
                        out=wg_sb[:, wslice].rearrange("p (k i) -> p k i", k=NDT),
                        in_=src), ("qa", "qb")[par], "s")
                P.sec("s", s_wl, dq=(("qa", "qb")[par],))

                def y_wl(y):
                    P.w(y, "y", bar4 + [ag_w, w_pe[par]])
                    src = sap(u_all_d, [[I, 128], [128 * I, NDT], [1, 128]],
                              it * 128)
                    P.i(y.dma_start(
                        out=wu_sb[:, wslice].rearrange("p (k i) -> p k i", k=NDT),
                        in_=src), ("qd", "qe")[par], "y")
                P.sec("y", y_wl, dq=(("qd", "qe")[par],))
                w_ld = P.tok(("qd", "qe")[par])
                w_ld_g = P.tok(("qa", "qb")[par])

                def t_f(t):
                    P.w(t, "t", [w_ld, w_ld_g, xl2T_tok, ev3])
                    for kt in range(NDT):
                        for tw in range(NTW):
                            mm = t.matmul(
                                PG_[:, tw * 512:(tw + 1) * 512],
                                wg_sb[:, par * NDT * 128 + kt * 128:
                                      par * NDT * 128 + (kt + 1) * 128],
                                xl2T_sb[:, kt * TPC + tw * 512:kt * TPC + (tw + 1) * 512],
                                start=(kt == 0), stop=(kt == NDT - 1))
                    P.i(mm, "sp")
                    for kt in range(NDT):
                        for tw in range(NTW):
                            mm = t.matmul(
                                PU_[:, tw * 512:(tw + 1) * 512],
                                wu_sb[:, par * NDT * 128 + kt * 128:
                                      par * NDT * 128 + (kt + 1) * 128],
                                xl2T_sb[:, kt * TPC + tw * 512:kt * TPC + (tw + 1) * 512],
                                start=(kt == 0), stop=(kt == NDT - 1))
                    P.i(mm, "sp")
                P.sec("t", t_f)
                u_mm = P.tok("sp")
                g_mm = ("sp", P.cnt["sp"] - 1)
                w_pe[par] = u_mm

                def s_sil(s):
                    P.w(s, "s", [g_mm, P.tok("sv")])
                    for tw in range(NTW):
                        P.i(s.activation(out=hs_sb[:, tw * 512:(tw + 1) * 512],
                                         in_=PG_[:, tw * 512:(tw + 1) * 512],
                                         func=AF.Silu), "ss")
                P.sec("s", s_sil)
                sil_tok = P.tok("ss")

                def v_h(v):
                    P.w(v, "v", [sil_tok, u_mm, P.tok("qg")])
                    for tw in range(NTW):
                        P.i(v.tensor_tensor(out=ho_sb[:, tw * 512:(tw + 1) * 512],
                                            in0=PU_[:, tw * 512:(tw + 1) * 512],
                                            in1=hs_sb[:, tw * 512:(tw + 1) * 512],
                                            op=ALU.mult), "sv")
                P.sec("v", v_h)
                ev3 = P.tok("sv")
                h_tok = P.tok("sv")

                def g_h(g):
                    P.w(g, "g", [h_tok])
                    P.i(g.dma_start(out=h_d[it * 128:(it + 1) * 128, :],
                                    in_=ho_sb[:]), "qg")
                P.sec("g", g_h)
                h_w = P.tok("qg")
            h_written = h_w
        b3.free()
        b2.free()

        b4 = nc.auto_arena()
        hsup_sb = b4.sbuf_tensor("hsup_sb", [128, NI * 512], BF16)
        KCH = -(-NI // 4)
        wd_sb = b4.sbuf_tensor("wd_sb", [128, 2 * KCH * 1024], BF16)
        xr4_sb = b4.sbuf_tensor("xr4_sb", [128, 1024], F32)
        o4_sb = b4.sbuf_tensor("o4_sb", [128, 1024], F32)
        b4.allocate()

        SUP4 = 512
        NSUP4 = TPC // SUP4
        SPW = min(1024, HID)
        NSP = HID // SPW
        bar5 = P.bar()
        with ExitStack() as _esb4:
            PD_ = _esb4.enter_context(nc.psum_tensor("PD_", [128, 8 * 512], F32))
            h_pe = None
            out_w = None
            ev4 = None
            wd_war = {0: None, 1: None}
            for sup in range(NSUP4):
                t0 = sup * SUP4

                def y_h(y):
                    P.w(y, "y", bar5 + [h_written, h_pe])
                    src = sap(h_d, [[TPC, 128], [128 * TPC, NI], [1, 512]], t0)
                    P.i(y.dma_start(
                        out=hsup_sb[:].rearrange("p (k t) -> p k t", k=NI),
                        in_=src), "qy")
                P.sec("y", y_h)
                h_ld = P.tok("qy")

                for sp in range(NSP):
                    wd_toks = []
                    nchunk = -(-NI // KCH)
                    for ch in range(nchunk):
                        k0 = ch * KCH
                        k1 = min(NI, k0 + KCH)
                        par = ch % 2

                        def s_wd(eng):
                            prev = (wd_toks[ch - 2][1] if ch >= 2
                                    else wd_war[par])
                            ekk = ("s", "y")[par]
                            P.w(eng, ekk, bar5 + [ag_w, prev])
                            src = sap(wd_all_d,
                                      [[HID, 128], [128 * HID, k1 - k0], [1, SPW]],
                                      k0 * 128 * HID + sp * SPW)
                            P.i(eng.dma_start(
                                out=wd_sb[:, par * KCH * 1024:
                                          par * KCH * 1024 + (k1 - k0) * SPW]
                                .rearrange("p (k w) -> p k w", k=k1 - k0),
                                in_=src), ("qa", "qd")[par], ("s", "y")[par])
                        P.sec(("s", "y")[par], s_wd, dq=(("qa", "qd")[par],))
                        wd_ld = P.tok(("qa", "qd")[par])

                        def t_dn(t):
                            P.w(t, "t", [wd_ld, h_ld, ev4])
                            for ki in range(k0, k1):
                                for tt in range(4):
                                    for j in range(SPW // 512):
                                        mm = t.matmul(
                                            PD_[:, (tt * (SPW // 512) + j) * 512:
                                                (tt * (SPW // 512) + j + 1) * 512],
                                            hsup_sb[:, ki * 512 + tt * 128:
                                                    ki * 512 + (tt + 1) * 128],
                                            wd_sb[:, par * KCH * 1024 + (ki - k0) * SPW
                                                  + j * 512:
                                                  par * KCH * 1024 + (ki - k0) * SPW
                                                  + (j + 1) * 512],
                                            start=(ki == 0), stop=(ki == NI - 1))
                            P.i(mm, "sp")
                        P.sec("t", t_dn)
                        wd_toks.append((wd_ld, P.tok("sp")))
                        wd_war[par] = P.tok("sp")
                    dn_mm = P.tok("sp")
                    h_pe = dn_mm

                    for tt in range(4):
                        def y_x4(y):
                            P.w(y, "y", [P.tok("sv")])
                            P.i(y.dma_start(
                                out=xr4_sb[:, 0:SPW],
                                in_=x_new_d[t0 + tt * 128:t0 + (tt + 1) * 128,
                                            sp * SPW:(sp + 1) * SPW]), "qy")
                        P.sec("y", y_x4)
                        xr_tok = P.tok("qy")

                        def v_o(v):
                            P.w(v, "v", [dn_mm, xr_tok, P.tok("qs")])
                            for j in range(SPW // 512):
                                P.i(v.tensor_tensor(
                                    out=o4_sb[:, j * 512:(j + 1) * 512],
                                    in0=PD_[:, (tt * (SPW // 512) + j) * 512:
                                            (tt * (SPW // 512) + j + 1) * 512],
                                    in1=xr4_sb[:, j * 512:(j + 1) * 512],
                                    op=ALU.add), "sv")
                        P.sec("v", v_o)
                        o_tok = P.tok("sv")
                        ev4 = o_tok

                        def s_o(s):
                            P.w(s, "s", [o_tok])
                            P.i(s.dma_start(
                                out=out_t[t0 + tt * 128:t0 + (tt + 1) * 128,
                                          sp * SPW:(sp + 1) * SPW],
                                in_=o4_sb[:, 0:SPW]), "qs")
                        P.sec("s", s_o)
                        out_w = P.tok("qs")
        b4.free()

        def s_end(s):
            P.w(s, "s", [out_w, P.tok("qg"), P.tok("qy")])
        P.sec("s", s_end)

        def y_end(y):
            P.w(y, "y", [out_w])
        P.sec("y", y_end)

    return nc



def shard_inputs(cfg, inputs):
    c = cfg
    HID, I, Bsz, L = c["HID"], c["I"], c["B"], c["L"]
    DIN, NST, HPC, DPC = c["DIN"], c["NST"], c["HPC"], c["DPC"]
    TPC, QG, G = c["TPC"], c["QG"], c["G"]
    DSH, NGRP = c["DSH"], c["NGRP"]
    ISH, NIT_SH, IPAD = c["ISH"], c["NIT_SH"], c["IPAD"]
    HP8 = HID // 8

    x = np.ascontiguousarray(
        np.asarray(inputs["L0_attn_hidden_states"]).reshape(Bsz * L, HID))
    wi = np.asarray(inputs["L0_attn_in_proj_w"])
    convw = np.asarray(inputs["L0_attn_conv_w"])[:, 0, :]
    convb = np.asarray(inputs["L0_attn_conv_b"])
    dtb = np.asarray(inputs["L0_attn_dt_bias"])
    alog = np.asarray(inputs["L0_attn_A_log"])
    dvec = np.asarray(inputs["L0_attn_D"])
    normw = np.asarray(inputs["L0_attn_norm_w"])
    wo = np.asarray(inputs["L0_attn_out_proj_w"])
    gq = np.asarray(inputs["L0_ffn_gate_qweight"])
    gs = np.asarray(inputs["L0_ffn_gate_scales"])
    gz = np.asarray(inputs["L0_ffn_gate_qzeros"])
    uq = np.asarray(inputs["L0_ffn_up_qweight"])
    us = np.asarray(inputs["L0_ffn_up_scales"])
    uz = np.asarray(inputs["L0_ffn_up_qzeros"])
    dq = np.asarray(inputs["L0_ffn_down_qweight"])
    ds = np.asarray(inputs["L0_ffn_down_scales"])
    dz = np.asarray(inputs["L0_ffn_down_qzeros"])

    dqp = np.zeros((IPAD, HP8), np.int32); dqp[:I] = dq
    dsp = np.zeros((IPAD // QG, HID), np.float32); dsp[:ds.shape[0]] = ds
    dzp = np.zeros((IPAD // QG, HP8), np.int32); dzp[:dz.shape[0]] = dz

    maps = []
    for cid in range(NCORES):
        xr = slice(DIN + DPC * cid, DIN + DPC * (cid + 1))
        br = slice(2 * DIN + NST * cid, 2 * DIN + NST * (cid + 1))
        cr = slice(2 * DIN + G * NST + NST * cid, 2 * DIN + G * NST + NST * (cid + 1))
        dtr = slice(DIN + c["CONVD"] + HPC * cid, DIN + c["CONVD"] + HPC * (cid + 1))
        gr = slice(DPC * cid, DPC * (cid + 1))
        m = dict(
            x=np.ascontiguousarray(x[TPC * cid:TPC * (cid + 1)]),
            wa1=np.ascontiguousarray(
                np.concatenate([wi[xr], wi[br], wi[cr], wi[dtr]], axis=0).T),
            wa2=np.ascontiguousarray(wi[gr].T),
            convw=np.ascontiguousarray(np.concatenate(
                [convw[DPC * cid:DPC * (cid + 1)],
                 convw[DIN + NST * cid:DIN + NST * (cid + 1)],
                 convw[DIN + G * NST + NST * cid:DIN + G * NST + NST * (cid + 1)]],
                axis=0).reshape(-1, 128, convw.shape[1]).transpose(1, 0, 2)
                .reshape(128, -1)),
            convb=np.ascontiguousarray(np.concatenate(
                [convb[DPC * cid:DPC * (cid + 1)],
                 convb[DIN + NST * cid:DIN + NST * (cid + 1)],
                 convb[DIN + G * NST + NST * cid:DIN + G * NST + NST * (cid + 1)]],
                axis=0).reshape(-1, 128).T),
            dtb=np.ascontiguousarray(dtb[HPC * cid:HPC * (cid + 1)])[:, None],
            alog=np.ascontiguousarray(alog[HPC * cid:HPC * (cid + 1)])[:, None],
            dvec=np.ascontiguousarray(dvec[HPC * cid:HPC * (cid + 1)])[:, None],
            normw=np.ascontiguousarray(
                normw[DPC * cid:DPC * (cid + 1)].reshape(-1, 128).T),
            woutt=np.ascontiguousarray(wo[:, DPC * cid:DPC * (cid + 1)].T),
            gq=np.ascontiguousarray(gq[DSH * cid:DSH * (cid + 1)]),
            gs=np.ascontiguousarray(gs[NGRP * cid:NGRP * (cid + 1)]),
            gz=np.ascontiguousarray(gz[NGRP * cid:NGRP * (cid + 1)]),
            uq=np.ascontiguousarray(uq[DSH * cid:DSH * (cid + 1)]),
            us=np.ascontiguousarray(us[NGRP * cid:NGRP * (cid + 1)]),
            uz=np.ascontiguousarray(uz[NGRP * cid:NGRP * (cid + 1)]),
            dq=np.ascontiguousarray(dqp[ISH * cid:ISH * (cid + 1)]),
            ds=np.ascontiguousarray(dsp[NIT_SH * cid:NIT_SH * (cid + 1)]),
            dz=np.ascontiguousarray(dzp[NIT_SH * cid:NIT_SH * (cid + 1)]),
        )
        maps.append(m)
    return maps


_CACHE = {}


def kernel(**inputs):
    cfg = REAL_CFG
    maps = shard_inputs(cfg, inputs)
    if "nc" not in _CACHE:
        _CACHE["nc"] = build_program(cfg)
    res = run_bass_kernel_spmd(_CACHE["nc"], maps, core_ids=list(range(NCORES)))
    out = np.concatenate([np.asarray(r["out"]) for r in res.results], axis=0)
    return np.ascontiguousarray(
        out.reshape(cfg["B"], cfg["L"], cfg["HID"]).astype(np.float32))
